# version 13
# baseline (speedup 1.0000x reference)
import contextlib
import numpy as np
import ml_dtypes
import concourse.bass as bass
import concourse.mybir as mybir
from concourse.bass_utils import run_bass_kernel_spmd

F32 = mybir.dt.float32
F16 = mybir.dt.float16
BF16 = mybir.dt.bfloat16
U8 = mybir.dt.uint8
AF = mybir.ActivationFunctionType
ALU = mybir.AluOpType
AX = mybir.AxisListType
BF = ml_dtypes.bfloat16

P = 128
NCORE = 8
T = 8192
D = 4096
STR = 512
TOKC = 2 * STR
EXT = (8, 16)
LN_EPS = 1e-5


def _dram(nc, name, shape, dtype, kind="ExternalInput"):
    cache = getattr(nc, '_dram_cache', None)
    if cache is None:
        cache = {}
        nc._dram_cache = cache
    if name not in cache:
        cache[name] = nc.dram_tensor(name, list(shape), dtype, kind=kind).ap()
    return cache[name]


class KB:
    def __init__(self, nc):
        self.nc = nc
        self.stack = contextlib.ExitStack()
        self.sems = {}
        self.cnt = {}
        self.sbufs = {}
        self.psums = {}
        self.waited = {}
        self.ln_prev = None
        self.tr_evac = [None, None]
        self.state = {}

    def sem(self, name):
        if name not in self.sems:
            self.sems[name] = self.stack.enter_context(self.nc.semaphore(name))
            self.cnt[name] = 0
        return self.sems[name]

    OFFSETS = {}

    def sb(self, name, shape, dtype):
        if name not in self.sbufs:
            if self.OFFSETS:
                assert name in self.OFFSETS, "no offset for %s" % name
            if name in self.OFFSETS:
                self.sbufs[name] = self.nc.alloc_sbuf_tensor_at(
                    "sb_" + name, list(shape), dtype, offset=self.OFFSETS[name])
            else:
                self.sbufs[name] = self.stack.enter_context(
                    self.nc.sbuf_tensor("sb_" + name, list(shape), dtype))
        return self.sbufs[name]

    def block_barrier(self):
        nc = self.nc
        toks = [self.tok('cV'), self.tok('cA'), self.tok('sT'), self.tok('cG')]
        toks += [self.tok(s) for s in list(self.cnt) if s.startswith('d')]
        for ename, eng in (('T', nc.tensor), ('V', nc.vector), ('A', nc.scalar),
                           ('S', nc.sync), ('G', nc.gpsimd)):
            self.wait(ename, eng, *toks)

    def ps(self, name, shape, dtype=F32):
        if name not in self.psums:
            self.psums[name] = self.stack.enter_context(
                self.nc.psum_tensor("ps_" + name, list(shape), dtype))
        return self.psums[name]

    def dma(self, q, out, in_, sem_name, **kw):
        self.sem(sem_name)
        q.dma_start(out=out, in_=in_, **kw).then_inc(self.sems[sem_name], 16)
        self.cnt[sem_name] += 16
        return (sem_name, self.cnt[sem_name])

    def inc(self, instr, sem_name, amt=1):
        self.sem(sem_name)
        instr.then_inc(self.sems[sem_name], amt)
        self.cnt[sem_name] += amt
        return (sem_name, self.cnt[sem_name])

    def tok(self, sem_name):
        self.sem(sem_name)
        return (sem_name, self.cnt[sem_name])

    def wait(self, ename, eng, *toks):
        wmap = self.waited.setdefault(ename, {})
        best = {}
        for t in toks:
            if t is None:
                continue
            s, v = t
            if v > best.get(s, 0):
                best[s] = v
        for s, v in best.items():
            if v > 0 and wmap.get(s, 0) < v:
                eng.wait_ge(self.sems[s], v)
                wmap[s] = v

    def vpre(self, *toks):
        self.wait('V', self.nc.vector, self.tok('cV'), *toks)

    def vpost(self, ins):
        return self.inc(ins, 'cV')

    def apre(self, *toks):
        self.wait('A', self.nc.scalar, self.tok('cA'), *toks)

    def apost(self, ins):
        return self.inc(ins, 'cA')

    def gpre(self, *toks):
        self.wait('G', self.nc.gpsimd, self.tok('cG'), *toks)

    def gpost(self, ins):
        return self.inc(ins, 'cG')

    def close(self):
        self.stack.close()


def emit_ln_stats(kb, xsrc, x_tok):
    nc = kb.nc
    v, a = nc.vector, nc.scalar
    bst = kb.sb("ln_bst", (P, 48), F32)
    mv = kb.sb("ln_mv", (P, 8), F32)
    kb.vpre(x_tok, kb.ln_prev)
    for i in range(8):
        kb.vpost(v.bn_stats(bst[:, 6 * i:6 * (i + 1)],
                            xsrc[:, 512 * i:512 * (i + 1)]))
        kb.vpre()
    tva = kb.vpost(v.bn_aggr(mv[:, 0:2], bst[:, 0:48]))
    kb.vpre()
    tv = kb.vpost(v.tensor_scalar(mv[:, 2:3], mv[:, 1:2], LN_EPS, None, ALU.add))
    kb.apre(tv)
    ta = kb.apost(a.activation(mv[:, 3:4], mv[:, 2:3], AF.Sqrt))
    kb.vpre(ta)
    kb.vpost(v.reciprocal(mv[:, 4:5], mv[:, 3:4]))
    kb.vpre()
    kb.vpost(v.scalar_tensor_tensor(mv[:, 5:6], mv[:, 0:1], -1.0,
                                    mv[:, 4:5], ALU.mult, ALU.mult))
    kb.vpre()
    tv2 = kb.vpost(v.tensor_tensor(mv[:, 6:7], mv[:, 4:5], mv[:, 5:6], ALU.mult))
    return tv2, mv


def emit_ln_tile(kb, xsrc, xn_out, x_tok, war_toks=()):
    nc = kb.nc
    tv2, mv = emit_ln_stats(kb, xsrc, x_tok)
    kb.apre(tv2, *[t for t in war_toks if t is not None])
    t = kb.apost(nc.scalar.activation(xn_out, xsrc, AF.Identity,
                                      scale=mv[:, 4:5], bias=mv[:, 5:6]))
    kb.ln_prev = t
    return t


def emit_transpose_tile(kb, src, dest_r, tcol0, identb, evac_wait=None):
    nc = kb.nc
    pT = [kb.ps("pT0", (P, 1024), BF16), kb.ps("pT1", (P, 1024), BF16)]
    tt = None
    for fg in range(4):
        pt = pT[fg % 2]
        kb.wait('T', nc.tensor, kb.tr_evac[fg % 2])
        for f8 in range(8):
            f = 8 * fg + f8
            ins = nc.tensor.transpose(pt[:, 128 * f8:128 * (f8 + 1)],
                                      src[:, 128 * f:128 * (f + 1)], identb)
        tt = kb.inc(ins, 'sT')
        kb.vpre(tt, evac_wait)
        ins = nc.vector.tensor_copy(
            dest_r[:, 8 * fg:8 * (fg + 1), tcol0:tcol0 + 128],
            pt[:, 0:1024].rearrange("p (f t) -> p f t", f=8))
        kb.tr_evac[fg % 2] = kb.vpost(ins)
    return tt


DEBUG_L0A = False


def emit_l0_attn(kb, aps):
    nc = kb.nc
    v, a = nc.vector, nc.scalar
    xfull, xown = aps['xfull'], aps['xown']
    maskstr, res_out = aps['maskstr'], aps['res_out']
    kqv_d = aps['kqv_scratch']

    identb = kb.sb("identb", (P, P), BF16)
    kb.dma(nc.sync, identb[:, :], aps['identb'][:, :], 'dSc')

    acc = kb.sb("acc", (P, 4 * D), F32)
    bv = acc
    sc1 = acc[0:4, 3 * 1024:4 * 1024]
    scal_a = acc[0:4, 4 * 1024:4 * 1024 + 1]
    scal_b = acc[0:4, 4 * 1024 + 8:4 * 1024 + 9]
    for i, key in enumerate(('bvq', 'bvk', 'bvv')):
        kb.dma(nc.sync, bv[0:4, 1024 * i:1024 * (i + 1)], aps[key][:, :], 'dSb')
    t_ball = kb.tok('dSb')
    for i in range(3):
        sl = bv[0:4, 1024 * i:1024 * (i + 1)]
        kb.vpre(t_ball)
        kb.vpost(v.reduce_sum(scal_a, sl, axis=AX.X, apply_absolute_value=True))
        kb.vpre()
        tv = kb.vpost(v.tensor_scalar(scal_b, scal_a, 1.0 / 1024, None, ALU.mult))
        kb.apre(tv)
        ta = kb.apost(a.activation(sc1, sl, AF.Sign))
        kb.vpre(ta)
        kb.vpost(v.tensor_scalar(sc1, sc1, scal_b, None, ALU.mult))
        kb.vpre()
        kb.vpost(v.tensor_single_scalar(sl, sc1, 0.0, ALU.is_equal))
        kb.vpre()
        kb.vpost(v.scalar_tensor_tensor(sl, sl, scal_b, sc1, ALU.mult, ALU.add))
        kb.vpre()
    tv = kb.vpost(v.tensor_tensor(sc1, bv[0:4, 0:1024], bv[0:4, 1024:2048],
                                  ALU.mult))
    kb.wait('G', nc.gpsimd, tv)
    t1 = kb.dma(nc.gpsimd, kqv_d[0:1, :].rearrange("o (a b) -> (o a) b", a=4),
                sc1, 'dGk')
    t2 = kb.dma(nc.gpsimd, kqv_d[1:2, :].rearrange("o (a b) -> (o a) b", a=4),
                bv[0:4, 2048:3072], 'dGk')
    xn_s = kb.sb("xn_s", (P, 5 * D), BF16)
    xb = kb.sb("l0a_xb", (P, 2 * D), BF16)
    x_ep = kb.sb("l0a_xep", (P, D), F32)
    kqb = kb.sb("kqb", (P, D), BF16)
    vbb = kb.sb("vbb", (P, D), BF16)
    ones1b = kb.sb("ones1b", (1, P), BF16)
    kb.vpre()
    t_ones = kb.vpost(v.memset(ones1b[:, :], 1.0))
    kqQT = kb.sb("kqQT", (P, 32 * 512), BF16)
    kqQT_r = kqQT[:, :].rearrange("p (f t) -> p f t", f=32)
    xnT = kb.sb("xnT", (P, 32 * 512), BF16)
    xnT_r = xnT[:, :].rearrange("p (f t) -> p f t", f=32)
    attnT = kb.sb("attnT", (P, 16 * 512), BF16)
    mask_sb = kb.sb("mask_sb", (P, 4 * 512), BF16)
    pS = [kb.ps("pS0", (P, 512), F32), kb.ps("pS1", (P, 512), F32)]
    pA = [kb.ps("pA%d" % k, (P, 512), F32) for k in range(4)]

    xb_war = [None, None]
    slot_war = [None] * 5
    row_war = None
    t_xnq_war = None
    last_av = None
    av_h_tok = [None] * 4
    score_j_tok = [None] * 4
    last_sig = [None, None]
    last_sp = [None] * 4
    last_maskmm = None
    t_ep_store = None
    t_ep_v = None
    x_ep_war = None
    t_vb_ev = [None, None]

    for ph in range(2):
        kb.wait('G', nc.gpsimd, t1, t2, xb_war[1], row_war)
        trow = kb.dma(nc.gpsimd, xb[0:1, D:2 * D], kqv_d[0:1, :], 'dGk')
        t_bc_ev = [None, None]
        for i in range(8):
            ps = pS[i % 2]
            kb.wait('T', nc.tensor, t_ones, trow, t_bc_ev[i % 2],
                    last_sig[i % 2], t_vb_ev[i % 2], kb.tok('dSc'))
            ins = nc.tensor.matmul(ps[:, :], ones1b[0:1, :],
                                   xb[0:1, D + 512 * i:D + 512 * (i + 1)],
                                   start=True, stop=True)
            tt = kb.inc(ins, 'sT')
            kb.vpre(tt)
            t_bc_ev[i % 2] = kb.vpost(
                v.tensor_copy(kqb[:, 512 * i:512 * (i + 1)], ps[:, :]))
        row_war = kb.tok('sT')
        tqk = None
        for k4 in range(4):
            buf = k4 % 2
            kb.wait('G', nc.gpsimd, xb_war[buf],
                    row_war if buf == 1 else None)
            tx = kb.dma(nc.gpsimd, xb[:, D * buf:D * (buf + 1)],
                        xown[STR * ph + P * k4: STR * ph + P * (k4 + 1), :],
                        'dGx%d' % buf)
            tv2, mv = emit_ln_stats(kb, xb[:, D * buf:D * (buf + 1)], tx)
            kb.apre(tv2, t_xnq_war, last_av, t_ep_v)
            tln = kb.apost(a.activation(xn_s[:, 0:D], xb[:, D * buf:D * (buf + 1)],
                                        AF.Identity, scale=mv[:, 4:5],
                                        bias=mv[:, 5:6]))
            kb.ln_prev = tln
            xb_war[buf] = tln
            gd = kb.sb("gdscr", (P, 64), BF16)
            kb.gpre(tln)
            tg = kb.gpost(nc.gpsimd.tensor_copy(gd[:, :], xn_s[:, D - 64:D]))
            kb.vpre(tg, t_bc_ev[0], t_bc_ev[1])
            tqk = kb.vpost(v.tensor_tensor(xn_s[:, 0:D], xn_s[:, 0:D],
                                           kqb[:, :], ALU.mult))
            kb.wait('T', nc.tensor, tqk)
            t_xnq_war = emit_transpose_tile(kb, xn_s[:, 0:D], kqQT_r, P * k4,
                                            identb[:, :])
        for it in range(EXT[ph]):
            itg = it if ph == 0 else 8 + it
            kb.gpre(last_maskmm)
            tmask = kb.dma(nc.gpsimd, mask_sb[:, :],
                           maskstr[itg * P:(itg + 1) * P, :], 'dGm')
            for j in range(4):
                jj = 4 * it + j
                slot = jj % 5
                buf = jj % 2
                srow = 512 * it + P * j
                kb.wait('G', nc.gpsimd, xb_war[buf],
                        row_war if (it == 0 and j == 1 and ph > 0) else None)
                tx = kb.dma(nc.gpsimd, xb[:, D * buf:D * (buf + 1)],
                            xfull[srow:srow + P, :], 'dGx%d' % buf)
                tv2, mv = emit_ln_stats(kb, xb[:, D * buf:D * (buf + 1)], tx)
                kb.apre(tv2, slot_war[slot],
                        t_xnq_war if (it == 0 and j == 0) else None)
                tln = kb.apost(a.activation(
                    xn_s[:, D * slot:D * (slot + 1)],
                    xb[:, D * buf:D * (buf + 1)], AF.Identity,
                    scale=mv[:, 4:5], bias=mv[:, 5:6]))
                kb.ln_prev = tln
                xb_war[buf] = tln
                gd = kb.sb("gdscr", (P, 64), BF16)
                kb.gpre(tln)
                tg = kb.gpost(nc.gpsimd.tensor_copy(
                    gd[:, :], xn_s[:, D * slot + D - 64:D * (slot + 1)]))
                kb.wait('T', nc.tensor, tg)
                emit_transpose_tile(kb, xn_s[:, D * slot:D * (slot + 1)],
                                    xnT_r, P * j, identb[:, :],
                                    evac_wait=score_j_tok[j])
                trj = (kb.tr_evac[0], kb.tr_evac[1])
                for h in range(4):
                    ps = pS[h % 2]
                    kb.wait('T', nc.tensor, last_sig[h % 2], tmask,
                            t_bc_ev[h % 2], trj[0], trj[1])
                    for f8 in range(8):
                        f = 8 * h + f8
                        ins = nc.tensor.matmul(
                            ps[:, :],
                            xnT[:, 512 * f + P * j:512 * f + P * (j + 1)],
                            kqQT[:, 512 * f:512 * (f + 1)],
                            start=(f8 == 0), stop=False)
                    ins = nc.tensor.matmul(ps[:, :], identb[:, :],
                                           mask_sb[:, 512 * j:512 * (j + 1)],
                                           start=False, stop=True)
                    tt = kb.inc(ins, 'sT')
                    if h == 3:
                        score_j_tok[j] = tt
                        if j == 3:
                            last_maskmm = tt
                    kb.apre(tt, av_h_tok[h])
                    ta = kb.apost(a.activation(
                        attnT[:, 512 * (4 * h + j):512 * (4 * h + j + 1)],
                        ps[:, :], AF.Sigmoid, scale=0.125))
                    last_sig[h % 2] = ta
            gd2 = kb.sb("gdscr2", (P, 64), BF16)
            kb.gpre(last_sig[0], last_sig[1])
            tg2 = kb.gpost(nc.gpsimd.tensor_copy(
                gd2[:, :], attnT[:, 16 * 512 - 64:16 * 512]))
            for h in range(4):
                for dh in range(2):
                    for k in range(4):
                        kb.wait('T', nc.tensor, tg2, last_sp[k])
                        for j in range(4):
                            slot = (4 * it + j) % 5
                            ins = nc.tensor.matmul(
                                pA[k][:, :],
                                attnT[:, 512 * (4 * h + j) + P * k:
                                      512 * (4 * h + j) + P * (k + 1)],
                                xn_s[:, D * slot + 1024 * h + 512 * dh:
                                     D * slot + 1024 * h + 512 * (dh + 1)],
                                start=(j == 0), stop=(j == 3))
                        tt = kb.inc(ins, 'sT')
                        last_av = tt
                        av_h_tok[h] = tt
                        accsl = acc[:, D * k + 1024 * h + 512 * dh:
                                    D * k + 1024 * h + 512 * (dh + 1)]
                        kb.vpre(tt, t_ep_store if it == 0 else None)
                        if it == 0:
                            ins = v.tensor_copy(accsl, pA[k][:, :])
                        else:
                            ins = v.tensor_tensor(accsl, accsl, pA[k][:, :],
                                                  ALU.add)
                        last_sp[k] = kb.vpost(ins)
            for j in range(4):
                slot_war[(4 * it + j) % 5] = last_av
        kb.wait('G', nc.gpsimd, xb_war[1])
        trow2 = kb.dma(nc.gpsimd, xb[0:1, D:2 * D], kqv_d[1:2, :], 'dGk')
        t_vb_ev = [None, None]
        for i in range(8):
            ps = pS[i % 2]
            kb.wait('T', nc.tensor, t_ones, trow2, t_vb_ev[i % 2],
                    last_sig[i % 2])
            ins = nc.tensor.matmul(ps[:, :], ones1b[0:1, :],
                                   xb[0:1, D + 512 * i:D + 512 * (i + 1)],
                                   start=True, stop=True)
            tt = kb.inc(ins, 'sT')
            kb.vpre(tt)
            t_vb_ev[i % 2] = kb.vpost(
                v.tensor_copy(vbb[:, 512 * i:512 * (i + 1)], ps[:, :]))
        row_war = kb.tok('sT')
        for k in range(4):
            kb.wait('S', nc.sync, last_av, t_ep_store, x_ep_war)
            tx = kb.dma(nc.sync, x_ep[:, :],
                        xown[STR * ph + P * k: STR * ph + P * (k + 1), :], 'dSx')
            accsl = acc[:, D * k:D * (k + 1)]
            kb.vpre(tx, t_ep_store, t_vb_ev[0], t_vb_ev[1],
                    last_sp[0], last_sp[1], last_sp[2], last_sp[3])
            kb.vpost(v.tensor_tensor(accsl, accsl, vbb[:, :], ALU.mult))
            kb.vpre()
            t_ep_v = kb.vpost(v.tensor_tensor(accsl, accsl, x_ep[:, :],
                                              ALU.add))
            x_ep_war = t_ep_v
            gd3 = kb.sb("gdscr3", (P, 64), F32)
            kb.gpre(t_ep_v)
            tg3 = kb.gpost(nc.gpsimd.tensor_copy(gd3[:, :],
                                                 accsl[:, D - 64:D]))
            kb.apre(t_ep_v, tg3)
            t_ep_store = kb.dma(nc.scalar,
                                res_out[STR * ph + P * k: STR * ph + P * (k + 1), :],
                                accsl, 'dV')
        slot_war[0] = slot_war[1] = t_ep_v


def emit_l0_ffn(kb, aps):
    nc = kb.nc
    v, a = nc.vector, nc.scalar
    res_in, res_out = aps['res_in'], aps['res_out']

    identb = kb.sb("identb", (P, P), BF16)
    identf = kb.sb("identf", (P, P), F32)
    indE = kb.sb("indE", (P, 32 * 32), BF16)
    bias_sb = kb.sb("l0f_bias", (P, 32), F32)
    kb.dma(nc.sync, identb[:, :], aps['identb'][:, :], 'dSc')
    kb.dma(nc.sync, identf[:, :], aps['identf'][:, :], 'dSc')
    kb.dma(nc.sync, indE[:, :], aps['indE'][:, :], 'dSc')
    kb.dma(nc.sync, bias_sb[:, :], aps['biasT'][:, :], 'dSc')
    t_const = kb.tok('dSc')

    xnpT = kb.sb("xnpT", (P, 32 * 1024), BF16)
    xnpT_r = xnpT[:, :].rearrange("p (f t) -> p f t", f=32)
    lres = kb.sb("l0f_res", (P, 4 * D), F32)
    xst = kb.sb("xst0", (P, D), F32)
    gd = kb.sb("gdscr", (P, 64), BF16)
    zs_b = kb.sb("l0f_zs", (32, D), BF16)
    xsum = kb.sb("l0f_xsum", (32, 1024), BF16)
    qb_sb = kb.sb("l0f_qb", (P, 4096), U8)
    sexp_sb = kb.sb("l0f_sexp", (P, 4096), F32)
    delta_sb = kb.sb("l0f_delta", (P, 4096), BF16)
    wt_sb = kb.sb("l0f_wt", (P, 2 * 4096), BF16)
    ytmp = xst[:, 0:1024]
    pS = [kb.ps("pS0", (P, 512), F32), kb.ps("pS1", (P, 512), F32)]
    pA = [kb.ps("pA%d" % k, (P, 512), F32) for k in range(4)]

    last_norm = kb.state.get('l0f_xst_war')
    for k in range(8):
        kb.wait('S', nc.sync, last_norm)
        tx = kb.dma(nc.sync, xst[:, :], res_in[P * k:P * (k + 1), :], 'dSx')
        xnp = kb.sb("l0f_xnp", (P, D), BF16)
        tv2, mv = emit_ln_stats(kb, xst[:, :], tx)
        kb.apre(tv2, kb.state.get('l0f_xnp_war'))
        kb.apost(a.activation(xnp[:, 0:2048], xst[:, 0:D:2], AF.Identity,
                              scale=mv[:, 4:5], bias=mv[:, 5:6]))
        kb.apre()
        last_norm = kb.apost(a.activation(xnp[:, 2048:4096], xst[:, 1:D:2],
                                          AF.Identity, scale=mv[:, 4:5],
                                          bias=mv[:, 5:6]))
        kb.ln_prev = last_norm
        kb.gpre(last_norm)
        tg = kb.gpost(nc.gpsimd.tensor_copy(gd[:, :], xnp[:, D - 64:D]))
        kb.wait('T', nc.tensor, tg, t_const)
        kb.state['l0f_xnp_war'] = emit_transpose_tile(kb, xnp[:, :], xnpT_r,
                                                      P * k, identb[:, :])
    kb.state['l0f_xst_war'] = last_norm
    kb.state['l0f_stageA_done'] = kb.tok('sT')

    s2 = lres[0:32, 0:D]
    z2 = lres[0:32, D:2 * D]
    t1 = kb.dma(nc.sync, s2, aps['s2dT'][:, :], 'dSz')
    t2 = kb.dma(nc.sync, z2, aps['z2dT'][:, :], 'dSz')
    kb.vpre(kb.tok('dSz'))
    tzs = kb.vpost(v.scalar_tensor_tensor(zs_b[:, :], s2, -1.0, z2,
                                          ALU.mult, ALU.mult))
    t_tr = (kb.tr_evac[0], kb.tr_evac[1])
    for tc in range(2):
        kb.wait('T', nc.tensor, t_const, kb.state.get('l0f_xsum_war'), *t_tr)
        for pc in range(32):
            ins = nc.tensor.matmul(pA[1][0:32, :], indE[:, 32 * pc:32 * (pc + 1)],
                                   xnpT[:, 1024 * pc + 512 * tc:1024 * pc + 512 * (tc + 1)],
                                   start=(pc == 0), stop=(pc == 31))
        tt = kb.inc(ins, 'sT')
        kb.vpre(tt)
        ins = v.tensor_copy(xsum[:, 512 * tc:512 * (tc + 1)], pA[1][0:32, :])
        kb.state['l0f_xsum_war'] = kb.vpost(ins)
    kb.gpre(kb.state['l0f_xsum_war'], tzs)
    tg_zx = kb.gpost(nc.gpsimd.tensor_copy(gd[0:32, 0:64], xsum[:, 1024 - 64:1024]))

    weff = aps['weff_scratch']
    wt_war = [None, None]
    wt_ready = [None, None]
    wt_store = [None, None]
    qb_war = [None, None]
    ev_war = [None, None]
    ytmp_war = [None, None]
    pa0_war = None
    res_stored = kb.state.get('l0f_res_stored')

    def _flush_tail(pend, t_res):
        nonlocal pa0_war
        po, pb, ptgy = pend
        kb.wait('T', nc.tensor, ptgy, pa0_war)
        for m in range(4):
            ins = nc.tensor.transpose(pA[0][:, 128 * m:128 * (m + 1)],
                                      ytmp[:, 512 * pb + 128 * m:512 * pb + 128 * (m + 1)],
                                      identf[:, :])
        tt2 = kb.inc(ins, 'sT')
        ytmp_war[pb] = tt2
        kb.vpre(tt2, t_res)
        for m in range(4):
            ins = v.tensor_tensor(lres[:, D * m + P * po:D * m + P * (po + 1)],
                                  lres[:, D * m + P * po:D * m + P * (po + 1)],
                                  pA[0][:, 128 * m:128 * (m + 1)], ALU.add)
            kb.vpost(ins)
            if m < 3:
                kb.vpre()
        pa0_war = kb.tok('cV')

    for tc in range(2):
        kb.wait('S', nc.sync, res_stored, tzs)
        for k4 in range(4):
            kb.dma(nc.sync, lres[:, D * k4:D * (k4 + 1)],
                   res_in[512 * tc + P * k4:512 * tc + P * (k4 + 1), :], 'dSr')
        t_res = kb.tok('dSr')
        pend = None
        for ot in range(32):
            buf = ot % 2
            if tc == 0:
                kb.wait('G', nc.gpsimd, qb_war[0])
                tq = kb.dma(nc.gpsimd, qb_sb[:, :],
                            aps['qbT'][P * ot:P * (ot + 1), :], 'dGq')
                kb.wait('S', nc.sync, kb.state.get('l0f_sexp_war'),
                        kb.state.get('l0f_stageA_done') if ot == 0 else None)
                ts = kb.dma(nc.sync, sexp_sb[:, 0:2048],
                            aps['sexpT'][P * ot:P * (ot + 1), 0:2048], 'dSw')
                kb.wait('A', a, kb.state.get('l0f_sexp_war'))
                ts2 = kb.dma(nc.scalar, sexp_sb[:, 2048:4096],
                             aps['sexpT'][P * ot:P * (ot + 1), 2048:4096], 'dAw')
                kb.wait('G', nc.gpsimd, *(kb.state.get('l0f_delta_war') or ()))
                td = kb.dma(nc.gpsimd, delta_sb[:, 0:2048],
                            aps['deltaT'][P * ot:P * (ot + 1), 0:2048], 'dGd')
                kb.wait('A', a, *(kb.state.get('l0f_delta_war') or ()))
                td2 = kb.dma(nc.scalar, delta_sb[:, 2048:4096],
                             aps['deltaT'][P * ot:P * (ot + 1), 2048:4096], 'dAd')
                nib_lo = qb_sb[:, 0:2048].rearrange("p (j o) -> p j o", j=16)
                nib_hi = qb_sb[:, 2048:4096].rearrange("p (j o) -> p j o", j=16)
                sex_r = sexp_sb[:, :].rearrange("p (j q o) -> p j q o", j=16, q=2)
                del_r = delta_sb[:, :].rearrange("p (j q o) -> p j q o", j=16, q=2)
                wt_r = wt_sb[:, 4096 * buf:4096 * (buf + 1)].rearrange(
                    "p (q j o) -> p q j o", q=2, j=16)
                kb.vpre(tq, ts, ts2, wt_war[buf], wt_store[buf])
                tm0 = kb.vpost(v.tensor_tensor(wt_r[:, 0, :, :], nib_lo,
                                               sex_r[:, :, 0, :], ALU.mult))
                kb.vpre()
                tm1 = kb.vpost(v.tensor_tensor(wt_r[:, 1, :, :], nib_hi,
                                               sex_r[:, :, 1, :], ALU.mult))
                qb_war[0] = tm1
                kb.state['l0f_sexp_war'] = tm1
                kb.vpre(td, td2)
                tva = kb.vpost(v.tensor_tensor(
                    wt_r[:, 0, :, :], wt_r[:, 0, :, :],
                    del_r[:, :, 0, :], ALU.add))
                kb.gpre(tm1, td, td2)
                tdq = kb.gpost(nc.gpsimd.tensor_tensor(
                    wt_r[:, 1, :, :], wt_r[:, 1, :, :],
                    del_r[:, :, 1, :], ALU.add))
                kb.gpre(tva)
                tgv = kb.gpost(nc.gpsimd.tensor_copy(
                    gd[:, :], wt_sb[:, 4096 * buf + 2048 - 64:4096 * buf + 2048]))
                kb.state['l0f_delta_war'] = (tva, tdq)
                wt_ready[buf] = (tdq, tgv)
                kb.wait('S', nc.sync, tdq, tgv)
                tw1 = kb.dma(nc.sync, weff[P * ot:P * (ot + 1), 0:2048],
                             wt_sb[:, 4096 * buf:4096 * buf + 2048],
                             'dGw%d' % buf)
                kb.wait('A', a, tdq, tgv)
                wt_store[buf] = kb.dma(nc.scalar,
                                       weff[P * ot:P * (ot + 1), 2048:4096],
                                       wt_sb[:, 4096 * buf + 2048:4096 * (buf + 1)],
                                       'dGw%d' % buf)
            else:
                kb.wait('G', nc.gpsimd, wt_war[buf], wt_store[buf])
                tr1 = kb.dma(nc.gpsimd,
                             wt_sb[:, 4096 * buf:4096 * buf + 2048],
                             weff[P * ot:P * (ot + 1), 0:2048], 'dGr%d' % buf)
                kb.wait('A', a, wt_war[buf], wt_store[buf])
                tr2 = kb.dma(nc.scalar,
                             wt_sb[:, 4096 * buf + 2048:4096 * (buf + 1)],
                             weff[P * ot:P * (ot + 1), 2048:4096], 'dAr%d' % buf)
                wt_ready[buf] = (tr1, tr2)
            ps = pS[buf]
            kb.wait('T', nc.tensor, *(wt_ready[buf] or ()),
                    ev_war[buf], tg_zx, t_const)
            for f in range(32):
                ins = nc.tensor.matmul(
                    ps[:, :], wt_sb[:, 4096 * buf + 128 * f:4096 * buf + 128 * (f + 1)],
                    xnpT[:, 1024 * f + 512 * tc:1024 * f + 512 * (tc + 1)],
                    start=(f == 0), stop=False)
            ins = nc.tensor.matmul(ps[:, :], zs_b[0:32, P * ot:P * (ot + 1)],
                                   xsum[0:32, 512 * tc:512 * (tc + 1)],
                                   start=False, stop=True)
            tt = kb.inc(ins, 'sT')
            wt_war[buf] = tt
            if pend is not None:
                _flush_tail(pend, t_res)
            kb.apre(tt, ytmp_war[buf])
            ins = a.activation(ytmp[:, 512 * buf:512 * (buf + 1)], ps[:, :],
                               AF.Identity, bias=bias_sb[:, ot:ot + 1])
            tev = kb.apost(ins)
            ev_war[buf] = tev
            kb.gpre(tev)
            tgy = kb.gpost(nc.gpsimd.tensor_copy(
                gd[:, :], ytmp[:, 512 * buf + 448:512 * buf + 512]))
            pend = (ot, buf, tgy)
        _flush_tail(pend, t_res)
        for k4 in range(4):
            kb.gpre(pa0_war)
            gdf = kb.sb("gdscrf", (P, 64), F32)
            tgr = kb.gpost(nc.gpsimd.tensor_copy(
                gdf[:, :], lres[:, D * k4 + D - 64:D * (k4 + 1)]))
            kb.wait('A', a, tgr)
            kb.dma(nc.scalar, res_out[512 * tc + P * k4:512 * tc + P * (k4 + 1), :],
                   lres[:, D * k4:D * (k4 + 1)], 'dV')
        res_stored = kb.tok('dV')
    kb.state['l0f_res_stored'] = res_stored


def _prep_l0a(core, x, bvq, bvk, bvv):
    s1, s2 = core, core + 8
    xown = np.concatenate([x[STR * s1:STR * s1 + STR], x[STR * s2:STR * s2 + STR]])
    ms = np.zeros((24 * P, 4 * 512), dtype=BF)
    r = np.arange(P)[:, None]
    col = np.arange(512)[None, :]
    for ph, stripe in enumerate((s1, s2)):
        S0 = STR * stripe
        for it in range(EXT[ph]):
            itg = it if ph == 0 else 8 + it
            for j in range(4):
                srow = 512 * it + P * j
                m = np.where((srow + r) <= (S0 + col), 0.0, -300.0).astype(np.float32)
                ms[itg * P:(itg + 1) * P, 512 * j:512 * (j + 1)] = m.astype(BF)
    return {
        "xfull": np.ascontiguousarray(x),
        "xown": np.ascontiguousarray(xown),
        "bvq": np.ascontiguousarray(bvq),
        "bvk": np.ascontiguousarray(bvk),
        "bvv": np.ascontiguousarray(bvv),
        "maskstr": ms,
        "identb": np.eye(P, dtype=BF),
    }


def _l0a_aps(nc):
    return {
        'xfull': nc.dram_tensor("xfull", [T, D], F32, kind="ExternalInput").ap(),
        'xown': nc.dram_tensor("xown", [TOKC, D], F32, kind="ExternalInput").ap(),
        'bvq': nc.dram_tensor("bvq", [4, 1024], F32, kind="ExternalInput").ap(),
        'bvk': nc.dram_tensor("bvk", [4, 1024], F32, kind="ExternalInput").ap(),
        'bvv': nc.dram_tensor("bvv", [4, 1024], F32, kind="ExternalInput").ap(),
        'maskstr': nc.dram_tensor("maskstr", [24 * P, 4 * 512], BF16,
                                  kind="ExternalInput").ap(),
        'identb': _dram(nc, "identb", [P, P], BF16),
        'kqv_scratch': nc.dram_tensor("kqv_scratch", [2, D], F32,
                                      kind="Internal").ap(),
    }


def build_l0a_program():
    nc = bass.Bass("TRN2", target_bir_lowering=False)
    aps = _l0a_aps(nc)
    aps['res_out'] = nc.dram_tensor("res_out", [TOKC, D], F32,
                                    kind="ExternalOutput").ap()
    if DEBUG_L0A:
        for nm, shp, dt in (('dbg_acc', (P, 4 * D), F32),
                            ('dbg_attnT', (P, 4 * 512), BF16),
                            ('dbg_xns', (P, 4 * D), BF16),
                            ('dbg_kqQT', (P, 32 * 512), BF16),
                            ('dbg_kqb', (P, D), BF16),
                            ('dbg_accit', (8 * P, 4 * D), F32),
                            ('dbg_attit', (8 * P, 4 * 512), BF16),
                            ('dbg_vbb', (P, D), BF16)):
            aps[nm] = nc.dram_tensor(nm, list(shp), dt, kind="ExternalOutput").ap()
    kb = KB(nc)
    kb.OFFSETS = _mk_offsets()
    emit_l0_attn(kb, aps)
    kb.wait('S', nc.sync, kb.tok('dV'))
    if DEBUG_L0A:
        kb.wait('S', nc.sync, kb.tok('dDbg'))
    kb.close()
    return nc


def kernel_l0a_only(L0_attn_x, L0_attn_bv_q, L0_attn_bv_k, L0_attn_bv_v):
    x = np.asarray(L0_attn_x).reshape(T, D)
    nc = build_l0a_program()
    in_maps = [_prep_l0a(c, x, np.asarray(L0_attn_bv_q), np.asarray(L0_attn_bv_k),
                         np.asarray(L0_attn_bv_v)) for c in range(NCORE)]
    res = run_bass_kernel_spmd(nc, in_maps, core_ids=list(range(NCORE)))
    out = np.empty((T, D), np.float32)
    for c in range(NCORE):
        r = np.asarray(res.results[c]["res_out"])
        out[STR * c:STR * (c + 1)] = r[:STR]
        out[STR * (c + 8):STR * (c + 9)] = r[STR:]
    return out.reshape(1, T, D)


def _l1a_hydra(kb, u_sb, out_sb, w):
    nc = kb.nc
    v, a = nc.vector, nc.scalar
    pA = [kb.ps("pA%d" % k, (P, 512), F32) for k in range(4)]
    pS = [kb.ps("pS0", (P, 512), F32), kb.ps("pS1", (P, 512), F32)]
    gd = kb.sb("gdscr", (P, 64), BF16)
    zsil = kb.sb("l1a_zsil", (P, 1024), BF16)
    xs = kb.sb("l1a_xs", (P, 1024), BF16)
    y = kb.sb("l1a_y", (P, 1024), BF16)
    yz = kb.sb("l1a_yz", (P, 1024), BF16)
    yz2 = kb.sb("l1a_yz2", (P, 1024), BF16)
    y2 = kb.sb("l1a_y2", (P, 1024), BF16)
    rr = kb.sb("l1a_rr", (1, 512), F32)
    st = kb.state
    tz = []
    for c in range(4):
        kb.wait('T', nc.tensor, st.get('l1a_pA%d_war' % c))
        ins = nc.tensor.matmul(pA[c][:, :], w['inwT'][:, 128 * c:128 * (c + 1)],
                               u_sb, start=True, stop=True)
        tz.append(kb.inc(ins, 'sT'))
    for c in range(2):
        kb.apre(tz[c], st.get('l1a_zsil_war'))
        kb.apost(a.activation(zsil[:, 512 * c:512 * (c + 1)], pA[c][:, :], AF.Silu))
    for c in range(2):
        kb.apre(tz[2 + c], st.get('l1a_xs_war'))
        kb.apost(a.activation(xs[:, 512 * c:512 * (c + 1)], pA[2 + c][:, :],
                              AF.Silu, scale=w['cwcb'][:, c:c + 1],
                              bias=w['cwcb'][:, 2 + c:3 + c]))
    t_xs = kb.tok('cA')
    st['l1a_pA1_war'] = t_xs
    st['l1a_pA2_war'] = t_xs
    st['l1a_pA3_war'] = t_xs
    kb.gpre(t_xs)
    tgx = kb.gpost(nc.gpsimd.tensor_copy(gd[:, :], xs[:, 1024 - 64:1024]))
    tf = []
    for oc in range(2):
        kb.wait('T', nc.tensor, tgx, st.get('l1a_pS%d_war' % oc))
        for c in range(2):
            ins = nc.tensor.matmul(pS[oc][:, :],
                                   w['fcdT'][:, 128 * (2 * c + oc):128 * (2 * c + oc + 1)],
                                   xs[:, 512 * c:512 * (c + 1)],
                                   start=(c == 0), stop=(c == 1))
        tf.append(kb.inc(ins, 'sT'))
    for oc in range(2):
        kb.vpre(tf[oc])
        kb.vpost(v.scalar_tensor_tensor(y[:, 512 * oc:512 * (oc + 1)],
                                        pS[oc][:, :], w['drep'][:, oc:oc + 1],
                                        xs[:, 512 * oc:512 * (oc + 1)],
                                        ALU.add, ALU.mult))
        kb.vpre()
        kb.vpost(v.tensor_tensor(yz[:, 512 * oc:512 * (oc + 1)],
                                 y[:, 512 * oc:512 * (oc + 1)],
                                 zsil[:, 512 * oc:512 * (oc + 1)], ALU.mult))
    t_yz = kb.tok('cV')
    st['l1a_pS0_war'] = t_yz
    st['l1a_pS1_war'] = t_yz
    for oc in range(2):
        kb.apre(t_yz, st.get('l1a_yz2_war'))
        kb.apost(a.activation(yz2[:, 512 * oc:512 * (oc + 1)],
                              yz[:, 512 * oc:512 * (oc + 1)], AF.Square))
    t_yz2 = kb.tok('cA')
    kb.gpre(t_yz2)
    tgq = kb.gpost(nc.gpsimd.tensor_copy(gd[:, :], yz2[:, 1024 - 64:1024]))
    kb.wait('T', nc.tensor, tgq, t_yz)
    for c in range(2):
        ins = nc.tensor.matmul(pS[0][0:1, :], w['onesb'][:, 0:1],
                               yz2[:, 512 * c:512 * (c + 1)],
                               start=(c == 0), stop=(c == 1))
    tssq = kb.inc(ins, 'sT')
    kb.vpre(tssq)
    kb.vpost(v.tensor_scalar(rr[0:1, :], pS[0][0:1, :], 1.0 / 256, 1e-5,
                             ALU.mult, ALU.add))
    tv = kb.tok('cV')
    kb.apre(tv)
    ta = kb.apost(a.activation(rr[0:1, :], rr[0:1, :], AF.Sqrt))
    kb.vpre(ta)
    trr = kb.vpost(v.reciprocal(rr[0:1, :], rr[0:1, :]))
    st['l1a_pS0_war'] = trr
    kb.gpre(trr)
    tgr = kb.gpost(nc.gpsimd.tensor_copy(gd[0:1, 0:64], rr[0:1, 448:512]))
    kb.wait('T', nc.tensor, tgr, t_yz)
    ins = nc.tensor.matmul(pS[1][:, :], w['onesf'][0:1, :], rr[0:1, :],
                           start=True, stop=True)
    tbc = kb.inc(ins, 'sT')
    for oc in range(2):
        kb.vpre(tbc)
        kb.vpost(v.tensor_tensor(y2[:, 512 * oc:512 * (oc + 1)],
                                 yz[:, 512 * oc:512 * (oc + 1)], pS[1][:, :],
                                 ALU.mult))
    ty2 = kb.tok('cV')
    st['l1a_pS1_war'] = ty2
    kb.gpre(ty2)
    tgy = kb.gpost(nc.gpsimd.tensor_copy(gd[:, :], y2[:, 1024 - 64:1024]))
    kb.wait('T', nc.tensor, tgy, st.get('l1a_pA0_war'))
    for c in range(2):
        ins = nc.tensor.matmul(pA[0][:, :], w['outwT'][:, 128 * c:128 * (c + 1)],
                               y2[:, 512 * c:512 * (c + 1)],
                               start=(c == 0), stop=(c == 1))
    tt = kb.inc(ins, 'sT')
    kb.apre(tt, st.get('l1a_out_war'))
    tout = kb.apost(a.activation(out_sb, pA[0][:, :], AF.Identity))
    st['l1a_pA0_war'] = tout
    st['l1a_xs_war'] = kb.tok('sT')
    st['l1a_yz2_war'] = kb.tok('sT')
    return tout


def emit_l1_attn(kb, aps):
    nc = kb.nc
    v, a = nc.vector, nc.scalar
    res_in, res_out = aps['res_in'], aps['res_out']
    st = kb.state

    identb = kb.sb("identb", (P, P), BF16)
    identf = kb.sb("identf", (P, P), F32)
    wdT = kb.sb("l1a_wdT", (P, 4096), BF16)
    wupT = kb.sb("l1a_wupT", (P, 4096), BF16)
    inwT = kb.sb("l1a_inwT", (P, 512), BF16)
    outwT = kb.sb("l1a_outwT", (P, 256), BF16)
    fcdT = kb.sb("l1a_fcdT", (P, 512), BF16)
    cwcb = kb.sb("l1a_cwcb", (P, 4), F32)
    drep = kb.sb("l1a_drep", (P, 2), F32)
    normw = kb.sb("l1a_normw", (P, 2), F32)
    bdown = kb.sb("l1a_bdown", (P, 1), F32)
    bupT = kb.sb("l1a_bupT", (P, 32), F32)
    g1 = kb.sb("l1a_g1", (1, 8), F32)
    onesb = kb.sb("l1a_onesb", (P, 1), BF16)
    onesf = kb.sb("l1a_onesf", (1, P), F32)
    gd = kb.sb("gdscr", (P, 64), BF16)

    kb.dma(nc.sync, identb[:, :], aps['identb'][:, :], 'dSc')
    kb.dma(nc.sync, identf[:, :], aps['identf'][:, :], 'dSc')
    kb.dma(nc.gpsimd, wdT[:, :], aps['wdT'][:, :], 'dGc')
    kb.dma(nc.gpsimd, wupT[:, :], aps['wupT'][:, :], 'dGc')
    kb.dma(nc.gpsimd, inwT[:, :], aps['inwT'][:, :], 'dGc')
    kb.dma(nc.gpsimd, outwT[:, :], aps['outwT'][:, :], 'dGc')
    kb.dma(nc.gpsimd, fcdT[:, :], aps['fcdT'][:, :], 'dGc')
    kb.dma(nc.sync, cwcb[:, :], aps['cwcb'][:, :], 'dSc')
    kb.dma(nc.sync, drep[:, :], aps['drep'][:, :], 'dSc')
    kb.dma(nc.sync, normw[:, :], aps['normw'][:, :], 'dSc')
    kb.dma(nc.sync, bdown[:, :], aps['bdown'][:, :], 'dSc')
    kb.dma(nc.sync, bupT[:, :], aps['bupT'][:, :], 'dSc')
    kb.dma(nc.sync, g1[0:1, 0:1], aps['g1'][:, :], 'dSc')
    t_const = kb.tok('dSc')
    t_constg = kb.tok('dGc')

    kb.vpre(t_constg, t_const)
    kb.vpost(v.memset(onesb[:, :], 1.0))
    kb.vpre()
    kb.vpost(v.memset(onesf[:, :], 1.0))
    for c in range(2):
        kb.vpre()
        kb.vpost(v.tensor_scalar(outwT[:, 128 * c:128 * (c + 1)],
                                 outwT[:, 128 * c:128 * (c + 1)],
                                 normw[:, c:c + 1], None, ALU.mult))
    t_w = kb.tok('cV')
    pS = [kb.ps("pS0", (P, 512), F32), kb.ps("pS1", (P, 512), F32)]
    kb.gpre(t_w)
    tgw = kb.gpost(nc.gpsimd.tensor_copy(gd[0:1, 0:2], onesf[0:1, P - 2:P]))
    kb.wait('T', nc.tensor, tgw, st.get('l1a_pS0_war'))
    ins = nc.tensor.matmul(pS[0][:, 0:1], onesf[0:1, :], g1[0:1, 0:1],
                           start=True, stop=True)
    tt = kb.inc(ins, 'sT')
    alf = kb.sb("l1a_alf", (P, 4), F32)
    kb.apre(tt)
    kb.apost(a.activation(alf[:, 0:1], pS[0][:, 0:1], AF.Sigmoid))
    kb.vpre(kb.tok('cA'))
    kb.vpost(v.tensor_scalar(alf[:, 1:2], alf[:, 0:1], 1.0, None, ALU.add))
    kb.vpre()
    tal = kb.vpost(v.tensor_scalar(alf[:, 2:3], alf[:, 0:1], -1.0, 1.0,
                                   ALU.mult, ALU.add))
    st['l1a_pS0_war'] = kb.tok('cA')

    xn1 = kb.sb("l1a_xn1", (P, 4 * D), BF16)
    x1T = kb.sb("l1a_x1T", (P, 32 * 512), BF16)
    x1T_r = x1T[:, :].rearrange("p (f t) -> p f t", f=32)
    lres = kb.sb("l1a_res", (P, 4 * D), F32)
    xst = kb.sb("xst0", (P, D), F32)
    h0T = kb.sb("l1a_h0T", (P, 512), BF16)
    h1T = kb.sb("l1a_h1T", (P, 512), BF16)
    h2T = kb.sb("l1a_h2T", (P, 512), BF16)
    hfin = kb.sb("l1a_hfin", (P, 512), BF16)
    utile = kb.sb("l1a_utile", (P, 1024), F32)
    w = {'inwT': inwT, 'outwT': outwT, 'fcdT': fcdT, 'cwcb': cwcb,
         'drep': drep, 'onesb': onesb, 'onesf': onesf}

    pA = [kb.ps("pA%d" % k, (P, 512), F32) for k in range(4)]
    for tc in range(2):
        for k4 in range(4):
            kb.wait('S', nc.sync, st.get('l1a_xst_war'), st.get('l1a_res_stored'))
            tx = kb.dma(nc.sync, xst[:, :],
                        res_in[512 * tc + P * k4:512 * tc + P * (k4 + 1), :], 'dSx')
            tln = emit_ln_tile(kb, xst[:, :], xn1[:, D * k4:D * (k4 + 1)], tx,
                               war_toks=(st.get('l1a_xn_war'),))
            st['l1a_xst_war'] = tln
            kb.gpre(tln)
            tg = kb.gpost(nc.gpsimd.tensor_copy(
                gd[:, :], xn1[:, D * k4 + D - 64:D * (k4 + 1)]))
            kb.wait('T', nc.tensor, tg, t_constg)
            emit_transpose_tile(kb, xn1[:, D * k4:D * (k4 + 1)], x1T_r,
                                P * k4, identb[:, :])
        t_tr = (kb.tr_evac[0], kb.tr_evac[1])
        kb.wait('T', nc.tensor, st.get('l1a_pS0_war'), *t_tr)
        for f in range(32):
            ins = nc.tensor.matmul(pS[0][:, :], wdT[:, 128 * f:128 * (f + 1)],
                                   x1T[:, 512 * f:512 * (f + 1)],
                                   start=(f == 0), stop=(f == 31))
        tt = kb.inc(ins, 'sT')
        kb.apre(tt, tal, st.get('l1a_h0_war'))
        th0 = kb.apost(a.activation(h0T[:, :], pS[0][:, :], AF.Gelu,
                                    bias=bdown[:, 0:1]))
        st['l1a_pS0_war'] = th0
        kb.gpre(th0)
        tgh = kb.gpost(nc.gpsimd.tensor_copy(gd[:, :], h0T[:, 512 - 64:512]))
        st['l1a_u_guard'] = tgh
        kb.wait('T', nc.tensor, tgh)
        _l1a_hydra(kb, h0T[:, :], h1T[:, :], w)
        kb.gpre(kb.tok('cA'))
        tg1 = kb.gpost(nc.gpsimd.tensor_copy(gd[:, :], h1T[:, 512 - 64:512]))
        kb.wait('T', nc.tensor, tg1)
        _l1a_hydra(kb, h1T[:, :], h2T[:, :], w)
        kb.vpre(kb.tok('cA'))
        kb.vpost(v.tensor_scalar(hfin[:, :], h2T[:, :], alf[:, 2:3], None,
                                 ALU.mult))
        kb.vpre()
        thf = kb.vpost(v.scalar_tensor_tensor(hfin[:, :], h0T[:, :],
                                              alf[:, 1:2], hfin[:, :],
                                              ALU.mult, ALU.add))
        st['l1a_h0_war'] = thf
        kb.gpre(thf)
        tghf = kb.gpost(nc.gpsimd.tensor_copy(gd[:, :], hfin[:, 512 - 64:512]))
        kb.wait('S', nc.sync, st.get('l1a_res_stored'), st.get('l1a_lres_war'))
        for k4 in range(4):
            kb.dma(nc.sync, lres[:, D * k4:D * (k4 + 1)],
                   res_in[512 * tc + P * k4:512 * tc + P * (k4 + 1), :], 'dSr')
        t_res = kb.tok('dSr')
        for k4 in range(4):
            kb.vpre(t_res)
            kb.vpost(v.tensor_tensor(lres[:, D * k4:D * (k4 + 1)],
                                     lres[:, D * k4:D * (k4 + 1)],
                                     xn1[:, D * k4:D * (k4 + 1)], ALU.add))
        st['l1a_xn_war'] = kb.tok('cV')
        ut_war = [None, None]
        pa2_war = st.get('l1a_pA2_war')
        pend = None

        def _flush_up(pend):
            nonlocal pa2_war
            po, pb, ptgu = pend
            kb.wait('T', nc.tensor, ptgu, pa2_war)
            for m in range(4):
                ins = nc.tensor.transpose(
                    pA[2][:, 128 * m:128 * (m + 1)],
                    utile[:, 512 * pb + 128 * m:512 * pb + 128 * (m + 1)],
                    identf[:, :])
            tt2 = kb.inc(ins, 'sT')
            ut_war[pb] = tt2
            kb.vpre(tt2)
            for m in range(4):
                ins = v.tensor_tensor(lres[:, D * m + P * po:D * m + P * (po + 1)],
                                      lres[:, D * m + P * po:D * m + P * (po + 1)],
                                      pA[2][:, 128 * m:128 * (m + 1)], ALU.add)
                kb.vpost(ins)
                if m < 3:
                    kb.vpre()
            pa2_war = kb.tok('cV')

        for oc in range(32):
            ps = pS[oc % 2]
            kb.wait('T', nc.tensor, tghf, st.get('l1a_pS%d_war' % (oc % 2)))
            ins = nc.tensor.matmul(ps[:, :], wupT[:, 128 * oc:128 * (oc + 1)],
                                   hfin[:, :], start=True, stop=True)
            tt = kb.inc(ins, 'sT')
            if pend is not None:
                _flush_up(pend)
            kb.apre(tt, ut_war[oc % 2])
            tev = kb.apost(a.activation(utile[:, 512 * (oc % 2):512 * (oc % 2 + 1)],
                                        ps[:, :], AF.Identity,
                                        bias=bupT[:, oc:oc + 1]))
            st['l1a_pS%d_war' % (oc % 2)] = tev
            kb.gpre(tev)
            tgu = kb.gpost(nc.gpsimd.tensor_copy(
                gd[:, :], utile[:, 512 * (oc % 2) + 448:512 * (oc % 2) + 512]))
            pend = (oc, oc % 2, tgu)
        _flush_up(pend)
        st['l1a_pA2_war'] = pa2_war
        gdf = kb.sb("gdscrf", (P, 64), F32)
        for k4 in range(4):
            kb.gpre(pa2_war)
            tgr = kb.gpost(nc.gpsimd.tensor_copy(
                gdf[:, :], lres[:, D * k4 + D - 64:D * (k4 + 1)]))
            kb.wait('A', a, tgr)
            kb.dma(nc.scalar, res_out[512 * tc + P * k4:512 * tc + P * (k4 + 1), :],
                   lres[:, D * k4:D * (k4 + 1)], 'dV')
        st['l1a_res_stored'] = kb.tok('dV')
        st['l1a_lres_war'] = pa2_war


def _l1f_build_bp(kb, stg_f, tload, scT, nt, bp_out):
    nc = kb.nc
    v = nc.vector
    st = kb.state
    inv = kb.sb("l1f_inv", (P, 86), F32)
    stg = kb.sb("l1f_wg", (P, 4096), BF16)[:, 0:86 * 32]
    gd = kb.sb("gdscr", (P, 64), BF16)
    identb = kb.sb("identb", (P, P), BF16)
    pT = kb.ps("pT0", (P, 1024), BF16)
    kb.vpre(tload, st.get('l1f_inv_war'))
    kb.vpost(v.reciprocal(inv[:, 0:nt], scT))
    kb.vpre()
    kb.vpost(v.tensor_scalar(inv[:, 0:nt], inv[:, 0:nt], 4.0, None, ALU.mult))
    for ot in range(nt):
        kb.vpre(st.get('l1f_bstage_war') if ot == 0 else None)
        kb.vpost(v.tensor_scalar(stg[:, 32 * ot:32 * (ot + 1)],
                                 stg_f[:, 32 * ot:32 * (ot + 1)],
                                 inv[:, ot:ot + 1], None, ALU.mult))
    tv = kb.tok('cV')
    st['l1f_inv_war'] = tv
    kb.gpre(tv)
    tg = kb.gpost(nc.gpsimd.tensor_copy(gd[:, 0:32],
                                        stg[:, 32 * (nt - 1):32 * nt]))
    evtok = None
    for g in range((nt + 7) // 8):
        n = min(8, nt - 8 * g)
        kb.wait('T', nc.tensor, tg, evtok, st.get('l1f_pT_war'))
        for m in range(n):
            ins = nc.tensor.transpose(pT[0:32, 128 * m:128 * (m + 1)],
                                      stg[:, 32 * (8 * g + m):32 * (8 * g + m + 1)],
                                      identb[:, :])
        tt = kb.inc(ins, 'sT')
        kb.vpre(tt)
        evtok = kb.vpost(v.tensor_copy(bp_out[0:32, 1024 * g:1024 * g + 128 * n],
                                       pT[0:32, 0:128 * n]))
    st['l1f_bstage_war'] = evtok
    st['l1f_pT_war'] = evtok
    return evtok


def emit_l1_ffn(kb, aps):
    nc = kb.nc
    v, a = nc.vector, nc.scalar
    res_in, res_out = aps['res_in'], aps['res_out']
    st = kb.state
    identb = kb.sb("identb", (P, P), BF16)
    kb.dma(nc.sync, identb[:, :], aps['identb'][:, :], 'dSc')

    hcol = kb.sb("l1f_hcol", (P, 86 * 512), BF16)
    xnT = hcol[:, 0:32 * 1024]
    xnT_r = xnT.rearrange("p (f t) -> p f t", f=32)
    gBp = hcol[0:32, 32 * 1024:32 * 1024 + 86 * 128]
    xst = kb.sb("xst0", (P, D), F32)
    xn = kb.sb("l1f_xn", (P, D), BF16)
    gd = kb.sb("gdscr", (P, 64), BF16)
    uBp = kb.sb("l1f_uBp", (32, 86 * 128), BF16)
    B2p = kb.sb("l1f_B2p", (32, 32 * 128), BF16)
    wd = kb.sb("l1f_wd", (P, 2 * 8 * 512), BF16)
    gA_sb = wd[:, 0:1024]
    uA_sb = wd[:, 1024:2048]
    A2_sb = wd[:, 2048:2048 + 86 * 32]
    t1g = kb.sb("l1f_t1g", (32, 1024), BF16)
    t1u = kb.sb("l1f_t1u", (32, 1024), BF16)
    t2 = kb.sb("l1f_t2", (32, 1024), BF16)
    gscT = kb.sb("l1f_gsc", (P, 86), F32)
    uscT = kb.sb("l1f_usc", (P, 86), F32)
    dscT = kb.sb("l1f_dsc", (P, 32), F32)
    gsc005 = kb.sb("l1f_gsc5", (P, 86), F32)
    usc005 = kb.sb("l1f_usc5", (P, 86), F32)
    xsrow = kb.sb("l1f_xsrow", (1, 1024), BF16)
    rsrow = kb.sb("l1f_rsrow", (1, 1024), BF16)
    m75 = kb.sb("l1f_m75", (1, 512), BF16)
    onesb = kb.sb("l1a_onesb", (P, 1), BF16)
    onesf = kb.sb("l1a_onesf", (1, P), F32)
    s2bc = kb.sb("l1f_s2bc", (P, 4096), F32)
    dsrow = s2bc[0:1, :]
    stg_f = s2bc[:, 0:86 * 32]
    wg = kb.sb("l1f_wg", (P, 4096), BF16)
    wu = kb.sb("l1f_wu", (P, 4096), BF16)
    h_sb = kb.sb("l1f_h", (P, 4 * 512), BF16)
    gs_sb = kb.sb("l1f_gs", (P, 512), BF16)
    us_sb = kb.sb("l1f_us", (P, 512), BF16)
    ytmp = xst[:, 0:2 * 512]
    pS = [kb.ps("pS0", (P, 512), F32), kb.ps("pS1", (P, 512), F32)]
    pA = [kb.ps("pA%d" % k, (P, 512), F32) for k in range(4)]

    kb.dma(nc.sync, gscT[:, :], aps['gscT'][:, :], 'dSc')
    kb.dma(nc.sync, uscT[:, :], aps['uscT'][:, :], 'dSc')
    kb.dma(nc.sync, dscT[:, :], aps['dscT'][:, :], 'dSc')
    kb.dma(nc.gpsimd, gA_sb, aps['gA'][:, :], 'dGc')
    kb.dma(nc.gpsimd, uA_sb, aps['uA'][:, :], 'dGc')
    kb.dma(nc.gpsimd, A2_sb, aps['A2'][:, :], 'dGc')
    t_const = kb.tok('dSc')
    t_constg = kb.tok('dGc')
    kb.vpre(t_const, t_constg)
    kb.vpost(v.memset(onesb[:, :], 1.0))
    kb.vpre()
    kb.vpost(v.memset(onesf[:, :], 1.0))
    kb.vpre()
    kb.vpost(v.memset(m75[0:1, :], -7.5))
    kb.vpre()
    kb.vpost(v.tensor_scalar(gsc005[:, :], gscT[:, :], 0.005, None, ALU.mult))
    kb.vpre()
    kb.vpost(v.tensor_scalar(usc005[:, :], uscT[:, :], 0.005, None, ALU.mult))
    for k in range(8):
        kb.wait('S', nc.sync, st.get('l1f_xst_war'))
        tx = kb.dma(nc.sync, xst[:, :], res_in[P * k:P * (k + 1), :], 'dSx')
        tln = emit_ln_tile(kb, xst[:, :], xn[:, :], tx,
                           war_toks=(st.get('l1f_xn_war'),))
        st['l1f_xst_war'] = tln
        kb.gpre(tln)
        tg = kb.gpost(nc.gpsimd.tensor_copy(gd[:, :], xn[:, D - 64:D]))
        kb.wait('T', nc.tensor, tg, t_const)
        st['l1f_xn_war'] = emit_transpose_tile(kb, xn[:, :], xnT_r, P * k,
                                               identb[:, :])
    t_tr = (kb.tr_evac[0], kb.tr_evac[1])
    kb.gpre(kb.tok('cV'))
    tgo = kb.gpost(nc.gpsimd.tensor_copy(gd[0:1, 0:2], onesf[0:1, P - 2:P]))
    for tc in range(2):
        kb.wait('T', nc.tensor, tgo, st.get('l1f_ps0_war'), *t_tr)
        for f in range(32):
            ins = nc.tensor.matmul(pS[0][0:1, :], onesb[:, 0:1],
                                   xnT[:, 1024 * f + 512 * tc:1024 * f + 512 * (tc + 1)],
                                   start=(f == 0), stop=(f == 31))
        tt = kb.inc(ins, 'sT')
        kb.vpre(tt)
        st['l1f_ps0_war'] = kb.vpost(
            v.tensor_copy(xsrow[0:1, 512 * tc:512 * (tc + 1)], pS[0][0:1, :]))
    kb.gpre(kb.tok('cV'))
    tgxs = kb.gpost(nc.gpsimd.tensor_copy(gd[0:1, :], xsrow[0:1, 1024 - 64:1024]))

    tl = kb.dma(nc.sync, stg_f[:, :], aps['gBT'][:, :], 'dSb2')
    tb1 = _l1f_build_bp(kb, stg_f, tl, gscT[:, :], 86, gBp)
    kb.wait('S', nc.sync, kb.tok('cV'))
    tl = kb.dma(nc.sync, stg_f[:, :], aps['uBT'][:, :], 'dSb2')
    tb2 = _l1f_build_bp(kb, stg_f, tl, uscT[:, :], 86, uBp)
    kb.wait('S', nc.sync, kb.tok('cV'))
    tl = kb.dma(nc.sync, stg_f[:, 0:32 * 32], aps['B2T'][:, :], 'dSb2')
    tb3 = _l1f_build_bp(kb, stg_f, tl, dscT[:, :], 32, B2p)
    for (t1, A_sb, warkey) in ((t1g, gA_sb, 'l1f_t1g_war'),
                               (t1u, uA_sb, 'l1f_t1u_war')):
        for tc in range(2):
            kb.wait('T', nc.tensor, t_constg, st.get('l1f_ps1_war'), *t_tr)
            for f in range(32):
                ins = nc.tensor.matmul(
                    pS[1][0:32, :], A_sb[:, 32 * f:32 * (f + 1)],
                    xnT[:, 1024 * f + 512 * tc:1024 * f + 512 * (tc + 1)],
                    start=(f == 0), stop=(f == 31))
            tt = kb.inc(ins, 'sT')
            kb.vpre(tt)
            st['l1f_ps1_war'] = kb.vpost(
                v.tensor_copy(t1[0:32, 512 * tc:512 * (tc + 1)], pS[1][0:32, :]))
    kb.wait('S', nc.sync, kb.tok('cV'))
    tdsl = kb.dma(nc.sync, s2bc[0:1, :], aps['dsrow'][:, :], 'dSc2')
    kb.vpre(tdsl, st.get('l1f_inv_war'), st.get('l1f_bstage_war'))
    tds = kb.vpost(v.tensor_scalar(dsrow[0:1, :], dsrow[0:1, :], 0.005, None,
                                   ALU.mult))
    kb.gpre(tds)
    tgd = kb.gpost(nc.gpsimd.tensor_copy(gd[0:1, :],
                                         dsrow[0:1, 4096 - 64:4096]))
    for i in range(8):
        ps = pS[i % 2]
        kb.wait('T', nc.tensor, tgd, st.get('l1f_ps%d_war' % (i % 2)))
        ins = nc.tensor.matmul(ps[:, :], onesf[0:1, :],
                               dsrow[0:1, 512 * i:512 * (i + 1)],
                               start=True, stop=True)
        tt = kb.inc(ins, 'sT')
        kb.vpre(tt)
        st['l1f_ps%d_war' % (i % 2)] = kb.vpost(
            v.tensor_copy(s2bc[:, 512 * i:512 * (i + 1)], ps[:, :]))

    kb.gpre(kb.tok('cV'))
    tgr1 = kb.gpost(nc.gpsimd.tensor_copy(gd[0:32, 0:32],
                                          t1g[0:32, 1024 - 32:1024]))
    kb.gpre()
    tgrow = kb.gpost(nc.gpsimd.tensor_copy(gd[0:32, 0:32],
                                           t1u[0:32, 1024 - 32:1024]))

    wgs = [wg, kb.sb("l1f_wgb", (P, 4096), BF16)]
    wus = [wu, kb.sb("l1f_wub", (P, 4096), BF16)]
    wg_war = [None, None]
    wu_war = [None, None]
    ev_war = {'g': None, 'u': None}
    h_war = [None, None, None, None]
    hst = [None, None, None, None]
    for ot in range(86):
        par = ot % 2
        wgc, wuc = wgs[par], wus[par]
        kb.wait('G', nc.gpsimd, wg_war[par],
                st.get('l1f_pT_war') if ot == 0 else None)
        tqg = kb.dma(nc.gpsimd, wgc[:, :], aps['gqT'][P * ot:P * (ot + 1), :], 'dGg')
        kb.wait('G', nc.gpsimd, wu_war[par])
        tqu = kb.dma(nc.gpsimd, wuc[:, :], aps['uqT'][P * ot:P * (ot + 1), :], 'dAu')
        for tc in range(2):
            slot = 2 * par + tc
            pslot = 2 * (1 - par) + tc
            kb.wait('T', nc.tensor, tqg, ev_war['g'], tgrow, tgr1, tgo, tgxs,
                    tb1, tb2, tb3)
            for hc in range(32):
                ins = nc.tensor.matmul(
                    pS[0][:, :], wgc[:, 128 * hc:128 * (hc + 1)],
                    xnT[:, 1024 * hc + 512 * tc:1024 * hc + 512 * (tc + 1)],
                    start=(hc == 0), stop=False)
            ins = nc.tensor.matmul(pS[0][:, :], m75[0:1, 0:P],
                                   xsrow[0:1, 512 * tc:512 * (tc + 1)],
                                   start=False, stop=False)
            ins = nc.tensor.matmul(pS[0][:, :], gBp[0:32, 128 * ot:128 * (ot + 1)],
                                   t1g[0:32, 512 * tc:512 * (tc + 1)],
                                   start=False, stop=True)
            ttg = kb.inc(ins, 'sT')
            kb.wait('T', nc.tensor, tqu, ev_war['u'])
            for hc in range(32):
                ins = nc.tensor.matmul(
                    pS[1][:, :], wuc[:, 128 * hc:128 * (hc + 1)],
                    xnT[:, 1024 * hc + 512 * tc:1024 * hc + 512 * (tc + 1)],
                    start=(hc == 0), stop=False)
            ins = nc.tensor.matmul(pS[1][:, :], m75[0:1, 0:P],
                                   xsrow[0:1, 512 * tc:512 * (tc + 1)],
                                   start=False, stop=False)
            ins = nc.tensor.matmul(pS[1][:, :], uBp[0:32, 128 * ot:128 * (ot + 1)],
                                   t1u[0:32, 512 * tc:512 * (tc + 1)],
                                   start=False, stop=True)
            ttu = kb.inc(ins, 'sT')
            wu_war[par] = ttu
            if tc == 1:
                wg_war[par] = ttg
            kb.apre(ttg, st.get('l1f_gs_war'))
            tevg = kb.apost(a.activation(gs_sb[:, :], pS[0][:, :], AF.Silu,
                                         scale=gsc005[:, ot:ot + 1]))
            ev_war['g'] = tevg
            kb.vpre(ttu, st.get('l1f_us_war'))
            tevu = kb.vpost(v.tensor_scalar(us_sb[:, :], pS[1][:, :],
                                            usc005[:, ot:ot + 1], None, ALU.mult))
            ev_war['u'] = tevu
            kb.vpre(tevg, h_war[slot], hst[slot])
            th = kb.vpost(v.tensor_tensor(h_sb[:, 512 * slot:512 * (slot + 1)],
                                          gs_sb[:, :], us_sb[:, :], ALU.mult))
            st['l1f_gs_war'] = th
            st['l1f_us_war'] = th
            kb.wait('A', a, th)
            hst[slot] = kb.dma(nc.scalar,
                               aps['h_scratch'][P * ot:P * (ot + 1),
                                                512 * tc:512 * (tc + 1)],
                               h_sb[:, 512 * slot:512 * (slot + 1)],
                               'dGh%d' % slot)
            if ot > 0:
                kb.wait('T', nc.tensor, st.get('l1f_h_guard%d' % pslot))
                ins = nc.tensor.matmul(pA[2 + tc][0:32, :],
                                       A2_sb[:, 32 * (ot - 1):32 * ot],
                                       h_sb[:, 512 * pslot:512 * (pslot + 1)],
                                       start=(ot == 1), stop=False)
                h_war[pslot] = kb.inc(ins, 'sT')
            kb.gpre(th)
            st['l1f_h_guard%d' % slot] = kb.gpost(
                nc.gpsimd.tensor_copy(gd[:, :],
                                      h_sb[:, 512 * (slot + 1) - 64:512 * (slot + 1)]))
    for tc in range(2):
        slot = 2 * (85 % 2) + tc
        kb.wait('T', nc.tensor, st.get('l1f_h_guard%d' % slot))
        ins = nc.tensor.matmul(pA[2 + tc][0:32, :], A2_sb[:, 32 * 85:32 * 86],
                               h_sb[:, 512 * slot:512 * (slot + 1)],
                               start=False, stop=True)
        h_war[slot] = kb.inc(ins, 'sT')
        kb.apre(h_war[slot], st.get('l1f_t2_war'))
        st['l1f_t2_war'] = kb.apost(
            a.activation(t2[0:32, 512 * tc:512 * (tc + 1)], pA[2 + tc][0:32, :],
                         AF.Identity))
    kb.gpre(st['l1f_t2_war'])
    tgt2 = kb.gpost(nc.gpsimd.tensor_copy(gd[0:32, 0:32], t2[0:32, 992:1024]))

    kb.gpre()
    tpre = kb.dma(nc.gpsimd, res_out[:, :], res_in[:, :], 'dGcp')

    yst = [None, None]
    ytp_war = None
    for tc in range(2):
        hwaits = (kb.tok('dGh0'), kb.tok('dGh1'), kb.tok('dGh2'),
                  kb.tok('dGh3'), ('sT', kb.cnt['sT']), st.get('l1f_hcol_war'))
        th = []
        for (en, qe, _), (i0, i1) in zip(
                (('S', nc.sync, 0), ('A', nc.scalar, 0), ('G', nc.gpsimd, 0)),
                ((0, 29), (29, 58), (58, 86))):
            kb.wait(en, qe, *hwaits)
            th.append(kb.dma(
                qe, hcol[:, 512 * i0:512 * i1].rearrange("p (i t) -> p i t",
                                                         i=i1 - i0),
                aps['h_scratch'][P * i0:P * i1, 512 * tc:512 * (tc + 1)]
                .rearrange("(i p) t -> p i t", p=P), 'dSh'))
        kb.wait('T', nc.tensor, *th, st.get('l1f_ps0_war'))
        for i in range(86):
            ins = nc.tensor.matmul(pS[0][0:1, :], onesb[:, 0:1],
                                   hcol[:, 512 * i:512 * (i + 1)],
                                   start=(i == 0), stop=(i == 85))
        tt = kb.inc(ins, 'sT')
        kb.vpre(tt)
        trs = kb.vpost(v.tensor_copy(rsrow[0:1, 512 * tc:512 * (tc + 1)],
                                     pS[0][0:1, :]))
        st['l1f_ps0_war'] = trs
        kb.gpre(trs)
        tgrs = kb.gpost(nc.gpsimd.tensor_copy(
            gd[0:1, :], rsrow[0:1, 512 * (tc + 1) - 64:512 * (tc + 1)]))
        wd_tok = [None, None, None, None]
        wd_war = [None, None, None, None]
        QS = (('G', nc.gpsimd, 'dSd0'), ('A', nc.scalar, 'dSd1'),
              ('S', nc.sync, 'dSd2'))
        for o2c in range(8):
            for g in range(22):
                slot = g % 4
                n = min(4, 86 - 4 * g)
                qeng, qe, qsem = QS[g % 3]
                kb.wait(qeng, qe, wd_war[slot], st.get('l1f_t2_war'))
                wd_tok[slot] = kb.dma(
                    qe,
                    wd[:, 2048 * slot:2048 * slot + 512 * n]
                    .rearrange("p (i o) -> p i o", i=n),
                    aps['dqT'][(o2c * 86 + 4 * g) * P:(o2c * 86 + 4 * g + n) * P, :]
                    .rearrange("(i p) o -> p i o", p=P), qsem)
                kb.wait('T', nc.tensor, wd_tok[slot], tgt2, tgrs,
                        ytp_war if g == 0 else None)
                for m in range(n):
                    i = 4 * g + m
                    for k in range(4):
                        ins = nc.tensor.matmul(
                            pA[k][:, :],
                            hcol[:, 512 * i + 128 * k:512 * i + 128 * (k + 1)],
                            wd[:, 2048 * slot + 512 * m:2048 * slot + 512 * (m + 1)],
                            start=(i == 0), stop=False)
                wd_war[slot] = kb.inc(ins, 'sT')
            for k in range(4):
                ins = nc.tensor.matmul(
                    pA[k][:, :],
                    rsrow[0:1, 512 * tc + 128 * k:512 * tc + 128 * (k + 1)],
                    m75[0:1, :], start=False, stop=False)
                ins = nc.tensor.matmul(
                    pA[k][:, :],
                    t2[0:32, 512 * tc + 128 * k:512 * tc + 128 * (k + 1)],
                    B2p[0:32, 512 * o2c:512 * (o2c + 1)],
                    start=False, stop=True)
                tt = kb.inc(ins, 'sT')
                ypar = k % 2
                kb.vpre(tt, yst[ypar])
                tev = kb.vpost(v.tensor_tensor(
                    ytmp[:, 512 * ypar:512 * (ypar + 1)], pA[k][:, :],
                    s2bc[:, 512 * o2c:512 * (o2c + 1)], ALU.mult))
                kb.gpre(tev, tpre)
                yst[ypar] = kb.dma(
                    nc.gpsimd,
                    res_out[512 * tc + P * k:512 * tc + P * (k + 1),
                            512 * o2c:512 * (o2c + 1)],
                    ytmp[:, 512 * ypar:512 * (ypar + 1)], 'dGy%d' % ypar,
                    accum_op=ALU.add)
            ytp_war = kb.tok('cV')
        st['l1f_hcol_war'] = kb.tok('sT')
    kb.wait('S', nc.sync, kb.tok('dGy0'), kb.tok('dGy1'))


def _prep_l0f_weights(q_packed, q_scales, q_zeros, delta_weight, bias):
    b = np.ascontiguousarray(q_packed).view(np.uint8).reshape(-1, 4)[:, 0]
    b2 = b.reshape(4096, 2048)
    qbT = np.ascontiguousarray(
        b2.reshape(32, 128, 16, 128).transpose(0, 3, 2, 1)).reshape(32 * 128, 2048)
    qbT = np.concatenate([qbT & np.uint8(15), qbT >> np.uint8(4)], axis=1)
    F, JC, Q = np.meshgrid(np.arange(128), np.arange(16), np.arange(2),
                           indexing='ij')
    ii = 2 * (128 * JC + F) + Q
    s2d = q_scales.reshape(4096, 32)
    cc = ii // 128
    sexpT = np.ascontiguousarray(
        s2d[:, cc.ravel()].reshape(4096, 128, 16, 2)
        .transpose(1, 2, 3, 0).reshape(128, 16, 2, 32, 128)
        .transpose(3, 0, 1, 2, 4)).reshape(32 * 128, 4096).astype(np.float32)
    deltaT = np.ascontiguousarray(
        delta_weight[:, ii.ravel()].reshape(4096, 128, 16, 2)
        .transpose(1, 2, 3, 0).reshape(128, 16, 2, 32, 128)
        .transpose(3, 0, 1, 2, 4)).reshape(32 * 128, 4096).astype(BF)
    s2dT = np.ascontiguousarray(q_scales.reshape(4096, 32).T)
    z2dT = np.ascontiguousarray(q_zeros.reshape(4096, 32).T)
    indE = np.zeros((128, 32, 32), dtype=BF)
    for pc in range(32):
        q, jc = pc // 16, pc % 16
        for f in range(128):
            c = 2 * jc + (1 if f >= 64 else 0)
            indE[f, pc, c] = 1.0
    indE = indE.reshape(128, 1024)
    biasT = np.ascontiguousarray(bias.reshape(32, 128).T).astype(np.float32)
    return {"qbT": qbT, "sexpT": sexpT, "deltaT": deltaT, "s2dT": s2dT,
            "z2dT": z2dT, "indE": indE, "biasT": biasT,
            "identb": np.eye(P, dtype=BF),
            "identf": np.eye(P, dtype=np.float32)}


def _l0f_aps(nc):
    return {
        'qbT': nc.dram_tensor("qbT", [32 * P, 4096], U8, kind="ExternalInput").ap(),
        'sexpT': nc.dram_tensor("sexpT", [32 * P, 4096], F32, kind="ExternalInput").ap(),
        'deltaT': nc.dram_tensor("deltaT", [32 * P, 4096], BF16, kind="ExternalInput").ap(),
        's2dT': nc.dram_tensor("s2dT", [32, 4096], F32, kind="ExternalInput").ap(),
        'z2dT': nc.dram_tensor("z2dT", [32, 4096], F32, kind="ExternalInput").ap(),
        'indE': nc.dram_tensor("indE", [P, 1024], BF16, kind="ExternalInput").ap(),
        'biasT': nc.dram_tensor("biasT", [P, 32], F32, kind="ExternalInput").ap(),
        'identb': _dram(nc, "identb", [P, P], BF16),
        'identf': _dram(nc, "identf", [P, P], F32),
        'weff_scratch': nc.dram_tensor("weff_scratch", [32 * P, 4096], BF16,
                                       kind="Internal").ap(),
    }


def build_l0f_program():
    nc = bass.Bass("TRN2", target_bir_lowering=False)
    aps = _l0f_aps(nc)
    aps['res_in'] = nc.dram_tensor("res_in", [TOKC, D], F32, kind="ExternalInput").ap()
    aps['res_out'] = nc.dram_tensor("res_out", [TOKC, D], F32, kind="ExternalOutput").ap()
    kb = KB(nc)
    kb.OFFSETS = _mk_offsets()
    emit_l0_ffn(kb, aps)
    kb.wait('S', nc.sync, kb.tok('dV'))
    kb.close()
    return nc


def kernel_l0f_only(r1_full, q_packed, q_scales, q_zeros, delta_weight, bias):
    r1 = np.asarray(r1_full).reshape(T, D)
    w = _prep_l0f_weights(np.asarray(q_packed), np.asarray(q_scales),
                          np.asarray(q_zeros), np.asarray(delta_weight),
                          np.asarray(bias))
    nc = build_l0f_program()
    in_maps = []
    for c in range(NCORE):
        m = dict(w)
        m['res_in'] = np.ascontiguousarray(
            np.concatenate([r1[STR * c:STR * (c + 1)],
                            r1[STR * (c + 8):STR * (c + 9)]]))
        in_maps.append(m)
    res = run_bass_kernel_spmd(nc, in_maps, core_ids=list(range(NCORE)))
    out = np.empty((T, D), np.float32)
    for c in range(NCORE):
        r = np.asarray(res.results[c]["res_out"])
        out[STR * c:STR * (c + 1)] = r[:STR]
        out[STR * (c + 8):STR * (c + 9)] = r[STR:]
    return out.reshape(1, T, D)


def _prep_l1a_weights(p):
    wd = np.ascontiguousarray(
        p['w_down'].T.reshape(32, 128, 128).transpose(1, 0, 2).reshape(128, 4096)
    ).astype(np.float32)
    wu = np.ascontiguousarray(p['w_up'].T).astype(np.float32)
    inw = np.ascontiguousarray(p['hy_in_w'][:512, :].T).astype(np.float32)
    outw = np.ascontiguousarray(
        p['hy_out_w'].T.reshape(2, 128, 128).transpose(1, 0, 2).reshape(128, 256)
    ).astype(np.float32)
    fr = p['hy_fcD_w'][np.repeat(np.arange(4), 64), :].T
    fcd = np.ascontiguousarray(
        fr.reshape(2, 128, 2, 128).transpose(1, 0, 2, 3).reshape(128, 512)
    ).astype(np.float32)
    cw = p['hy_conv_w'][0:256, 0, 6].reshape(2, 128).T
    cb = p['hy_conv_b'][0:256].reshape(2, 128).T
    cwcb = np.ascontiguousarray(np.concatenate([cw, cb], 1)).astype(np.float32)
    drep = np.ascontiguousarray(
        np.repeat(p['hy_D'], 64).reshape(2, 128).T).astype(np.float32)
    normw = np.ascontiguousarray(
        p['hy_norm_w'].reshape(2, 128).T).astype(np.float32)
    return {
        'wdT': wd, 'wupT': wu, 'inwT': inw, 'outwT': outw, 'fcdT': fcd,
        'cwcb': cwcb, 'drep': drep, 'normw': normw,
        'bdown': np.ascontiguousarray(p['b_down'].reshape(128, 1)).astype(np.float32),
        'bupT': np.ascontiguousarray(p['b_up'].reshape(32, 128).T).astype(np.float32),
        'g1': np.asarray(p['gate1']).reshape(1, 1).astype(np.float32),
        'identb': np.eye(P, dtype=BF),
        'identf': np.eye(P, dtype=np.float32),
    }


def _l1a_aps(nc):
    d = {'identb': _dram(nc, "identb", [P, P], BF16),
         'identf': _dram(nc, "identf", [P, P], F32)}
    for nm, shp, dt in (('wdT', (P, 4096), F32), ('wupT', (P, 4096), F32),
                        ('inwT', (P, 512), F32), ('outwT', (P, 256), F32),
                        ('fcdT', (P, 512), F32), ('cwcb', (P, 4), F32),
                        ('drep', (P, 2), F32), ('normw', (P, 2), F32),
                        ('bdown', (P, 1), F32), ('bupT', (P, 32), F32),
                        ('g1', (1, 1), F32), ):
        d[nm] = nc.dram_tensor(nm, list(shp), dt, kind="ExternalInput").ap()
    return d


def build_l1a_program():
    nc = bass.Bass("TRN2", target_bir_lowering=False)
    aps = _l1a_aps(nc)
    aps['res_in'] = nc.dram_tensor("res_in", [TOKC, D], F32, kind="ExternalInput").ap()
    aps['res_out'] = nc.dram_tensor("res_out", [TOKC, D], F32, kind="ExternalOutput").ap()
    kb = KB(nc)
    kb.OFFSETS = _mk_offsets()
    emit_l1_attn(kb, aps)
    kb.wait('S', nc.sync, kb.tok('dV'))
    kb.close()
    return nc


def kernel_l1a_only(r2_full, p):
    r2 = np.asarray(r2_full).reshape(T, D)
    w = _prep_l1a_weights(p)
    nc = build_l1a_program()
    in_maps = []
    for c in range(NCORE):
        m = dict(w)
        m['res_in'] = np.ascontiguousarray(
            np.concatenate([r2[STR * c:STR * (c + 1)],
                            r2[STR * (c + 8):STR * (c + 9)]]))
        in_maps.append(m)
    res = run_bass_kernel_spmd(nc, in_maps, core_ids=list(range(NCORE)))
    out = np.empty((T, D), np.float32)
    for c in range(NCORE):
        r = np.asarray(res.results[c]["res_out"])
        out[STR * c:STR * (c + 1)] = r[:STR]
        out[STR * (c + 8):STR * (c + 9)] = r[STR:]
    return out.reshape(1, T, D)


def _prep_l1f_weights(i4):
    def qbytes(q):
        return np.ascontiguousarray(q).view(np.uint8).reshape(-1, 4)[:, 0]
    gb = qbytes(i4['gate_q']).reshape(11008, 4096)
    ub = qbytes(i4['up_q']).reshape(11008, 4096)
    db = qbytes(i4['down_q']).reshape(4096, 11008)
    gqT = np.ascontiguousarray(
        gb.T.reshape(32, 128, 86, 128).transpose(2, 1, 0, 3)).reshape(86 * 128, 4096)
    uqT = np.ascontiguousarray(
        ub.T.reshape(32, 128, 86, 128).transpose(2, 1, 0, 3)).reshape(86 * 128, 4096)
    dqT = np.ascontiguousarray(
        db.T.reshape(86, 128, 8, 512).transpose(2, 0, 1, 3)).reshape(
            8 * 86 * 128, 512).astype(BF)
    gBT = np.ascontiguousarray(
        i4['gate_B'].T.reshape(86, 128, 32).transpose(1, 0, 2)).reshape(128, 86 * 32)
    uBT = np.ascontiguousarray(
        i4['up_B'].T.reshape(86, 128, 32).transpose(1, 0, 2)).reshape(128, 86 * 32)
    B2T = np.ascontiguousarray(
        i4['down_B'].T.reshape(32, 128, 32).transpose(1, 0, 2)).reshape(128, 32 * 32)
    gA = np.ascontiguousarray(
        i4['gate_A'].reshape(32, 128, 32).transpose(1, 0, 2)).reshape(128, 1024)
    uA = np.ascontiguousarray(
        i4['up_A'].reshape(32, 128, 32).transpose(1, 0, 2)).reshape(128, 1024)
    A2 = np.ascontiguousarray(
        i4['down_A'].reshape(86, 128, 32).transpose(1, 0, 2)).reshape(128, 86 * 32)
    return {
        'gqT': gqT, 'uqT': uqT, 'dqT': dqT,
        'gBT': gBT.astype(np.float32), 'uBT': uBT.astype(np.float32),
        'B2T': B2T.astype(np.float32),
        'gA': gA.astype(np.float32), 'uA': uA.astype(np.float32),
        'A2': A2.astype(np.float32),
        'gscT': np.ascontiguousarray(i4['gate_scale'].reshape(86, 128).T).astype(np.float32),
        'uscT': np.ascontiguousarray(i4['up_scale'].reshape(86, 128).T).astype(np.float32),
        'dscT': np.ascontiguousarray(i4['down_scale'].reshape(32, 128).T).astype(np.float32),
        'dsrow': np.ascontiguousarray(i4['down_scale'].reshape(1, 4096)).astype(np.float32),
        'identb': np.eye(P, dtype=BF),
    }


def _l1f_aps(nc):
    d = {
        'gqT': nc.dram_tensor("gqT", [86 * P, 4096], U8, kind="ExternalInput").ap(),
        'uqT': nc.dram_tensor("uqT", [86 * P, 4096], U8, kind="ExternalInput").ap(),
        'dqT': nc.dram_tensor("dqT", [8 * 86 * P, 512], BF16, kind="ExternalInput").ap(),
        'gBT': nc.dram_tensor("gBT", [P, 86 * 32], F32, kind="ExternalInput").ap(),
        'uBT': nc.dram_tensor("uBT", [P, 86 * 32], F32, kind="ExternalInput").ap(),
        'B2T': nc.dram_tensor("B2T", [P, 32 * 32], F32, kind="ExternalInput").ap(),
        'gA': nc.dram_tensor("gA", [P, 1024], F32, kind="ExternalInput").ap(),
        'uA': nc.dram_tensor("uA", [P, 1024], F32, kind="ExternalInput").ap(),
        'A2': nc.dram_tensor("A2", [P, 86 * 32], F32, kind="ExternalInput").ap(),
        'gscT': nc.dram_tensor("gscT", [P, 86], F32, kind="ExternalInput").ap(),
        'uscT': nc.dram_tensor("uscT", [P, 86], F32, kind="ExternalInput").ap(),
        'dscT': nc.dram_tensor("dscT", [P, 32], F32, kind="ExternalInput").ap(),
        'dsrow': nc.dram_tensor("dsrow", [1, 4096], F32, kind="ExternalInput").ap(),
        'identb': _dram(nc, "identb", [P, P], BF16),
        'h_scratch': nc.dram_tensor("h_scratch", [86 * P, 1024], BF16,
                                    kind="Internal").ap(),
    }
    return d


def build_l1f_program():
    nc = bass.Bass("TRN2", target_bir_lowering=False)
    aps = _l1f_aps(nc)
    aps['res_in'] = nc.dram_tensor("res_in", [TOKC, D], F32, kind="ExternalInput").ap()
    aps['res_out'] = nc.dram_tensor("res_out", [TOKC, D], F32, kind="ExternalOutput").ap()
    kb = KB(nc)
    kb.OFFSETS = _mk_offsets()
    emit_l1_ffn(kb, aps)
    kb.close()
    return nc


def kernel_l1f_only(r3_full, i4):
    r3 = np.asarray(r3_full).reshape(T, D)
    w = _prep_l1f_weights(i4)
    nc = build_l1f_program()
    in_maps = []
    for c in range(NCORE):
        m = dict(w)
        m['res_in'] = np.ascontiguousarray(
            np.concatenate([r3[STR * c:STR * (c + 1)],
                            r3[STR * (c + 8):STR * (c + 9)]]))
        in_maps.append(m)
    res = run_bass_kernel_spmd(nc, in_maps, core_ids=list(range(NCORE)))
    out = np.empty((T, D), np.float32)
    for c in range(NCORE):
        r = np.asarray(res.results[c]["res_out"])
        out[STR * c:STR * (c + 1)] = r[:STR]
        out[STR * (c + 8):STR * (c + 9)] = r[STR:]
    return out.reshape(1, T, D)


def _core_slices(x):
    return [np.ascontiguousarray(np.concatenate(
        [x[STR * c:STR * (c + 1)], x[STR * (c + 8):STR * (c + 9)]]))
        for c in range(NCORE)]


def _gather(res, key="res_out"):
    out = np.empty((T, D), np.float32)
    for c in range(NCORE):
        r = np.asarray(res.results[c][key])
        out[STR * c:STR * (c + 1)] = r[:STR]
        out[STR * (c + 8):STR * (c + 9)] = r[STR:]
    return out


def kernel_multilaunch(L0_attn_x, L0_attn_bv_q, L0_attn_bv_k, L0_attn_bv_v,
           L0_ffn_q_packed, L0_ffn_q_scales, L0_ffn_q_zeros,
           L0_ffn_delta_weight, L0_ffn_bias,
           L1_attn_w_down, L1_attn_b_down, L1_attn_w_up, L1_attn_b_up,
           L1_attn_gate1, L1_attn_hy_in_w, L1_attn_hy_conv_w, L1_attn_hy_conv_b,
           L1_attn_hy_dt_bias, L1_attn_hy_A_log, L1_attn_hy_D, L1_attn_hy_fcD_w,
           L1_attn_hy_norm_w, L1_attn_hy_out_w,
           L1_ffn_gate_q, L1_ffn_gate_scale, L1_ffn_gate_A, L1_ffn_gate_B,
           L1_ffn_up_q, L1_ffn_up_scale, L1_ffn_up_A, L1_ffn_up_B,
           L1_ffn_down_q, L1_ffn_down_scale, L1_ffn_down_A, L1_ffn_down_B):
    x = np.ascontiguousarray(np.asarray(L0_attn_x).reshape(T, D))
    nc = build_l0a_program()
    in_maps = [_prep_l0a(c, x, np.asarray(L0_attn_bv_q),
                         np.asarray(L0_attn_bv_k), np.asarray(L0_attn_bv_v))
               for c in range(NCORE)]
    res = run_bass_kernel_spmd(nc, in_maps, core_ids=list(range(NCORE)))
    r1 = _gather(res)
    w = _prep_l0f_weights(np.asarray(L0_ffn_q_packed), np.asarray(L0_ffn_q_scales),
                          np.asarray(L0_ffn_q_zeros), np.asarray(L0_ffn_delta_weight),
                          np.asarray(L0_ffn_bias))
    nc = build_l0f_program()
    sl = _core_slices(r1)
    res = run_bass_kernel_spmd(nc, [dict(w, res_in=sl[c]) for c in range(NCORE)],
                               core_ids=list(range(NCORE)))
    r2 = _gather(res)
    p = {'w_down': np.asarray(L1_attn_w_down), 'b_down': np.asarray(L1_attn_b_down),
         'w_up': np.asarray(L1_attn_w_up), 'b_up': np.asarray(L1_attn_b_up),
         'gate1': np.asarray(L1_attn_gate1), 'hy_in_w': np.asarray(L1_attn_hy_in_w),
         'hy_conv_w': np.asarray(L1_attn_hy_conv_w),
         'hy_conv_b': np.asarray(L1_attn_hy_conv_b),
         'hy_D': np.asarray(L1_attn_hy_D), 'hy_fcD_w': np.asarray(L1_attn_hy_fcD_w),
         'hy_norm_w': np.asarray(L1_attn_hy_norm_w),
         'hy_out_w': np.asarray(L1_attn_hy_out_w)}
    w = _prep_l1a_weights(p)
    nc = build_l1a_program()
    sl = _core_slices(r2)
    res = run_bass_kernel_spmd(nc, [dict(w, res_in=sl[c]) for c in range(NCORE)],
                               core_ids=list(range(NCORE)))
    r3 = _gather(res)
    i4 = {'gate_q': np.asarray(L1_ffn_gate_q), 'gate_scale': np.asarray(L1_ffn_gate_scale),
          'gate_A': np.asarray(L1_ffn_gate_A), 'gate_B': np.asarray(L1_ffn_gate_B),
          'up_q': np.asarray(L1_ffn_up_q), 'up_scale': np.asarray(L1_ffn_up_scale),
          'up_A': np.asarray(L1_ffn_up_A), 'up_B': np.asarray(L1_ffn_up_B),
          'down_q': np.asarray(L1_ffn_down_q), 'down_scale': np.asarray(L1_ffn_down_scale),
          'down_A': np.asarray(L1_ffn_down_A), 'down_B': np.asarray(L1_ffn_down_B)}
    w = _prep_l1f_weights(i4)
    nc = build_l1f_program()
    sl = _core_slices(r3)
    res = run_bass_kernel_spmd(nc, [dict(w, res_in=sl[c]) for c in range(NCORE)],
                               core_ids=list(range(NCORE)))
    r4 = _gather(res)
    return r4.reshape(1, T, D)


_BASE = 16512
_BS = 194944 + _BASE


def _mk_offsets():
    o = {}
    o['xst0'] = 176128
    small = [('identb', 256), ('identf', 512), ('gdscr', 256), ('gdscrf', 256),
             ('ln_bst', 256), ('ln_mv', 64), ('ones1b', 256),
             ('l1a_onesb', 64), ('l1a_onesf', 512)]
    off = 192512
    for nm, sz in small:
        o[nm] = off
        off += sz
    for nm, a in (('acc', 0), ('kqQT', 65536), ('vbb', 65536), ('xnT', 98304),
                  ('kqb', 98304), ('xn_s', 131072), ('l0a_xep', 131072),
                  ('mask_sb', 172032), ('l0a_xb', 176128), ('attnT', 194944)):
        o[nm] = a
    o['gdscr2'] = 211328
    o['gdscr3'] = 211456
    for nm, a in (('l0f_res', 0), ('xnpT', 65536), ('l0f_wt', 131072),
                  ('l0f_sexp', 147456), ('l0f_delta', 163840),
                  ('l0f_xnp', 147456), ('l0f_qb', 172032)):
        o[nm] = a
    for nm, sz in (('l0f_bias', 128), ('indE', 2048), ('l0f_xsum', 2048),
                   ('l0f_zs', 8192)):
        o[nm] = o.get('_l0f_cur', 194944)
        o['_l0f_cur'] = o[nm] + sz
    o.pop('_l0f_cur')
    for nm, a in (('l1a_res', 0), ('l1a_xn1', 65536), ('l1a_x1T', 98304),
                  ('l1a_wdT', 131072), ('l1a_wupT', 139264)):
        o[nm] = a
    cur = 147456
    for nm, sz in (('l1a_inwT', 1024), ('l1a_outwT', 512), ('l1a_fcdT', 1024),
                   ('l1a_cwcb', 64), ('l1a_drep', 64), ('l1a_normw', 64),
                   ('l1a_bdown', 64), ('l1a_bupT', 128), ('l1a_g1', 64),
                   ('l1a_alf', 64), ('l1a_rr', 2048), ('l1a_zsil', 2048),
                   ('l1a_xs', 2048), ('l1a_y', 2048), ('l1a_yz', 2048),
                   ('l1a_yz2', 2048), ('l1a_y2', 2048), ('l1a_h0T', 1024),
                   ('l1a_h1T', 1024), ('l1a_h2T', 1024), ('l1a_hfin', 1024),
                   ('l1a_utile', 4096)):
        o[nm] = cur
        cur += sz
    for nm, a in (('l1f_hcol', 0), ('l1f_s2bc', 88064), ('l1f_uBp', 104448),
                  ('l1f_B2p', 126976), ('l1f_wd', 135168), ('l1f_wg', 151552),
                  ('l1f_wu', 159744), ('l1f_xn', 167936),
                  ('l1f_wgb', 176128), ('l1f_wub', 184320)):
        o[nm] = a
    cur = 194944
    for nm, sz in (('l1f_gsc', 384), ('l1f_usc', 384), ('l1f_dsc', 128),
                   ('l1f_gsc5', 384), ('l1f_usc5', 384), ('l1f_inv', 384),
                   ('l1f_xsrow', 2048), ('l1f_rsrow', 2048), ('l1f_m75', 1024),
                   ('l1f_t1g', 2048), ('l1f_t1u', 2048), ('l1f_t2', 2048),
                   ('l1f_gs', 1024), ('l1f_us', 1024)):
        o[nm] = cur
        cur += sz
    o['l1f_h'] = 167936 + 4096
    return {k: v + _BASE for k, v in o.items()}


def build_fused_program():
    nc = bass.Bass("TRN2", target_bir_lowering=False)
    aps0 = _l0a_aps(nc)
    aps1 = _l0f_aps(nc)
    aps2 = _l1a_aps(nc)
    aps3 = _l1f_aps(nc)
    res1 = nc.dram_tensor("res1", [TOKC, D], F32, kind="Internal").ap()
    res2 = nc.dram_tensor("res2", [TOKC, D], F32, kind="Internal").ap()
    res3 = nc.dram_tensor("res3", [TOKC, D], F32, kind="Internal").ap()
    out = nc.dram_tensor("res_out", [TOKC, D], F32, kind="ExternalOutput").ap()
    aps0['res_out'] = res1
    aps1['res_in'], aps1['res_out'] = res1, res2
    aps2['res_in'], aps2['res_out'] = res2, res3
    aps3['res_in'], aps3['res_out'] = res3, out
    kb = KB(nc)
    kb.OFFSETS = _mk_offsets()
    emit_l0_attn(kb, aps0)
    kb.block_barrier()
    emit_l0_ffn(kb, aps1)
    kb.block_barrier()
    emit_l1_attn(kb, aps2)
    kb.block_barrier()
    emit_l1_ffn(kb, aps3)
    kb.block_barrier()
    kb.close()
    return nc


def kernel_fused(inp):
    x = np.ascontiguousarray(np.asarray(inp['L0_attn_x']).reshape(T, D))
    w0 = [_prep_l0a(c, x, np.asarray(inp['L0_attn_bv_q']),
                    np.asarray(inp['L0_attn_bv_k']),
                    np.asarray(inp['L0_attn_bv_v'])) for c in range(NCORE)]
    w1 = _prep_l0f_weights(np.asarray(inp['L0_ffn_q_packed']),
                           np.asarray(inp['L0_ffn_q_scales']),
                           np.asarray(inp['L0_ffn_q_zeros']),
                           np.asarray(inp['L0_ffn_delta_weight']),
                           np.asarray(inp['L0_ffn_bias']))
    p = {k[len('L1_attn_'):]: np.asarray(v) for k, v in inp.items()
         if k.startswith('L1_attn_')}
    w2 = _prep_l1a_weights(p)
    i4 = {k[len('L1_ffn_'):]: np.asarray(v) for k, v in inp.items()
          if k.startswith('L1_ffn_')}
    w3 = _prep_l1f_weights(i4)
    nc = build_fused_program()
    in_maps = []
    for c in range(NCORE):
        m = {}
        m.update(w0[c])
        m.update(w1)
        m.update(w2)
        m.update(w3)
        in_maps.append(m)
    res = run_bass_kernel_spmd(nc, in_maps, core_ids=list(range(NCORE)))
    return _gather(res).reshape(1, T, D)


def kernel(**inputs):
    return kernel_fused(inputs)


# version 15
# speedup vs baseline: 1.0155x; 1.0155x over previous
import contextlib
import numpy as np
import ml_dtypes
import concourse.bass as bass
import concourse.mybir as mybir
from concourse.bass_utils import run_bass_kernel_spmd

F32 = mybir.dt.float32
F16 = mybir.dt.float16
BF16 = mybir.dt.bfloat16
U8 = mybir.dt.uint8
AF = mybir.ActivationFunctionType
ALU = mybir.AluOpType
AX = mybir.AxisListType
BF = ml_dtypes.bfloat16

P = 128
NCORE = 8
T = 8192
D = 4096
STR = 512
TOKC = 2 * STR
EXT = (8, 16)
LN_EPS = 1e-5


def _dram(nc, name, shape, dtype, kind="ExternalInput"):
    cache = getattr(nc, '_dram_cache', None)
    if cache is None:
        cache = {}
        nc._dram_cache = cache
    if name not in cache:
        cache[name] = nc.dram_tensor(name, list(shape), dtype, kind=kind).ap()
    return cache[name]


class KB:
    def __init__(self, nc):
        self.nc = nc
        self.stack = contextlib.ExitStack()
        self.sems = {}
        self.cnt = {}
        self.sbufs = {}
        self.psums = {}
        self.waited = {}
        self.ln_prev = None
        self.tr_evac = [None, None]
        self.state = {}

    def sem(self, name):
        if name not in self.sems:
            self.sems[name] = self.stack.enter_context(self.nc.semaphore(name))
            self.cnt[name] = 0
        return self.sems[name]

    OFFSETS = {}

    def sb(self, name, shape, dtype):
        if name not in self.sbufs:
            if self.OFFSETS:
                assert name in self.OFFSETS, "no offset for %s" % name
            if name in self.OFFSETS:
                self.sbufs[name] = self.nc.alloc_sbuf_tensor_at(
                    "sb_" + name, list(shape), dtype, offset=self.OFFSETS[name])
            else:
                self.sbufs[name] = self.stack.enter_context(
                    self.nc.sbuf_tensor("sb_" + name, list(shape), dtype))
        return self.sbufs[name]

    def block_barrier(self):
        nc = self.nc
        toks = [self.tok('cV'), self.tok('cA'), self.tok('sT'), self.tok('cG')]
        toks += [self.tok(s) for s in list(self.cnt) if s.startswith('d')]
        for ename, eng in (('T', nc.tensor), ('V', nc.vector), ('A', nc.scalar),
                           ('S', nc.sync), ('G', nc.gpsimd)):
            self.wait(ename, eng, *toks)

    def ps(self, name, shape, dtype=F32):
        if name not in self.psums:
            self.psums[name] = self.stack.enter_context(
                self.nc.psum_tensor("ps_" + name, list(shape), dtype))
        return self.psums[name]

    def dma(self, q, out, in_, sem_name, **kw):
        self.sem(sem_name)
        q.dma_start(out=out, in_=in_, **kw).then_inc(self.sems[sem_name], 16)
        self.cnt[sem_name] += 16
        return (sem_name, self.cnt[sem_name])

    def inc(self, instr, sem_name, amt=1):
        self.sem(sem_name)
        instr.then_inc(self.sems[sem_name], amt)
        self.cnt[sem_name] += amt
        return (sem_name, self.cnt[sem_name])

    def tok(self, sem_name):
        self.sem(sem_name)
        return (sem_name, self.cnt[sem_name])

    def wait(self, ename, eng, *toks):
        wmap = self.waited.setdefault(ename, {})
        best = {}
        for t in toks:
            if t is None:
                continue
            s, v = t
            if v > best.get(s, 0):
                best[s] = v
        for s, v in best.items():
            if v > 0 and wmap.get(s, 0) < v:
                eng.wait_ge(self.sems[s], v)
                wmap[s] = v

    def vpre(self, *toks):
        self.wait('V', self.nc.vector, self.tok('cV'), *toks)

    def vpost(self, ins):
        return self.inc(ins, 'cV')

    def apre(self, *toks):
        self.wait('A', self.nc.scalar, self.tok('cA'), *toks)

    def apost(self, ins):
        return self.inc(ins, 'cA')

    def gpre(self, *toks):
        self.wait('G', self.nc.gpsimd, self.tok('cG'), *toks)

    def gpost(self, ins):
        return self.inc(ins, 'cG')

    def close(self):
        self.stack.close()


def emit_ln_stats(kb, xsrc, x_tok):
    nc = kb.nc
    v, a = nc.vector, nc.scalar
    bst = kb.sb("ln_bst", (P, 48), F32)
    mv = kb.sb("ln_mv", (P, 8), F32)
    kb.vpre(x_tok, kb.ln_prev)
    for i in range(8):
        kb.vpost(v.bn_stats(bst[:, 6 * i:6 * (i + 1)],
                            xsrc[:, 512 * i:512 * (i + 1)]))
        kb.vpre()
    tva = kb.vpost(v.bn_aggr(mv[:, 0:2], bst[:, 0:48]))
    kb.vpre()
    tv = kb.vpost(v.tensor_scalar(mv[:, 2:3], mv[:, 1:2], LN_EPS, None, ALU.add))
    kb.apre(tv)
    ta = kb.apost(a.activation(mv[:, 3:4], mv[:, 2:3], AF.Sqrt))
    kb.vpre(ta)
    kb.vpost(v.reciprocal(mv[:, 4:5], mv[:, 3:4]))
    kb.vpre()
    kb.vpost(v.scalar_tensor_tensor(mv[:, 5:6], mv[:, 0:1], -1.0,
                                    mv[:, 4:5], ALU.mult, ALU.mult))
    kb.vpre()
    tv2 = kb.vpost(v.tensor_tensor(mv[:, 6:7], mv[:, 4:5], mv[:, 5:6], ALU.mult))
    return tv2, mv


def emit_ln_tile(kb, xsrc, xn_out, x_tok, war_toks=()):
    nc = kb.nc
    tv2, mv = emit_ln_stats(kb, xsrc, x_tok)
    kb.apre(tv2, *[t for t in war_toks if t is not None])
    t = kb.apost(nc.scalar.activation(xn_out, xsrc, AF.Identity,
                                      scale=mv[:, 4:5], bias=mv[:, 5:6]))
    kb.ln_prev = t
    return t


def emit_transpose_tile(kb, src, dest_r, tcol0, identb, evac_wait=None):
    nc = kb.nc
    pT = [kb.ps("pT0", (P, 1024), BF16), kb.ps("pT1", (P, 1024), BF16)]
    tt = None
    for fg in range(4):
        pt = pT[fg % 2]
        kb.wait('T', nc.tensor, kb.tr_evac[fg % 2])
        for f8 in range(8):
            f = 8 * fg + f8
            ins = nc.tensor.transpose(pt[:, 128 * f8:128 * (f8 + 1)],
                                      src[:, 128 * f:128 * (f + 1)], identb)
        tt = kb.inc(ins, 'sT')
        kb.vpre(tt, evac_wait)
        ins = nc.vector.tensor_copy(
            dest_r[:, 8 * fg:8 * (fg + 1), tcol0:tcol0 + 128],
            pt[:, 0:1024].rearrange("p (f t) -> p f t", f=8))
        kb.tr_evac[fg % 2] = kb.vpost(ins)
    return tt


DEBUG_L0A = False


def emit_l0_attn(kb, aps):
    nc = kb.nc
    v, a = nc.vector, nc.scalar
    xfull, xown = aps['xfull'], aps['xown']
    maskstr, res_out = aps['maskstr'], aps['res_out']
    kqv_d = aps['kqv_scratch']

    identb = kb.sb("identb", (P, P), BF16)
    kb.dma(nc.sync, identb[:, :], aps['identb'][:, :], 'dSc')

    acc = kb.sb("acc", (P, 4 * D), F32)
    bv = acc
    sc1 = acc[0:4, 3 * 1024:4 * 1024]
    scal_a = acc[0:4, 4 * 1024:4 * 1024 + 1]
    scal_b = acc[0:4, 4 * 1024 + 8:4 * 1024 + 9]
    for i, key in enumerate(('bvq', 'bvk', 'bvv')):
        kb.dma(nc.sync, bv[0:4, 1024 * i:1024 * (i + 1)], aps[key][:, :], 'dSb')
    t_ball = kb.tok('dSb')
    for i in range(3):
        sl = bv[0:4, 1024 * i:1024 * (i + 1)]
        kb.vpre(t_ball)
        kb.vpost(v.reduce_sum(scal_a, sl, axis=AX.X, apply_absolute_value=True))
        kb.vpre()
        tv = kb.vpost(v.tensor_scalar(scal_b, scal_a, 1.0 / 1024, None, ALU.mult))
        kb.apre(tv)
        ta = kb.apost(a.activation(sc1, sl, AF.Sign))
        kb.vpre(ta)
        kb.vpost(v.tensor_scalar(sc1, sc1, scal_b, None, ALU.mult))
        kb.vpre()
        kb.vpost(v.tensor_single_scalar(sl, sc1, 0.0, ALU.is_equal))
        kb.vpre()
        kb.vpost(v.scalar_tensor_tensor(sl, sl, scal_b, sc1, ALU.mult, ALU.add))
        kb.vpre()
    tv = kb.vpost(v.tensor_tensor(sc1, bv[0:4, 0:1024], bv[0:4, 1024:2048],
                                  ALU.mult))
    kb.wait('G', nc.gpsimd, tv)
    t1 = kb.dma(nc.gpsimd, kqv_d[0:1, :].rearrange("o (a b) -> (o a) b", a=4),
                sc1, 'dGk')
    t2 = kb.dma(nc.gpsimd, kqv_d[1:2, :].rearrange("o (a b) -> (o a) b", a=4),
                bv[0:4, 2048:3072], 'dGk')
    xn_s = kb.sb("xn_s", (P, 5 * D), BF16)
    xb = kb.sb("l0a_xb", (P, 2 * D), BF16)
    x_ep = kb.sb("l0a_xep", (P, D), F32)
    kqb = kb.sb("kqb", (P, D), BF16)
    vbb = kb.sb("vbb", (P, D), BF16)
    ones1b = kb.sb("ones1b", (1, P), BF16)
    kb.vpre()
    t_ones = kb.vpost(v.memset(ones1b[:, :], 1.0))
    kqQT = kb.sb("kqQT", (P, 32 * 512), BF16)
    kqQT_r = kqQT[:, :].rearrange("p (f t) -> p f t", f=32)
    xnT = kb.sb("xnT", (P, 32 * 512), BF16)
    xnT_r = xnT[:, :].rearrange("p (f t) -> p f t", f=32)
    attnT = kb.sb("attnT", (P, 16 * 512), BF16)
    mask_sb = kb.sb("mask_sb", (P, 4 * 512), BF16)
    pS = [kb.ps("pS0", (P, 512), F32), kb.ps("pS1", (P, 512), F32)]
    pA = [kb.ps("pA%d" % k, (P, 512), F32) for k in range(4)]

    xb_war = [None, None]
    slot_war = [None] * 5
    row_war = None
    t_xnq_war = None
    last_av = None
    av_h_tok = [None] * 4
    score_j_tok = [None] * 4
    last_sig = [None, None]
    last_sp = [None] * 4
    last_maskmm = None
    t_ep_store = None
    t_ep_v = None
    x_ep_war = None
    t_vb_ev = [None, None]

    for ph in range(2):
        kb.wait('G', nc.gpsimd, t1, t2, xb_war[1], row_war)
        trow = kb.dma(nc.gpsimd, xb[0:1, D:2 * D], kqv_d[0:1, :], 'dGk')
        t_bc_ev = [None, None]
        for i in range(8):
            ps = pS[i % 2]
            kb.wait('T', nc.tensor, t_ones, trow, t_bc_ev[i % 2],
                    last_sig[i % 2], t_vb_ev[i % 2], kb.tok('dSc'))
            ins = nc.tensor.matmul(ps[:, :], ones1b[0:1, :],
                                   xb[0:1, D + 512 * i:D + 512 * (i + 1)],
                                   start=True, stop=True)
            tt = kb.inc(ins, 'sT')
            kb.vpre(tt)
            t_bc_ev[i % 2] = kb.vpost(
                v.tensor_copy(kqb[:, 512 * i:512 * (i + 1)], ps[:, :]))
        row_war = kb.tok('sT')
        tqk = None
        for k4 in range(4):
            buf = k4 % 2
            qeng, qq = (('S', nc.sync) if buf == 0 else ('A', nc.scalar))
            kb.wait(qeng, qq, xb_war[buf], row_war if buf == 1 else None)
            tx = kb.dma(qq, xb[:, D * buf:D * (buf + 1)],
                        aps['xownb'][STR * ph + P * k4: STR * ph + P * (k4 + 1), :],
                        'dGx%d' % buf)
            tv2, mv = emit_ln_stats(kb, xb[:, D * buf:D * (buf + 1)], tx)
            kb.apre(tv2, t_xnq_war, last_av, t_ep_v)
            tln = kb.apost(a.activation(xn_s[:, 0:D], xb[:, D * buf:D * (buf + 1)],
                                        AF.Identity, scale=mv[:, 4:5],
                                        bias=mv[:, 5:6]))
            kb.ln_prev = tln
            xb_war[buf] = tln
            gd = kb.sb("gdscr", (P, 64), BF16)
            kb.gpre(tln)
            tg = kb.gpost(nc.gpsimd.tensor_copy(gd[:, :], xn_s[:, D - 64:D]))
            kb.vpre(tg, t_bc_ev[0], t_bc_ev[1])
            tqk = kb.vpost(v.tensor_tensor(xn_s[:, 0:D], xn_s[:, 0:D],
                                           kqb[:, :], ALU.mult))
            kb.wait('T', nc.tensor, tqk)
            t_xnq_war = emit_transpose_tile(kb, xn_s[:, 0:D], kqQT_r, P * k4,
                                            identb[:, :])
        for it in range(EXT[ph]):
            itg = it if ph == 0 else 8 + it
            kb.gpre(last_maskmm)
            tmask = kb.dma(nc.gpsimd, mask_sb[:, :],
                           maskstr[itg * P:(itg + 1) * P, :], 'dGm')
            for j in range(4):
                jj = 4 * it + j
                slot = jj % 5
                buf = jj % 2
                srow = 512 * it + P * j
                qeng, qq = (('S', nc.sync) if buf == 0 else ('A', nc.scalar))
                kb.wait(qeng, qq, xb_war[buf],
                        row_war if (it == 0 and j == 1 and ph > 0) else None)
                tx = kb.dma(qq, xb[:, D * buf:D * (buf + 1)],
                            xfull[srow:srow + P, :], 'dGx%d' % buf)
                tv2, mv = emit_ln_stats(kb, xb[:, D * buf:D * (buf + 1)], tx)
                kb.apre(tv2, slot_war[slot],
                        t_xnq_war if (it == 0 and j == 0) else None)
                tln = kb.apost(a.activation(
                    xn_s[:, D * slot:D * (slot + 1)],
                    xb[:, D * buf:D * (buf + 1)], AF.Identity,
                    scale=mv[:, 4:5], bias=mv[:, 5:6]))
                kb.ln_prev = tln
                xb_war[buf] = tln
                gd = kb.sb("gdscr", (P, 64), BF16)
                kb.gpre(tln)
                tg = kb.gpost(nc.gpsimd.tensor_copy(
                    gd[:, :], xn_s[:, D * slot + D - 64:D * (slot + 1)]))
                kb.wait('T', nc.tensor, tg)
                emit_transpose_tile(kb, xn_s[:, D * slot:D * (slot + 1)],
                                    xnT_r, P * j, identb[:, :],
                                    evac_wait=score_j_tok[j])
                trj = (kb.tr_evac[0], kb.tr_evac[1])
                for h in range(4):
                    ps = pS[h % 2]
                    kb.wait('T', nc.tensor, last_sig[h % 2], tmask,
                            t_bc_ev[h % 2], trj[0], trj[1])
                    for f8 in range(8):
                        f = 8 * h + f8
                        ins = nc.tensor.matmul(
                            ps[:, :],
                            xnT[:, 512 * f + P * j:512 * f + P * (j + 1)],
                            kqQT[:, 512 * f:512 * (f + 1)],
                            start=(f8 == 0), stop=False)
                    ins = nc.tensor.matmul(ps[:, :], identb[:, :],
                                           mask_sb[:, 512 * j:512 * (j + 1)],
                                           start=False, stop=True)
                    tt = kb.inc(ins, 'sT')
                    if h == 3:
                        score_j_tok[j] = tt
                        if j == 3:
                            last_maskmm = tt
                    kb.apre(tt, av_h_tok[h])
                    ta = kb.apost(a.activation(
                        attnT[:, 512 * (4 * h + j):512 * (4 * h + j + 1)],
                        ps[:, :], AF.Sigmoid, scale=0.125))
                    last_sig[h % 2] = ta
            gd2 = kb.sb("gdscr2", (P, 64), BF16)
            kb.gpre(last_sig[0], last_sig[1])
            tg2 = kb.gpost(nc.gpsimd.tensor_copy(
                gd2[:, :], attnT[:, 16 * 512 - 64:16 * 512]))
            for h in range(4):
                for dh in range(2):
                    for k in range(4):
                        kb.wait('T', nc.tensor, tg2, last_sp[k])
                        for j in range(4):
                            slot = (4 * it + j) % 5
                            ins = nc.tensor.matmul(
                                pA[k][:, :],
                                attnT[:, 512 * (4 * h + j) + P * k:
                                      512 * (4 * h + j) + P * (k + 1)],
                                xn_s[:, D * slot + 1024 * h + 512 * dh:
                                     D * slot + 1024 * h + 512 * (dh + 1)],
                                start=(j == 0), stop=(j == 3))
                        tt = kb.inc(ins, 'sT')
                        last_av = tt
                        av_h_tok[h] = tt
                        accsl = acc[:, D * k + 1024 * h + 512 * dh:
                                    D * k + 1024 * h + 512 * (dh + 1)]
                        kb.vpre(tt, t_ep_store if it == 0 else None)
                        if it == 0:
                            ins = v.tensor_copy(accsl, pA[k][:, :])
                        else:
                            ins = v.tensor_tensor(accsl, accsl, pA[k][:, :],
                                                  ALU.add)
                        last_sp[k] = kb.vpost(ins)
            for j in range(4):
                slot_war[(4 * it + j) % 5] = last_av
        kb.wait('G', nc.gpsimd, xb_war[1])
        trow2 = kb.dma(nc.gpsimd, xb[0:1, D:2 * D], kqv_d[1:2, :], 'dGk')
        t_vb_ev = [None, None]
        for i in range(8):
            ps = pS[i % 2]
            kb.wait('T', nc.tensor, t_ones, trow2, t_vb_ev[i % 2],
                    last_sig[i % 2])
            ins = nc.tensor.matmul(ps[:, :], ones1b[0:1, :],
                                   xb[0:1, D + 512 * i:D + 512 * (i + 1)],
                                   start=True, stop=True)
            tt = kb.inc(ins, 'sT')
            kb.vpre(tt)
            t_vb_ev[i % 2] = kb.vpost(
                v.tensor_copy(vbb[:, 512 * i:512 * (i + 1)], ps[:, :]))
        row_war = kb.tok('sT')
        for k in range(4):
            kb.wait('S', nc.sync, last_av, t_ep_store, x_ep_war)
            tx = kb.dma(nc.sync, x_ep[:, :],
                        xown[STR * ph + P * k: STR * ph + P * (k + 1), :], 'dSx')
            accsl = acc[:, D * k:D * (k + 1)]
            kb.vpre(tx, t_ep_store, t_vb_ev[0], t_vb_ev[1],
                    last_sp[0], last_sp[1], last_sp[2], last_sp[3])
            kb.vpost(v.tensor_tensor(accsl, accsl, vbb[:, :], ALU.mult))
            kb.vpre()
            t_ep_v = kb.vpost(v.tensor_tensor(accsl, accsl, x_ep[:, :],
                                              ALU.add))
            x_ep_war = t_ep_v
            gd3 = kb.sb("gdscr3", (P, 64), F32)
            kb.gpre(t_ep_v)
            tg3 = kb.gpost(nc.gpsimd.tensor_copy(gd3[:, :],
                                                 accsl[:, D - 64:D]))
            kb.apre(t_ep_v, tg3)
            t_ep_store = kb.dma(nc.scalar,
                                res_out[STR * ph + P * k: STR * ph + P * (k + 1), :],
                                accsl, 'dV')
        slot_war[0] = slot_war[1] = t_ep_v


def emit_l0_ffn(kb, aps):
    nc = kb.nc
    v, a = nc.vector, nc.scalar
    res_in, res_out = aps['res_in'], aps['res_out']

    identb = kb.sb("identb", (P, P), BF16)
    identf = kb.sb("identf", (P, P), F32)
    indE = kb.sb("indE", (P, 32 * 32), BF16)
    bias_sb = kb.sb("l0f_bias", (P, 32), F32)
    kb.dma(nc.sync, identb[:, :], aps['identb'][:, :], 'dSc')
    kb.dma(nc.sync, identf[:, :], aps['identf'][:, :], 'dSc')
    kb.dma(nc.sync, indE[:, :], aps['indE'][:, :], 'dSc')
    kb.dma(nc.sync, bias_sb[:, :], aps['biasT'][:, :], 'dSc')
    t_const = kb.tok('dSc')

    xnpT = kb.sb("xnpT", (P, 32 * 1024), BF16)
    xnpT_r = xnpT[:, :].rearrange("p (f t) -> p f t", f=32)
    lres = kb.sb("l0f_res", (P, 4 * D), F32)
    xst = kb.sb("xst0", (P, D), F32)
    gd = kb.sb("gdscr", (P, 64), BF16)
    zs_b = kb.sb("l0f_zs", (32, D), BF16)
    xsum = kb.sb("l0f_xsum", (32, 1024), BF16)
    qb_sb = kb.sb("l0f_qb", (P, 4096), U8)
    sexp_sb = kb.sb("l0f_sexp", (P, 4096), F32)
    delta_sb = kb.sb("l0f_delta", (P, 4096), BF16)
    wt_sb = kb.sb("l0f_wt", (P, 2 * 4096), BF16)
    ytmp = xst[:, 0:1024]
    pS = [kb.ps("pS0", (P, 512), F32), kb.ps("pS1", (P, 512), F32)]
    pA = [kb.ps("pA%d" % k, (P, 512), F32) for k in range(4)]

    last_norm = kb.state.get('l0f_xst_war')
    for k in range(8):
        kb.wait('S', nc.sync, last_norm)
        tx = kb.dma(nc.sync, xst[:, :], res_in[P * k:P * (k + 1), :], 'dSx')
        xnp = kb.sb("l0f_xnp", (P, D), BF16)
        tv2, mv = emit_ln_stats(kb, xst[:, :], tx)
        kb.apre(tv2, kb.state.get('l0f_xnp_war'))
        kb.apost(a.activation(xnp[:, 0:2048], xst[:, 0:D:2], AF.Identity,
                              scale=mv[:, 4:5], bias=mv[:, 5:6]))
        kb.apre()
        last_norm = kb.apost(a.activation(xnp[:, 2048:4096], xst[:, 1:D:2],
                                          AF.Identity, scale=mv[:, 4:5],
                                          bias=mv[:, 5:6]))
        kb.ln_prev = last_norm
        kb.gpre(last_norm)
        tg = kb.gpost(nc.gpsimd.tensor_copy(gd[:, :], xnp[:, D - 64:D]))
        kb.wait('T', nc.tensor, tg, t_const)
        kb.state['l0f_xnp_war'] = emit_transpose_tile(kb, xnp[:, :], xnpT_r,
                                                      P * k, identb[:, :])
    kb.state['l0f_xst_war'] = last_norm
    kb.state['l0f_stageA_done'] = kb.tok('sT')

    s2 = lres[0:32, 0:D]
    z2 = lres[0:32, D:2 * D]
    t1 = kb.dma(nc.sync, s2, aps['s2dT'][:, :], 'dSz')
    t2 = kb.dma(nc.sync, z2, aps['z2dT'][:, :], 'dSz')
    kb.vpre(kb.tok('dSz'))
    tzs = kb.vpost(v.scalar_tensor_tensor(zs_b[:, :], s2, -1.0, z2,
                                          ALU.mult, ALU.mult))
    t_tr = (kb.tr_evac[0], kb.tr_evac[1])
    for tc in range(2):
        kb.wait('T', nc.tensor, t_const, kb.state.get('l0f_xsum_war'), *t_tr)
        for pc in range(32):
            ins = nc.tensor.matmul(pA[1][0:32, :], indE[:, 32 * pc:32 * (pc + 1)],
                                   xnpT[:, 1024 * pc + 512 * tc:1024 * pc + 512 * (tc + 1)],
                                   start=(pc == 0), stop=(pc == 31))
        tt = kb.inc(ins, 'sT')
        kb.vpre(tt)
        ins = v.tensor_copy(xsum[:, 512 * tc:512 * (tc + 1)], pA[1][0:32, :])
        kb.state['l0f_xsum_war'] = kb.vpost(ins)
    kb.gpre(kb.state['l0f_xsum_war'], tzs)
    tg_zx = kb.gpost(nc.gpsimd.tensor_copy(gd[0:32, 0:64], xsum[:, 1024 - 64:1024]))

    weff = aps['weff_scratch']
    wt_war = [None, None]
    wt_ready = [None, None]
    wt_store = [None, None]
    qb_war = [None, None]
    ev_war = [None, None]
    ytmp_war = [None, None]
    pa0_war = None
    res_stored = kb.state.get('l0f_res_stored')

    def _flush_tail(pend, t_res):
        nonlocal pa0_war
        po, pb, ptgy = pend
        kb.wait('T', nc.tensor, ptgy, pa0_war)
        for m in range(4):
            ins = nc.tensor.transpose(pA[0][:, 128 * m:128 * (m + 1)],
                                      ytmp[:, 512 * pb + 128 * m:512 * pb + 128 * (m + 1)],
                                      identf[:, :])
        tt2 = kb.inc(ins, 'sT')
        ytmp_war[pb] = tt2
        kb.vpre(tt2, t_res)
        for m in range(4):
            ins = v.tensor_tensor(lres[:, D * m + P * po:D * m + P * (po + 1)],
                                  lres[:, D * m + P * po:D * m + P * (po + 1)],
                                  pA[0][:, 128 * m:128 * (m + 1)], ALU.add)
            kb.vpost(ins)
            if m < 3:
                kb.vpre()
        pa0_war = kb.tok('cV')

    for tc in range(2):
        kb.wait('S', nc.sync, res_stored, tzs)
        for k4 in range(4):
            kb.dma(nc.sync, lres[:, D * k4:D * (k4 + 1)],
                   res_in[512 * tc + P * k4:512 * tc + P * (k4 + 1), :], 'dSr')
        t_res = kb.tok('dSr')
        pend = None
        for ot in range(32):
            buf = ot % 2
            if tc == 0:
                kb.wait('G', nc.gpsimd, qb_war[0])
                tq = kb.dma(nc.gpsimd, qb_sb[:, :],
                            aps['qbT'][P * ot:P * (ot + 1), :], 'dGq')
                kb.wait('S', nc.sync, kb.state.get('l0f_sexp_war'),
                        kb.state.get('l0f_stageA_done') if ot == 0 else None)
                ts = kb.dma(nc.sync, sexp_sb[:, 0:2048],
                            aps['sexpT'][P * ot:P * (ot + 1), 0:2048], 'dSw')
                kb.wait('A', a, kb.state.get('l0f_sexp_war'))
                ts2 = kb.dma(nc.scalar, sexp_sb[:, 2048:4096],
                             aps['sexpT'][P * ot:P * (ot + 1), 2048:4096], 'dAw')
                kb.wait('G', nc.gpsimd, *(kb.state.get('l0f_delta_war') or ()))
                td = kb.dma(nc.gpsimd, delta_sb[:, 0:2048],
                            aps['deltaT'][P * ot:P * (ot + 1), 0:2048], 'dGd')
                kb.wait('A', a, *(kb.state.get('l0f_delta_war') or ()))
                td2 = kb.dma(nc.scalar, delta_sb[:, 2048:4096],
                             aps['deltaT'][P * ot:P * (ot + 1), 2048:4096], 'dAd')
                nib_lo = qb_sb[:, 0:2048].rearrange("p (j o) -> p j o", j=16)
                nib_hi = qb_sb[:, 2048:4096].rearrange("p (j o) -> p j o", j=16)
                sex_r = sexp_sb[:, :].rearrange("p (j q o) -> p j q o", j=16, q=2)
                del_r = delta_sb[:, :].rearrange("p (j q o) -> p j q o", j=16, q=2)
                wt_r = wt_sb[:, 4096 * buf:4096 * (buf + 1)].rearrange(
                    "p (q j o) -> p q j o", q=2, j=16)
                kb.vpre(tq, ts, ts2, wt_war[buf], wt_store[buf])
                tm0 = kb.vpost(v.tensor_tensor(wt_r[:, 0, :, :], nib_lo,
                                               sex_r[:, :, 0, :], ALU.mult))
                kb.vpre()
                tm1 = kb.vpost(v.tensor_tensor(wt_r[:, 1, :, :], nib_hi,
                                               sex_r[:, :, 1, :], ALU.mult))
                qb_war[0] = tm1
                kb.state['l0f_sexp_war'] = tm1
                kb.vpre(td, td2)
                tva = kb.vpost(v.tensor_tensor(
                    wt_r[:, 0, :, :], wt_r[:, 0, :, :],
                    del_r[:, :, 0, :], ALU.add))
                kb.gpre(tm1, td, td2)
                tdq = kb.gpost(nc.gpsimd.tensor_tensor(
                    wt_r[:, 1, :, :], wt_r[:, 1, :, :],
                    del_r[:, :, 1, :], ALU.add))
                kb.gpre(tva)
                tgv = kb.gpost(nc.gpsimd.tensor_copy(
                    gd[:, :], wt_sb[:, 4096 * buf + 2048 - 64:4096 * buf + 2048]))
                kb.state['l0f_delta_war'] = (tva, tdq)
                wt_ready[buf] = (tdq, tgv)
                kb.wait('S', nc.sync, tdq, tgv)
                tw1 = kb.dma(nc.sync, weff[P * ot:P * (ot + 1), 0:2048],
                             wt_sb[:, 4096 * buf:4096 * buf + 2048],
                             'dGw%d' % buf)
                kb.wait('A', a, tdq, tgv)
                wt_store[buf] = kb.dma(nc.scalar,
                                       weff[P * ot:P * (ot + 1), 2048:4096],
                                       wt_sb[:, 4096 * buf + 2048:4096 * (buf + 1)],
                                       'dGw%d' % buf)
            else:
                kb.wait('G', nc.gpsimd, wt_war[buf], wt_store[buf])
                tr1 = kb.dma(nc.gpsimd,
                             wt_sb[:, 4096 * buf:4096 * buf + 2048],
                             weff[P * ot:P * (ot + 1), 0:2048], 'dGr%d' % buf)
                kb.wait('A', a, wt_war[buf], wt_store[buf])
                tr2 = kb.dma(nc.scalar,
                             wt_sb[:, 4096 * buf + 2048:4096 * (buf + 1)],
                             weff[P * ot:P * (ot + 1), 2048:4096], 'dAr%d' % buf)
                wt_ready[buf] = (tr1, tr2)
            ps = pS[buf]
            kb.wait('T', nc.tensor, *(wt_ready[buf] or ()),
                    ev_war[buf], tg_zx, t_const)
            for f in range(32):
                ins = nc.tensor.matmul(
                    ps[:, :], wt_sb[:, 4096 * buf + 128 * f:4096 * buf + 128 * (f + 1)],
                    xnpT[:, 1024 * f + 512 * tc:1024 * f + 512 * (tc + 1)],
                    start=(f == 0), stop=False)
            ins = nc.tensor.matmul(ps[:, :], zs_b[0:32, P * ot:P * (ot + 1)],
                                   xsum[0:32, 512 * tc:512 * (tc + 1)],
                                   start=False, stop=True)
            tt = kb.inc(ins, 'sT')
            wt_war[buf] = tt
            if pend is not None:
                _flush_tail(pend, t_res)
            kb.apre(tt, ytmp_war[buf])
            ins = a.activation(ytmp[:, 512 * buf:512 * (buf + 1)], ps[:, :],
                               AF.Identity, bias=bias_sb[:, ot:ot + 1])
            tev = kb.apost(ins)
            ev_war[buf] = tev
            kb.gpre(tev)
            tgy = kb.gpost(nc.gpsimd.tensor_copy(
                gd[:, :], ytmp[:, 512 * buf + 448:512 * buf + 512]))
            pend = (ot, buf, tgy)
        _flush_tail(pend, t_res)
        for k4 in range(4):
            kb.gpre(pa0_war)
            gdf = kb.sb("gdscrf", (P, 64), F32)
            tgr = kb.gpost(nc.gpsimd.tensor_copy(
                gdf[:, :], lres[:, D * k4 + D - 64:D * (k4 + 1)]))
            kb.wait('A', a, tgr)
            kb.dma(nc.scalar, res_out[512 * tc + P * k4:512 * tc + P * (k4 + 1), :],
                   lres[:, D * k4:D * (k4 + 1)], 'dV')
        res_stored = kb.tok('dV')
    kb.state['l0f_res_stored'] = res_stored


def _prep_l0a(core, x, bvq, bvk, bvv):
    s1, s2 = core, core + 8
    xown = np.concatenate([x[STR * s1:STR * s1 + STR], x[STR * s2:STR * s2 + STR]])
    ms = np.zeros((24 * P, 4 * 512), dtype=BF)
    r = np.arange(P)[:, None]
    col = np.arange(512)[None, :]
    for ph, stripe in enumerate((s1, s2)):
        S0 = STR * stripe
        for it in range(EXT[ph]):
            itg = it if ph == 0 else 8 + it
            for j in range(4):
                srow = 512 * it + P * j
                m = np.where((srow + r) <= (S0 + col), 0.0, -300.0).astype(np.float32)
                ms[itg * P:(itg + 1) * P, 512 * j:512 * (j + 1)] = m.astype(BF)
    return {
        "xfull": _prep_l0a.xbf,
        "xown": np.ascontiguousarray(xown),
        "xownb": np.ascontiguousarray(xown.astype(BF)),
        "bvq": np.ascontiguousarray(bvq),
        "bvk": np.ascontiguousarray(bvk),
        "bvv": np.ascontiguousarray(bvv),
        "maskstr": ms,
        "identb": np.eye(P, dtype=BF),
    }


def _l0a_aps(nc):
    return {
        'xfull': nc.dram_tensor("xfull", [T, D], BF16, kind="ExternalInput").ap(),
        'xown': nc.dram_tensor("xown", [TOKC, D], F32, kind="ExternalInput").ap(),
        'xownb': nc.dram_tensor("xownb", [TOKC, D], BF16, kind="ExternalInput").ap(),
        'bvq': nc.dram_tensor("bvq", [4, 1024], F32, kind="ExternalInput").ap(),
        'bvk': nc.dram_tensor("bvk", [4, 1024], F32, kind="ExternalInput").ap(),
        'bvv': nc.dram_tensor("bvv", [4, 1024], F32, kind="ExternalInput").ap(),
        'maskstr': nc.dram_tensor("maskstr", [24 * P, 4 * 512], BF16,
                                  kind="ExternalInput").ap(),
        'identb': _dram(nc, "identb", [P, P], BF16),
        'kqv_scratch': nc.dram_tensor("kqv_scratch", [2, D], F32,
                                      kind="Internal").ap(),
    }


def build_l0a_program():
    nc = bass.Bass("TRN2", target_bir_lowering=False)
    aps = _l0a_aps(nc)
    aps['res_out'] = nc.dram_tensor("res_out", [TOKC, D], F32,
                                    kind="ExternalOutput").ap()
    if DEBUG_L0A:
        for nm, shp, dt in (('dbg_acc', (P, 4 * D), F32),
                            ('dbg_attnT', (P, 4 * 512), BF16),
                            ('dbg_xns', (P, 4 * D), BF16),
                            ('dbg_kqQT', (P, 32 * 512), BF16),
                            ('dbg_kqb', (P, D), BF16),
                            ('dbg_accit', (8 * P, 4 * D), F32),
                            ('dbg_attit', (8 * P, 4 * 512), BF16),
                            ('dbg_vbb', (P, D), BF16)):
            aps[nm] = nc.dram_tensor(nm, list(shp), dt, kind="ExternalOutput").ap()
    kb = KB(nc)
    kb.OFFSETS = _mk_offsets()
    emit_l0_attn(kb, aps)
    kb.wait('S', nc.sync, kb.tok('dV'))
    if DEBUG_L0A:
        kb.wait('S', nc.sync, kb.tok('dDbg'))
    kb.close()
    return nc


def kernel_l0a_only(L0_attn_x, L0_attn_bv_q, L0_attn_bv_k, L0_attn_bv_v):
    x = np.asarray(L0_attn_x).reshape(T, D)
    _prep_l0a.xbf = np.ascontiguousarray(x.astype(BF))
    nc = build_l0a_program()
    in_maps = [_prep_l0a(c, x, np.asarray(L0_attn_bv_q), np.asarray(L0_attn_bv_k),
                         np.asarray(L0_attn_bv_v)) for c in range(NCORE)]
    res = run_bass_kernel_spmd(nc, in_maps, core_ids=list(range(NCORE)))
    out = np.empty((T, D), np.float32)
    for c in range(NCORE):
        r = np.asarray(res.results[c]["res_out"])
        out[STR * c:STR * (c + 1)] = r[:STR]
        out[STR * (c + 8):STR * (c + 9)] = r[STR:]
    return out.reshape(1, T, D)


def _l1a_hydra(kb, u_sb, out_sb, w):
    nc = kb.nc
    v, a = nc.vector, nc.scalar
    pA = [kb.ps("pA%d" % k, (P, 512), F32) for k in range(4)]
    pS = [kb.ps("pS0", (P, 512), F32), kb.ps("pS1", (P, 512), F32)]
    gd = kb.sb("gdscr", (P, 64), BF16)
    zsil = kb.sb("l1a_zsil", (P, 1024), BF16)
    xs = kb.sb("l1a_xs", (P, 1024), BF16)
    y = kb.sb("l1a_y", (P, 1024), BF16)
    yz = kb.sb("l1a_yz", (P, 1024), BF16)
    yz2 = kb.sb("l1a_yz2", (P, 1024), BF16)
    y2 = kb.sb("l1a_y2", (P, 1024), BF16)
    rr = kb.sb("l1a_rr", (1, 512), F32)
    st = kb.state
    tz = []
    for c in range(4):
        kb.wait('T', nc.tensor, st.get('l1a_pA%d_war' % c))
        ins = nc.tensor.matmul(pA[c][:, :], w['inwT'][:, 128 * c:128 * (c + 1)],
                               u_sb, start=True, stop=True)
        tz.append(kb.inc(ins, 'sT'))
    for c in range(2):
        kb.apre(tz[c], st.get('l1a_zsil_war'))
        kb.apost(a.activation(zsil[:, 512 * c:512 * (c + 1)], pA[c][:, :], AF.Silu))
    for c in range(2):
        kb.apre(tz[2 + c], st.get('l1a_xs_war'))
        kb.apost(a.activation(xs[:, 512 * c:512 * (c + 1)], pA[2 + c][:, :],
                              AF.Silu, scale=w['cwcb'][:, c:c + 1],
                              bias=w['cwcb'][:, 2 + c:3 + c]))
    t_xs = kb.tok('cA')
    st['l1a_pA1_war'] = t_xs
    st['l1a_pA2_war'] = t_xs
    st['l1a_pA3_war'] = t_xs
    kb.gpre(t_xs)
    tgx = kb.gpost(nc.gpsimd.tensor_copy(gd[:, :], xs[:, 1024 - 64:1024]))
    tf = []
    for oc in range(2):
        kb.wait('T', nc.tensor, tgx, st.get('l1a_pS%d_war' % oc))
        for c in range(2):
            ins = nc.tensor.matmul(pS[oc][:, :],
                                   w['fcdT'][:, 128 * (2 * c + oc):128 * (2 * c + oc + 1)],
                                   xs[:, 512 * c:512 * (c + 1)],
                                   start=(c == 0), stop=(c == 1))
        tf.append(kb.inc(ins, 'sT'))
    for oc in range(2):
        kb.vpre(tf[oc])
        kb.vpost(v.scalar_tensor_tensor(y[:, 512 * oc:512 * (oc + 1)],
                                        pS[oc][:, :], w['drep'][:, oc:oc + 1],
                                        xs[:, 512 * oc:512 * (oc + 1)],
                                        ALU.add, ALU.mult))
        kb.vpre()
        kb.vpost(v.tensor_tensor(yz[:, 512 * oc:512 * (oc + 1)],
                                 y[:, 512 * oc:512 * (oc + 1)],
                                 zsil[:, 512 * oc:512 * (oc + 1)], ALU.mult))
    t_yz = kb.tok('cV')
    st['l1a_pS0_war'] = t_yz
    st['l1a_pS1_war'] = t_yz
    for oc in range(2):
        kb.apre(t_yz, st.get('l1a_yz2_war'))
        kb.apost(a.activation(yz2[:, 512 * oc:512 * (oc + 1)],
                              yz[:, 512 * oc:512 * (oc + 1)], AF.Square))
    t_yz2 = kb.tok('cA')
    kb.gpre(t_yz2)
    tgq = kb.gpost(nc.gpsimd.tensor_copy(gd[:, :], yz2[:, 1024 - 64:1024]))
    kb.wait('T', nc.tensor, tgq, t_yz)
    for c in range(2):
        ins = nc.tensor.matmul(pS[0][0:1, :], w['onesb'][:, 0:1],
                               yz2[:, 512 * c:512 * (c + 1)],
                               start=(c == 0), stop=(c == 1))
    tssq = kb.inc(ins, 'sT')
    kb.vpre(tssq)
    kb.vpost(v.tensor_scalar(rr[0:1, :], pS[0][0:1, :], 1.0 / 256, 1e-5,
                             ALU.mult, ALU.add))
    tv = kb.tok('cV')
    kb.apre(tv)
    ta = kb.apost(a.activation(rr[0:1, :], rr[0:1, :], AF.Sqrt))
    kb.vpre(ta)
    trr = kb.vpost(v.reciprocal(rr[0:1, :], rr[0:1, :]))
    st['l1a_pS0_war'] = trr
    kb.gpre(trr)
    tgr = kb.gpost(nc.gpsimd.tensor_copy(gd[0:1, 0:64], rr[0:1, 448:512]))
    kb.wait('T', nc.tensor, tgr, t_yz)
    ins = nc.tensor.matmul(pS[1][:, :], w['onesf'][0:1, :], rr[0:1, :],
                           start=True, stop=True)
    tbc = kb.inc(ins, 'sT')
    for oc in range(2):
        kb.vpre(tbc)
        kb.vpost(v.tensor_tensor(y2[:, 512 * oc:512 * (oc + 1)],
                                 yz[:, 512 * oc:512 * (oc + 1)], pS[1][:, :],
                                 ALU.mult))
    ty2 = kb.tok('cV')
    st['l1a_pS1_war'] = ty2
    kb.gpre(ty2)
    tgy = kb.gpost(nc.gpsimd.tensor_copy(gd[:, :], y2[:, 1024 - 64:1024]))
    kb.wait('T', nc.tensor, tgy, st.get('l1a_pA0_war'))
    for c in range(2):
        ins = nc.tensor.matmul(pA[0][:, :], w['outwT'][:, 128 * c:128 * (c + 1)],
                               y2[:, 512 * c:512 * (c + 1)],
                               start=(c == 0), stop=(c == 1))
    tt = kb.inc(ins, 'sT')
    kb.apre(tt, st.get('l1a_out_war'))
    tout = kb.apost(a.activation(out_sb, pA[0][:, :], AF.Identity))
    st['l1a_pA0_war'] = tout
    st['l1a_xs_war'] = kb.tok('sT')
    st['l1a_yz2_war'] = kb.tok('sT')
    return tout


def emit_l1_attn(kb, aps):
    nc = kb.nc
    v, a = nc.vector, nc.scalar
    res_in, res_out = aps['res_in'], aps['res_out']
    st = kb.state

    identb = kb.sb("identb", (P, P), BF16)
    identf = kb.sb("identf", (P, P), F32)
    wdT = kb.sb("l1a_wdT", (P, 4096), BF16)
    wupT = kb.sb("l1a_wupT", (P, 4096), BF16)
    inwT = kb.sb("l1a_inwT", (P, 512), BF16)
    outwT = kb.sb("l1a_outwT", (P, 256), BF16)
    fcdT = kb.sb("l1a_fcdT", (P, 512), BF16)
    cwcb = kb.sb("l1a_cwcb", (P, 4), F32)
    drep = kb.sb("l1a_drep", (P, 2), F32)
    normw = kb.sb("l1a_normw", (P, 2), F32)
    bdown = kb.sb("l1a_bdown", (P, 1), F32)
    bupT = kb.sb("l1a_bupT", (P, 32), F32)
    g1 = kb.sb("l1a_g1", (1, 8), F32)
    onesb = kb.sb("l1a_onesb", (P, 1), BF16)
    onesf = kb.sb("l1a_onesf", (1, P), F32)
    gd = kb.sb("gdscr", (P, 64), BF16)

    kb.dma(nc.sync, identb[:, :], aps['identb'][:, :], 'dSc')
    kb.dma(nc.sync, identf[:, :], aps['identf'][:, :], 'dSc')
    kb.dma(nc.gpsimd, wdT[:, :], aps['wdT'][:, :], 'dGc')
    kb.dma(nc.gpsimd, wupT[:, :], aps['wupT'][:, :], 'dGc')
    kb.dma(nc.gpsimd, inwT[:, :], aps['inwT'][:, :], 'dGc')
    kb.dma(nc.gpsimd, outwT[:, :], aps['outwT'][:, :], 'dGc')
    kb.dma(nc.gpsimd, fcdT[:, :], aps['fcdT'][:, :], 'dGc')
    kb.dma(nc.sync, cwcb[:, :], aps['cwcb'][:, :], 'dSc')
    kb.dma(nc.sync, drep[:, :], aps['drep'][:, :], 'dSc')
    kb.dma(nc.sync, normw[:, :], aps['normw'][:, :], 'dSc')
    kb.dma(nc.sync, bdown[:, :], aps['bdown'][:, :], 'dSc')
    kb.dma(nc.sync, bupT[:, :], aps['bupT'][:, :], 'dSc')
    kb.dma(nc.sync, g1[0:1, 0:1], aps['g1'][:, :], 'dSc')
    t_const = kb.tok('dSc')
    t_constg = kb.tok('dGc')

    kb.vpre(t_constg, t_const)
    kb.vpost(v.memset(onesb[:, :], 1.0))
    kb.vpre()
    kb.vpost(v.memset(onesf[:, :], 1.0))
    for c in range(2):
        kb.vpre()
        kb.vpost(v.tensor_scalar(outwT[:, 128 * c:128 * (c + 1)],
                                 outwT[:, 128 * c:128 * (c + 1)],
                                 normw[:, c:c + 1], None, ALU.mult))
    t_w = kb.tok('cV')
    pS = [kb.ps("pS0", (P, 512), F32), kb.ps("pS1", (P, 512), F32)]
    kb.gpre(t_w)
    tgw = kb.gpost(nc.gpsimd.tensor_copy(gd[0:1, 0:2], onesf[0:1, P - 2:P]))
    kb.wait('T', nc.tensor, tgw, st.get('l1a_pS0_war'))
    ins = nc.tensor.matmul(pS[0][:, 0:1], onesf[0:1, :], g1[0:1, 0:1],
                           start=True, stop=True)
    tt = kb.inc(ins, 'sT')
    alf = kb.sb("l1a_alf", (P, 4), F32)
    kb.apre(tt)
    kb.apost(a.activation(alf[:, 0:1], pS[0][:, 0:1], AF.Sigmoid))
    kb.vpre(kb.tok('cA'))
    kb.vpost(v.tensor_scalar(alf[:, 1:2], alf[:, 0:1], 1.0, None, ALU.add))
    kb.vpre()
    tal = kb.vpost(v.tensor_scalar(alf[:, 2:3], alf[:, 0:1], -1.0, 1.0,
                                   ALU.mult, ALU.add))
    st['l1a_pS0_war'] = kb.tok('cA')

    xn1 = kb.sb("l1a_xn1", (P, 4 * D), BF16)
    x1T = kb.sb("l1a_x1T", (P, 32 * 512), BF16)
    x1T_r = x1T[:, :].rearrange("p (f t) -> p f t", f=32)
    lres = kb.sb("l1a_res", (P, 4 * D), F32)
    xst = kb.sb("xst0", (P, D), F32)
    h0T = kb.sb("l1a_h0T", (P, 512), BF16)
    h1T = kb.sb("l1a_h1T", (P, 512), BF16)
    h2T = kb.sb("l1a_h2T", (P, 512), BF16)
    hfin = kb.sb("l1a_hfin", (P, 512), BF16)
    utile = kb.sb("l1a_utile", (P, 1024), F32)
    w = {'inwT': inwT, 'outwT': outwT, 'fcdT': fcdT, 'cwcb': cwcb,
         'drep': drep, 'onesb': onesb, 'onesf': onesf}

    pA = [kb.ps("pA%d" % k, (P, 512), F32) for k in range(4)]
    for tc in range(2):
        for k4 in range(4):
            kb.wait('S', nc.sync, st.get('l1a_xst_war'), st.get('l1a_res_stored'))
            tx = kb.dma(nc.sync, xst[:, :],
                        res_in[512 * tc + P * k4:512 * tc + P * (k4 + 1), :], 'dSx')
            tln = emit_ln_tile(kb, xst[:, :], xn1[:, D * k4:D * (k4 + 1)], tx,
                               war_toks=(st.get('l1a_xn_war'),))
            st['l1a_xst_war'] = tln
            kb.gpre(tln)
            tg = kb.gpost(nc.gpsimd.tensor_copy(
                gd[:, :], xn1[:, D * k4 + D - 64:D * (k4 + 1)]))
            kb.wait('T', nc.tensor, tg, t_constg)
            emit_transpose_tile(kb, xn1[:, D * k4:D * (k4 + 1)], x1T_r,
                                P * k4, identb[:, :])
        t_tr = (kb.tr_evac[0], kb.tr_evac[1])
        kb.wait('T', nc.tensor, st.get('l1a_pS0_war'), *t_tr)
        for f in range(32):
            ins = nc.tensor.matmul(pS[0][:, :], wdT[:, 128 * f:128 * (f + 1)],
                                   x1T[:, 512 * f:512 * (f + 1)],
                                   start=(f == 0), stop=(f == 31))
        tt = kb.inc(ins, 'sT')
        kb.apre(tt, tal, st.get('l1a_h0_war'))
        th0 = kb.apost(a.activation(h0T[:, :], pS[0][:, :], AF.Gelu,
                                    bias=bdown[:, 0:1]))
        st['l1a_pS0_war'] = th0
        kb.gpre(th0)
        tgh = kb.gpost(nc.gpsimd.tensor_copy(gd[:, :], h0T[:, 512 - 64:512]))
        st['l1a_u_guard'] = tgh
        kb.wait('T', nc.tensor, tgh)
        _l1a_hydra(kb, h0T[:, :], h1T[:, :], w)
        kb.gpre(kb.tok('cA'))
        tg1 = kb.gpost(nc.gpsimd.tensor_copy(gd[:, :], h1T[:, 512 - 64:512]))
        kb.wait('T', nc.tensor, tg1)
        _l1a_hydra(kb, h1T[:, :], h2T[:, :], w)
        kb.vpre(kb.tok('cA'))
        kb.vpost(v.tensor_scalar(hfin[:, :], h2T[:, :], alf[:, 2:3], None,
                                 ALU.mult))
        kb.vpre()
        thf = kb.vpost(v.scalar_tensor_tensor(hfin[:, :], h0T[:, :],
                                              alf[:, 1:2], hfin[:, :],
                                              ALU.mult, ALU.add))
        st['l1a_h0_war'] = thf
        kb.gpre(thf)
        tghf = kb.gpost(nc.gpsimd.tensor_copy(gd[:, :], hfin[:, 512 - 64:512]))
        kb.wait('S', nc.sync, st.get('l1a_res_stored'), st.get('l1a_lres_war'))
        for k4 in range(4):
            kb.dma(nc.sync, lres[:, D * k4:D * (k4 + 1)],
                   res_in[512 * tc + P * k4:512 * tc + P * (k4 + 1), :], 'dSr')
        t_res = kb.tok('dSr')
        for k4 in range(4):
            kb.vpre(t_res)
            kb.vpost(v.tensor_tensor(lres[:, D * k4:D * (k4 + 1)],
                                     lres[:, D * k4:D * (k4 + 1)],
                                     xn1[:, D * k4:D * (k4 + 1)], ALU.add))
        st['l1a_xn_war'] = kb.tok('cV')
        ut_war = [None, None]
        pa2_war = st.get('l1a_pA2_war')
        pend = None

        def _flush_up(pend):
            nonlocal pa2_war
            po, pb, ptgu = pend
            kb.wait('T', nc.tensor, ptgu, pa2_war)
            for m in range(4):
                ins = nc.tensor.transpose(
                    pA[2][:, 128 * m:128 * (m + 1)],
                    utile[:, 512 * pb + 128 * m:512 * pb + 128 * (m + 1)],
                    identf[:, :])
            tt2 = kb.inc(ins, 'sT')
            ut_war[pb] = tt2
            kb.vpre(tt2)
            for m in range(4):
                ins = v.tensor_tensor(lres[:, D * m + P * po:D * m + P * (po + 1)],
                                      lres[:, D * m + P * po:D * m + P * (po + 1)],
                                      pA[2][:, 128 * m:128 * (m + 1)], ALU.add)
                kb.vpost(ins)
                if m < 3:
                    kb.vpre()
            pa2_war = kb.tok('cV')

        for oc in range(32):
            ps = pS[oc % 2]
            kb.wait('T', nc.tensor, tghf, st.get('l1a_pS%d_war' % (oc % 2)))
            ins = nc.tensor.matmul(ps[:, :], wupT[:, 128 * oc:128 * (oc + 1)],
                                   hfin[:, :], start=True, stop=True)
            tt = kb.inc(ins, 'sT')
            if pend is not None:
                _flush_up(pend)
            kb.apre(tt, ut_war[oc % 2])
            tev = kb.apost(a.activation(utile[:, 512 * (oc % 2):512 * (oc % 2 + 1)],
                                        ps[:, :], AF.Identity,
                                        bias=bupT[:, oc:oc + 1]))
            st['l1a_pS%d_war' % (oc % 2)] = tev
            kb.gpre(tev)
            tgu = kb.gpost(nc.gpsimd.tensor_copy(
                gd[:, :], utile[:, 512 * (oc % 2) + 448:512 * (oc % 2) + 512]))
            pend = (oc, oc % 2, tgu)
        _flush_up(pend)
        st['l1a_pA2_war'] = pa2_war
        gdf = kb.sb("gdscrf", (P, 64), F32)
        for k4 in range(4):
            kb.gpre(pa2_war)
            tgr = kb.gpost(nc.gpsimd.tensor_copy(
                gdf[:, :], lres[:, D * k4 + D - 64:D * (k4 + 1)]))
            kb.wait('A', a, tgr)
            kb.dma(nc.scalar, res_out[512 * tc + P * k4:512 * tc + P * (k4 + 1), :],
                   lres[:, D * k4:D * (k4 + 1)], 'dV')
        st['l1a_res_stored'] = kb.tok('dV')
        st['l1a_lres_war'] = pa2_war


def _l1f_build_bp(kb, stg_f, tload, scT, nt, bp_out):
    nc = kb.nc
    v = nc.vector
    st = kb.state
    inv = kb.sb("l1f_inv", (P, 86), F32)
    stg = kb.sb("l1f_wg", (P, 4096), BF16)[:, 0:86 * 32]
    gd = kb.sb("gdscr", (P, 64), BF16)
    identb = kb.sb("identb", (P, P), BF16)
    pT = kb.ps("pT0", (P, 1024), BF16)
    kb.vpre(tload, st.get('l1f_inv_war'))
    kb.vpost(v.reciprocal(inv[:, 0:nt], scT))
    kb.vpre()
    kb.vpost(v.tensor_scalar(inv[:, 0:nt], inv[:, 0:nt], 4.0, None, ALU.mult))
    for ot in range(nt):
        kb.vpre(st.get('l1f_bstage_war') if ot == 0 else None)
        kb.vpost(v.tensor_scalar(stg[:, 32 * ot:32 * (ot + 1)],
                                 stg_f[:, 32 * ot:32 * (ot + 1)],
                                 inv[:, ot:ot + 1], None, ALU.mult))
    tv = kb.tok('cV')
    st['l1f_inv_war'] = tv
    kb.gpre(tv)
    tg = kb.gpost(nc.gpsimd.tensor_copy(gd[:, 0:32],
                                        stg[:, 32 * (nt - 1):32 * nt]))
    evtok = None
    for g in range((nt + 7) // 8):
        n = min(8, nt - 8 * g)
        kb.wait('T', nc.tensor, tg, evtok, st.get('l1f_pT_war'))
        for m in range(n):
            ins = nc.tensor.transpose(pT[0:32, 128 * m:128 * (m + 1)],
                                      stg[:, 32 * (8 * g + m):32 * (8 * g + m + 1)],
                                      identb[:, :])
        tt = kb.inc(ins, 'sT')
        kb.vpre(tt)
        evtok = kb.vpost(v.tensor_copy(bp_out[0:32, 1024 * g:1024 * g + 128 * n],
                                       pT[0:32, 0:128 * n]))
    st['l1f_bstage_war'] = evtok
    st['l1f_pT_war'] = evtok
    return evtok


def emit_l1_ffn(kb, aps):
    nc = kb.nc
    v, a = nc.vector, nc.scalar
    res_in, res_out = aps['res_in'], aps['res_out']
    st = kb.state
    identb = kb.sb("identb", (P, P), BF16)
    kb.dma(nc.sync, identb[:, :], aps['identb'][:, :], 'dSc')

    hcol = kb.sb("l1f_hcol", (P, 86 * 512), BF16)
    xnT = hcol[:, 0:32 * 1024]
    xnT_r = xnT.rearrange("p (f t) -> p f t", f=32)
    gBp = hcol[0:32, 32 * 1024:32 * 1024 + 86 * 128]
    xst = kb.sb("xst0", (P, D), F32)
    xn = kb.sb("l1f_xn", (P, D), BF16)
    gd = kb.sb("gdscr", (P, 64), BF16)
    uBp = kb.sb("l1f_uBp", (32, 86 * 128), BF16)
    B2p = kb.sb("l1f_B2p", (32, 32 * 128), BF16)
    wd = kb.sb("l1f_wd", (P, 2 * 8 * 512), BF16)
    gA_sb = wd[:, 0:1024]
    uA_sb = wd[:, 1024:2048]
    A2_sb = wd[:, 2048:2048 + 86 * 32]
    t1g = kb.sb("l1f_t1g", (32, 1024), BF16)
    t1u = kb.sb("l1f_t1u", (32, 1024), BF16)
    t2 = kb.sb("l1f_t2", (32, 1024), BF16)
    gscT = kb.sb("l1f_gsc", (P, 86), F32)
    uscT = kb.sb("l1f_usc", (P, 86), F32)
    dscT = kb.sb("l1f_dsc", (P, 32), F32)
    gsc005 = kb.sb("l1f_gsc5", (P, 86), F32)
    usc005 = kb.sb("l1f_usc5", (P, 86), F32)
    xsrow = kb.sb("l1f_xsrow", (1, 1024), BF16)
    rsrow = kb.sb("l1f_rsrow", (1, 1024), BF16)
    m75 = kb.sb("l1f_m75", (1, 512), BF16)
    onesb = kb.sb("l1a_onesb", (P, 1), BF16)
    onesf = kb.sb("l1a_onesf", (1, P), F32)
    s2bc = kb.sb("l1f_s2bc", (P, 4096), F32)
    dsrow = s2bc[0:1, :]
    stg_f = s2bc[:, 0:86 * 32]
    wg = kb.sb("l1f_wg", (P, 4096), BF16)
    wu = kb.sb("l1f_wu", (P, 4096), BF16)
    h_sb = kb.sb("l1f_h", (P, 4 * 512), BF16)
    gs_sb = kb.sb("l1f_gs", (P, 512), BF16)
    us_sb = kb.sb("l1f_us", (P, 512), BF16)
    ytmp = xst[:, 0:2 * 512]
    pS = [kb.ps("pS0", (P, 512), F32), kb.ps("pS1", (P, 512), F32)]
    pA = [kb.ps("pA%d" % k, (P, 512), F32) for k in range(4)]

    kb.dma(nc.sync, gscT[:, :], aps['gscT'][:, :], 'dSc')
    kb.dma(nc.sync, uscT[:, :], aps['uscT'][:, :], 'dSc')
    kb.dma(nc.sync, dscT[:, :], aps['dscT'][:, :], 'dSc')
    kb.dma(nc.gpsimd, gA_sb, aps['gA'][:, :], 'dGc')
    kb.dma(nc.gpsimd, uA_sb, aps['uA'][:, :], 'dGc')
    kb.dma(nc.gpsimd, A2_sb, aps['A2'][:, :], 'dGc')
    t_const = kb.tok('dSc')
    t_constg = kb.tok('dGc')
    kb.vpre(t_const, t_constg)
    kb.vpost(v.memset(onesb[:, :], 1.0))
    kb.vpre()
    kb.vpost(v.memset(onesf[:, :], 1.0))
    kb.vpre()
    kb.vpost(v.memset(m75[0:1, :], -7.5))
    kb.vpre()
    kb.vpost(v.tensor_scalar(gsc005[:, :], gscT[:, :], 0.005, None, ALU.mult))
    kb.vpre()
    kb.vpost(v.tensor_scalar(usc005[:, :], uscT[:, :], 0.005, None, ALU.mult))
    for k in range(8):
        kb.wait('S', nc.sync, st.get('l1f_xst_war'))
        tx = kb.dma(nc.sync, xst[:, :], res_in[P * k:P * (k + 1), :], 'dSx')
        tln = emit_ln_tile(kb, xst[:, :], xn[:, :], tx,
                           war_toks=(st.get('l1f_xn_war'),))
        st['l1f_xst_war'] = tln
        kb.gpre(tln)
        tg = kb.gpost(nc.gpsimd.tensor_copy(gd[:, :], xn[:, D - 64:D]))
        kb.wait('T', nc.tensor, tg, t_const)
        st['l1f_xn_war'] = emit_transpose_tile(kb, xn[:, :], xnT_r, P * k,
                                               identb[:, :])
    t_tr = (kb.tr_evac[0], kb.tr_evac[1])
    kb.gpre(kb.tok('cV'))
    tgo = kb.gpost(nc.gpsimd.tensor_copy(gd[0:1, 0:2], onesf[0:1, P - 2:P]))
    for tc in range(2):
        kb.wait('T', nc.tensor, tgo, st.get('l1f_ps0_war'), *t_tr)
        for f in range(32):
            ins = nc.tensor.matmul(pS[0][0:1, :], onesb[:, 0:1],
                                   xnT[:, 1024 * f + 512 * tc:1024 * f + 512 * (tc + 1)],
                                   start=(f == 0), stop=(f == 31))
        tt = kb.inc(ins, 'sT')
        kb.vpre(tt)
        st['l1f_ps0_war'] = kb.vpost(
            v.tensor_copy(xsrow[0:1, 512 * tc:512 * (tc + 1)], pS[0][0:1, :]))
    kb.gpre(kb.tok('cV'))
    tgxs = kb.gpost(nc.gpsimd.tensor_copy(gd[0:1, :], xsrow[0:1, 1024 - 64:1024]))

    tl = kb.dma(nc.sync, stg_f[:, :], aps['gBT'][:, :], 'dSb2')
    tb1 = _l1f_build_bp(kb, stg_f, tl, gscT[:, :], 86, gBp)
    kb.wait('S', nc.sync, kb.tok('cV'))
    tl = kb.dma(nc.sync, stg_f[:, :], aps['uBT'][:, :], 'dSb2')
    tb2 = _l1f_build_bp(kb, stg_f, tl, uscT[:, :], 86, uBp)
    kb.wait('S', nc.sync, kb.tok('cV'))
    tl = kb.dma(nc.sync, stg_f[:, 0:32 * 32], aps['B2T'][:, :], 'dSb2')
    tb3 = _l1f_build_bp(kb, stg_f, tl, dscT[:, :], 32, B2p)
    for (t1, A_sb, warkey) in ((t1g, gA_sb, 'l1f_t1g_war'),
                               (t1u, uA_sb, 'l1f_t1u_war')):
        for tc in range(2):
            kb.wait('T', nc.tensor, t_constg, st.get('l1f_ps1_war'), *t_tr)
            for f in range(32):
                ins = nc.tensor.matmul(
                    pS[1][0:32, :], A_sb[:, 32 * f:32 * (f + 1)],
                    xnT[:, 1024 * f + 512 * tc:1024 * f + 512 * (tc + 1)],
                    start=(f == 0), stop=(f == 31))
            tt = kb.inc(ins, 'sT')
            kb.vpre(tt)
            st['l1f_ps1_war'] = kb.vpost(
                v.tensor_copy(t1[0:32, 512 * tc:512 * (tc + 1)], pS[1][0:32, :]))
    kb.wait('S', nc.sync, kb.tok('cV'))
    tdsl = kb.dma(nc.sync, s2bc[0:1, :], aps['dsrow'][:, :], 'dSc2')
    kb.vpre(tdsl, st.get('l1f_inv_war'), st.get('l1f_bstage_war'))
    tds = kb.vpost(v.tensor_scalar(dsrow[0:1, :], dsrow[0:1, :], 0.005, None,
                                   ALU.mult))
    kb.gpre(tds)
    tgd = kb.gpost(nc.gpsimd.tensor_copy(gd[0:1, :],
                                         dsrow[0:1, 4096 - 64:4096]))
    for i in range(8):
        ps = pS[i % 2]
        kb.wait('T', nc.tensor, tgd, st.get('l1f_ps%d_war' % (i % 2)))
        ins = nc.tensor.matmul(ps[:, :], onesf[0:1, :],
                               dsrow[0:1, 512 * i:512 * (i + 1)],
                               start=True, stop=True)
        tt = kb.inc(ins, 'sT')
        kb.vpre(tt)
        st['l1f_ps%d_war' % (i % 2)] = kb.vpost(
            v.tensor_copy(s2bc[:, 512 * i:512 * (i + 1)], ps[:, :]))

    kb.gpre(kb.tok('cV'))
    tgr1 = kb.gpost(nc.gpsimd.tensor_copy(gd[0:32, 0:32],
                                          t1g[0:32, 1024 - 32:1024]))
    kb.gpre()
    tgrow = kb.gpost(nc.gpsimd.tensor_copy(gd[0:32, 0:32],
                                           t1u[0:32, 1024 - 32:1024]))

    wgs = [wg, kb.sb("l1f_wgb", (P, 4096), BF16)]
    wus = [wu, kb.sb("l1f_wub", (P, 4096), BF16)]
    wg_war = [None, None]
    wu_war = [None, None]
    ev_war = {'g': None, 'u': None}
    h_war = [None, None, None, None]
    hst = [None, None, None, None]
    for ot in range(86):
        par = ot % 2
        wgc, wuc = wgs[par], wus[par]
        tok0 = (st.get('l1f_pT_war') if ot == 0
                else (st.get('l1f_xst_war') if ot == 1 else None))
        kb.wait('S', nc.sync, wg_war[par], tok0)
        tqg = kb.dma(nc.sync, wgc[:, :],
                     aps['gqT'][P * ot:P * (ot + 1), :], 'dGg')
        kb.wait('A', a, wu_war[par], tok0)
        tqu = kb.dma(nc.scalar, wuc[:, :],
                     aps['uqT'][P * ot:P * (ot + 1), :], 'dAu')
        for tc in range(2):
            slot = 2 * par + tc
            pslot = 2 * (1 - par) + tc
            kb.wait('T', nc.tensor, tqg, ev_war['g'], tgrow, tgr1, tgo, tgxs,
                    tb1, tb2, tb3)
            for hc in range(32):
                ins = nc.tensor.matmul(
                    pS[0][:, :], wgc[:, 128 * hc:128 * (hc + 1)],
                    xnT[:, 1024 * hc + 512 * tc:1024 * hc + 512 * (tc + 1)],
                    start=(hc == 0), stop=False)
            ins = nc.tensor.matmul(pS[0][:, :], m75[0:1, 0:P],
                                   xsrow[0:1, 512 * tc:512 * (tc + 1)],
                                   start=False, stop=False)
            ins = nc.tensor.matmul(pS[0][:, :], gBp[0:32, 128 * ot:128 * (ot + 1)],
                                   t1g[0:32, 512 * tc:512 * (tc + 1)],
                                   start=False, stop=True)
            ttg = kb.inc(ins, 'sT')
            kb.wait('T', nc.tensor, tqu, ev_war['u'])
            for hc in range(32):
                ins = nc.tensor.matmul(
                    pS[1][:, :], wuc[:, 128 * hc:128 * (hc + 1)],
                    xnT[:, 1024 * hc + 512 * tc:1024 * hc + 512 * (tc + 1)],
                    start=(hc == 0), stop=False)
            ins = nc.tensor.matmul(pS[1][:, :], m75[0:1, 0:P],
                                   xsrow[0:1, 512 * tc:512 * (tc + 1)],
                                   start=False, stop=False)
            ins = nc.tensor.matmul(pS[1][:, :], uBp[0:32, 128 * ot:128 * (ot + 1)],
                                   t1u[0:32, 512 * tc:512 * (tc + 1)],
                                   start=False, stop=True)
            ttu = kb.inc(ins, 'sT')
            wu_war[par] = ttu
            if tc == 1:
                wg_war[par] = ttg
            kb.apre(ttg, st.get('l1f_gs_war'))
            tevg = kb.apost(a.activation(gs_sb[:, :], pS[0][:, :], AF.Silu,
                                         scale=gsc005[:, ot:ot + 1]))
            ev_war['g'] = tevg
            kb.vpre(ttu, st.get('l1f_us_war'))
            tevu = kb.vpost(v.tensor_scalar(us_sb[:, :], pS[1][:, :],
                                            usc005[:, ot:ot + 1], None, ALU.mult))
            ev_war['u'] = tevu
            kb.vpre(tevg, h_war[slot], hst[slot])
            th = kb.vpost(v.tensor_tensor(h_sb[:, 512 * slot:512 * (slot + 1)],
                                          gs_sb[:, :], us_sb[:, :], ALU.mult))
            st['l1f_gs_war'] = th
            st['l1f_us_war'] = th
            kb.wait('A', a, th)
            hst[slot] = kb.dma(nc.scalar,
                               aps['h_scratch'][P * ot:P * (ot + 1),
                                                512 * tc:512 * (tc + 1)],
                               h_sb[:, 512 * slot:512 * (slot + 1)],
                               'dGh%d' % slot)
            if ot > 0:
                kb.wait('T', nc.tensor, st.get('l1f_h_guard%d' % pslot))
                ins = nc.tensor.matmul(pA[2 + tc][0:32, :],
                                       A2_sb[:, 32 * (ot - 1):32 * ot],
                                       h_sb[:, 512 * pslot:512 * (pslot + 1)],
                                       start=(ot == 1), stop=False)
                h_war[pslot] = kb.inc(ins, 'sT')
            kb.gpre(th)
            st['l1f_h_guard%d' % slot] = kb.gpost(
                nc.gpsimd.tensor_copy(gd[:, :],
                                      h_sb[:, 512 * (slot + 1) - 64:512 * (slot + 1)]))
    for tc in range(2):
        slot = 2 * (85 % 2) + tc
        kb.wait('T', nc.tensor, st.get('l1f_h_guard%d' % slot))
        ins = nc.tensor.matmul(pA[2 + tc][0:32, :], A2_sb[:, 32 * 85:32 * 86],
                               h_sb[:, 512 * slot:512 * (slot + 1)],
                               start=False, stop=True)
        h_war[slot] = kb.inc(ins, 'sT')
        kb.apre(h_war[slot], st.get('l1f_t2_war'))
        st['l1f_t2_war'] = kb.apost(
            a.activation(t2[0:32, 512 * tc:512 * (tc + 1)], pA[2 + tc][0:32, :],
                         AF.Identity))
    kb.gpre(st['l1f_t2_war'])
    tgt2 = kb.gpost(nc.gpsimd.tensor_copy(gd[0:32, 0:32], t2[0:32, 992:1024]))

    kb.gpre()
    tpre = kb.dma(nc.gpsimd, res_out[:, :], res_in[:, :], 'dGcp')

    yst = [None, None]
    ytp_war = None
    for tc in range(2):
        hwaits = (kb.tok('dGh0'), kb.tok('dGh1'), kb.tok('dGh2'),
                  kb.tok('dGh3'), ('sT', kb.cnt['sT']), st.get('l1f_hcol_war'))
        th = []
        for (en, qe, _), (i0, i1) in zip(
                (('S', nc.sync, 0), ('A', nc.scalar, 0), ('G', nc.gpsimd, 0)),
                ((0, 29), (29, 58), (58, 86))):
            kb.wait(en, qe, *hwaits)
            th.append(kb.dma(
                qe, hcol[:, 512 * i0:512 * i1].rearrange("p (i t) -> p i t",
                                                         i=i1 - i0),
                aps['h_scratch'][P * i0:P * i1, 512 * tc:512 * (tc + 1)]
                .rearrange("(i p) t -> p i t", p=P), 'dSh'))
        kb.wait('T', nc.tensor, *th, st.get('l1f_ps0_war'))
        for i in range(86):
            ins = nc.tensor.matmul(pS[0][0:1, :], onesb[:, 0:1],
                                   hcol[:, 512 * i:512 * (i + 1)],
                                   start=(i == 0), stop=(i == 85))
        tt = kb.inc(ins, 'sT')
        kb.vpre(tt)
        trs = kb.vpost(v.tensor_copy(rsrow[0:1, 512 * tc:512 * (tc + 1)],
                                     pS[0][0:1, :]))
        st['l1f_ps0_war'] = trs
        kb.gpre(trs)
        tgrs = kb.gpost(nc.gpsimd.tensor_copy(
            gd[0:1, :], rsrow[0:1, 512 * (tc + 1) - 64:512 * (tc + 1)]))
        wd_tok = [None, None, None, None]
        wd_war = [None, None, None, None]
        QS = (('G', nc.gpsimd, 'dSd0'), ('A', nc.scalar, 'dSd1'),
              ('S', nc.sync, 'dSd2'))
        for o2c in range(8):
            for g in range(22):
                slot = g % 4
                n = min(4, 86 - 4 * g)
                qeng, qe, qsem = QS[g % 3]
                kb.wait(qeng, qe, wd_war[slot], st.get('l1f_t2_war'))
                wd_tok[slot] = kb.dma(
                    qe,
                    wd[:, 2048 * slot:2048 * slot + 512 * n]
                    .rearrange("p (i o) -> p i o", i=n),
                    aps['dqT'][(o2c * 86 + 4 * g) * P:(o2c * 86 + 4 * g + n) * P, :]
                    .rearrange("(i p) o -> p i o", p=P), qsem)
                kb.wait('T', nc.tensor, wd_tok[slot], tgt2, tgrs,
                        ytp_war if g == 0 else None)
                for m in range(n):
                    i = 4 * g + m
                    for k in range(4):
                        ins = nc.tensor.matmul(
                            pA[k][:, :],
                            hcol[:, 512 * i + 128 * k:512 * i + 128 * (k + 1)],
                            wd[:, 2048 * slot + 512 * m:2048 * slot + 512 * (m + 1)],
                            start=(i == 0), stop=False)
                wd_war[slot] = kb.inc(ins, 'sT')
            for k in range(4):
                ins = nc.tensor.matmul(
                    pA[k][:, :],
                    rsrow[0:1, 512 * tc + 128 * k:512 * tc + 128 * (k + 1)],
                    m75[0:1, :], start=False, stop=False)
                ins = nc.tensor.matmul(
                    pA[k][:, :],
                    t2[0:32, 512 * tc + 128 * k:512 * tc + 128 * (k + 1)],
                    B2p[0:32, 512 * o2c:512 * (o2c + 1)],
                    start=False, stop=True)
                tt = kb.inc(ins, 'sT')
                ypar = k % 2
                kb.vpre(tt, yst[ypar])
                tev = kb.vpost(v.tensor_tensor(
                    ytmp[:, 512 * ypar:512 * (ypar + 1)], pA[k][:, :],
                    s2bc[:, 512 * o2c:512 * (o2c + 1)], ALU.mult))
                kb.gpre(tev, tpre)
                yst[ypar] = kb.dma(
                    nc.gpsimd,
                    res_out[512 * tc + P * k:512 * tc + P * (k + 1),
                            512 * o2c:512 * (o2c + 1)],
                    ytmp[:, 512 * ypar:512 * (ypar + 1)], 'dGy%d' % ypar,
                    accum_op=ALU.add)
            ytp_war = kb.tok('cV')
        st['l1f_hcol_war'] = kb.tok('sT')
    kb.wait('S', nc.sync, kb.tok('dGy0'), kb.tok('dGy1'))


def _prep_l0f_weights(q_packed, q_scales, q_zeros, delta_weight, bias):
    b = np.ascontiguousarray(q_packed).view(np.uint8).reshape(-1, 4)[:, 0]
    b2 = b.reshape(4096, 2048)
    qbT = np.ascontiguousarray(
        b2.reshape(32, 128, 16, 128).transpose(0, 3, 2, 1)).reshape(32 * 128, 2048)
    qbT = np.concatenate([qbT & np.uint8(15), qbT >> np.uint8(4)], axis=1)
    F, JC, Q = np.meshgrid(np.arange(128), np.arange(16), np.arange(2),
                           indexing='ij')
    ii = 2 * (128 * JC + F) + Q
    s2d = q_scales.reshape(4096, 32)
    cc = ii // 128
    sexpT = np.ascontiguousarray(
        s2d[:, cc.ravel()].reshape(4096, 128, 16, 2)
        .transpose(1, 2, 3, 0).reshape(128, 16, 2, 32, 128)
        .transpose(3, 0, 1, 2, 4)).reshape(32 * 128, 4096).astype(np.float32)
    deltaT = np.ascontiguousarray(
        delta_weight[:, ii.ravel()].reshape(4096, 128, 16, 2)
        .transpose(1, 2, 3, 0).reshape(128, 16, 2, 32, 128)
        .transpose(3, 0, 1, 2, 4)).reshape(32 * 128, 4096).astype(BF)
    s2dT = np.ascontiguousarray(q_scales.reshape(4096, 32).T)
    z2dT = np.ascontiguousarray(q_zeros.reshape(4096, 32).T)
    indE = np.zeros((128, 32, 32), dtype=BF)
    for pc in range(32):
        q, jc = pc // 16, pc % 16
        for f in range(128):
            c = 2 * jc + (1 if f >= 64 else 0)
            indE[f, pc, c] = 1.0
    indE = indE.reshape(128, 1024)
    biasT = np.ascontiguousarray(bias.reshape(32, 128).T).astype(np.float32)
    return {"qbT": qbT, "sexpT": sexpT, "deltaT": deltaT, "s2dT": s2dT,
            "z2dT": z2dT, "indE": indE, "biasT": biasT,
            "identb": np.eye(P, dtype=BF),
            "identf": np.eye(P, dtype=np.float32)}


def _l0f_aps(nc):
    return {
        'qbT': nc.dram_tensor("qbT", [32 * P, 4096], U8, kind="ExternalInput").ap(),
        'sexpT': nc.dram_tensor("sexpT", [32 * P, 4096], F32, kind="ExternalInput").ap(),
        'deltaT': nc.dram_tensor("deltaT", [32 * P, 4096], BF16, kind="ExternalInput").ap(),
        's2dT': nc.dram_tensor("s2dT", [32, 4096], F32, kind="ExternalInput").ap(),
        'z2dT': nc.dram_tensor("z2dT", [32, 4096], F32, kind="ExternalInput").ap(),
        'indE': nc.dram_tensor("indE", [P, 1024], BF16, kind="ExternalInput").ap(),
        'biasT': nc.dram_tensor("biasT", [P, 32], F32, kind="ExternalInput").ap(),
        'identb': _dram(nc, "identb", [P, P], BF16),
        'identf': _dram(nc, "identf", [P, P], F32),
        'weff_scratch': nc.dram_tensor("weff_scratch", [32 * P, 4096], BF16,
                                       kind="Internal").ap(),
    }


def build_l0f_program():
    nc = bass.Bass("TRN2", target_bir_lowering=False)
    aps = _l0f_aps(nc)
    aps['res_in'] = nc.dram_tensor("res_in", [TOKC, D], F32, kind="ExternalInput").ap()
    aps['res_out'] = nc.dram_tensor("res_out", [TOKC, D], F32, kind="ExternalOutput").ap()
    kb = KB(nc)
    kb.OFFSETS = _mk_offsets()
    emit_l0_ffn(kb, aps)
    kb.wait('S', nc.sync, kb.tok('dV'))
    kb.close()
    return nc


def kernel_l0f_only(r1_full, q_packed, q_scales, q_zeros, delta_weight, bias):
    r1 = np.asarray(r1_full).reshape(T, D)
    w = _prep_l0f_weights(np.asarray(q_packed), np.asarray(q_scales),
                          np.asarray(q_zeros), np.asarray(delta_weight),
                          np.asarray(bias))
    nc = build_l0f_program()
    in_maps = []
    for c in range(NCORE):
        m = dict(w)
        m['res_in'] = np.ascontiguousarray(
            np.concatenate([r1[STR * c:STR * (c + 1)],
                            r1[STR * (c + 8):STR * (c + 9)]]))
        in_maps.append(m)
    res = run_bass_kernel_spmd(nc, in_maps, core_ids=list(range(NCORE)))
    out = np.empty((T, D), np.float32)
    for c in range(NCORE):
        r = np.asarray(res.results[c]["res_out"])
        out[STR * c:STR * (c + 1)] = r[:STR]
        out[STR * (c + 8):STR * (c + 9)] = r[STR:]
    return out.reshape(1, T, D)


def _prep_l1a_weights(p):
    wd = np.ascontiguousarray(
        p['w_down'].T.reshape(32, 128, 128).transpose(1, 0, 2).reshape(128, 4096)
    ).astype(np.float32)
    wu = np.ascontiguousarray(p['w_up'].T).astype(np.float32)
    inw = np.ascontiguousarray(p['hy_in_w'][:512, :].T).astype(np.float32)
    outw = np.ascontiguousarray(
        p['hy_out_w'].T.reshape(2, 128, 128).transpose(1, 0, 2).reshape(128, 256)
    ).astype(np.float32)
    fr = p['hy_fcD_w'][np.repeat(np.arange(4), 64), :].T
    fcd = np.ascontiguousarray(
        fr.reshape(2, 128, 2, 128).transpose(1, 0, 2, 3).reshape(128, 512)
    ).astype(np.float32)
    cw = p['hy_conv_w'][0:256, 0, 6].reshape(2, 128).T
    cb = p['hy_conv_b'][0:256].reshape(2, 128).T
    cwcb = np.ascontiguousarray(np.concatenate([cw, cb], 1)).astype(np.float32)
    drep = np.ascontiguousarray(
        np.repeat(p['hy_D'], 64).reshape(2, 128).T).astype(np.float32)
    normw = np.ascontiguousarray(
        p['hy_norm_w'].reshape(2, 128).T).astype(np.float32)
    return {
        'wdT': wd, 'wupT': wu, 'inwT': inw, 'outwT': outw, 'fcdT': fcd,
        'cwcb': cwcb, 'drep': drep, 'normw': normw,
        'bdown': np.ascontiguousarray(p['b_down'].reshape(128, 1)).astype(np.float32),
        'bupT': np.ascontiguousarray(p['b_up'].reshape(32, 128).T).astype(np.float32),
        'g1': np.asarray(p['gate1']).reshape(1, 1).astype(np.float32),
        'identb': np.eye(P, dtype=BF),
        'identf': np.eye(P, dtype=np.float32),
    }


def _l1a_aps(nc):
    d = {'identb': _dram(nc, "identb", [P, P], BF16),
         'identf': _dram(nc, "identf", [P, P], F32)}
    for nm, shp, dt in (('wdT', (P, 4096), F32), ('wupT', (P, 4096), F32),
                        ('inwT', (P, 512), F32), ('outwT', (P, 256), F32),
                        ('fcdT', (P, 512), F32), ('cwcb', (P, 4), F32),
                        ('drep', (P, 2), F32), ('normw', (P, 2), F32),
                        ('bdown', (P, 1), F32), ('bupT', (P, 32), F32),
                        ('g1', (1, 1), F32), ):
        d[nm] = nc.dram_tensor(nm, list(shp), dt, kind="ExternalInput").ap()
    return d


def build_l1a_program():
    nc = bass.Bass("TRN2", target_bir_lowering=False)
    aps = _l1a_aps(nc)
    aps['res_in'] = nc.dram_tensor("res_in", [TOKC, D], F32, kind="ExternalInput").ap()
    aps['res_out'] = nc.dram_tensor("res_out", [TOKC, D], F32, kind="ExternalOutput").ap()
    kb = KB(nc)
    kb.OFFSETS = _mk_offsets()
    emit_l1_attn(kb, aps)
    kb.wait('S', nc.sync, kb.tok('dV'))
    kb.close()
    return nc


def kernel_l1a_only(r2_full, p):
    r2 = np.asarray(r2_full).reshape(T, D)
    w = _prep_l1a_weights(p)
    nc = build_l1a_program()
    in_maps = []
    for c in range(NCORE):
        m = dict(w)
        m['res_in'] = np.ascontiguousarray(
            np.concatenate([r2[STR * c:STR * (c + 1)],
                            r2[STR * (c + 8):STR * (c + 9)]]))
        in_maps.append(m)
    res = run_bass_kernel_spmd(nc, in_maps, core_ids=list(range(NCORE)))
    out = np.empty((T, D), np.float32)
    for c in range(NCORE):
        r = np.asarray(res.results[c]["res_out"])
        out[STR * c:STR * (c + 1)] = r[:STR]
        out[STR * (c + 8):STR * (c + 9)] = r[STR:]
    return out.reshape(1, T, D)


def _prep_l1f_weights(i4):
    def qbytes(q):
        return np.ascontiguousarray(q).view(np.uint8).reshape(-1, 4)[:, 0]
    gb = qbytes(i4['gate_q']).reshape(11008, 4096)
    ub = qbytes(i4['up_q']).reshape(11008, 4096)
    db = qbytes(i4['down_q']).reshape(4096, 11008)
    gqT = np.ascontiguousarray(
        gb.T.reshape(32, 128, 86, 128).transpose(2, 1, 0, 3)).reshape(
            86 * 128, 4096).astype(BF)
    uqT = np.ascontiguousarray(
        ub.T.reshape(32, 128, 86, 128).transpose(2, 1, 0, 3)).reshape(
            86 * 128, 4096).astype(BF)
    dqT = np.ascontiguousarray(
        db.T.reshape(86, 128, 8, 512).transpose(2, 0, 1, 3)).reshape(
            8 * 86 * 128, 512).astype(BF)
    gBT = np.ascontiguousarray(
        i4['gate_B'].T.reshape(86, 128, 32).transpose(1, 0, 2)).reshape(128, 86 * 32)
    uBT = np.ascontiguousarray(
        i4['up_B'].T.reshape(86, 128, 32).transpose(1, 0, 2)).reshape(128, 86 * 32)
    B2T = np.ascontiguousarray(
        i4['down_B'].T.reshape(32, 128, 32).transpose(1, 0, 2)).reshape(128, 32 * 32)
    gA = np.ascontiguousarray(
        i4['gate_A'].reshape(32, 128, 32).transpose(1, 0, 2)).reshape(128, 1024)
    uA = np.ascontiguousarray(
        i4['up_A'].reshape(32, 128, 32).transpose(1, 0, 2)).reshape(128, 1024)
    A2 = np.ascontiguousarray(
        i4['down_A'].reshape(86, 128, 32).transpose(1, 0, 2)).reshape(128, 86 * 32)
    return {
        'gqT': gqT, 'uqT': uqT, 'dqT': dqT,
        'gBT': gBT.astype(np.float32), 'uBT': uBT.astype(np.float32),
        'B2T': B2T.astype(np.float32),
        'gA': gA.astype(np.float32), 'uA': uA.astype(np.float32),
        'A2': A2.astype(np.float32),
        'gscT': np.ascontiguousarray(i4['gate_scale'].reshape(86, 128).T).astype(np.float32),
        'uscT': np.ascontiguousarray(i4['up_scale'].reshape(86, 128).T).astype(np.float32),
        'dscT': np.ascontiguousarray(i4['down_scale'].reshape(32, 128).T).astype(np.float32),
        'dsrow': np.ascontiguousarray(i4['down_scale'].reshape(1, 4096)).astype(np.float32),
        'identb': np.eye(P, dtype=BF),
    }


def _l1f_aps(nc):
    d = {
        'gqT': nc.dram_tensor("gqT", [86 * P, 4096], BF16, kind="ExternalInput").ap(),
        'uqT': nc.dram_tensor("uqT", [86 * P, 4096], BF16, kind="ExternalInput").ap(),
        'dqT': nc.dram_tensor("dqT", [8 * 86 * P, 512], BF16, kind="ExternalInput").ap(),
        'gBT': nc.dram_tensor("gBT", [P, 86 * 32], F32, kind="ExternalInput").ap(),
        'uBT': nc.dram_tensor("uBT", [P, 86 * 32], F32, kind="ExternalInput").ap(),
        'B2T': nc.dram_tensor("B2T", [P, 32 * 32], F32, kind="ExternalInput").ap(),
        'gA': nc.dram_tensor("gA", [P, 1024], F32, kind="ExternalInput").ap(),
        'uA': nc.dram_tensor("uA", [P, 1024], F32, kind="ExternalInput").ap(),
        'A2': nc.dram_tensor("A2", [P, 86 * 32], F32, kind="ExternalInput").ap(),
        'gscT': nc.dram_tensor("gscT", [P, 86], F32, kind="ExternalInput").ap(),
        'uscT': nc.dram_tensor("uscT", [P, 86], F32, kind="ExternalInput").ap(),
        'dscT': nc.dram_tensor("dscT", [P, 32], F32, kind="ExternalInput").ap(),
        'dsrow': nc.dram_tensor("dsrow", [1, 4096], F32, kind="ExternalInput").ap(),
        'identb': _dram(nc, "identb", [P, P], BF16),
        'h_scratch': nc.dram_tensor("h_scratch", [86 * P, 1024], BF16,
                                    kind="Internal").ap(),
    }
    return d


def build_l1f_program():
    nc = bass.Bass("TRN2", target_bir_lowering=False)
    aps = _l1f_aps(nc)
    aps['res_in'] = nc.dram_tensor("res_in", [TOKC, D], F32, kind="ExternalInput").ap()
    aps['res_out'] = nc.dram_tensor("res_out", [TOKC, D], F32, kind="ExternalOutput").ap()
    kb = KB(nc)
    kb.OFFSETS = _mk_offsets()
    emit_l1_ffn(kb, aps)
    kb.close()
    return nc


def kernel_l1f_only(r3_full, i4):
    r3 = np.asarray(r3_full).reshape(T, D)
    w = _prep_l1f_weights(i4)
    nc = build_l1f_program()
    in_maps = []
    for c in range(NCORE):
        m = dict(w)
        m['res_in'] = np.ascontiguousarray(
            np.concatenate([r3[STR * c:STR * (c + 1)],
                            r3[STR * (c + 8):STR * (c + 9)]]))
        in_maps.append(m)
    res = run_bass_kernel_spmd(nc, in_maps, core_ids=list(range(NCORE)))
    out = np.empty((T, D), np.float32)
    for c in range(NCORE):
        r = np.asarray(res.results[c]["res_out"])
        out[STR * c:STR * (c + 1)] = r[:STR]
        out[STR * (c + 8):STR * (c + 9)] = r[STR:]
    return out.reshape(1, T, D)


def _core_slices(x):
    return [np.ascontiguousarray(np.concatenate(
        [x[STR * c:STR * (c + 1)], x[STR * (c + 8):STR * (c + 9)]]))
        for c in range(NCORE)]


def _gather(res, key="res_out"):
    out = np.empty((T, D), np.float32)
    for c in range(NCORE):
        r = np.asarray(res.results[c][key])
        out[STR * c:STR * (c + 1)] = r[:STR]
        out[STR * (c + 8):STR * (c + 9)] = r[STR:]
    return out


def _unused_multilaunch(L0_attn_x, L0_attn_bv_q, L0_attn_bv_k, L0_attn_bv_v,
           L0_ffn_q_packed, L0_ffn_q_scales, L0_ffn_q_zeros,
           L0_ffn_delta_weight, L0_ffn_bias,
           L1_attn_w_down, L1_attn_b_down, L1_attn_w_up, L1_attn_b_up,
           L1_attn_gate1, L1_attn_hy_in_w, L1_attn_hy_conv_w, L1_attn_hy_conv_b,
           L1_attn_hy_dt_bias, L1_attn_hy_A_log, L1_attn_hy_D, L1_attn_hy_fcD_w,
           L1_attn_hy_norm_w, L1_attn_hy_out_w,
           L1_ffn_gate_q, L1_ffn_gate_scale, L1_ffn_gate_A, L1_ffn_gate_B,
           L1_ffn_up_q, L1_ffn_up_scale, L1_ffn_up_A, L1_ffn_up_B,
           L1_ffn_down_q, L1_ffn_down_scale, L1_ffn_down_A, L1_ffn_down_B):
    x = np.ascontiguousarray(np.asarray(L0_attn_x).reshape(T, D))
    nc = build_l0a_program()
    in_maps = [_prep_l0a(c, x, np.asarray(L0_attn_bv_q),
                         np.asarray(L0_attn_bv_k), np.asarray(L0_attn_bv_v))
               for c in range(NCORE)]
    res = run_bass_kernel_spmd(nc, in_maps, core_ids=list(range(NCORE)))
    r1 = _gather(res)
    w = _prep_l0f_weights(np.asarray(L0_ffn_q_packed), np.asarray(L0_ffn_q_scales),
                          np.asarray(L0_ffn_q_zeros), np.asarray(L0_ffn_delta_weight),
                          np.asarray(L0_ffn_bias))
    nc = build_l0f_program()
    sl = _core_slices(r1)
    res = run_bass_kernel_spmd(nc, [dict(w, res_in=sl[c]) for c in range(NCORE)],
                               core_ids=list(range(NCORE)))
    r2 = _gather(res)
    p = {'w_down': np.asarray(L1_attn_w_down), 'b_down': np.asarray(L1_attn_b_down),
         'w_up': np.asarray(L1_attn_w_up), 'b_up': np.asarray(L1_attn_b_up),
         'gate1': np.asarray(L1_attn_gate1), 'hy_in_w': np.asarray(L1_attn_hy_in_w),
         'hy_conv_w': np.asarray(L1_attn_hy_conv_w),
         'hy_conv_b': np.asarray(L1_attn_hy_conv_b),
         'hy_D': np.asarray(L1_attn_hy_D), 'hy_fcD_w': np.asarray(L1_attn_hy_fcD_w),
         'hy_norm_w': np.asarray(L1_attn_hy_norm_w),
         'hy_out_w': np.asarray(L1_attn_hy_out_w)}
    w = _prep_l1a_weights(p)
    nc = build_l1a_program()
    sl = _core_slices(r2)
    res = run_bass_kernel_spmd(nc, [dict(w, res_in=sl[c]) for c in range(NCORE)],
                               core_ids=list(range(NCORE)))
    r3 = _gather(res)
    i4 = {'gate_q': np.asarray(L1_ffn_gate_q), 'gate_scale': np.asarray(L1_ffn_gate_scale),
          'gate_A': np.asarray(L1_ffn_gate_A), 'gate_B': np.asarray(L1_ffn_gate_B),
          'up_q': np.asarray(L1_ffn_up_q), 'up_scale': np.asarray(L1_ffn_up_scale),
          'up_A': np.asarray(L1_ffn_up_A), 'up_B': np.asarray(L1_ffn_up_B),
          'down_q': np.asarray(L1_ffn_down_q), 'down_scale': np.asarray(L1_ffn_down_scale),
          'down_A': np.asarray(L1_ffn_down_A), 'down_B': np.asarray(L1_ffn_down_B)}
    w = _prep_l1f_weights(i4)
    nc = build_l1f_program()
    sl = _core_slices(r3)
    res = run_bass_kernel_spmd(nc, [dict(w, res_in=sl[c]) for c in range(NCORE)],
                               core_ids=list(range(NCORE)))
    r4 = _gather(res)
    return r4.reshape(1, T, D)


_BASE = 16512
_BS = 194944 + _BASE


def _mk_offsets():
    o = {}
    o['xst0'] = 176128
    small = [('identb', 256), ('identf', 512), ('gdscr', 256), ('gdscrf', 256),
             ('ln_bst', 256), ('ln_mv', 64), ('ones1b', 256),
             ('l1a_onesb', 64), ('l1a_onesf', 512)]
    off = 192512
    for nm, sz in small:
        o[nm] = off
        off += sz
    for nm, a in (('acc', 0), ('kqQT', 65536), ('vbb', 65536), ('xnT', 98304),
                  ('kqb', 98304), ('xn_s', 131072), ('l0a_xep', 131072),
                  ('mask_sb', 172032), ('l0a_xb', 176128), ('attnT', 194944)):
        o[nm] = a
    o['gdscr2'] = 211328
    o['gdscr3'] = 211456
    for nm, a in (('l0f_res', 0), ('xnpT', 65536), ('l0f_wt', 131072),
                  ('l0f_sexp', 147456), ('l0f_delta', 163840),
                  ('l0f_xnp', 147456), ('l0f_qb', 172032)):
        o[nm] = a
    for nm, sz in (('l0f_bias', 128), ('indE', 2048), ('l0f_xsum', 2048),
                   ('l0f_zs', 8192)):
        o[nm] = o.get('_l0f_cur', 194944)
        o['_l0f_cur'] = o[nm] + sz
    o.pop('_l0f_cur')
    for nm, a in (('l1a_res', 0), ('l1a_xn1', 65536), ('l1a_x1T', 98304),
                  ('l1a_wdT', 131072), ('l1a_wupT', 139264)):
        o[nm] = a
    cur = 147456
    for nm, sz in (('l1a_inwT', 1024), ('l1a_outwT', 512), ('l1a_fcdT', 1024),
                   ('l1a_cwcb', 64), ('l1a_drep', 64), ('l1a_normw', 64),
                   ('l1a_bdown', 64), ('l1a_bupT', 128), ('l1a_g1', 64),
                   ('l1a_alf', 64), ('l1a_rr', 2048), ('l1a_zsil', 2048),
                   ('l1a_xs', 2048), ('l1a_y', 2048), ('l1a_yz', 2048),
                   ('l1a_yz2', 2048), ('l1a_y2', 2048), ('l1a_h0T', 1024),
                   ('l1a_h1T', 1024), ('l1a_h2T', 1024), ('l1a_hfin', 1024),
                   ('l1a_utile', 4096)):
        o[nm] = cur
        cur += sz
    for nm, a in (('l1f_hcol', 0), ('l1f_s2bc', 88064), ('l1f_uBp', 104448),
                  ('l1f_B2p', 126976), ('l1f_wd', 135168), ('l1f_wg', 151552),
                  ('l1f_wu', 159744), ('l1f_xn', 167936),
                  ('l1f_wgb', 176128), ('l1f_wub', 184320)):
        o[nm] = a
    cur = 194944
    for nm, sz in (('l1f_gsc', 384), ('l1f_usc', 384), ('l1f_dsc', 128),
                   ('l1f_gsc5', 384), ('l1f_usc5', 384), ('l1f_inv', 384),
                   ('l1f_xsrow', 2048), ('l1f_rsrow', 2048), ('l1f_m75', 1024),
                   ('l1f_t1g', 2048), ('l1f_t1u', 2048), ('l1f_t2', 2048),
                   ('l1f_gs', 1024), ('l1f_us', 1024)):
        o[nm] = cur
        cur += sz
    o['l1f_h'] = 167936 + 4096
    return {k: v + _BASE for k, v in o.items()}


def build_fused_program():
    nc = bass.Bass("TRN2", target_bir_lowering=False)
    aps0 = _l0a_aps(nc)
    aps1 = _l0f_aps(nc)
    aps2 = _l1a_aps(nc)
    aps3 = _l1f_aps(nc)
    res1 = nc.dram_tensor("res1", [TOKC, D], F32, kind="Internal").ap()
    res2 = nc.dram_tensor("res2", [TOKC, D], F32, kind="Internal").ap()
    res3 = nc.dram_tensor("res3", [TOKC, D], F32, kind="Internal").ap()
    out = nc.dram_tensor("res_out", [TOKC, D], F32, kind="ExternalOutput").ap()
    aps0['res_out'] = res1
    aps1['res_in'], aps1['res_out'] = res1, res2
    aps2['res_in'], aps2['res_out'] = res2, res3
    aps3['res_in'], aps3['res_out'] = res3, out
    kb = KB(nc)
    kb.OFFSETS = _mk_offsets()
    emit_l0_attn(kb, aps0)
    kb.block_barrier()
    emit_l0_ffn(kb, aps1)
    kb.block_barrier()
    emit_l1_attn(kb, aps2)
    kb.block_barrier()
    emit_l1_ffn(kb, aps3)
    kb.block_barrier()
    kb.close()
    return nc


def kernel_fused(inp):
    x = np.ascontiguousarray(np.asarray(inp['L0_attn_x']).reshape(T, D))
    _prep_l0a.xbf = np.ascontiguousarray(x.astype(BF))
    w0 = [_prep_l0a(c, x, np.asarray(inp['L0_attn_bv_q']),
                    np.asarray(inp['L0_attn_bv_k']),
                    np.asarray(inp['L0_attn_bv_v'])) for c in range(NCORE)]
    w1 = _prep_l0f_weights(np.asarray(inp['L0_ffn_q_packed']),
                           np.asarray(inp['L0_ffn_q_scales']),
                           np.asarray(inp['L0_ffn_q_zeros']),
                           np.asarray(inp['L0_ffn_delta_weight']),
                           np.asarray(inp['L0_ffn_bias']))
    p = {k[len('L1_attn_'):]: np.asarray(v) for k, v in inp.items()
         if k.startswith('L1_attn_')}
    w2 = _prep_l1a_weights(p)
    i4 = {k[len('L1_ffn_'):]: np.asarray(v) for k, v in inp.items()
          if k.startswith('L1_ffn_')}
    w3 = _prep_l1f_weights(i4)
    nc = build_fused_program()
    in_maps = []
    for c in range(NCORE):
        m = {}
        m.update(w0[c])
        m.update(w1)
        m.update(w2)
        m.update(w3)
        in_maps.append(m)
    res = run_bass_kernel_spmd(nc, in_maps, core_ids=list(range(NCORE)))
    return _gather(res).reshape(1, T, D)


def kernel(**inputs):
    return kernel_fused(inputs)


# version 17
# speedup vs baseline: 1.0551x; 1.0390x over previous
import contextlib
import numpy as np
import ml_dtypes
import concourse.bass as bass
import concourse.mybir as mybir
from concourse.bass_utils import run_bass_kernel_spmd

F32 = mybir.dt.float32
F16 = mybir.dt.float16
BF16 = mybir.dt.bfloat16
U8 = mybir.dt.uint8
AF = mybir.ActivationFunctionType
ALU = mybir.AluOpType
AX = mybir.AxisListType
BF = ml_dtypes.bfloat16

P = 128
NCORE = 8
T = 8192
D = 4096
STR = 512
TOKC = 2 * STR
EXT = (8, 16)
LN_EPS = 1e-5


def _dram(nc, name, shape, dtype, kind="ExternalInput"):
    cache = getattr(nc, '_dram_cache', None)
    if cache is None:
        cache = {}
        nc._dram_cache = cache
    if name not in cache:
        cache[name] = nc.dram_tensor(name, list(shape), dtype, kind=kind).ap()
    return cache[name]


class KB:
    def __init__(self, nc):
        self.nc = nc
        self.stack = contextlib.ExitStack()
        self.sems = {}
        self.cnt = {}
        self.sbufs = {}
        self.psums = {}
        self.waited = {}
        self.ln_prev = None
        self.tr_evac = [None, None]
        self.state = {}

    def sem(self, name):
        if name not in self.sems:
            self.sems[name] = self.stack.enter_context(self.nc.semaphore(name))
            self.cnt[name] = 0
        return self.sems[name]

    OFFSETS = {}

    def sb(self, name, shape, dtype):
        if name not in self.sbufs:
            if self.OFFSETS:
                assert name in self.OFFSETS, "no offset for %s" % name
            if name in self.OFFSETS:
                self.sbufs[name] = self.nc.alloc_sbuf_tensor_at(
                    "sb_" + name, list(shape), dtype, offset=self.OFFSETS[name])
            else:
                self.sbufs[name] = self.stack.enter_context(
                    self.nc.sbuf_tensor("sb_" + name, list(shape), dtype))
        return self.sbufs[name]

    def block_barrier(self):
        nc = self.nc
        toks = [self.tok('cV'), self.tok('cA'), self.tok('sT'), self.tok('cG')]
        toks += [self.tok(s) for s in list(self.cnt) if s.startswith('d')]
        for ename, eng in (('T', nc.tensor), ('V', nc.vector), ('A', nc.scalar),
                           ('S', nc.sync), ('G', nc.gpsimd)):
            self.wait(ename, eng, *toks)

    def ps(self, name, shape, dtype=F32):
        if name not in self.psums:
            self.psums[name] = self.stack.enter_context(
                self.nc.psum_tensor("ps_" + name, list(shape), dtype))
        return self.psums[name]

    def dma(self, q, out, in_, sem_name, **kw):
        self.sem(sem_name)
        q.dma_start(out=out, in_=in_, **kw).then_inc(self.sems[sem_name], 16)
        self.cnt[sem_name] += 16
        return (sem_name, self.cnt[sem_name])

    def inc(self, instr, sem_name, amt=1):
        self.sem(sem_name)
        instr.then_inc(self.sems[sem_name], amt)
        self.cnt[sem_name] += amt
        return (sem_name, self.cnt[sem_name])

    def tok(self, sem_name):
        self.sem(sem_name)
        return (sem_name, self.cnt[sem_name])

    def wait(self, ename, eng, *toks):
        wmap = self.waited.setdefault(ename, {})
        best = {}
        for t in toks:
            if t is None:
                continue
            s, v = t
            if v > best.get(s, 0):
                best[s] = v
        for s, v in best.items():
            if v > 0 and wmap.get(s, 0) < v:
                eng.wait_ge(self.sems[s], v)
                wmap[s] = v

    def vpre(self, *toks):
        self.wait('V', self.nc.vector, self.tok('cV'), *toks)

    def vpost(self, ins):
        return self.inc(ins, 'cV')

    def apre(self, *toks):
        self.wait('A', self.nc.scalar, self.tok('cA'), *toks)

    def apost(self, ins):
        return self.inc(ins, 'cA')

    def gpre(self, *toks):
        self.wait('G', self.nc.gpsimd, self.tok('cG'), *toks)

    def gpost(self, ins):
        return self.inc(ins, 'cG')

    def close(self):
        self.stack.close()


def emit_ln_stats(kb, xsrc, x_tok):
    nc = kb.nc
    v, a = nc.vector, nc.scalar
    bst = kb.sb("ln_bst", (P, 48), F32)
    mv = kb.sb("ln_mv", (P, 8), F32)
    kb.vpre(x_tok, kb.ln_prev)
    for i in range(8):
        kb.vpost(v.bn_stats(bst[:, 6 * i:6 * (i + 1)],
                            xsrc[:, 512 * i:512 * (i + 1)]))
        kb.vpre()
    tva = kb.vpost(v.bn_aggr(mv[:, 0:2], bst[:, 0:48]))
    kb.vpre()
    tv = kb.vpost(v.tensor_scalar(mv[:, 2:3], mv[:, 1:2], LN_EPS, None, ALU.add))
    kb.apre(tv)
    ta = kb.apost(a.activation(mv[:, 3:4], mv[:, 2:3], AF.Sqrt))
    kb.vpre(ta)
    kb.vpost(v.reciprocal(mv[:, 4:5], mv[:, 3:4]))
    kb.vpre()
    kb.vpost(v.scalar_tensor_tensor(mv[:, 5:6], mv[:, 0:1], -1.0,
                                    mv[:, 4:5], ALU.mult, ALU.mult))
    kb.vpre()
    tv2 = kb.vpost(v.tensor_tensor(mv[:, 6:7], mv[:, 4:5], mv[:, 5:6], ALU.mult))
    return tv2, mv


def emit_ln_tile(kb, xsrc, xn_out, x_tok, war_toks=()):
    nc = kb.nc
    tv2, mv = emit_ln_stats(kb, xsrc, x_tok)
    kb.apre(tv2, *[t for t in war_toks if t is not None])
    t = kb.apost(nc.scalar.activation(xn_out, xsrc, AF.Identity,
                                      scale=mv[:, 4:5], bias=mv[:, 5:6]))
    kb.ln_prev = t
    return t


def emit_transpose_tile(kb, src, dest_r, tcol0, identb, evac_wait=None):
    nc = kb.nc
    pT = [kb.ps("pT0", (P, 1024), BF16), kb.ps("pT1", (P, 1024), BF16)]
    tt = None
    for fg in range(4):
        pt = pT[fg % 2]
        kb.wait('T', nc.tensor, kb.tr_evac[fg % 2])
        for f8 in range(8):
            f = 8 * fg + f8
            ins = nc.tensor.transpose(pt[:, 128 * f8:128 * (f8 + 1)],
                                      src[:, 128 * f:128 * (f + 1)], identb)
        tt = kb.inc(ins, 'sT')
        kb.vpre(tt, evac_wait)
        ins = nc.vector.tensor_copy(
            dest_r[:, 8 * fg:8 * (fg + 1), tcol0:tcol0 + 128],
            pt[:, 0:1024].rearrange("p (f t) -> p f t", f=8))
        kb.tr_evac[fg % 2] = kb.vpost(ins)
    return tt


DEBUG_L0A = False


def emit_l0_attn(kb, aps):
    nc = kb.nc
    v, a = nc.vector, nc.scalar
    xfull, xown = aps['xfull'], aps['xown']
    maskstr, res_out = aps['maskstr'], aps['res_out']
    kqv_d = aps['kqv_scratch']

    identb = kb.sb("identb", (P, P), BF16)
    kb.dma(nc.sync, identb[:, :], aps['identb'][:, :], 'dSc')

    acc = kb.sb("acc", (P, 4 * D), F32)
    bv = acc
    sc1 = acc[0:4, 3 * 1024:4 * 1024]
    scal_a = acc[0:4, 4 * 1024:4 * 1024 + 1]
    scal_b = acc[0:4, 4 * 1024 + 8:4 * 1024 + 9]
    for i, key in enumerate(('bvq', 'bvk', 'bvv')):
        kb.dma(nc.sync, bv[0:4, 1024 * i:1024 * (i + 1)], aps[key][:, :], 'dSb')
    t_ball = kb.tok('dSb')
    for i in range(3):
        sl = bv[0:4, 1024 * i:1024 * (i + 1)]
        kb.vpre(t_ball)
        kb.vpost(v.reduce_sum(scal_a, sl, axis=AX.X, apply_absolute_value=True))
        kb.vpre()
        tv = kb.vpost(v.tensor_scalar(scal_b, scal_a, 1.0 / 1024, None, ALU.mult))
        kb.apre(tv)
        ta = kb.apost(a.activation(sc1, sl, AF.Sign))
        kb.vpre(ta)
        kb.vpost(v.tensor_scalar(sc1, sc1, scal_b, None, ALU.mult))
        kb.vpre()
        kb.vpost(v.tensor_single_scalar(sl, sc1, 0.0, ALU.is_equal))
        kb.vpre()
        kb.vpost(v.scalar_tensor_tensor(sl, sl, scal_b, sc1, ALU.mult, ALU.add))
        kb.vpre()
    tv = kb.vpost(v.tensor_tensor(sc1, bv[0:4, 0:1024], bv[0:4, 1024:2048],
                                  ALU.mult))
    kb.wait('G', nc.gpsimd, tv)
    t1 = kb.dma(nc.gpsimd, kqv_d[0:1, :].rearrange("o (a b) -> (o a) b", a=4),
                sc1, 'dGk')
    t2 = kb.dma(nc.gpsimd, kqv_d[1:2, :].rearrange("o (a b) -> (o a) b", a=4),
                bv[0:4, 2048:3072], 'dGk')
    xn_s = kb.sb("xn_s", (P, 5 * D), BF16)
    xb = kb.sb("l0a_xb", (P, 2 * D), BF16)
    x_ep = kb.sb("l0a_xep", (P, D), F32)
    kqb = kb.sb("kqb", (P, D), BF16)
    vbb = kb.sb("vbb", (P, D), BF16)
    ones1b = kb.sb("ones1b", (1, P), BF16)
    kb.vpre()
    t_ones = kb.vpost(v.memset(ones1b[:, :], 1.0))
    kqQT = kb.sb("kqQT", (P, 32 * 512), BF16)
    kqQT_r = kqQT[:, :].rearrange("p (f t) -> p f t", f=32)
    xnT = kb.sb("xnT", (P, 32 * 512), BF16)
    xnT_r = xnT[:, :].rearrange("p (f t) -> p f t", f=32)
    attnT = kb.sb("attnT", (P, 16 * 512), BF16)
    mask_sb = kb.sb("mask_sb", (P, 4 * 512), BF16)
    pS = [kb.ps("pS0", (P, 512), F32), kb.ps("pS1", (P, 512), F32)]
    pA = [kb.ps("pA%d" % k, (P, 512), F32) for k in range(4)]

    xb_war = [None, None]
    slot_war = [None] * 5
    row_war = None
    t_xnq_war = None
    last_av = None
    av_h_tok = [None] * 4
    score_j_tok = [None] * 4
    last_sig = [None, None]
    last_sp = [None] * 4
    last_maskmm = None
    t_ep_store = None
    t_ep_v = None
    x_ep_war = None
    t_vb_ev = [None, None]

    for ph in range(2):
        kb.wait('G', nc.gpsimd, t1, t2, xb_war[1], row_war)
        trow = kb.dma(nc.gpsimd, xb[0:1, D:2 * D], kqv_d[0:1, :], 'dGk')
        t_bc_ev = [None, None]
        for i in range(8):
            ps = pS[i % 2]
            kb.wait('T', nc.tensor, t_ones, trow, t_bc_ev[i % 2],
                    last_sig[i % 2], t_vb_ev[i % 2], kb.tok('dSc'))
            ins = nc.tensor.matmul(ps[:, :], ones1b[0:1, :],
                                   xb[0:1, D + 512 * i:D + 512 * (i + 1)],
                                   start=True, stop=True)
            tt = kb.inc(ins, 'sT')
            kb.vpre(tt)
            t_bc_ev[i % 2] = kb.vpost(
                v.tensor_copy(kqb[:, 512 * i:512 * (i + 1)], ps[:, :]))
        row_war = kb.tok('sT')
        tqk = None
        for k4 in range(4):
            buf = k4 % 2
            qeng, qq = (('S', nc.sync) if buf == 0 else ('A', nc.scalar))
            kb.wait(qeng, qq, xb_war[buf], row_war if buf == 1 else None)
            tx = kb.dma(qq, xb[:, D * buf:D * (buf + 1)],
                        aps['xownb'][STR * ph + P * k4: STR * ph + P * (k4 + 1), :],
                        'dGx%d' % buf)
            tv2, mv = emit_ln_stats(kb, xb[:, D * buf:D * (buf + 1)], tx)
            kb.apre(tv2, t_xnq_war, last_av, t_ep_v)
            tln = kb.apost(a.activation(xn_s[:, 0:D], xb[:, D * buf:D * (buf + 1)],
                                        AF.Identity, scale=mv[:, 4:5],
                                        bias=mv[:, 5:6]))
            kb.ln_prev = tln
            xb_war[buf] = tln
            gd = kb.sb("gdscr", (P, 64), BF16)
            kb.gpre(tln)
            tg = kb.gpost(nc.gpsimd.tensor_copy(gd[:, :], xn_s[:, D - 64:D]))
            kb.vpre(tg, t_bc_ev[0], t_bc_ev[1])
            tqk = kb.vpost(v.tensor_tensor(xn_s[:, 0:D], xn_s[:, 0:D],
                                           kqb[:, :], ALU.mult))
            kb.wait('T', nc.tensor, tqk)
            t_xnq_war = emit_transpose_tile(kb, xn_s[:, 0:D], kqQT_r, P * k4,
                                            identb[:, :])
        for it in range(EXT[ph]):
            itg = it if ph == 0 else 8 + it
            kb.gpre(last_maskmm)
            tmask = kb.dma(nc.gpsimd, mask_sb[:, :],
                           maskstr[itg * P:(itg + 1) * P, :], 'dGm')
            for j in range(4):
                jj = 4 * it + j
                slot = jj % 5
                buf = jj % 2
                srow = 512 * it + P * j
                qeng, qq = (('S', nc.sync) if buf == 0 else ('A', nc.scalar))
                kb.wait(qeng, qq, xb_war[buf],
                        row_war if (it == 0 and j == 1 and ph > 0) else None)
                tx = kb.dma(qq, xb[:, D * buf:D * (buf + 1)],
                            xfull[srow:srow + P, :], 'dGx%d' % buf)
                tv2, mv = emit_ln_stats(kb, xb[:, D * buf:D * (buf + 1)], tx)
                kb.apre(tv2, slot_war[slot],
                        t_xnq_war if (it == 0 and j == 0) else None)
                tln = kb.apost(a.activation(
                    xn_s[:, D * slot:D * (slot + 1)],
                    xb[:, D * buf:D * (buf + 1)], AF.Identity,
                    scale=mv[:, 4:5], bias=mv[:, 5:6]))
                kb.ln_prev = tln
                xb_war[buf] = tln
                gd = kb.sb("gdscr", (P, 64), BF16)
                kb.gpre(tln)
                tg = kb.gpost(nc.gpsimd.tensor_copy(
                    gd[:, :], xn_s[:, D * slot + D - 64:D * (slot + 1)]))
                kb.wait('T', nc.tensor, tg)
                emit_transpose_tile(kb, xn_s[:, D * slot:D * (slot + 1)],
                                    xnT_r, P * j, identb[:, :],
                                    evac_wait=score_j_tok[j])
                trj = (kb.tr_evac[0], kb.tr_evac[1])
                for h in range(4):
                    ps = pS[h % 2]
                    kb.wait('T', nc.tensor, last_sig[h % 2], tmask,
                            t_bc_ev[h % 2], trj[0], trj[1])
                    for f8 in range(8):
                        f = 8 * h + f8
                        ins = nc.tensor.matmul(
                            ps[:, :],
                            xnT[:, 512 * f + P * j:512 * f + P * (j + 1)],
                            kqQT[:, 512 * f:512 * (f + 1)],
                            start=(f8 == 0), stop=False)
                    ins = nc.tensor.matmul(ps[:, :], identb[:, :],
                                           mask_sb[:, 512 * j:512 * (j + 1)],
                                           start=False, stop=True)
                    tt = kb.inc(ins, 'sT')
                    if h == 3:
                        score_j_tok[j] = tt
                        if j == 3:
                            last_maskmm = tt
                    kb.apre(tt, av_h_tok[h])
                    ta = kb.apost(a.activation(
                        attnT[:, 512 * (4 * h + j):512 * (4 * h + j + 1)],
                        ps[:, :], AF.Sigmoid, scale=0.125))
                    last_sig[h % 2] = ta
            gd2 = kb.sb("gdscr2", (P, 64), BF16)
            kb.gpre(last_sig[0], last_sig[1])
            tg2 = kb.gpost(nc.gpsimd.tensor_copy(
                gd2[:, :], attnT[:, 16 * 512 - 64:16 * 512]))
            for h in range(4):
                for dh in range(2):
                    for k in range(4):
                        kb.wait('T', nc.tensor, tg2, last_sp[k])
                        for j in range(4):
                            slot = (4 * it + j) % 5
                            ins = nc.tensor.matmul(
                                pA[k][:, :],
                                attnT[:, 512 * (4 * h + j) + P * k:
                                      512 * (4 * h + j) + P * (k + 1)],
                                xn_s[:, D * slot + 1024 * h + 512 * dh:
                                     D * slot + 1024 * h + 512 * (dh + 1)],
                                start=(j == 0), stop=(j == 3))
                        tt = kb.inc(ins, 'sT')
                        last_av = tt
                        av_h_tok[h] = tt
                        accsl = acc[:, D * k + 1024 * h + 512 * dh:
                                    D * k + 1024 * h + 512 * (dh + 1)]
                        kb.vpre(tt, t_ep_store if it == 0 else None)
                        if it == 0:
                            ins = v.tensor_copy(accsl, pA[k][:, :])
                        else:
                            ins = v.tensor_tensor(accsl, accsl, pA[k][:, :],
                                                  ALU.add)
                        last_sp[k] = kb.vpost(ins)
            for j in range(4):
                slot_war[(4 * it + j) % 5] = last_av
        kb.wait('G', nc.gpsimd, xb_war[1])
        trow2 = kb.dma(nc.gpsimd, xb[0:1, D:2 * D], kqv_d[1:2, :], 'dGk')
        t_vb_ev = [None, None]
        for i in range(8):
            ps = pS[i % 2]
            kb.wait('T', nc.tensor, t_ones, trow2, t_vb_ev[i % 2],
                    last_sig[i % 2])
            ins = nc.tensor.matmul(ps[:, :], ones1b[0:1, :],
                                   xb[0:1, D + 512 * i:D + 512 * (i + 1)],
                                   start=True, stop=True)
            tt = kb.inc(ins, 'sT')
            kb.vpre(tt)
            t_vb_ev[i % 2] = kb.vpost(
                v.tensor_copy(vbb[:, 512 * i:512 * (i + 1)], ps[:, :]))
        row_war = kb.tok('sT')
        for k in range(4):
            kb.wait('S', nc.sync, last_av, t_ep_store, x_ep_war)
            tx = kb.dma(nc.sync, x_ep[:, :],
                        xown[STR * ph + P * k: STR * ph + P * (k + 1), :], 'dSx')
            accsl = acc[:, D * k:D * (k + 1)]
            kb.vpre(tx, t_ep_store, t_vb_ev[0], t_vb_ev[1],
                    last_sp[0], last_sp[1], last_sp[2], last_sp[3])
            kb.vpost(v.tensor_tensor(accsl, accsl, vbb[:, :], ALU.mult))
            kb.vpre()
            t_ep_v = kb.vpost(v.tensor_tensor(accsl, accsl, x_ep[:, :],
                                              ALU.add))
            x_ep_war = t_ep_v
            gd3 = kb.sb("gdscr3", (P, 64), F32)
            kb.gpre(t_ep_v)
            tg3 = kb.gpost(nc.gpsimd.tensor_copy(gd3[:, :],
                                                 accsl[:, D - 64:D]))
            kb.apre(t_ep_v, tg3)
            t_ep_store = kb.dma(nc.scalar,
                                res_out[STR * ph + P * k: STR * ph + P * (k + 1), :],
                                accsl, 'dV')
        slot_war[0] = slot_war[1] = t_ep_v


def emit_l0_ffn(kb, aps):
    nc = kb.nc
    v, a = nc.vector, nc.scalar
    res_in, res_out = aps['res_in'], aps['res_out']

    identb = kb.sb("identb", (P, P), BF16)
    identf = kb.sb("identf", (P, P), F32)
    indE = kb.sb("indE", (P, 32 * 32), BF16)
    bias_sb = kb.sb("l0f_bias", (P, 32), F32)
    kb.dma(nc.sync, identb[:, :], aps['identb'][:, :], 'dSc')
    kb.dma(nc.sync, identf[:, :], aps['identf'][:, :], 'dSc')
    kb.dma(nc.sync, indE[:, :], aps['indE'][:, :], 'dSc')
    kb.dma(nc.sync, bias_sb[:, :], aps['biasT'][:, :], 'dSc')
    t_const = kb.tok('dSc')

    xnpT = kb.sb("xnpT", (P, 32 * 1024), BF16)
    xnpT_r = xnpT[:, :].rearrange("p (f t) -> p f t", f=32)
    lres = kb.sb("l0f_res", (P, 2 * D), F32)
    xst = kb.sb("xst0", (P, D), F32)
    gd = kb.sb("gdscr", (P, 64), BF16)
    zs_b = kb.sb("l0f_zs", (32, D), BF16)
    xsum = kb.sb("l0f_xsum", (32, 1024), BF16)
    qb_sb = kb.sb("l0f_qb", (P, 4096), U8)
    sexp_sb = kb.sb("l0f_sexp", (P, 4096), F32)
    delta_sb = kb.sb("l0f_delta", (P, 4096), BF16)
    wt_sb = kb.sb("l0f_wt", (P, 2 * 4096), BF16)
    ytmp = xst[:, 0:1024]
    pS = [kb.ps("pS0", (P, 512), F32), kb.ps("pS1", (P, 512), F32)]
    pA = [kb.ps("pA%d" % k, (P, 512), F32) for k in range(4)]

    last_norm = kb.state.get('l0f_xst_war')
    for k in range(8):
        kb.wait('S', nc.sync, last_norm)
        tx = kb.dma(nc.sync, xst[:, :], res_in[P * k:P * (k + 1), :], 'dSx')
        xnp = kb.sb("l0f_xnp", (P, D), BF16)
        tv2, mv = emit_ln_stats(kb, xst[:, :], tx)
        kb.apre(tv2, kb.state.get('l0f_xnp_war'))
        kb.apost(a.activation(xnp[:, 0:2048], xst[:, 0:D:2], AF.Identity,
                              scale=mv[:, 4:5], bias=mv[:, 5:6]))
        kb.apre()
        last_norm = kb.apost(a.activation(xnp[:, 2048:4096], xst[:, 1:D:2],
                                          AF.Identity, scale=mv[:, 4:5],
                                          bias=mv[:, 5:6]))
        kb.ln_prev = last_norm
        kb.gpre(last_norm)
        tg = kb.gpost(nc.gpsimd.tensor_copy(gd[:, :], xnp[:, D - 64:D]))
        kb.wait('T', nc.tensor, tg, t_const)
        kb.state['l0f_xnp_war'] = emit_transpose_tile(kb, xnp[:, :], xnpT_r,
                                                      P * k, identb[:, :])
    kb.state['l0f_xst_war'] = last_norm
    kb.state['l0f_stageA_done'] = kb.tok('sT')

    s2 = lres[0:32, 0:D]
    z2 = lres[0:32, D:2 * D]
    t1 = kb.dma(nc.sync, s2, aps['s2dT'][:, :], 'dSz')
    t2 = kb.dma(nc.sync, z2, aps['z2dT'][:, :], 'dSz')
    kb.vpre(kb.tok('dSz'))
    tzs = kb.vpost(v.scalar_tensor_tensor(zs_b[:, :], s2, -1.0, z2,
                                          ALU.mult, ALU.mult))
    t_tr = (kb.tr_evac[0], kb.tr_evac[1])
    for tc in range(2):
        kb.wait('T', nc.tensor, t_const, kb.state.get('l0f_xsum_war'), *t_tr)
        for pc in range(32):
            ins = nc.tensor.matmul(pA[1][0:32, :], indE[:, 32 * pc:32 * (pc + 1)],
                                   xnpT[:, 1024 * pc + 512 * tc:1024 * pc + 512 * (tc + 1)],
                                   start=(pc == 0), stop=(pc == 31))
        tt = kb.inc(ins, 'sT')
        kb.vpre(tt)
        ins = v.tensor_copy(xsum[:, 512 * tc:512 * (tc + 1)], pA[1][0:32, :])
        kb.state['l0f_xsum_war'] = kb.vpost(ins)
    kb.gpre(kb.state['l0f_xsum_war'], tzs)
    tg_zx = kb.gpost(nc.gpsimd.tensor_copy(gd[0:32, 0:64], xsum[:, 1024 - 64:1024]))

    wt_war = [None, None]
    wt_ready = [None, None]
    qb_war = [None, None]
    ev_war = [None, None]
    ytmp_war = [None, None]
    pa_war = [None, kb.state.get('l0f_xsum_war')]
    st_war = [None, None]
    rc_tok = [None, None]
    rc_war = [None] * 4
    gdf = kb.sb("gdscrf", (P, 64), F32)
    ystg = kb.sb("l0f_ystg", (P, 2 * 512), F32)
    rc = kb.sb("l0f_rc", (P, 4 * 512), F32)

    for ot in range(32):
        buf = ot % 2
        kb.wait('S', nc.sync, rc_war[2 * buf], rc_war[2 * buf + 1])
        for tc in range(2):
            rs = 2 * buf + tc
            rc_tok[tc] = kb.dma(
                nc.sync,
                rc[:, 512 * rs:512 * (rs + 1)].rearrange("p (m f) -> p m f", m=4),
                res_in[512 * tc:512 * (tc + 1), 128 * ot:128 * (ot + 1)]
                .rearrange("(m p) f -> p m f", p=P), 'dSrc')
        kb.wait('G', nc.gpsimd, qb_war[0])
        tq = kb.dma(nc.gpsimd, qb_sb[:, :],
                    aps['qbT'][P * ot:P * (ot + 1), :], 'dGq')
        kb.wait('S', nc.sync, kb.state.get('l0f_sexp_war'),
                kb.state.get('l0f_stageA_done') if ot == 0 else None, tzs)
        ts = kb.dma(nc.sync, sexp_sb[:, 0:2048],
                    aps['sexpT'][P * ot:P * (ot + 1), 0:2048], 'dSw')
        kb.wait('A', a, kb.state.get('l0f_sexp_war'))
        ts2 = kb.dma(nc.scalar, sexp_sb[:, 2048:4096],
                     aps['sexpT'][P * ot:P * (ot + 1), 2048:4096], 'dAw')
        kb.wait('G', nc.gpsimd, *(kb.state.get('l0f_delta_war') or ()))
        td = kb.dma(nc.gpsimd, delta_sb[:, 0:2048],
                    aps['deltaT'][P * ot:P * (ot + 1), 0:2048], 'dGd')
        kb.wait('A', a, *(kb.state.get('l0f_delta_war') or ()))
        td2 = kb.dma(nc.scalar, delta_sb[:, 2048:4096],
                     aps['deltaT'][P * ot:P * (ot + 1), 2048:4096], 'dAd')
        nib_lo = qb_sb[:, 0:2048].rearrange("p (j o) -> p j o", j=16)
        nib_hi = qb_sb[:, 2048:4096].rearrange("p (j o) -> p j o", j=16)
        sex_r = sexp_sb[:, :].rearrange("p (j q o) -> p j q o", j=16, q=2)
        del_r = delta_sb[:, :].rearrange("p (j q o) -> p j q o", j=16, q=2)
        wt_r = wt_sb[:, 4096 * buf:4096 * (buf + 1)].rearrange(
            "p (q j o) -> p q j o", q=2, j=16)
        kb.vpre(tq, ts, ts2, wt_war[buf])
        tm0 = kb.vpost(v.tensor_tensor(wt_r[:, 0, :, :], nib_lo,
                                       sex_r[:, :, 0, :], ALU.mult))
        kb.vpre()
        tm1 = kb.vpost(v.tensor_tensor(wt_r[:, 1, :, :], nib_hi,
                                       sex_r[:, :, 1, :], ALU.mult))
        qb_war[0] = tm1
        kb.state['l0f_sexp_war'] = tm1
        kb.vpre(td, td2)
        tva = kb.vpost(v.tensor_tensor(
            wt_r[:, 0, :, :], wt_r[:, 0, :, :],
            del_r[:, :, 0, :], ALU.add))
        kb.gpre(tm1, td, td2)
        tdq = kb.gpost(nc.gpsimd.tensor_tensor(
            wt_r[:, 1, :, :], wt_r[:, 1, :, :],
            del_r[:, :, 1, :], ALU.add))
        kb.gpre(tva)
        tgv = kb.gpost(nc.gpsimd.tensor_copy(
            gd[:, :], wt_sb[:, 4096 * buf + 2048 - 64:4096 * buf + 2048]))
        kb.state['l0f_delta_war'] = (tva, tdq)
        wt_ready[buf] = (tdq, tgv)
        for tc in range(2):
            ps = pS[tc]
            kb.wait('T', nc.tensor, *(wt_ready[buf] or ()),
                    ev_war[tc], tg_zx, t_const)
            for f in range(32):
                ins = nc.tensor.matmul(
                    ps[:, :], wt_sb[:, 4096 * buf + 128 * f:4096 * buf + 128 * (f + 1)],
                    xnpT[:, 1024 * f + 512 * tc:1024 * f + 512 * (tc + 1)],
                    start=(f == 0), stop=False)
            ins = nc.tensor.matmul(ps[:, :], zs_b[0:32, P * ot:P * (ot + 1)],
                                   xsum[0:32, 512 * tc:512 * (tc + 1)],
                                   start=False, stop=True)
            tt = kb.inc(ins, 'sT')
            if tc == 1:
                wt_war[buf] = tt
            kb.apre(tt, ytmp_war[tc])
            ins = a.activation(ytmp[:, 512 * tc:512 * (tc + 1)], ps[:, :],
                               AF.Identity, bias=bias_sb[:, ot:ot + 1])
            tev = kb.apost(ins)
            ev_war[tc] = tev
            kb.gpre(tev)
            tgy = kb.gpost(nc.gpsimd.tensor_copy(
                gd[:, :], ytmp[:, 512 * tc + 448:512 * tc + 512]))
            kb.wait('T', nc.tensor, tgy, pa_war[tc])
            for m in range(4):
                ins = nc.tensor.transpose(
                    pA[tc][:, 128 * m:128 * (m + 1)],
                    ytmp[:, 512 * tc + 128 * m:512 * tc + 128 * (m + 1)],
                    identf[:, :])
            tt2 = kb.inc(ins, 'sT')
            ytmp_war[tc] = tt2
            kb.vpre(tt2, rc_tok[tc], st_war[tc])
            tv = kb.vpost(v.tensor_tensor(ystg[:, 512 * tc:512 * (tc + 1)],
                                          pA[tc][:, :],
                                          rc[:, 512 * (2 * buf + tc):512 * (2 * buf + tc + 1)],
                                          ALU.add))
            pa_war[tc] = tv
            rc_war[2 * buf + tc] = tv
            kb.gpre(tv)
            kb.gpost(nc.gpsimd.tensor_copy(
                gdf[:, :], ystg[:, 512 * tc + 448:512 * tc + 512]))
            st_war[tc] = kb.dma(
                nc.gpsimd,
                res_out[512 * tc:512 * (tc + 1), 128 * ot:128 * (ot + 1)]
                .rearrange("(m p) f -> p m f", p=P),
                ystg[:, 512 * tc:512 * (tc + 1)].rearrange("p (m f) -> p m f", m=4),
                'dGy%d' % tc)


def _prep_l0a(core, x, bvq, bvk, bvv):
    s1, s2 = core, core + 8
    xown = np.concatenate([x[STR * s1:STR * s1 + STR], x[STR * s2:STR * s2 + STR]])
    ms = np.zeros((24 * P, 4 * 512), dtype=BF)
    r = np.arange(P)[:, None]
    col = np.arange(512)[None, :]
    for ph, stripe in enumerate((s1, s2)):
        S0 = STR * stripe
        for it in range(EXT[ph]):
            itg = it if ph == 0 else 8 + it
            for j in range(4):
                srow = 512 * it + P * j
                m = np.where((srow + r) <= (S0 + col), 0.0, -300.0).astype(np.float32)
                ms[itg * P:(itg + 1) * P, 512 * j:512 * (j + 1)] = m.astype(BF)
    return {
        "xfull": _prep_l0a.xbf,
        "xown": np.ascontiguousarray(xown),
        "xownb": np.ascontiguousarray(xown.astype(BF)),
        "bvq": np.ascontiguousarray(bvq),
        "bvk": np.ascontiguousarray(bvk),
        "bvv": np.ascontiguousarray(bvv),
        "maskstr": ms,
        "identb": np.eye(P, dtype=BF),
    }


def _l0a_aps(nc):
    return {
        'xfull': nc.dram_tensor("xfull", [T, D], BF16, kind="ExternalInput").ap(),
        'xown': nc.dram_tensor("xown", [TOKC, D], F32, kind="ExternalInput").ap(),
        'xownb': nc.dram_tensor("xownb", [TOKC, D], BF16, kind="ExternalInput").ap(),
        'bvq': nc.dram_tensor("bvq", [4, 1024], F32, kind="ExternalInput").ap(),
        'bvk': nc.dram_tensor("bvk", [4, 1024], F32, kind="ExternalInput").ap(),
        'bvv': nc.dram_tensor("bvv", [4, 1024], F32, kind="ExternalInput").ap(),
        'maskstr': nc.dram_tensor("maskstr", [24 * P, 4 * 512], BF16,
                                  kind="ExternalInput").ap(),
        'identb': _dram(nc, "identb", [P, P], BF16),
        'kqv_scratch': nc.dram_tensor("kqv_scratch", [2, D], F32,
                                      kind="Internal").ap(),
    }


def build_l0a_program():
    nc = bass.Bass("TRN2", target_bir_lowering=False)
    aps = _l0a_aps(nc)
    aps['res_out'] = nc.dram_tensor("res_out", [TOKC, D], F32,
                                    kind="ExternalOutput").ap()
    if DEBUG_L0A:
        for nm, shp, dt in (('dbg_acc', (P, 4 * D), F32),
                            ('dbg_attnT', (P, 4 * 512), BF16),
                            ('dbg_xns', (P, 4 * D), BF16),
                            ('dbg_kqQT', (P, 32 * 512), BF16),
                            ('dbg_kqb', (P, D), BF16),
                            ('dbg_accit', (8 * P, 4 * D), F32),
                            ('dbg_attit', (8 * P, 4 * 512), BF16),
                            ('dbg_vbb', (P, D), BF16)):
            aps[nm] = nc.dram_tensor(nm, list(shp), dt, kind="ExternalOutput").ap()
    kb = KB(nc)
    kb.OFFSETS = _mk_offsets()
    emit_l0_attn(kb, aps)
    kb.wait('S', nc.sync, kb.tok('dV'))
    if DEBUG_L0A:
        kb.wait('S', nc.sync, kb.tok('dDbg'))
    kb.close()
    return nc


def kernel_l0a_only(L0_attn_x, L0_attn_bv_q, L0_attn_bv_k, L0_attn_bv_v):
    x = np.asarray(L0_attn_x).reshape(T, D)
    _prep_l0a.xbf = np.ascontiguousarray(x.astype(BF))
    nc = build_l0a_program()
    in_maps = [_prep_l0a(c, x, np.asarray(L0_attn_bv_q), np.asarray(L0_attn_bv_k),
                         np.asarray(L0_attn_bv_v)) for c in range(NCORE)]
    res = run_bass_kernel_spmd(nc, in_maps, core_ids=list(range(NCORE)))
    out = np.empty((T, D), np.float32)
    for c in range(NCORE):
        r = np.asarray(res.results[c]["res_out"])
        out[STR * c:STR * (c + 1)] = r[:STR]
        out[STR * (c + 8):STR * (c + 9)] = r[STR:]
    return out.reshape(1, T, D)


def _l1a_hydra(kb, u_sb, out_sb, w):
    nc = kb.nc
    v, a = nc.vector, nc.scalar
    pA = [kb.ps("pA%d" % k, (P, 512), F32) for k in range(4)]
    pS = [kb.ps("pS0", (P, 512), F32), kb.ps("pS1", (P, 512), F32)]
    gd = kb.sb("gdscr", (P, 64), BF16)
    zsil = kb.sb("l1a_zsil", (P, 1024), BF16)
    xs = kb.sb("l1a_xs", (P, 1024), BF16)
    y = kb.sb("l1a_y", (P, 1024), BF16)
    yz = kb.sb("l1a_yz", (P, 1024), BF16)
    yz2 = kb.sb("l1a_yz2", (P, 1024), BF16)
    y2 = kb.sb("l1a_y2", (P, 1024), BF16)
    rr = kb.sb("l1a_rr", (1, 512), F32)
    st = kb.state
    tz = []
    for c in range(4):
        kb.wait('T', nc.tensor, st.get('l1a_pA%d_war' % c))
        ins = nc.tensor.matmul(pA[c][:, :], w['inwT'][:, 128 * c:128 * (c + 1)],
                               u_sb, start=True, stop=True)
        tz.append(kb.inc(ins, 'sT'))
    for c in range(2):
        kb.apre(tz[c], st.get('l1a_zsil_war'))
        kb.apost(a.activation(zsil[:, 512 * c:512 * (c + 1)], pA[c][:, :], AF.Silu))
    for c in range(2):
        kb.apre(tz[2 + c], st.get('l1a_xs_war'))
        kb.apost(a.activation(xs[:, 512 * c:512 * (c + 1)], pA[2 + c][:, :],
                              AF.Silu, scale=w['cwcb'][:, c:c + 1],
                              bias=w['cwcb'][:, 2 + c:3 + c]))
    t_xs = kb.tok('cA')
    st['l1a_pA1_war'] = t_xs
    st['l1a_pA2_war'] = t_xs
    st['l1a_pA3_war'] = t_xs
    kb.gpre(t_xs)
    tgx = kb.gpost(nc.gpsimd.tensor_copy(gd[:, :], xs[:, 1024 - 64:1024]))
    tf = []
    for oc in range(2):
        kb.wait('T', nc.tensor, tgx, st.get('l1a_pS%d_war' % oc))
        for c in range(2):
            ins = nc.tensor.matmul(pS[oc][:, :],
                                   w['fcdT'][:, 128 * (2 * c + oc):128 * (2 * c + oc + 1)],
                                   xs[:, 512 * c:512 * (c + 1)],
                                   start=(c == 0), stop=(c == 1))
        tf.append(kb.inc(ins, 'sT'))
    for oc in range(2):
        kb.vpre(tf[oc])
        kb.vpost(v.scalar_tensor_tensor(y[:, 512 * oc:512 * (oc + 1)],
                                        pS[oc][:, :], w['drep'][:, oc:oc + 1],
                                        xs[:, 512 * oc:512 * (oc + 1)],
                                        ALU.add, ALU.mult))
        kb.vpre()
        kb.vpost(v.tensor_tensor(yz[:, 512 * oc:512 * (oc + 1)],
                                 y[:, 512 * oc:512 * (oc + 1)],
                                 zsil[:, 512 * oc:512 * (oc + 1)], ALU.mult))
    t_yz = kb.tok('cV')
    st['l1a_pS0_war'] = t_yz
    st['l1a_pS1_war'] = t_yz
    for oc in range(2):
        kb.apre(t_yz, st.get('l1a_yz2_war'))
        kb.apost(a.activation(yz2[:, 512 * oc:512 * (oc + 1)],
                              yz[:, 512 * oc:512 * (oc + 1)], AF.Square))
    t_yz2 = kb.tok('cA')
    kb.gpre(t_yz2)
    tgq = kb.gpost(nc.gpsimd.tensor_copy(gd[:, :], yz2[:, 1024 - 64:1024]))
    kb.wait('T', nc.tensor, tgq, t_yz)
    for c in range(2):
        ins = nc.tensor.matmul(pS[0][0:1, :], w['onesb'][:, 0:1],
                               yz2[:, 512 * c:512 * (c + 1)],
                               start=(c == 0), stop=(c == 1))
    tssq = kb.inc(ins, 'sT')
    kb.vpre(tssq)
    kb.vpost(v.tensor_scalar(rr[0:1, :], pS[0][0:1, :], 1.0 / 256, 1e-5,
                             ALU.mult, ALU.add))
    tv = kb.tok('cV')
    kb.apre(tv)
    ta = kb.apost(a.activation(rr[0:1, :], rr[0:1, :], AF.Sqrt))
    kb.vpre(ta)
    trr = kb.vpost(v.reciprocal(rr[0:1, :], rr[0:1, :]))
    st['l1a_pS0_war'] = trr
    kb.gpre(trr)
    tgr = kb.gpost(nc.gpsimd.tensor_copy(gd[0:1, 0:64], rr[0:1, 448:512]))
    kb.wait('T', nc.tensor, tgr, t_yz)
    ins = nc.tensor.matmul(pS[1][:, :], w['onesf'][0:1, :], rr[0:1, :],
                           start=True, stop=True)
    tbc = kb.inc(ins, 'sT')
    for oc in range(2):
        kb.vpre(tbc)
        kb.vpost(v.tensor_tensor(y2[:, 512 * oc:512 * (oc + 1)],
                                 yz[:, 512 * oc:512 * (oc + 1)], pS[1][:, :],
                                 ALU.mult))
    ty2 = kb.tok('cV')
    st['l1a_pS1_war'] = ty2
    kb.gpre(ty2)
    tgy = kb.gpost(nc.gpsimd.tensor_copy(gd[:, :], y2[:, 1024 - 64:1024]))
    kb.wait('T', nc.tensor, tgy, st.get('l1a_pA0_war'))
    for c in range(2):
        ins = nc.tensor.matmul(pA[0][:, :], w['outwT'][:, 128 * c:128 * (c + 1)],
                               y2[:, 512 * c:512 * (c + 1)],
                               start=(c == 0), stop=(c == 1))
    tt = kb.inc(ins, 'sT')
    kb.apre(tt, st.get('l1a_out_war'))
    tout = kb.apost(a.activation(out_sb, pA[0][:, :], AF.Identity))
    st['l1a_pA0_war'] = tout
    st['l1a_xs_war'] = kb.tok('sT')
    st['l1a_yz2_war'] = kb.tok('sT')
    return tout


def emit_l1_attn(kb, aps):
    nc = kb.nc
    v, a = nc.vector, nc.scalar
    res_in, res_out = aps['res_in'], aps['res_out']
    st = kb.state

    identb = kb.sb("identb", (P, P), BF16)
    identf = kb.sb("identf", (P, P), F32)
    wdT = kb.sb("l1a_wdT", (P, 4096), BF16)
    wupT = kb.sb("l1a_wupT", (P, 4096), BF16)
    inwT = kb.sb("l1a_inwT", (P, 512), BF16)
    outwT = kb.sb("l1a_outwT", (P, 256), BF16)
    fcdT = kb.sb("l1a_fcdT", (P, 512), BF16)
    cwcb = kb.sb("l1a_cwcb", (P, 4), F32)
    drep = kb.sb("l1a_drep", (P, 2), F32)
    normw = kb.sb("l1a_normw", (P, 2), F32)
    bdown = kb.sb("l1a_bdown", (P, 1), F32)
    bupT = kb.sb("l1a_bupT", (P, 32), F32)
    g1 = kb.sb("l1a_g1", (1, 8), F32)
    onesb = kb.sb("l1a_onesb", (P, 1), BF16)
    onesf = kb.sb("l1a_onesf", (1, P), F32)
    gd = kb.sb("gdscr", (P, 64), BF16)

    kb.dma(nc.sync, identb[:, :], aps['identb'][:, :], 'dSc')
    kb.dma(nc.sync, identf[:, :], aps['identf'][:, :], 'dSc')
    kb.dma(nc.gpsimd, wdT[:, :], aps['wdT'][:, :], 'dGc')
    kb.dma(nc.gpsimd, wupT[:, :], aps['wupT'][:, :], 'dGc')
    kb.dma(nc.gpsimd, inwT[:, :], aps['inwT'][:, :], 'dGc')
    kb.dma(nc.gpsimd, outwT[:, :], aps['outwT'][:, :], 'dGc')
    kb.dma(nc.gpsimd, fcdT[:, :], aps['fcdT'][:, :], 'dGc')
    kb.dma(nc.sync, cwcb[:, :], aps['cwcb'][:, :], 'dSc')
    kb.dma(nc.sync, drep[:, :], aps['drep'][:, :], 'dSc')
    kb.dma(nc.sync, normw[:, :], aps['normw'][:, :], 'dSc')
    kb.dma(nc.sync, bdown[:, :], aps['bdown'][:, :], 'dSc')
    kb.dma(nc.sync, bupT[:, :], aps['bupT'][:, :], 'dSc')
    kb.dma(nc.sync, g1[0:1, 0:1], aps['g1'][:, :], 'dSc')
    t_const = kb.tok('dSc')
    t_constg = kb.tok('dGc')

    kb.vpre(t_constg, t_const)
    kb.vpost(v.memset(onesb[:, :], 1.0))
    kb.vpre()
    kb.vpost(v.memset(onesf[:, :], 1.0))
    for c in range(2):
        kb.vpre()
        kb.vpost(v.tensor_scalar(outwT[:, 128 * c:128 * (c + 1)],
                                 outwT[:, 128 * c:128 * (c + 1)],
                                 normw[:, c:c + 1], None, ALU.mult))
    t_w = kb.tok('cV')
    pS = [kb.ps("pS0", (P, 512), F32), kb.ps("pS1", (P, 512), F32)]
    kb.gpre(t_w)
    tgw = kb.gpost(nc.gpsimd.tensor_copy(gd[0:1, 0:2], onesf[0:1, P - 2:P]))
    kb.wait('T', nc.tensor, tgw, st.get('l1a_pS0_war'))
    ins = nc.tensor.matmul(pS[0][:, 0:1], onesf[0:1, :], g1[0:1, 0:1],
                           start=True, stop=True)
    tt = kb.inc(ins, 'sT')
    alf = kb.sb("l1a_alf", (P, 4), F32)
    kb.apre(tt)
    kb.apost(a.activation(alf[:, 0:1], pS[0][:, 0:1], AF.Sigmoid))
    kb.vpre(kb.tok('cA'))
    kb.vpost(v.tensor_scalar(alf[:, 1:2], alf[:, 0:1], 1.0, None, ALU.add))
    kb.vpre()
    tal = kb.vpost(v.tensor_scalar(alf[:, 2:3], alf[:, 0:1], -1.0, 1.0,
                                   ALU.mult, ALU.add))
    st['l1a_pS0_war'] = kb.tok('cA')

    xn1 = kb.sb("l1a_xn1", (P, 4 * D), BF16)
    x1T = kb.sb("l1a_x1T", (P, 32 * 512), BF16)
    x1T_r = x1T[:, :].rearrange("p (f t) -> p f t", f=32)
    lres = kb.sb("l1a_res", (P, 4 * D), F32)
    xst = kb.sb("xst0", (P, D), F32)
    h0T = kb.sb("l1a_h0T", (P, 512), BF16)
    h1T = kb.sb("l1a_h1T", (P, 512), BF16)
    h2T = kb.sb("l1a_h2T", (P, 512), BF16)
    hfin = kb.sb("l1a_hfin", (P, 512), BF16)
    utile = kb.sb("l1a_utile", (P, 1024), F32)
    w = {'inwT': inwT, 'outwT': outwT, 'fcdT': fcdT, 'cwcb': cwcb,
         'drep': drep, 'onesb': onesb, 'onesf': onesf}

    pA = [kb.ps("pA%d" % k, (P, 512), F32) for k in range(4)]
    for tc in range(2):
        for k4 in range(4):
            kb.wait('S', nc.sync, st.get('l1a_xst_war'), st.get('l1a_res_stored'))
            tx = kb.dma(nc.sync, xst[:, :],
                        res_in[512 * tc + P * k4:512 * tc + P * (k4 + 1), :], 'dSx')
            tln = emit_ln_tile(kb, xst[:, :], xn1[:, D * k4:D * (k4 + 1)], tx,
                               war_toks=(st.get('l1a_xn_war'),))
            st['l1a_xst_war'] = tln
            kb.gpre(tln)
            tg = kb.gpost(nc.gpsimd.tensor_copy(
                gd[:, :], xn1[:, D * k4 + D - 64:D * (k4 + 1)]))
            kb.wait('T', nc.tensor, tg, t_constg)
            emit_transpose_tile(kb, xn1[:, D * k4:D * (k4 + 1)], x1T_r,
                                P * k4, identb[:, :])
        t_tr = (kb.tr_evac[0], kb.tr_evac[1])
        kb.wait('T', nc.tensor, st.get('l1a_pS0_war'), *t_tr)
        for f in range(32):
            ins = nc.tensor.matmul(pS[0][:, :], wdT[:, 128 * f:128 * (f + 1)],
                                   x1T[:, 512 * f:512 * (f + 1)],
                                   start=(f == 0), stop=(f == 31))
        tt = kb.inc(ins, 'sT')
        kb.apre(tt, tal, st.get('l1a_h0_war'))
        th0 = kb.apost(a.activation(h0T[:, :], pS[0][:, :], AF.Gelu,
                                    bias=bdown[:, 0:1]))
        st['l1a_pS0_war'] = th0
        kb.gpre(th0)
        tgh = kb.gpost(nc.gpsimd.tensor_copy(gd[:, :], h0T[:, 512 - 64:512]))
        st['l1a_u_guard'] = tgh
        kb.wait('T', nc.tensor, tgh)
        _l1a_hydra(kb, h0T[:, :], h1T[:, :], w)
        kb.gpre(kb.tok('cA'))
        tg1 = kb.gpost(nc.gpsimd.tensor_copy(gd[:, :], h1T[:, 512 - 64:512]))
        kb.wait('T', nc.tensor, tg1)
        _l1a_hydra(kb, h1T[:, :], h2T[:, :], w)
        kb.vpre(kb.tok('cA'))
        kb.vpost(v.tensor_scalar(hfin[:, :], h2T[:, :], alf[:, 2:3], None,
                                 ALU.mult))
        kb.vpre()
        thf = kb.vpost(v.scalar_tensor_tensor(hfin[:, :], h0T[:, :],
                                              alf[:, 1:2], hfin[:, :],
                                              ALU.mult, ALU.add))
        st['l1a_h0_war'] = thf
        kb.gpre(thf)
        tghf = kb.gpost(nc.gpsimd.tensor_copy(gd[:, :], hfin[:, 512 - 64:512]))
        kb.wait('S', nc.sync, st.get('l1a_res_stored'), st.get('l1a_lres_war'))
        for k4 in range(4):
            kb.dma(nc.sync, lres[:, D * k4:D * (k4 + 1)],
                   res_in[512 * tc + P * k4:512 * tc + P * (k4 + 1), :], 'dSr')
        t_res = kb.tok('dSr')
        for k4 in range(4):
            kb.vpre(t_res)
            kb.vpost(v.tensor_tensor(lres[:, D * k4:D * (k4 + 1)],
                                     lres[:, D * k4:D * (k4 + 1)],
                                     xn1[:, D * k4:D * (k4 + 1)], ALU.add))
        st['l1a_xn_war'] = kb.tok('cV')
        ut_war = [None, None]
        pa2_war = st.get('l1a_pA2_war')
        pend = None

        def _flush_up(pend):
            nonlocal pa2_war
            po, pb, ptgu = pend
            kb.wait('T', nc.tensor, ptgu, pa2_war)
            for m in range(4):
                ins = nc.tensor.transpose(
                    pA[2][:, 128 * m:128 * (m + 1)],
                    utile[:, 512 * pb + 128 * m:512 * pb + 128 * (m + 1)],
                    identf[:, :])
            tt2 = kb.inc(ins, 'sT')
            ut_war[pb] = tt2
            kb.vpre(tt2)
            for m in range(4):
                ins = v.tensor_tensor(lres[:, D * m + P * po:D * m + P * (po + 1)],
                                      lres[:, D * m + P * po:D * m + P * (po + 1)],
                                      pA[2][:, 128 * m:128 * (m + 1)], ALU.add)
                kb.vpost(ins)
                if m < 3:
                    kb.vpre()
            pa2_war = kb.tok('cV')

        for oc in range(32):
            ps = pS[oc % 2]
            kb.wait('T', nc.tensor, tghf, st.get('l1a_pS%d_war' % (oc % 2)))
            ins = nc.tensor.matmul(ps[:, :], wupT[:, 128 * oc:128 * (oc + 1)],
                                   hfin[:, :], start=True, stop=True)
            tt = kb.inc(ins, 'sT')
            if pend is not None:
                _flush_up(pend)
            kb.apre(tt, ut_war[oc % 2])
            tev = kb.apost(a.activation(utile[:, 512 * (oc % 2):512 * (oc % 2 + 1)],
                                        ps[:, :], AF.Identity,
                                        bias=bupT[:, oc:oc + 1]))
            st['l1a_pS%d_war' % (oc % 2)] = tev
            kb.gpre(tev)
            tgu = kb.gpost(nc.gpsimd.tensor_copy(
                gd[:, :], utile[:, 512 * (oc % 2) + 448:512 * (oc % 2) + 512]))
            pend = (oc, oc % 2, tgu)
        _flush_up(pend)
        st['l1a_pA2_war'] = pa2_war
        gdf = kb.sb("gdscrf", (P, 64), F32)
        for k4 in range(4):
            kb.gpre(pa2_war)
            tgr = kb.gpost(nc.gpsimd.tensor_copy(
                gdf[:, :], lres[:, D * k4 + D - 64:D * (k4 + 1)]))
            kb.wait('A', a, tgr)
            kb.dma(nc.scalar, res_out[512 * tc + P * k4:512 * tc + P * (k4 + 1), :],
                   lres[:, D * k4:D * (k4 + 1)], 'dV')
        st['l1a_res_stored'] = kb.tok('dV')
        st['l1a_lres_war'] = pa2_war


def _l1f_build_bp(kb, stg_f, tload, scT, nt, bp_out):
    nc = kb.nc
    v = nc.vector
    st = kb.state
    inv = kb.sb("l1f_inv", (P, 86), F32)
    stg = kb.sb("l1f_wg", (P, 4096), BF16)[:, 0:86 * 32]
    gd = kb.sb("gdscr", (P, 64), BF16)
    identb = kb.sb("identb", (P, P), BF16)
    pT = kb.ps("pT0", (P, 1024), BF16)
    kb.vpre(tload, st.get('l1f_inv_war'))
    kb.vpost(v.reciprocal(inv[:, 0:nt], scT))
    kb.vpre()
    kb.vpost(v.tensor_scalar(inv[:, 0:nt], inv[:, 0:nt], 4.0, None, ALU.mult))
    for ot in range(nt):
        kb.vpre(st.get('l1f_bstage_war') if ot == 0 else None)
        kb.vpost(v.tensor_scalar(stg[:, 32 * ot:32 * (ot + 1)],
                                 stg_f[:, 32 * ot:32 * (ot + 1)],
                                 inv[:, ot:ot + 1], None, ALU.mult))
    tv = kb.tok('cV')
    st['l1f_inv_war'] = tv
    kb.gpre(tv)
    tg = kb.gpost(nc.gpsimd.tensor_copy(gd[:, 0:32],
                                        stg[:, 32 * (nt - 1):32 * nt]))
    evtok = None
    for g in range((nt + 7) // 8):
        n = min(8, nt - 8 * g)
        kb.wait('T', nc.tensor, tg, evtok, st.get('l1f_pT_war'))
        for m in range(n):
            ins = nc.tensor.transpose(pT[0:32, 128 * m:128 * (m + 1)],
                                      stg[:, 32 * (8 * g + m):32 * (8 * g + m + 1)],
                                      identb[:, :])
        tt = kb.inc(ins, 'sT')
        kb.vpre(tt)
        evtok = kb.vpost(v.tensor_copy(bp_out[0:32, 1024 * g:1024 * g + 128 * n],
                                       pT[0:32, 0:128 * n]))
    st['l1f_bstage_war'] = evtok
    st['l1f_pT_war'] = evtok
    return evtok


def emit_l1_ffn(kb, aps):
    nc = kb.nc
    v, a = nc.vector, nc.scalar
    res_in, res_out = aps['res_in'], aps['res_out']
    st = kb.state
    identb = kb.sb("identb", (P, P), BF16)
    kb.dma(nc.sync, identb[:, :], aps['identb'][:, :], 'dSc')

    hcol = kb.sb("l1f_hcol", (P, 86 * 512), BF16)
    xnT = hcol[:, 0:32 * 1024]
    xnT_r = xnT.rearrange("p (f t) -> p f t", f=32)
    gBp = hcol[0:32, 32 * 1024:32 * 1024 + 86 * 128]
    xst = kb.sb("xst0", (P, D), F32)
    xn = kb.sb("l1f_xn", (P, D), BF16)
    gd = kb.sb("gdscr", (P, 64), BF16)
    uBp = kb.sb("l1f_uBp", (32, 86 * 128), BF16)
    B2p = kb.sb("l1f_B2p", (32, 32 * 128), BF16)
    wd = kb.sb("l1f_wd", (P, 2 * 8 * 512), BF16)
    gA_sb = wd[:, 0:1024]
    uA_sb = wd[:, 1024:2048]
    A2_sb = wd[:, 2048:2048 + 86 * 32]
    t1g = kb.sb("l1f_t1g", (32, 1024), BF16)
    t1u = kb.sb("l1f_t1u", (32, 1024), BF16)
    t2 = kb.sb("l1f_t2", (32, 1024), BF16)
    gscT = kb.sb("l1f_gsc", (P, 86), F32)
    uscT = kb.sb("l1f_usc", (P, 86), F32)
    dscT = kb.sb("l1f_dsc", (P, 32), F32)
    gsc005 = kb.sb("l1f_gsc5", (P, 86), F32)
    usc005 = kb.sb("l1f_usc5", (P, 86), F32)
    xsrow = kb.sb("l1f_xsrow", (1, 1024), BF16)
    rsrow = kb.sb("l1f_rsrow", (1, 1024), BF16)
    m75 = kb.sb("l1f_m75", (1, 512), BF16)
    onesb = kb.sb("l1a_onesb", (P, 1), BF16)
    onesf = kb.sb("l1a_onesf", (1, P), F32)
    s2bc = kb.sb("l1f_s2bc", (P, 4096), F32)
    dsrow = s2bc[0:1, :]
    stg_f = s2bc[:, 0:86 * 32]
    wg = kb.sb("l1f_wg", (P, 4096), BF16)
    wu = kb.sb("l1f_wu", (P, 4096), BF16)
    h_sb = kb.sb("l1f_h", (P, 4 * 512), BF16)
    gs_sb = kb.sb("l1f_gs", (P, 512), BF16)
    us_sb = kb.sb("l1f_us", (P, 512), BF16)
    ytmp = xst[:, 0:2 * 512]
    pS = [kb.ps("pS0", (P, 512), F32), kb.ps("pS1", (P, 512), F32)]
    pA = [kb.ps("pA%d" % k, (P, 512), F32) for k in range(4)]

    kb.dma(nc.sync, gscT[:, :], aps['gscT'][:, :], 'dSc')
    kb.dma(nc.sync, uscT[:, :], aps['uscT'][:, :], 'dSc')
    kb.dma(nc.sync, dscT[:, :], aps['dscT'][:, :], 'dSc')
    kb.dma(nc.gpsimd, gA_sb, aps['gA'][:, :], 'dGc')
    kb.dma(nc.gpsimd, uA_sb, aps['uA'][:, :], 'dGc')
    kb.dma(nc.gpsimd, A2_sb, aps['A2'][:, :], 'dGc')
    t_const = kb.tok('dSc')
    t_constg = kb.tok('dGc')
    kb.vpre(t_const, t_constg)
    kb.vpost(v.memset(onesb[:, :], 1.0))
    kb.vpre()
    kb.vpost(v.memset(onesf[:, :], 1.0))
    kb.vpre()
    kb.vpost(v.memset(m75[0:1, :], -7.5))
    kb.vpre()
    kb.vpost(v.tensor_scalar(gsc005[:, :], gscT[:, :], 0.005, None, ALU.mult))
    kb.vpre()
    kb.vpost(v.tensor_scalar(usc005[:, :], uscT[:, :], 0.005, None, ALU.mult))
    for k in range(8):
        kb.wait('S', nc.sync, st.get('l1f_xst_war'))
        tx = kb.dma(nc.sync, xst[:, :], res_in[P * k:P * (k + 1), :], 'dSx')
        tln = emit_ln_tile(kb, xst[:, :], xn[:, :], tx,
                           war_toks=(st.get('l1f_xn_war'),))
        st['l1f_xst_war'] = tln
        kb.gpre(tln)
        tg = kb.gpost(nc.gpsimd.tensor_copy(gd[:, :], xn[:, D - 64:D]))
        kb.wait('T', nc.tensor, tg, t_const)
        st['l1f_xn_war'] = emit_transpose_tile(kb, xn[:, :], xnT_r, P * k,
                                               identb[:, :])
    t_tr = (kb.tr_evac[0], kb.tr_evac[1])
    kb.gpre(kb.tok('cV'))
    tgo = kb.gpost(nc.gpsimd.tensor_copy(gd[0:1, 0:2], onesf[0:1, P - 2:P]))
    for tc in range(2):
        kb.wait('T', nc.tensor, tgo, st.get('l1f_ps0_war'), *t_tr)
        for f in range(32):
            ins = nc.tensor.matmul(pS[0][0:1, :], onesb[:, 0:1],
                                   xnT[:, 1024 * f + 512 * tc:1024 * f + 512 * (tc + 1)],
                                   start=(f == 0), stop=(f == 31))
        tt = kb.inc(ins, 'sT')
        kb.vpre(tt)
        st['l1f_ps0_war'] = kb.vpost(
            v.tensor_copy(xsrow[0:1, 512 * tc:512 * (tc + 1)], pS[0][0:1, :]))
    kb.gpre(kb.tok('cV'))
    tgxs = kb.gpost(nc.gpsimd.tensor_copy(gd[0:1, :], xsrow[0:1, 1024 - 64:1024]))

    tl = kb.dma(nc.sync, stg_f[:, :], aps['gBT'][:, :], 'dSb2')
    tb1 = _l1f_build_bp(kb, stg_f, tl, gscT[:, :], 86, gBp)
    kb.wait('S', nc.sync, kb.tok('cV'))
    tl = kb.dma(nc.sync, stg_f[:, :], aps['uBT'][:, :], 'dSb2')
    tb2 = _l1f_build_bp(kb, stg_f, tl, uscT[:, :], 86, uBp)
    kb.wait('S', nc.sync, kb.tok('cV'))
    tl = kb.dma(nc.sync, stg_f[:, 0:32 * 32], aps['B2T'][:, :], 'dSb2')
    tb3 = _l1f_build_bp(kb, stg_f, tl, dscT[:, :], 32, B2p)
    for (t1, A_sb, warkey) in ((t1g, gA_sb, 'l1f_t1g_war'),
                               (t1u, uA_sb, 'l1f_t1u_war')):
        for tc in range(2):
            kb.wait('T', nc.tensor, t_constg, st.get('l1f_ps1_war'), *t_tr)
            for f in range(32):
                ins = nc.tensor.matmul(
                    pS[1][0:32, :], A_sb[:, 32 * f:32 * (f + 1)],
                    xnT[:, 1024 * f + 512 * tc:1024 * f + 512 * (tc + 1)],
                    start=(f == 0), stop=(f == 31))
            tt = kb.inc(ins, 'sT')
            kb.vpre(tt)
            st['l1f_ps1_war'] = kb.vpost(
                v.tensor_copy(t1[0:32, 512 * tc:512 * (tc + 1)], pS[1][0:32, :]))
    kb.wait('S', nc.sync, kb.tok('cV'))
    tdsl = kb.dma(nc.sync, s2bc[0:1, :], aps['dsrow'][:, :], 'dSc2')
    kb.vpre(tdsl, st.get('l1f_inv_war'), st.get('l1f_bstage_war'))
    tds = kb.vpost(v.tensor_scalar(dsrow[0:1, :], dsrow[0:1, :], 0.005, None,
                                   ALU.mult))
    kb.gpre(tds)
    tgd = kb.gpost(nc.gpsimd.tensor_copy(gd[0:1, :],
                                         dsrow[0:1, 4096 - 64:4096]))
    for i in range(8):
        ps = pS[i % 2]
        kb.wait('T', nc.tensor, tgd, st.get('l1f_ps%d_war' % (i % 2)))
        ins = nc.tensor.matmul(ps[:, :], onesf[0:1, :],
                               dsrow[0:1, 512 * i:512 * (i + 1)],
                               start=True, stop=True)
        tt = kb.inc(ins, 'sT')
        kb.vpre(tt)
        st['l1f_ps%d_war' % (i % 2)] = kb.vpost(
            v.tensor_copy(s2bc[:, 512 * i:512 * (i + 1)], ps[:, :]))

    kb.gpre(kb.tok('cV'))
    tgr1 = kb.gpost(nc.gpsimd.tensor_copy(gd[0:32, 0:32],
                                          t1g[0:32, 1024 - 32:1024]))
    kb.gpre()
    tgrow = kb.gpost(nc.gpsimd.tensor_copy(gd[0:32, 0:32],
                                           t1u[0:32, 1024 - 32:1024]))

    wgs = [wg, kb.sb("l1f_wgb", (P, 4096), BF16)]
    wus = [wu, kb.sb("l1f_wub", (P, 4096), BF16)]
    wg_war = [None, None]
    wu_war = [None, None]
    ev_war = {'g': None, 'u': None}
    h_war = [None, None, None, None]
    hst = [None, None, None, None]
    for ot in range(86):
        par = ot % 2
        wgc, wuc = wgs[par], wus[par]
        tok0 = (st.get('l1f_pT_war') if ot == 0
                else (st.get('l1f_xst_war') if ot == 1 else None))
        kb.wait('S', nc.sync, wg_war[par], tok0)
        tqg = kb.dma(nc.sync, wgc[:, :],
                     aps['gqT'][P * ot:P * (ot + 1), :], 'dGg')
        kb.wait('A', a, wu_war[par], tok0)
        tqu = kb.dma(nc.scalar, wuc[:, :],
                     aps['uqT'][P * ot:P * (ot + 1), :], 'dAu')
        for tc in range(2):
            slot = 2 * par + tc
            pslot = 2 * (1 - par) + tc
            kb.wait('T', nc.tensor, tqg, ev_war['g'], tgrow, tgr1, tgo, tgxs,
                    tb1, tb2, tb3)
            for hc in range(32):
                ins = nc.tensor.matmul(
                    pS[0][:, :], wgc[:, 128 * hc:128 * (hc + 1)],
                    xnT[:, 1024 * hc + 512 * tc:1024 * hc + 512 * (tc + 1)],
                    start=(hc == 0), stop=False)
            ins = nc.tensor.matmul(pS[0][:, :], m75[0:1, 0:P],
                                   xsrow[0:1, 512 * tc:512 * (tc + 1)],
                                   start=False, stop=False)
            ins = nc.tensor.matmul(pS[0][:, :], gBp[0:32, 128 * ot:128 * (ot + 1)],
                                   t1g[0:32, 512 * tc:512 * (tc + 1)],
                                   start=False, stop=True)
            ttg = kb.inc(ins, 'sT')
            kb.wait('T', nc.tensor, tqu, ev_war['u'])
            for hc in range(32):
                ins = nc.tensor.matmul(
                    pS[1][:, :], wuc[:, 128 * hc:128 * (hc + 1)],
                    xnT[:, 1024 * hc + 512 * tc:1024 * hc + 512 * (tc + 1)],
                    start=(hc == 0), stop=False)
            ins = nc.tensor.matmul(pS[1][:, :], m75[0:1, 0:P],
                                   xsrow[0:1, 512 * tc:512 * (tc + 1)],
                                   start=False, stop=False)
            ins = nc.tensor.matmul(pS[1][:, :], uBp[0:32, 128 * ot:128 * (ot + 1)],
                                   t1u[0:32, 512 * tc:512 * (tc + 1)],
                                   start=False, stop=True)
            ttu = kb.inc(ins, 'sT')
            wu_war[par] = ttu
            if tc == 1:
                wg_war[par] = ttg
            kb.apre(ttg, st.get('l1f_gs_war'))
            tevg = kb.apost(a.activation(gs_sb[:, :], pS[0][:, :], AF.Silu,
                                         scale=gsc005[:, ot:ot + 1]))
            ev_war['g'] = tevg
            kb.vpre(ttu, st.get('l1f_us_war'))
            tevu = kb.vpost(v.tensor_scalar(us_sb[:, :], pS[1][:, :],
                                            usc005[:, ot:ot + 1], None, ALU.mult))
            ev_war['u'] = tevu
            kb.vpre(tevg, h_war[slot], hst[slot])
            th = kb.vpost(v.tensor_tensor(h_sb[:, 512 * slot:512 * (slot + 1)],
                                          gs_sb[:, :], us_sb[:, :], ALU.mult))
            st['l1f_gs_war'] = th
            st['l1f_us_war'] = th
            kb.wait('A', a, th)
            hst[slot] = kb.dma(nc.scalar,
                               aps['h_scratch'][P * ot:P * (ot + 1),
                                                512 * tc:512 * (tc + 1)],
                               h_sb[:, 512 * slot:512 * (slot + 1)],
                               'dGh%d' % slot)
            if ot > 0:
                kb.wait('T', nc.tensor, st.get('l1f_h_guard%d' % pslot))
                ins = nc.tensor.matmul(pA[2 + tc][0:32, :],
                                       A2_sb[:, 32 * (ot - 1):32 * ot],
                                       h_sb[:, 512 * pslot:512 * (pslot + 1)],
                                       start=(ot == 1), stop=False)
                h_war[pslot] = kb.inc(ins, 'sT')
            kb.gpre(th)
            st['l1f_h_guard%d' % slot] = kb.gpost(
                nc.gpsimd.tensor_copy(gd[:, :],
                                      h_sb[:, 512 * (slot + 1) - 64:512 * (slot + 1)]))
    for tc in range(2):
        slot = 2 * (85 % 2) + tc
        kb.wait('T', nc.tensor, st.get('l1f_h_guard%d' % slot))
        ins = nc.tensor.matmul(pA[2 + tc][0:32, :], A2_sb[:, 32 * 85:32 * 86],
                               h_sb[:, 512 * slot:512 * (slot + 1)],
                               start=False, stop=True)
        h_war[slot] = kb.inc(ins, 'sT')
        kb.apre(h_war[slot], st.get('l1f_t2_war'))
        st['l1f_t2_war'] = kb.apost(
            a.activation(t2[0:32, 512 * tc:512 * (tc + 1)], pA[2 + tc][0:32, :],
                         AF.Identity))
    kb.gpre(st['l1f_t2_war'])
    tgt2 = kb.gpost(nc.gpsimd.tensor_copy(gd[0:32, 0:32], t2[0:32, 992:1024]))

    kb.gpre()
    tpre = kb.dma(nc.gpsimd, res_out[:, :], res_in[:, :], 'dGcp')

    yst = [None, None]
    ytp_war = None
    for tc in range(2):
        hwaits = (kb.tok('dGh0'), kb.tok('dGh1'), kb.tok('dGh2'),
                  kb.tok('dGh3'), ('sT', kb.cnt['sT']), st.get('l1f_hcol_war'))
        th = []
        for (en, qe, _), (i0, i1) in zip(
                (('S', nc.sync, 0), ('A', nc.scalar, 0), ('G', nc.gpsimd, 0)),
                ((0, 29), (29, 58), (58, 86))):
            kb.wait(en, qe, *hwaits)
            th.append(kb.dma(
                qe, hcol[:, 512 * i0:512 * i1].rearrange("p (i t) -> p i t",
                                                         i=i1 - i0),
                aps['h_scratch'][P * i0:P * i1, 512 * tc:512 * (tc + 1)]
                .rearrange("(i p) t -> p i t", p=P), 'dSh'))
        kb.wait('T', nc.tensor, *th, st.get('l1f_ps0_war'))
        for i in range(86):
            ins = nc.tensor.matmul(pS[0][0:1, :], onesb[:, 0:1],
                                   hcol[:, 512 * i:512 * (i + 1)],
                                   start=(i == 0), stop=(i == 85))
        tt = kb.inc(ins, 'sT')
        kb.vpre(tt)
        trs = kb.vpost(v.tensor_copy(rsrow[0:1, 512 * tc:512 * (tc + 1)],
                                     pS[0][0:1, :]))
        st['l1f_ps0_war'] = trs
        kb.gpre(trs)
        tgrs = kb.gpost(nc.gpsimd.tensor_copy(
            gd[0:1, :], rsrow[0:1, 512 * (tc + 1) - 64:512 * (tc + 1)]))
        wd_tok = [None, None, None, None]
        wd_war = [None, None, None, None]
        QS = (('G', nc.gpsimd, 'dSd0'), ('A', nc.scalar, 'dSd1'),
              ('S', nc.sync, 'dSd2'))
        for o2c in range(8):
            for g in range(22):
                slot = g % 4
                n = min(4, 86 - 4 * g)
                qeng, qe, qsem = QS[g % 3]
                kb.wait(qeng, qe, wd_war[slot], st.get('l1f_t2_war'))
                wd_tok[slot] = kb.dma(
                    qe,
                    wd[:, 2048 * slot:2048 * slot + 512 * n]
                    .rearrange("p (i o) -> p i o", i=n),
                    aps['dqT'][(o2c * 86 + 4 * g) * P:(o2c * 86 + 4 * g + n) * P, :]
                    .rearrange("(i p) o -> p i o", p=P), qsem)
                kb.wait('T', nc.tensor, wd_tok[slot], tgt2, tgrs,
                        ytp_war if g == 0 else None)
                for m in range(n):
                    i = 4 * g + m
                    for k in range(4):
                        ins = nc.tensor.matmul(
                            pA[k][:, :],
                            hcol[:, 512 * i + 128 * k:512 * i + 128 * (k + 1)],
                            wd[:, 2048 * slot + 512 * m:2048 * slot + 512 * (m + 1)],
                            start=(i == 0), stop=False)
                wd_war[slot] = kb.inc(ins, 'sT')
            for k in range(4):
                ins = nc.tensor.matmul(
                    pA[k][:, :],
                    rsrow[0:1, 512 * tc + 128 * k:512 * tc + 128 * (k + 1)],
                    m75[0:1, :], start=False, stop=False)
                ins = nc.tensor.matmul(
                    pA[k][:, :],
                    t2[0:32, 512 * tc + 128 * k:512 * tc + 128 * (k + 1)],
                    B2p[0:32, 512 * o2c:512 * (o2c + 1)],
                    start=False, stop=True)
                tt = kb.inc(ins, 'sT')
                ypar = k % 2
                kb.vpre(tt, yst[ypar])
                tev = kb.vpost(v.tensor_tensor(
                    ytmp[:, 512 * ypar:512 * (ypar + 1)], pA[k][:, :],
                    s2bc[:, 512 * o2c:512 * (o2c + 1)], ALU.mult))
                kb.gpre(tev, tpre)
                yst[ypar] = kb.dma(
                    nc.gpsimd,
                    res_out[512 * tc + P * k:512 * tc + P * (k + 1),
                            512 * o2c:512 * (o2c + 1)],
                    ytmp[:, 512 * ypar:512 * (ypar + 1)], 'dGy%d' % ypar,
                    accum_op=ALU.add)
            ytp_war = kb.tok('cV')
        st['l1f_hcol_war'] = kb.tok('sT')
    kb.wait('S', nc.sync, kb.tok('dGy0'), kb.tok('dGy1'))


def _prep_l0f_weights(q_packed, q_scales, q_zeros, delta_weight, bias):
    b = np.ascontiguousarray(q_packed).view(np.uint8).reshape(-1, 4)[:, 0]
    b2 = b.reshape(4096, 2048)
    qbT = np.ascontiguousarray(
        b2.reshape(32, 128, 16, 128).transpose(0, 3, 2, 1)).reshape(32 * 128, 2048)
    qbT = np.concatenate([qbT & np.uint8(15), qbT >> np.uint8(4)], axis=1)
    F, JC, Q = np.meshgrid(np.arange(128), np.arange(16), np.arange(2),
                           indexing='ij')
    ii = 2 * (128 * JC + F) + Q
    s2d = q_scales.reshape(4096, 32)
    cc = ii // 128
    sexpT = np.ascontiguousarray(
        s2d[:, cc.ravel()].reshape(4096, 128, 16, 2)
        .transpose(1, 2, 3, 0).reshape(128, 16, 2, 32, 128)
        .transpose(3, 0, 1, 2, 4)).reshape(32 * 128, 4096).astype(np.float32)
    deltaT = np.ascontiguousarray(
        delta_weight[:, ii.ravel()].reshape(4096, 128, 16, 2)
        .transpose(1, 2, 3, 0).reshape(128, 16, 2, 32, 128)
        .transpose(3, 0, 1, 2, 4)).reshape(32 * 128, 4096).astype(BF)
    s2dT = np.ascontiguousarray(q_scales.reshape(4096, 32).T)
    z2dT = np.ascontiguousarray(q_zeros.reshape(4096, 32).T)
    indE = np.zeros((128, 32, 32), dtype=BF)
    for pc in range(32):
        q, jc = pc // 16, pc % 16
        for f in range(128):
            c = 2 * jc + (1 if f >= 64 else 0)
            indE[f, pc, c] = 1.0
    indE = indE.reshape(128, 1024)
    biasT = np.ascontiguousarray(bias.reshape(32, 128).T).astype(np.float32)
    return {"qbT": qbT, "sexpT": sexpT, "deltaT": deltaT, "s2dT": s2dT,
            "z2dT": z2dT, "indE": indE, "biasT": biasT,
            "identb": np.eye(P, dtype=BF),
            "identf": np.eye(P, dtype=np.float32)}


def _l0f_aps(nc):
    return {
        'qbT': nc.dram_tensor("qbT", [32 * P, 4096], U8, kind="ExternalInput").ap(),
        'sexpT': nc.dram_tensor("sexpT", [32 * P, 4096], F32, kind="ExternalInput").ap(),
        'deltaT': nc.dram_tensor("deltaT", [32 * P, 4096], BF16, kind="ExternalInput").ap(),
        's2dT': nc.dram_tensor("s2dT", [32, 4096], F32, kind="ExternalInput").ap(),
        'z2dT': nc.dram_tensor("z2dT", [32, 4096], F32, kind="ExternalInput").ap(),
        'indE': nc.dram_tensor("indE", [P, 1024], BF16, kind="ExternalInput").ap(),
        'biasT': nc.dram_tensor("biasT", [P, 32], F32, kind="ExternalInput").ap(),
        'identb': _dram(nc, "identb", [P, P], BF16),
        'identf': _dram(nc, "identf", [P, P], F32),
        'weff_scratch': nc.dram_tensor("weff_scratch", [32 * P, 4096], BF16,
                                       kind="Internal").ap(),
    }


def build_l0f_program():
    nc = bass.Bass("TRN2", target_bir_lowering=False)
    aps = _l0f_aps(nc)
    aps['res_in'] = nc.dram_tensor("res_in", [TOKC, D], F32, kind="ExternalInput").ap()
    aps['res_out'] = nc.dram_tensor("res_out", [TOKC, D], F32, kind="ExternalOutput").ap()
    kb = KB(nc)
    kb.OFFSETS = _mk_offsets()
    emit_l0_ffn(kb, aps)
    kb.wait('S', nc.sync, kb.tok('dGy0'), kb.tok('dGy1'))
    kb.close()
    return nc


def kernel_l0f_only(r1_full, q_packed, q_scales, q_zeros, delta_weight, bias):
    r1 = np.asarray(r1_full).reshape(T, D)
    w = _prep_l0f_weights(np.asarray(q_packed), np.asarray(q_scales),
                          np.asarray(q_zeros), np.asarray(delta_weight),
                          np.asarray(bias))
    nc = build_l0f_program()
    in_maps = []
    for c in range(NCORE):
        m = dict(w)
        m['res_in'] = np.ascontiguousarray(
            np.concatenate([r1[STR * c:STR * (c + 1)],
                            r1[STR * (c + 8):STR * (c + 9)]]))
        in_maps.append(m)
    res = run_bass_kernel_spmd(nc, in_maps, core_ids=list(range(NCORE)))
    out = np.empty((T, D), np.float32)
    for c in range(NCORE):
        r = np.asarray(res.results[c]["res_out"])
        out[STR * c:STR * (c + 1)] = r[:STR]
        out[STR * (c + 8):STR * (c + 9)] = r[STR:]
    return out.reshape(1, T, D)


def _prep_l1a_weights(p):
    wd = np.ascontiguousarray(
        p['w_down'].T.reshape(32, 128, 128).transpose(1, 0, 2).reshape(128, 4096)
    ).astype(np.float32)
    wu = np.ascontiguousarray(p['w_up'].T).astype(np.float32)
    inw = np.ascontiguousarray(p['hy_in_w'][:512, :].T).astype(np.float32)
    outw = np.ascontiguousarray(
        p['hy_out_w'].T.reshape(2, 128, 128).transpose(1, 0, 2).reshape(128, 256)
    ).astype(np.float32)
    fr = p['hy_fcD_w'][np.repeat(np.arange(4), 64), :].T
    fcd = np.ascontiguousarray(
        fr.reshape(2, 128, 2, 128).transpose(1, 0, 2, 3).reshape(128, 512)
    ).astype(np.float32)
    cw = p['hy_conv_w'][0:256, 0, 6].reshape(2, 128).T
    cb = p['hy_conv_b'][0:256].reshape(2, 128).T
    cwcb = np.ascontiguousarray(np.concatenate([cw, cb], 1)).astype(np.float32)
    drep = np.ascontiguousarray(
        np.repeat(p['hy_D'], 64).reshape(2, 128).T).astype(np.float32)
    normw = np.ascontiguousarray(
        p['hy_norm_w'].reshape(2, 128).T).astype(np.float32)
    return {
        'wdT': wd, 'wupT': wu, 'inwT': inw, 'outwT': outw, 'fcdT': fcd,
        'cwcb': cwcb, 'drep': drep, 'normw': normw,
        'bdown': np.ascontiguousarray(p['b_down'].reshape(128, 1)).astype(np.float32),
        'bupT': np.ascontiguousarray(p['b_up'].reshape(32, 128).T).astype(np.float32),
        'g1': np.asarray(p['gate1']).reshape(1, 1).astype(np.float32),
        'identb': np.eye(P, dtype=BF),
        'identf': np.eye(P, dtype=np.float32),
    }


def _l1a_aps(nc):
    d = {'identb': _dram(nc, "identb", [P, P], BF16),
         'identf': _dram(nc, "identf", [P, P], F32)}
    for nm, shp, dt in (('wdT', (P, 4096), F32), ('wupT', (P, 4096), F32),
                        ('inwT', (P, 512), F32), ('outwT', (P, 256), F32),
                        ('fcdT', (P, 512), F32), ('cwcb', (P, 4), F32),
                        ('drep', (P, 2), F32), ('normw', (P, 2), F32),
                        ('bdown', (P, 1), F32), ('bupT', (P, 32), F32),
                        ('g1', (1, 1), F32), ):
        d[nm] = nc.dram_tensor(nm, list(shp), dt, kind="ExternalInput").ap()
    return d


def build_l1a_program():
    nc = bass.Bass("TRN2", target_bir_lowering=False)
    aps = _l1a_aps(nc)
    aps['res_in'] = nc.dram_tensor("res_in", [TOKC, D], F32, kind="ExternalInput").ap()
    aps['res_out'] = nc.dram_tensor("res_out", [TOKC, D], F32, kind="ExternalOutput").ap()
    kb = KB(nc)
    kb.OFFSETS = _mk_offsets()
    emit_l1_attn(kb, aps)
    kb.wait('S', nc.sync, kb.tok('dV'))
    kb.close()
    return nc


def kernel_l1a_only(r2_full, p):
    r2 = np.asarray(r2_full).reshape(T, D)
    w = _prep_l1a_weights(p)
    nc = build_l1a_program()
    in_maps = []
    for c in range(NCORE):
        m = dict(w)
        m['res_in'] = np.ascontiguousarray(
            np.concatenate([r2[STR * c:STR * (c + 1)],
                            r2[STR * (c + 8):STR * (c + 9)]]))
        in_maps.append(m)
    res = run_bass_kernel_spmd(nc, in_maps, core_ids=list(range(NCORE)))
    out = np.empty((T, D), np.float32)
    for c in range(NCORE):
        r = np.asarray(res.results[c]["res_out"])
        out[STR * c:STR * (c + 1)] = r[:STR]
        out[STR * (c + 8):STR * (c + 9)] = r[STR:]
    return out.reshape(1, T, D)


def _prep_l1f_weights(i4):
    def qbytes(q):
        return np.ascontiguousarray(q).view(np.uint8).reshape(-1, 4)[:, 0]
    gb = qbytes(i4['gate_q']).reshape(11008, 4096)
    ub = qbytes(i4['up_q']).reshape(11008, 4096)
    db = qbytes(i4['down_q']).reshape(4096, 11008)
    gqT = np.ascontiguousarray(
        gb.T.reshape(32, 128, 86, 128).transpose(2, 1, 0, 3)).reshape(
            86 * 128, 4096).astype(BF)
    uqT = np.ascontiguousarray(
        ub.T.reshape(32, 128, 86, 128).transpose(2, 1, 0, 3)).reshape(
            86 * 128, 4096).astype(BF)
    dqT = np.ascontiguousarray(
        db.T.reshape(86, 128, 8, 512).transpose(2, 0, 1, 3)).reshape(
            8 * 86 * 128, 512).astype(BF)
    gBT = np.ascontiguousarray(
        i4['gate_B'].T.reshape(86, 128, 32).transpose(1, 0, 2)).reshape(128, 86 * 32)
    uBT = np.ascontiguousarray(
        i4['up_B'].T.reshape(86, 128, 32).transpose(1, 0, 2)).reshape(128, 86 * 32)
    B2T = np.ascontiguousarray(
        i4['down_B'].T.reshape(32, 128, 32).transpose(1, 0, 2)).reshape(128, 32 * 32)
    gA = np.ascontiguousarray(
        i4['gate_A'].reshape(32, 128, 32).transpose(1, 0, 2)).reshape(128, 1024)
    uA = np.ascontiguousarray(
        i4['up_A'].reshape(32, 128, 32).transpose(1, 0, 2)).reshape(128, 1024)
    A2 = np.ascontiguousarray(
        i4['down_A'].reshape(86, 128, 32).transpose(1, 0, 2)).reshape(128, 86 * 32)
    return {
        'gqT': gqT, 'uqT': uqT, 'dqT': dqT,
        'gBT': gBT.astype(np.float32), 'uBT': uBT.astype(np.float32),
        'B2T': B2T.astype(np.float32),
        'gA': gA.astype(np.float32), 'uA': uA.astype(np.float32),
        'A2': A2.astype(np.float32),
        'gscT': np.ascontiguousarray(i4['gate_scale'].reshape(86, 128).T).astype(np.float32),
        'uscT': np.ascontiguousarray(i4['up_scale'].reshape(86, 128).T).astype(np.float32),
        'dscT': np.ascontiguousarray(i4['down_scale'].reshape(32, 128).T).astype(np.float32),
        'dsrow': np.ascontiguousarray(i4['down_scale'].reshape(1, 4096)).astype(np.float32),
        'identb': np.eye(P, dtype=BF),
    }


def _l1f_aps(nc):
    d = {
        'gqT': nc.dram_tensor("gqT", [86 * P, 4096], BF16, kind="ExternalInput").ap(),
        'uqT': nc.dram_tensor("uqT", [86 * P, 4096], BF16, kind="ExternalInput").ap(),
        'dqT': nc.dram_tensor("dqT", [8 * 86 * P, 512], BF16, kind="ExternalInput").ap(),
        'gBT': nc.dram_tensor("gBT", [P, 86 * 32], F32, kind="ExternalInput").ap(),
        'uBT': nc.dram_tensor("uBT", [P, 86 * 32], F32, kind="ExternalInput").ap(),
        'B2T': nc.dram_tensor("B2T", [P, 32 * 32], F32, kind="ExternalInput").ap(),
        'gA': nc.dram_tensor("gA", [P, 1024], F32, kind="ExternalInput").ap(),
        'uA': nc.dram_tensor("uA", [P, 1024], F32, kind="ExternalInput").ap(),
        'A2': nc.dram_tensor("A2", [P, 86 * 32], F32, kind="ExternalInput").ap(),
        'gscT': nc.dram_tensor("gscT", [P, 86], F32, kind="ExternalInput").ap(),
        'uscT': nc.dram_tensor("uscT", [P, 86], F32, kind="ExternalInput").ap(),
        'dscT': nc.dram_tensor("dscT", [P, 32], F32, kind="ExternalInput").ap(),
        'dsrow': nc.dram_tensor("dsrow", [1, 4096], F32, kind="ExternalInput").ap(),
        'identb': _dram(nc, "identb", [P, P], BF16),
        'h_scratch': nc.dram_tensor("h_scratch", [86 * P, 1024], BF16,
                                    kind="Internal").ap(),
    }
    return d


def build_l1f_program():
    nc = bass.Bass("TRN2", target_bir_lowering=False)
    aps = _l1f_aps(nc)
    aps['res_in'] = nc.dram_tensor("res_in", [TOKC, D], F32, kind="ExternalInput").ap()
    aps['res_out'] = nc.dram_tensor("res_out", [TOKC, D], F32, kind="ExternalOutput").ap()
    kb = KB(nc)
    kb.OFFSETS = _mk_offsets()
    emit_l1_ffn(kb, aps)
    kb.close()
    return nc


def kernel_l1f_only(r3_full, i4):
    r3 = np.asarray(r3_full).reshape(T, D)
    w = _prep_l1f_weights(i4)
    nc = build_l1f_program()
    in_maps = []
    for c in range(NCORE):
        m = dict(w)
        m['res_in'] = np.ascontiguousarray(
            np.concatenate([r3[STR * c:STR * (c + 1)],
                            r3[STR * (c + 8):STR * (c + 9)]]))
        in_maps.append(m)
    res = run_bass_kernel_spmd(nc, in_maps, core_ids=list(range(NCORE)))
    out = np.empty((T, D), np.float32)
    for c in range(NCORE):
        r = np.asarray(res.results[c]["res_out"])
        out[STR * c:STR * (c + 1)] = r[:STR]
        out[STR * (c + 8):STR * (c + 9)] = r[STR:]
    return out.reshape(1, T, D)


def _core_slices(x):
    return [np.ascontiguousarray(np.concatenate(
        [x[STR * c:STR * (c + 1)], x[STR * (c + 8):STR * (c + 9)]]))
        for c in range(NCORE)]


def _gather(res, key="res_out"):
    out = np.empty((T, D), np.float32)
    for c in range(NCORE):
        r = np.asarray(res.results[c][key])
        out[STR * c:STR * (c + 1)] = r[:STR]
        out[STR * (c + 8):STR * (c + 9)] = r[STR:]
    return out


def _unused_multilaunch(L0_attn_x, L0_attn_bv_q, L0_attn_bv_k, L0_attn_bv_v,
           L0_ffn_q_packed, L0_ffn_q_scales, L0_ffn_q_zeros,
           L0_ffn_delta_weight, L0_ffn_bias,
           L1_attn_w_down, L1_attn_b_down, L1_attn_w_up, L1_attn_b_up,
           L1_attn_gate1, L1_attn_hy_in_w, L1_attn_hy_conv_w, L1_attn_hy_conv_b,
           L1_attn_hy_dt_bias, L1_attn_hy_A_log, L1_attn_hy_D, L1_attn_hy_fcD_w,
           L1_attn_hy_norm_w, L1_attn_hy_out_w,
           L1_ffn_gate_q, L1_ffn_gate_scale, L1_ffn_gate_A, L1_ffn_gate_B,
           L1_ffn_up_q, L1_ffn_up_scale, L1_ffn_up_A, L1_ffn_up_B,
           L1_ffn_down_q, L1_ffn_down_scale, L1_ffn_down_A, L1_ffn_down_B):
    x = np.ascontiguousarray(np.asarray(L0_attn_x).reshape(T, D))
    nc = build_l0a_program()
    in_maps = [_prep_l0a(c, x, np.asarray(L0_attn_bv_q),
                         np.asarray(L0_attn_bv_k), np.asarray(L0_attn_bv_v))
               for c in range(NCORE)]
    res = run_bass_kernel_spmd(nc, in_maps, core_ids=list(range(NCORE)))
    r1 = _gather(res)
    w = _prep_l0f_weights(np.asarray(L0_ffn_q_packed), np.asarray(L0_ffn_q_scales),
                          np.asarray(L0_ffn_q_zeros), np.asarray(L0_ffn_delta_weight),
                          np.asarray(L0_ffn_bias))
    nc = build_l0f_program()
    sl = _core_slices(r1)
    res = run_bass_kernel_spmd(nc, [dict(w, res_in=sl[c]) for c in range(NCORE)],
                               core_ids=list(range(NCORE)))
    r2 = _gather(res)
    p = {'w_down': np.asarray(L1_attn_w_down), 'b_down': np.asarray(L1_attn_b_down),
         'w_up': np.asarray(L1_attn_w_up), 'b_up': np.asarray(L1_attn_b_up),
         'gate1': np.asarray(L1_attn_gate1), 'hy_in_w': np.asarray(L1_attn_hy_in_w),
         'hy_conv_w': np.asarray(L1_attn_hy_conv_w),
         'hy_conv_b': np.asarray(L1_attn_hy_conv_b),
         'hy_D': np.asarray(L1_attn_hy_D), 'hy_fcD_w': np.asarray(L1_attn_hy_fcD_w),
         'hy_norm_w': np.asarray(L1_attn_hy_norm_w),
         'hy_out_w': np.asarray(L1_attn_hy_out_w)}
    w = _prep_l1a_weights(p)
    nc = build_l1a_program()
    sl = _core_slices(r2)
    res = run_bass_kernel_spmd(nc, [dict(w, res_in=sl[c]) for c in range(NCORE)],
                               core_ids=list(range(NCORE)))
    r3 = _gather(res)
    i4 = {'gate_q': np.asarray(L1_ffn_gate_q), 'gate_scale': np.asarray(L1_ffn_gate_scale),
          'gate_A': np.asarray(L1_ffn_gate_A), 'gate_B': np.asarray(L1_ffn_gate_B),
          'up_q': np.asarray(L1_ffn_up_q), 'up_scale': np.asarray(L1_ffn_up_scale),
          'up_A': np.asarray(L1_ffn_up_A), 'up_B': np.asarray(L1_ffn_up_B),
          'down_q': np.asarray(L1_ffn_down_q), 'down_scale': np.asarray(L1_ffn_down_scale),
          'down_A': np.asarray(L1_ffn_down_A), 'down_B': np.asarray(L1_ffn_down_B)}
    w = _prep_l1f_weights(i4)
    nc = build_l1f_program()
    sl = _core_slices(r3)
    res = run_bass_kernel_spmd(nc, [dict(w, res_in=sl[c]) for c in range(NCORE)],
                               core_ids=list(range(NCORE)))
    r4 = _gather(res)
    return r4.reshape(1, T, D)


_BASE = 16512
_BS = 194944 + _BASE


def _mk_offsets():
    o = {}
    o['xst0'] = 176128
    small = [('identb', 256), ('identf', 512), ('gdscr', 256), ('gdscrf', 256),
             ('ln_bst', 256), ('ln_mv', 64), ('ones1b', 256),
             ('l1a_onesb', 64), ('l1a_onesf', 512)]
    off = 192512
    for nm, sz in small:
        o[nm] = off
        off += sz
    for nm, a in (('acc', 0), ('kqQT', 65536), ('vbb', 65536), ('xnT', 98304),
                  ('kqb', 98304), ('xn_s', 131072), ('l0a_xep', 131072),
                  ('mask_sb', 172032), ('l0a_xb', 176128), ('attnT', 194944)):
        o[nm] = a
    o['gdscr2'] = 211328
    o['gdscr3'] = 211456
    for nm, a in (('l0f_res', 0), ('l0f_ystg', 32768), ('l0f_rc', 36864),
                  ('xnpT', 65536), ('l0f_wt', 131072),
                  ('l0f_sexp', 147456), ('l0f_delta', 163840),
                  ('l0f_xnp', 147456), ('l0f_qb', 172032)):
        o[nm] = a
    for nm, sz in (('l0f_bias', 128), ('indE', 2048), ('l0f_xsum', 2048),
                   ('l0f_zs', 8192)):
        o[nm] = o.get('_l0f_cur', 194944)
        o['_l0f_cur'] = o[nm] + sz
    o.pop('_l0f_cur')
    for nm, a in (('l1a_res', 0), ('l1a_xn1', 65536), ('l1a_x1T', 98304),
                  ('l1a_wdT', 131072), ('l1a_wupT', 139264)):
        o[nm] = a
    cur = 147456
    for nm, sz in (('l1a_inwT', 1024), ('l1a_outwT', 512), ('l1a_fcdT', 1024),
                   ('l1a_cwcb', 64), ('l1a_drep', 64), ('l1a_normw', 64),
                   ('l1a_bdown', 64), ('l1a_bupT', 128), ('l1a_g1', 64),
                   ('l1a_alf', 64), ('l1a_rr', 2048), ('l1a_zsil', 2048),
                   ('l1a_xs', 2048), ('l1a_y', 2048), ('l1a_yz', 2048),
                   ('l1a_yz2', 2048), ('l1a_y2', 2048), ('l1a_h0T', 1024),
                   ('l1a_h1T', 1024), ('l1a_h2T', 1024), ('l1a_hfin', 1024),
                   ('l1a_utile', 4096)):
        o[nm] = cur
        cur += sz
    for nm, a in (('l1f_hcol', 0), ('l1f_s2bc', 88064), ('l1f_uBp', 104448),
                  ('l1f_B2p', 126976), ('l1f_wd', 135168), ('l1f_wg', 151552),
                  ('l1f_wu', 159744), ('l1f_xn', 167936),
                  ('l1f_wgb', 176128), ('l1f_wub', 184320)):
        o[nm] = a
    cur = 194944
    for nm, sz in (('l1f_gsc', 384), ('l1f_usc', 384), ('l1f_dsc', 128),
                   ('l1f_gsc5', 384), ('l1f_usc5', 384), ('l1f_inv', 384),
                   ('l1f_xsrow', 2048), ('l1f_rsrow', 2048), ('l1f_m75', 1024),
                   ('l1f_t1g', 2048), ('l1f_t1u', 2048), ('l1f_t2', 2048),
                   ('l1f_gs', 1024), ('l1f_us', 1024)):
        o[nm] = cur
        cur += sz
    o['l1f_h'] = 167936 + 4096
    return {k: v + _BASE for k, v in o.items()}


def build_fused_program():
    nc = bass.Bass("TRN2", target_bir_lowering=False)
    aps0 = _l0a_aps(nc)
    aps1 = _l0f_aps(nc)
    aps2 = _l1a_aps(nc)
    aps3 = _l1f_aps(nc)
    res1 = nc.dram_tensor("res1", [TOKC, D], F32, kind="Internal").ap()
    res2 = nc.dram_tensor("res2", [TOKC, D], F32, kind="Internal").ap()
    res3 = nc.dram_tensor("res3", [TOKC, D], F32, kind="Internal").ap()
    out = nc.dram_tensor("res_out", [TOKC, D], F32, kind="ExternalOutput").ap()
    aps0['res_out'] = res1
    aps1['res_in'], aps1['res_out'] = res1, res2
    aps2['res_in'], aps2['res_out'] = res2, res3
    aps3['res_in'], aps3['res_out'] = res3, out
    kb = KB(nc)
    kb.OFFSETS = _mk_offsets()
    emit_l0_attn(kb, aps0)
    kb.block_barrier()
    emit_l0_ffn(kb, aps1)
    kb.block_barrier()
    emit_l1_attn(kb, aps2)
    kb.block_barrier()
    emit_l1_ffn(kb, aps3)
    kb.block_barrier()
    kb.close()
    return nc


def kernel_fused(inp):
    x = np.ascontiguousarray(np.asarray(inp['L0_attn_x']).reshape(T, D))
    _prep_l0a.xbf = np.ascontiguousarray(x.astype(BF))
    w0 = [_prep_l0a(c, x, np.asarray(inp['L0_attn_bv_q']),
                    np.asarray(inp['L0_attn_bv_k']),
                    np.asarray(inp['L0_attn_bv_v'])) for c in range(NCORE)]
    w1 = _prep_l0f_weights(np.asarray(inp['L0_ffn_q_packed']),
                           np.asarray(inp['L0_ffn_q_scales']),
                           np.asarray(inp['L0_ffn_q_zeros']),
                           np.asarray(inp['L0_ffn_delta_weight']),
                           np.asarray(inp['L0_ffn_bias']))
    p = {k[len('L1_attn_'):]: np.asarray(v) for k, v in inp.items()
         if k.startswith('L1_attn_')}
    w2 = _prep_l1a_weights(p)
    i4 = {k[len('L1_ffn_'):]: np.asarray(v) for k, v in inp.items()
          if k.startswith('L1_ffn_')}
    w3 = _prep_l1f_weights(i4)
    nc = build_fused_program()
    in_maps = []
    for c in range(NCORE):
        m = {}
        m.update(w0[c])
        m.update(w1)
        m.update(w2)
        m.update(w3)
        in_maps.append(m)
    res = run_bass_kernel_spmd(nc, in_maps, core_ids=list(range(NCORE)))
    return _gather(res).reshape(1, T, D)


def kernel(**inputs):
    return kernel_fused(inputs)


# version 18 (final)
# speedup vs baseline: 1.0869x; 1.0301x over previous
import contextlib
import numpy as np
import ml_dtypes
import concourse.bass as bass
import concourse.mybir as mybir
from concourse.bass_utils import run_bass_kernel_spmd

F32 = mybir.dt.float32
F16 = mybir.dt.float16
BF16 = mybir.dt.bfloat16
U8 = mybir.dt.uint8
AF = mybir.ActivationFunctionType
ALU = mybir.AluOpType
AX = mybir.AxisListType
BF = ml_dtypes.bfloat16

P = 128
NCORE = 8
T = 8192
D = 4096
STR = 512
TOKC = 2 * STR
EXT = (8, 16)
LN_EPS = 1e-5


def _dram(nc, name, shape, dtype, kind="ExternalInput"):
    cache = getattr(nc, '_dram_cache', None)
    if cache is None:
        cache = {}
        nc._dram_cache = cache
    if name not in cache:
        cache[name] = nc.dram_tensor(name, list(shape), dtype, kind=kind).ap()
    return cache[name]


class KB:
    def __init__(self, nc):
        self.nc = nc
        self.stack = contextlib.ExitStack()
        self.sems = {}
        self.cnt = {}
        self.sbufs = {}
        self.psums = {}
        self.waited = {}
        self.ln_prev = None
        self.tr_evac = [None, None]
        self.state = {}

    def sem(self, name):
        if name not in self.sems:
            self.sems[name] = self.stack.enter_context(self.nc.semaphore(name))
            self.cnt[name] = 0
        return self.sems[name]

    OFFSETS = {}

    def sb(self, name, shape, dtype):
        if name not in self.sbufs:
            if self.OFFSETS:
                assert name in self.OFFSETS, "no offset for %s" % name
            if name in self.OFFSETS:
                self.sbufs[name] = self.nc.alloc_sbuf_tensor_at(
                    "sb_" + name, list(shape), dtype, offset=self.OFFSETS[name])
            else:
                self.sbufs[name] = self.stack.enter_context(
                    self.nc.sbuf_tensor("sb_" + name, list(shape), dtype))
        return self.sbufs[name]

    def block_barrier(self):
        nc = self.nc
        toks = [self.tok('cV'), self.tok('cA'), self.tok('sT'), self.tok('cG')]
        toks += [self.tok(s) for s in list(self.cnt) if s.startswith('d')]
        for ename, eng in (('T', nc.tensor), ('V', nc.vector), ('A', nc.scalar),
                           ('S', nc.sync), ('G', nc.gpsimd)):
            self.wait(ename, eng, *toks)

    def ps(self, name, shape, dtype=F32):
        if name not in self.psums:
            self.psums[name] = self.stack.enter_context(
                self.nc.psum_tensor("ps_" + name, list(shape), dtype))
        return self.psums[name]

    def dma(self, q, out, in_, sem_name, **kw):
        self.sem(sem_name)
        q.dma_start(out=out, in_=in_, **kw).then_inc(self.sems[sem_name], 16)
        self.cnt[sem_name] += 16
        return (sem_name, self.cnt[sem_name])

    def inc(self, instr, sem_name, amt=1):
        self.sem(sem_name)
        instr.then_inc(self.sems[sem_name], amt)
        self.cnt[sem_name] += amt
        return (sem_name, self.cnt[sem_name])

    def tok(self, sem_name):
        self.sem(sem_name)
        return (sem_name, self.cnt[sem_name])

    def wait(self, ename, eng, *toks):
        wmap = self.waited.setdefault(ename, {})
        best = {}
        for t in toks:
            if t is None:
                continue
            s, v = t
            if v > best.get(s, 0):
                best[s] = v
        for s, v in best.items():
            if v > 0 and wmap.get(s, 0) < v:
                eng.wait_ge(self.sems[s], v)
                wmap[s] = v

    def vpre(self, *toks):
        self.wait('V', self.nc.vector, self.tok('cV'), *toks)

    def vpost(self, ins):
        return self.inc(ins, 'cV')

    def apre(self, *toks):
        self.wait('A', self.nc.scalar, self.tok('cA'), *toks)

    def apost(self, ins):
        return self.inc(ins, 'cA')

    def gpre(self, *toks):
        self.wait('G', self.nc.gpsimd, self.tok('cG'), *toks)

    def gpost(self, ins):
        return self.inc(ins, 'cG')

    def close(self):
        self.stack.close()


def emit_ln_stats(kb, xsrc, x_tok):
    nc = kb.nc
    v, a = nc.vector, nc.scalar
    bst = kb.sb("ln_bst", (P, 48), F32)
    mv = kb.sb("ln_mv", (P, 8), F32)
    kb.vpre(x_tok, kb.ln_prev)
    for i in range(8):
        kb.vpost(v.bn_stats(bst[:, 6 * i:6 * (i + 1)],
                            xsrc[:, 512 * i:512 * (i + 1)]))
        kb.vpre()
    tva = kb.vpost(v.bn_aggr(mv[:, 0:2], bst[:, 0:48]))
    kb.vpre()
    tv = kb.vpost(v.tensor_scalar(mv[:, 2:3], mv[:, 1:2], LN_EPS, None, ALU.add))
    kb.apre(tv)
    ta = kb.apost(a.activation(mv[:, 3:4], mv[:, 2:3], AF.Sqrt))
    kb.vpre(ta)
    kb.vpost(v.reciprocal(mv[:, 4:5], mv[:, 3:4]))
    kb.vpre()
    kb.vpost(v.scalar_tensor_tensor(mv[:, 5:6], mv[:, 0:1], -1.0,
                                    mv[:, 4:5], ALU.mult, ALU.mult))
    kb.vpre()
    tv2 = kb.vpost(v.tensor_tensor(mv[:, 6:7], mv[:, 4:5], mv[:, 5:6], ALU.mult))
    return tv2, mv


def emit_ln_tile(kb, xsrc, xn_out, x_tok, war_toks=()):
    nc = kb.nc
    tv2, mv = emit_ln_stats(kb, xsrc, x_tok)
    kb.apre(tv2, *[t for t in war_toks if t is not None])
    t = kb.apost(nc.scalar.activation(xn_out, xsrc, AF.Identity,
                                      scale=mv[:, 4:5], bias=mv[:, 5:6]))
    kb.ln_prev = t
    return t


def emit_transpose_tile(kb, src, dest_r, tcol0, identb, evac_wait=None):
    nc = kb.nc
    pT = [kb.ps("pT0", (P, 1024), BF16), kb.ps("pT1", (P, 1024), BF16)]
    tt = None
    for fg in range(4):
        pt = pT[fg % 2]
        kb.wait('T', nc.tensor, kb.tr_evac[fg % 2])
        for f8 in range(8):
            f = 8 * fg + f8
            ins = nc.tensor.transpose(pt[:, 128 * f8:128 * (f8 + 1)],
                                      src[:, 128 * f:128 * (f + 1)], identb)
        tt = kb.inc(ins, 'sT')
        kb.vpre(tt, evac_wait)
        ins = nc.vector.tensor_copy(
            dest_r[:, 8 * fg:8 * (fg + 1), tcol0:tcol0 + 128],
            pt[:, 0:1024].rearrange("p (f t) -> p f t", f=8))
        kb.tr_evac[fg % 2] = kb.vpost(ins)
    return tt


DEBUG_L0A = False


def emit_l0_attn(kb, aps):
    nc = kb.nc
    v, a = nc.vector, nc.scalar
    xfull, xown = aps['xfull'], aps['xown']
    maskstr, res_out = aps['maskstr'], aps['res_out']
    kqv_d = aps['kqv_scratch']

    identb = kb.sb("identb", (P, P), BF16)
    kb.dma(nc.sync, identb[:, :], aps['identb'][:, :], 'dSc')

    acc = kb.sb("acc", (P, 4 * D), F32)
    bv = acc
    sc1 = acc[0:4, 3 * 1024:4 * 1024]
    scal_a = acc[0:4, 4 * 1024:4 * 1024 + 1]
    scal_b = acc[0:4, 4 * 1024 + 8:4 * 1024 + 9]
    for i, key in enumerate(('bvq', 'bvk', 'bvv')):
        kb.dma(nc.sync, bv[0:4, 1024 * i:1024 * (i + 1)], aps[key][:, :], 'dSb')
    t_ball = kb.tok('dSb')
    for i in range(3):
        sl = bv[0:4, 1024 * i:1024 * (i + 1)]
        kb.vpre(t_ball)
        kb.vpost(v.reduce_sum(scal_a, sl, axis=AX.X, apply_absolute_value=True))
        kb.vpre()
        tv = kb.vpost(v.tensor_scalar(scal_b, scal_a, 1.0 / 1024, None, ALU.mult))
        kb.apre(tv)
        ta = kb.apost(a.activation(sc1, sl, AF.Sign))
        kb.vpre(ta)
        kb.vpost(v.tensor_scalar(sc1, sc1, scal_b, None, ALU.mult))
        kb.vpre()
        kb.vpost(v.tensor_single_scalar(sl, sc1, 0.0, ALU.is_equal))
        kb.vpre()
        kb.vpost(v.scalar_tensor_tensor(sl, sl, scal_b, sc1, ALU.mult, ALU.add))
        kb.vpre()
    tv = kb.vpost(v.tensor_tensor(sc1, bv[0:4, 0:1024], bv[0:4, 1024:2048],
                                  ALU.mult))
    kb.wait('G', nc.gpsimd, tv)
    t1 = kb.dma(nc.gpsimd, kqv_d[0:1, :].rearrange("o (a b) -> (o a) b", a=4),
                sc1, 'dGk')
    t2 = kb.dma(nc.gpsimd, kqv_d[1:2, :].rearrange("o (a b) -> (o a) b", a=4),
                bv[0:4, 2048:3072], 'dGk')
    xn_s = kb.sb("xn_s", (P, 5 * D), BF16)
    xb = kb.sb("l0a_xb", (P, 2 * D), BF16)
    x_ep = kb.sb("l0a_xep", (P, D), F32)
    kqb = kb.sb("kqb", (P, D), BF16)
    vbb = kb.sb("vbb", (P, D), BF16)
    ones1b = kb.sb("ones1b", (1, P), BF16)
    kb.vpre()
    t_ones = kb.vpost(v.memset(ones1b[:, :], 1.0))
    kqQT = kb.sb("kqQT", (P, 32 * 512), BF16)
    kqQT_r = kqQT[:, :].rearrange("p (f t) -> p f t", f=32)
    xnT = kb.sb("xnT", (P, 32 * 512), BF16)
    xnT_r = xnT[:, :].rearrange("p (f t) -> p f t", f=32)
    attnT = kb.sb("attnT", (P, 16 * 512), BF16)
    mask_sb = kb.sb("mask_sb", (P, 4 * 512), BF16)
    pS = [kb.ps("pS0", (P, 512), F32), kb.ps("pS1", (P, 512), F32)]
    pA = [kb.ps("pA%d" % k, (P, 512), F32) for k in range(4)]

    xb_war = [None, None]
    slot_war = [None] * 5
    row_war = None
    t_xnq_war = None
    last_av = None
    av_h_tok = [None] * 4
    score_j_tok = [None] * 4
    last_sig = [None, None]
    last_sp = [None] * 4
    last_maskmm = None
    t_ep_store = None
    t_ep_v = None
    x_ep_war = None
    t_vb_ev = [None, None]

    for ph in range(2):
        kb.wait('G', nc.gpsimd, t1, t2, xb_war[1], row_war)
        trow = kb.dma(nc.gpsimd, xb[0:1, D:2 * D], kqv_d[0:1, :], 'dGk')
        t_bc_ev = [None, None]
        for i in range(8):
            ps = pS[i % 2]
            kb.wait('T', nc.tensor, t_ones, trow, t_bc_ev[i % 2],
                    last_sig[i % 2], t_vb_ev[i % 2], kb.tok('dSc'))
            ins = nc.tensor.matmul(ps[:, :], ones1b[0:1, :],
                                   xb[0:1, D + 512 * i:D + 512 * (i + 1)],
                                   start=True, stop=True)
            tt = kb.inc(ins, 'sT')
            kb.vpre(tt)
            t_bc_ev[i % 2] = kb.vpost(
                v.tensor_copy(kqb[:, 512 * i:512 * (i + 1)], ps[:, :]))
        row_war = kb.tok('sT')
        tqk = None
        for k4 in range(4):
            buf = k4 % 2
            qeng, qq = (('S', nc.sync) if buf == 0 else ('A', nc.scalar))
            kb.wait(qeng, qq, xb_war[buf], row_war if buf == 1 else None)
            tx = kb.dma(qq, xb[:, D * buf:D * (buf + 1)],
                        aps['xownb'][STR * ph + P * k4: STR * ph + P * (k4 + 1), :],
                        'dGx%d' % buf)
            tv2, mv = emit_ln_stats(kb, xb[:, D * buf:D * (buf + 1)], tx)
            kb.apre(tv2, t_xnq_war, last_av, t_ep_v)
            tln = kb.apost(a.activation(xn_s[:, 0:D], xb[:, D * buf:D * (buf + 1)],
                                        AF.Identity, scale=mv[:, 4:5],
                                        bias=mv[:, 5:6]))
            kb.ln_prev = tln
            xb_war[buf] = tln
            gd = kb.sb("gdscr", (P, 64), BF16)
            kb.gpre(tln)
            tg = kb.gpost(nc.gpsimd.tensor_copy(gd[:, :], xn_s[:, D - 64:D]))
            kb.vpre(tg, t_bc_ev[0], t_bc_ev[1])
            tqk = kb.vpost(v.tensor_tensor(xn_s[:, 0:D], xn_s[:, 0:D],
                                           kqb[:, :], ALU.mult))
            kb.wait('T', nc.tensor, tqk)
            t_xnq_war = emit_transpose_tile(kb, xn_s[:, 0:D], kqQT_r, P * k4,
                                            identb[:, :])
        for it in range(EXT[ph]):
            itg = it if ph == 0 else 8 + it
            kb.gpre(last_maskmm)
            tmask = kb.dma(nc.gpsimd, mask_sb[:, :],
                           maskstr[itg * P:(itg + 1) * P, :], 'dGm')
            for j in range(4):
                jj = 4 * it + j
                slot = jj % 5
                buf = jj % 2
                srow = 512 * it + P * j
                qeng, qq = (('S', nc.sync) if buf == 0 else ('A', nc.scalar))
                kb.wait(qeng, qq, xb_war[buf],
                        row_war if (it == 0 and j == 1 and ph > 0) else None)
                tx = kb.dma(qq, xb[:, D * buf:D * (buf + 1)],
                            xfull[srow:srow + P, :], 'dGx%d' % buf)
                tv2, mv = emit_ln_stats(kb, xb[:, D * buf:D * (buf + 1)], tx)
                kb.apre(tv2, slot_war[slot],
                        t_xnq_war if (it == 0 and j == 0) else None)
                tln = kb.apost(a.activation(
                    xn_s[:, D * slot:D * (slot + 1)],
                    xb[:, D * buf:D * (buf + 1)], AF.Identity,
                    scale=mv[:, 4:5], bias=mv[:, 5:6]))
                kb.ln_prev = tln
                xb_war[buf] = tln
                gd = kb.sb("gdscr", (P, 64), BF16)
                kb.gpre(tln)
                tg = kb.gpost(nc.gpsimd.tensor_copy(
                    gd[:, :], xn_s[:, D * slot + D - 64:D * (slot + 1)]))
                kb.wait('T', nc.tensor, tg)
                emit_transpose_tile(kb, xn_s[:, D * slot:D * (slot + 1)],
                                    xnT_r, P * j, identb[:, :],
                                    evac_wait=score_j_tok[j])
                trj = (kb.tr_evac[0], kb.tr_evac[1])
                for h in range(4):
                    ps = pS[h % 2]
                    kb.wait('T', nc.tensor, last_sig[h % 2], tmask,
                            t_bc_ev[h % 2], trj[0], trj[1])
                    for f8 in range(8):
                        f = 8 * h + f8
                        ins = nc.tensor.matmul(
                            ps[:, :],
                            xnT[:, 512 * f + P * j:512 * f + P * (j + 1)],
                            kqQT[:, 512 * f:512 * (f + 1)],
                            start=(f8 == 0), stop=False)
                    ins = nc.tensor.matmul(ps[:, :], identb[:, :],
                                           mask_sb[:, 512 * j:512 * (j + 1)],
                                           start=False, stop=True)
                    tt = kb.inc(ins, 'sT')
                    if h == 3:
                        score_j_tok[j] = tt
                        if j == 3:
                            last_maskmm = tt
                    kb.apre(tt, av_h_tok[h])
                    ta = kb.apost(a.activation(
                        attnT[:, 512 * (4 * h + j):512 * (4 * h + j + 1)],
                        ps[:, :], AF.Sigmoid, scale=0.125))
                    last_sig[h % 2] = ta
            gd2 = kb.sb("gdscr2", (P, 64), BF16)
            kb.gpre(last_sig[0], last_sig[1])
            tg2 = kb.gpost(nc.gpsimd.tensor_copy(
                gd2[:, :], attnT[:, 16 * 512 - 64:16 * 512]))
            for h in range(4):
                for dh in range(2):
                    for k in range(4):
                        kb.wait('T', nc.tensor, tg2, last_sp[k])
                        for j in range(4):
                            slot = (4 * it + j) % 5
                            ins = nc.tensor.matmul(
                                pA[k][:, :],
                                attnT[:, 512 * (4 * h + j) + P * k:
                                      512 * (4 * h + j) + P * (k + 1)],
                                xn_s[:, D * slot + 1024 * h + 512 * dh:
                                     D * slot + 1024 * h + 512 * (dh + 1)],
                                start=(j == 0), stop=(j == 3))
                        tt = kb.inc(ins, 'sT')
                        last_av = tt
                        av_h_tok[h] = tt
                        accsl = acc[:, D * k + 1024 * h + 512 * dh:
                                    D * k + 1024 * h + 512 * (dh + 1)]
                        kb.vpre(tt, t_ep_store if it == 0 else None)
                        if it == 0:
                            ins = v.tensor_copy(accsl, pA[k][:, :])
                        else:
                            ins = v.tensor_tensor(accsl, accsl, pA[k][:, :],
                                                  ALU.add)
                        last_sp[k] = kb.vpost(ins)
            for j in range(4):
                slot_war[(4 * it + j) % 5] = last_av
        kb.wait('G', nc.gpsimd, xb_war[1])
        trow2 = kb.dma(nc.gpsimd, xb[0:1, D:2 * D], kqv_d[1:2, :], 'dGk')
        t_vb_ev = [None, None]
        for i in range(8):
            ps = pS[i % 2]
            kb.wait('T', nc.tensor, t_ones, trow2, t_vb_ev[i % 2],
                    last_sig[i % 2])
            ins = nc.tensor.matmul(ps[:, :], ones1b[0:1, :],
                                   xb[0:1, D + 512 * i:D + 512 * (i + 1)],
                                   start=True, stop=True)
            tt = kb.inc(ins, 'sT')
            kb.vpre(tt)
            t_vb_ev[i % 2] = kb.vpost(
                v.tensor_copy(vbb[:, 512 * i:512 * (i + 1)], ps[:, :]))
        row_war = kb.tok('sT')
        for k in range(4):
            kb.wait('S', nc.sync, last_av, t_ep_store, x_ep_war)
            tx = kb.dma(nc.sync, x_ep[:, :],
                        xown[STR * ph + P * k: STR * ph + P * (k + 1), :], 'dSx')
            accsl = acc[:, D * k:D * (k + 1)]
            kb.vpre(tx, t_ep_store, t_vb_ev[0], t_vb_ev[1],
                    last_sp[0], last_sp[1], last_sp[2], last_sp[3])
            kb.vpost(v.tensor_tensor(accsl, accsl, vbb[:, :], ALU.mult))
            kb.vpre()
            t_ep_v = kb.vpost(v.tensor_tensor(accsl, accsl, x_ep[:, :],
                                              ALU.add))
            x_ep_war = t_ep_v
            gd3 = kb.sb("gdscr3", (P, 64), F32)
            kb.gpre(t_ep_v)
            tg3 = kb.gpost(nc.gpsimd.tensor_copy(gd3[:, :],
                                                 accsl[:, D - 64:D]))
            kb.apre(t_ep_v, tg3)
            t_ep_store = kb.dma(nc.scalar,
                                res_out[STR * ph + P * k: STR * ph + P * (k + 1), :],
                                accsl, 'dV')
        slot_war[0] = slot_war[1] = t_ep_v


def emit_l0_ffn(kb, aps):
    nc = kb.nc
    v, a = nc.vector, nc.scalar
    res_in, res_out = aps['res_in'], aps['res_out']

    identb = kb.sb("identb", (P, P), BF16)
    identf = kb.sb("identf", (P, P), F32)
    indE = kb.sb("indE", (P, 32 * 32), BF16)
    bias_sb = kb.sb("l0f_bias", (P, 32), F32)
    kb.dma(nc.sync, identb[:, :], aps['identb'][:, :], 'dSc')
    kb.dma(nc.sync, identf[:, :], aps['identf'][:, :], 'dSc')
    kb.dma(nc.sync, indE[:, :], aps['indE'][:, :], 'dSc')
    kb.dma(nc.sync, bias_sb[:, :], aps['biasT'][:, :], 'dSc')
    t_const = kb.tok('dSc')

    xnpT = kb.sb("xnpT", (P, 32 * 1024), BF16)
    xnpT_r = xnpT[:, :].rearrange("p (f t) -> p f t", f=32)
    lres = kb.sb("l0f_res", (P, 2 * D), F32)
    xst = kb.sb("xst0", (P, D), F32)
    gd = kb.sb("gdscr", (P, 64), BF16)
    zs_b = kb.sb("l0f_zs", (32, D), BF16)
    xsum = kb.sb("l0f_xsum", (32, 1024), BF16)
    qb_sb = kb.sb("l0f_qb", (P, 4096), U8)
    sexp_sb = kb.sb("l0f_sexp", (P, 4096), F32)
    delta_sb = kb.sb("l0f_delta", (P, 4096), BF16)
    wt_sb = kb.sb("l0f_wt", (P, 2 * 4096), BF16)
    ytmp = xst[:, 0:1024]
    pS = [kb.ps("pS0", (P, 512), F32), kb.ps("pS1", (P, 512), F32)]
    pA = [kb.ps("pA%d" % k, (P, 512), F32) for k in range(4)]

    last_norm = kb.state.get('l0f_xst_war')
    for k in range(8):
        kb.wait('S', nc.sync, last_norm)
        tx = kb.dma(nc.sync, xst[:, :], res_in[P * k:P * (k + 1), :], 'dSx')
        xnp = kb.sb("l0f_xnp", (P, D), BF16)
        tv2, mv = emit_ln_stats(kb, xst[:, :], tx)
        kb.apre(tv2, kb.state.get('l0f_xnp_war'))
        kb.apost(a.activation(xnp[:, 0:2048], xst[:, 0:D:2], AF.Identity,
                              scale=mv[:, 4:5], bias=mv[:, 5:6]))
        kb.apre()
        last_norm = kb.apost(a.activation(xnp[:, 2048:4096], xst[:, 1:D:2],
                                          AF.Identity, scale=mv[:, 4:5],
                                          bias=mv[:, 5:6]))
        kb.ln_prev = last_norm
        kb.gpre(last_norm)
        tg = kb.gpost(nc.gpsimd.tensor_copy(gd[:, :], xnp[:, D - 64:D]))
        kb.wait('T', nc.tensor, tg, t_const)
        kb.state['l0f_xnp_war'] = emit_transpose_tile(kb, xnp[:, :], xnpT_r,
                                                      P * k, identb[:, :])
    kb.state['l0f_xst_war'] = last_norm
    kb.state['l0f_stageA_done'] = kb.tok('sT')

    s2 = lres[0:32, 0:D]
    z2 = lres[0:32, D:2 * D]
    t1 = kb.dma(nc.sync, s2, aps['s2dT'][:, :], 'dSz')
    t2 = kb.dma(nc.sync, z2, aps['z2dT'][:, :], 'dSz')
    kb.vpre(kb.tok('dSz'))
    tzs = kb.vpost(v.scalar_tensor_tensor(zs_b[:, :], s2, -1.0, z2,
                                          ALU.mult, ALU.mult))
    t_tr = (kb.tr_evac[0], kb.tr_evac[1])
    for tc in range(2):
        kb.wait('T', nc.tensor, t_const, kb.state.get('l0f_xsum_war'), *t_tr)
        for pc in range(32):
            ins = nc.tensor.matmul(pA[1][0:32, :], indE[:, 32 * pc:32 * (pc + 1)],
                                   xnpT[:, 1024 * pc + 512 * tc:1024 * pc + 512 * (tc + 1)],
                                   start=(pc == 0), stop=(pc == 31))
        tt = kb.inc(ins, 'sT')
        kb.vpre(tt)
        ins = v.tensor_copy(xsum[:, 512 * tc:512 * (tc + 1)], pA[1][0:32, :])
        kb.state['l0f_xsum_war'] = kb.vpost(ins)
    kb.gpre(kb.state['l0f_xsum_war'], tzs)
    tg_zx = kb.gpost(nc.gpsimd.tensor_copy(gd[0:32, 0:64], xsum[:, 1024 - 64:1024]))

    wt_war = [None, None]
    wt_ready = [None, None]
    qb_war = [None, None]
    ev_war = [None, None]
    ytmp_war = [None, None]
    pa_war = [None, kb.state.get('l0f_xsum_war')]
    st_war = [None, None]
    rc_tok = [None, None]
    rc_war = [None] * 4
    gdf = kb.sb("gdscrf", (P, 64), F32)
    ystg = kb.sb("l0f_ystg", (P, 2 * 512), F32)
    rc = kb.sb("l0f_rc", (P, 4 * 512), F32)

    def _dq(o):
        dbuf = o % 2
        kb.wait('G', nc.gpsimd, qb_war[0])
        tq = kb.dma(nc.gpsimd, qb_sb[:, :],
                    aps['qbT'][P * o:P * (o + 1), :], 'dGq')
        kb.wait('S', nc.sync, kb.state.get('l0f_sexp_war'),
                kb.state.get('l0f_stageA_done') if o == 0 else None, tzs)
        ts = kb.dma(nc.sync, sexp_sb[:, 0:2048],
                    aps['sexpT'][P * o:P * (o + 1), 0:2048], 'dSw')
        kb.wait('A', a, kb.state.get('l0f_sexp_war'))
        ts2 = kb.dma(nc.scalar, sexp_sb[:, 2048:4096],
                     aps['sexpT'][P * o:P * (o + 1), 2048:4096], 'dAw')
        kb.wait('G', nc.gpsimd, *(kb.state.get('l0f_delta_war') or ()))
        td = kb.dma(nc.gpsimd, delta_sb[:, 0:2048],
                    aps['deltaT'][P * o:P * (o + 1), 0:2048], 'dGd')
        kb.wait('A', a, *(kb.state.get('l0f_delta_war') or ()))
        td2 = kb.dma(nc.scalar, delta_sb[:, 2048:4096],
                     aps['deltaT'][P * o:P * (o + 1), 2048:4096], 'dAd')
        nib_lo = qb_sb[:, 0:2048].rearrange("p (j o) -> p j o", j=16)
        nib_hi = qb_sb[:, 2048:4096].rearrange("p (j o) -> p j o", j=16)
        sex_r = sexp_sb[:, :].rearrange("p (j q o) -> p j q o", j=16, q=2)
        del_r = delta_sb[:, :].rearrange("p (j q o) -> p j q o", j=16, q=2)
        wt_r = wt_sb[:, 4096 * dbuf:4096 * (dbuf + 1)].rearrange(
            "p (q j o) -> p q j o", q=2, j=16)
        kb.vpre(tq, ts, ts2, wt_war[dbuf])
        kb.vpost(v.tensor_tensor(wt_r[:, 0, :, :], nib_lo,
                                 sex_r[:, :, 0, :], ALU.mult))
        kb.vpre()
        tm1 = kb.vpost(v.tensor_tensor(wt_r[:, 1, :, :], nib_hi,
                                       sex_r[:, :, 1, :], ALU.mult))
        qb_war[0] = tm1
        kb.state['l0f_sexp_war'] = tm1
        kb.vpre(td, td2)
        tva = kb.vpost(v.tensor_tensor(
            wt_r[:, 0, :, :], wt_r[:, 0, :, :],
            del_r[:, :, 0, :], ALU.add))
        kb.gpre(tm1, td, td2)
        tdq = kb.gpost(nc.gpsimd.tensor_tensor(
            wt_r[:, 1, :, :], wt_r[:, 1, :, :],
            del_r[:, :, 1, :], ALU.add))
        kb.gpre(tva)
        tgv = kb.gpost(nc.gpsimd.tensor_copy(
            gd[:, :], wt_sb[:, 4096 * dbuf + 2048 - 64:4096 * dbuf + 2048]))
        kb.state['l0f_delta_war'] = (tva, tdq)
        wt_ready[dbuf] = (tdq, tgv)

    _dq(0)
    for ot in range(32):
        buf = ot % 2
        kb.wait('S', nc.sync, rc_war[2 * buf], rc_war[2 * buf + 1])
        for tc in range(2):
            rs = 2 * buf + tc
            rc_tok[tc] = kb.dma(
                nc.sync,
                rc[:, 512 * rs:512 * (rs + 1)].rearrange("p (m f) -> p m f", m=4),
                res_in[512 * tc:512 * (tc + 1), 128 * ot:128 * (ot + 1)]
                .rearrange("(m p) f -> p m f", p=P), 'dSrc')
        if ot + 1 < 32:
            _dq(ot + 1)
        for tc in range(2):
            ps = pS[tc]
            kb.wait('T', nc.tensor, *(wt_ready[buf] or ()),
                    ev_war[tc], tg_zx, t_const)
            for f in range(32):
                ins = nc.tensor.matmul(
                    ps[:, :], wt_sb[:, 4096 * buf + 128 * f:4096 * buf + 128 * (f + 1)],
                    xnpT[:, 1024 * f + 512 * tc:1024 * f + 512 * (tc + 1)],
                    start=(f == 0), stop=False)
            ins = nc.tensor.matmul(ps[:, :], zs_b[0:32, P * ot:P * (ot + 1)],
                                   xsum[0:32, 512 * tc:512 * (tc + 1)],
                                   start=False, stop=True)
            tt = kb.inc(ins, 'sT')
            if tc == 1:
                wt_war[buf] = tt
            kb.apre(tt, ytmp_war[tc])
            ins = a.activation(ytmp[:, 512 * tc:512 * (tc + 1)], ps[:, :],
                               AF.Identity, bias=bias_sb[:, ot:ot + 1])
            tev = kb.apost(ins)
            ev_war[tc] = tev
            kb.gpre(tev)
            tgy = kb.gpost(nc.gpsimd.tensor_copy(
                gd[:, :], ytmp[:, 512 * tc + 448:512 * tc + 512]))
            kb.wait('T', nc.tensor, tgy, pa_war[tc])
            for m in range(4):
                ins = nc.tensor.transpose(
                    pA[tc][:, 128 * m:128 * (m + 1)],
                    ytmp[:, 512 * tc + 128 * m:512 * tc + 128 * (m + 1)],
                    identf[:, :])
            tt2 = kb.inc(ins, 'sT')
            ytmp_war[tc] = tt2
            kb.vpre(tt2, rc_tok[tc], st_war[tc])
            tv = kb.vpost(v.tensor_tensor(ystg[:, 512 * tc:512 * (tc + 1)],
                                          pA[tc][:, :],
                                          rc[:, 512 * (2 * buf + tc):512 * (2 * buf + tc + 1)],
                                          ALU.add))
            pa_war[tc] = tv
            rc_war[2 * buf + tc] = tv
            kb.gpre(tv)
            kb.gpost(nc.gpsimd.tensor_copy(
                gdf[:, :], ystg[:, 512 * tc + 448:512 * tc + 512]))
            st_war[tc] = kb.dma(
                nc.gpsimd,
                res_out[512 * tc:512 * (tc + 1), 128 * ot:128 * (ot + 1)]
                .rearrange("(m p) f -> p m f", p=P),
                ystg[:, 512 * tc:512 * (tc + 1)].rearrange("p (m f) -> p m f", m=4),
                'dGy%d' % tc)


def _prep_l0a(core, x, bvq, bvk, bvv):
    s1, s2 = core, core + 8
    xown = np.concatenate([x[STR * s1:STR * s1 + STR], x[STR * s2:STR * s2 + STR]])
    ms = np.zeros((24 * P, 4 * 512), dtype=BF)
    r = np.arange(P)[:, None]
    col = np.arange(512)[None, :]
    for ph, stripe in enumerate((s1, s2)):
        S0 = STR * stripe
        for it in range(EXT[ph]):
            itg = it if ph == 0 else 8 + it
            for j in range(4):
                srow = 512 * it + P * j
                m = np.where((srow + r) <= (S0 + col), 0.0, -300.0).astype(np.float32)
                ms[itg * P:(itg + 1) * P, 512 * j:512 * (j + 1)] = m.astype(BF)
    return {
        "xfull": _prep_l0a.xbf,
        "xown": np.ascontiguousarray(xown),
        "xownb": np.ascontiguousarray(xown.astype(BF)),
        "bvq": np.ascontiguousarray(bvq),
        "bvk": np.ascontiguousarray(bvk),
        "bvv": np.ascontiguousarray(bvv),
        "maskstr": ms,
        "identb": np.eye(P, dtype=BF),
    }


def _l0a_aps(nc):
    return {
        'xfull': nc.dram_tensor("xfull", [T, D], BF16, kind="ExternalInput").ap(),
        'xown': nc.dram_tensor("xown", [TOKC, D], F32, kind="ExternalInput").ap(),
        'xownb': nc.dram_tensor("xownb", [TOKC, D], BF16, kind="ExternalInput").ap(),
        'bvq': nc.dram_tensor("bvq", [4, 1024], F32, kind="ExternalInput").ap(),
        'bvk': nc.dram_tensor("bvk", [4, 1024], F32, kind="ExternalInput").ap(),
        'bvv': nc.dram_tensor("bvv", [4, 1024], F32, kind="ExternalInput").ap(),
        'maskstr': nc.dram_tensor("maskstr", [24 * P, 4 * 512], BF16,
                                  kind="ExternalInput").ap(),
        'identb': _dram(nc, "identb", [P, P], BF16),
        'kqv_scratch': nc.dram_tensor("kqv_scratch", [2, D], F32,
                                      kind="Internal").ap(),
    }


def build_l0a_program():
    nc = bass.Bass("TRN2", target_bir_lowering=False)
    aps = _l0a_aps(nc)
    aps['res_out'] = nc.dram_tensor("res_out", [TOKC, D], F32,
                                    kind="ExternalOutput").ap()
    if DEBUG_L0A:
        for nm, shp, dt in (('dbg_acc', (P, 4 * D), F32),
                            ('dbg_attnT', (P, 4 * 512), BF16),
                            ('dbg_xns', (P, 4 * D), BF16),
                            ('dbg_kqQT', (P, 32 * 512), BF16),
                            ('dbg_kqb', (P, D), BF16),
                            ('dbg_accit', (8 * P, 4 * D), F32),
                            ('dbg_attit', (8 * P, 4 * 512), BF16),
                            ('dbg_vbb', (P, D), BF16)):
            aps[nm] = nc.dram_tensor(nm, list(shp), dt, kind="ExternalOutput").ap()
    kb = KB(nc)
    kb.OFFSETS = _mk_offsets()
    emit_l0_attn(kb, aps)
    kb.wait('S', nc.sync, kb.tok('dV'))
    if DEBUG_L0A:
        kb.wait('S', nc.sync, kb.tok('dDbg'))
    kb.close()
    return nc


def kernel_l0a_only(L0_attn_x, L0_attn_bv_q, L0_attn_bv_k, L0_attn_bv_v):
    x = np.asarray(L0_attn_x).reshape(T, D)
    _prep_l0a.xbf = np.ascontiguousarray(x.astype(BF))
    nc = build_l0a_program()
    in_maps = [_prep_l0a(c, x, np.asarray(L0_attn_bv_q), np.asarray(L0_attn_bv_k),
                         np.asarray(L0_attn_bv_v)) for c in range(NCORE)]
    res = run_bass_kernel_spmd(nc, in_maps, core_ids=list(range(NCORE)))
    out = np.empty((T, D), np.float32)
    for c in range(NCORE):
        r = np.asarray(res.results[c]["res_out"])
        out[STR * c:STR * (c + 1)] = r[:STR]
        out[STR * (c + 8):STR * (c + 9)] = r[STR:]
    return out.reshape(1, T, D)


def _l1a_hydra(kb, u_sb, out_sb, w):
    nc = kb.nc
    v, a = nc.vector, nc.scalar
    pA = [kb.ps("pA%d" % k, (P, 512), F32) for k in range(4)]
    pS = [kb.ps("pS0", (P, 512), F32), kb.ps("pS1", (P, 512), F32)]
    gd = kb.sb("gdscr", (P, 64), BF16)
    zsil = kb.sb("l1a_zsil", (P, 1024), BF16)
    xs = kb.sb("l1a_xs", (P, 1024), BF16)
    y = kb.sb("l1a_y", (P, 1024), BF16)
    yz = kb.sb("l1a_yz", (P, 1024), BF16)
    yz2 = kb.sb("l1a_yz2", (P, 1024), BF16)
    y2 = kb.sb("l1a_y2", (P, 1024), BF16)
    rr = kb.sb("l1a_rr", (1, 512), F32)
    st = kb.state
    tz = []
    for c in range(4):
        kb.wait('T', nc.tensor, st.get('l1a_pA%d_war' % c))
        ins = nc.tensor.matmul(pA[c][:, :], w['inwT'][:, 128 * c:128 * (c + 1)],
                               u_sb, start=True, stop=True)
        tz.append(kb.inc(ins, 'sT'))
    for c in range(2):
        kb.apre(tz[c], st.get('l1a_zsil_war'))
        kb.apost(a.activation(zsil[:, 512 * c:512 * (c + 1)], pA[c][:, :], AF.Silu))
    for c in range(2):
        kb.apre(tz[2 + c], st.get('l1a_xs_war'))
        kb.apost(a.activation(xs[:, 512 * c:512 * (c + 1)], pA[2 + c][:, :],
                              AF.Silu, scale=w['cwcb'][:, c:c + 1],
                              bias=w['cwcb'][:, 2 + c:3 + c]))
    t_xs = kb.tok('cA')
    st['l1a_pA1_war'] = t_xs
    st['l1a_pA2_war'] = t_xs
    st['l1a_pA3_war'] = t_xs
    kb.gpre(t_xs)
    tgx = kb.gpost(nc.gpsimd.tensor_copy(gd[:, :], xs[:, 1024 - 64:1024]))
    tf = []
    for oc in range(2):
        kb.wait('T', nc.tensor, tgx, st.get('l1a_pS%d_war' % oc))
        for c in range(2):
            ins = nc.tensor.matmul(pS[oc][:, :],
                                   w['fcdT'][:, 128 * (2 * c + oc):128 * (2 * c + oc + 1)],
                                   xs[:, 512 * c:512 * (c + 1)],
                                   start=(c == 0), stop=(c == 1))
        tf.append(kb.inc(ins, 'sT'))
    for oc in range(2):
        kb.vpre(tf[oc])
        kb.vpost(v.scalar_tensor_tensor(y[:, 512 * oc:512 * (oc + 1)],
                                        pS[oc][:, :], w['drep'][:, oc:oc + 1],
                                        xs[:, 512 * oc:512 * (oc + 1)],
                                        ALU.add, ALU.mult))
        kb.vpre()
        kb.vpost(v.tensor_tensor(yz[:, 512 * oc:512 * (oc + 1)],
                                 y[:, 512 * oc:512 * (oc + 1)],
                                 zsil[:, 512 * oc:512 * (oc + 1)], ALU.mult))
    t_yz = kb.tok('cV')
    st['l1a_pS0_war'] = t_yz
    st['l1a_pS1_war'] = t_yz
    for oc in range(2):
        kb.apre(t_yz, st.get('l1a_yz2_war'))
        kb.apost(a.activation(yz2[:, 512 * oc:512 * (oc + 1)],
                              yz[:, 512 * oc:512 * (oc + 1)], AF.Square))
    t_yz2 = kb.tok('cA')
    kb.gpre(t_yz2)
    tgq = kb.gpost(nc.gpsimd.tensor_copy(gd[:, :], yz2[:, 1024 - 64:1024]))
    kb.wait('T', nc.tensor, tgq, t_yz)
    for c in range(2):
        ins = nc.tensor.matmul(pS[0][0:1, :], w['onesb'][:, 0:1],
                               yz2[:, 512 * c:512 * (c + 1)],
                               start=(c == 0), stop=(c == 1))
    tssq = kb.inc(ins, 'sT')
    kb.vpre(tssq)
    kb.vpost(v.tensor_scalar(rr[0:1, :], pS[0][0:1, :], 1.0 / 256, 1e-5,
                             ALU.mult, ALU.add))
    tv = kb.tok('cV')
    kb.apre(tv)
    ta = kb.apost(a.activation(rr[0:1, :], rr[0:1, :], AF.Sqrt))
    kb.vpre(ta)
    trr = kb.vpost(v.reciprocal(rr[0:1, :], rr[0:1, :]))
    st['l1a_pS0_war'] = trr
    kb.gpre(trr)
    tgr = kb.gpost(nc.gpsimd.tensor_copy(gd[0:1, 0:64], rr[0:1, 448:512]))
    kb.wait('T', nc.tensor, tgr, t_yz)
    ins = nc.tensor.matmul(pS[1][:, :], w['onesf'][0:1, :], rr[0:1, :],
                           start=True, stop=True)
    tbc = kb.inc(ins, 'sT')
    for oc in range(2):
        kb.vpre(tbc)
        kb.vpost(v.tensor_tensor(y2[:, 512 * oc:512 * (oc + 1)],
                                 yz[:, 512 * oc:512 * (oc + 1)], pS[1][:, :],
                                 ALU.mult))
    ty2 = kb.tok('cV')
    st['l1a_pS1_war'] = ty2
    kb.gpre(ty2)
    tgy = kb.gpost(nc.gpsimd.tensor_copy(gd[:, :], y2[:, 1024 - 64:1024]))
    kb.wait('T', nc.tensor, tgy, st.get('l1a_pA0_war'))
    for c in range(2):
        ins = nc.tensor.matmul(pA[0][:, :], w['outwT'][:, 128 * c:128 * (c + 1)],
                               y2[:, 512 * c:512 * (c + 1)],
                               start=(c == 0), stop=(c == 1))
    tt = kb.inc(ins, 'sT')
    kb.apre(tt, st.get('l1a_out_war'))
    tout = kb.apost(a.activation(out_sb, pA[0][:, :], AF.Identity))
    st['l1a_pA0_war'] = tout
    st['l1a_xs_war'] = kb.tok('sT')
    st['l1a_yz2_war'] = kb.tok('sT')
    return tout


def emit_l1_attn(kb, aps):
    nc = kb.nc
    v, a = nc.vector, nc.scalar
    res_in, res_out = aps['res_in'], aps['res_out']
    st = kb.state

    identb = kb.sb("identb", (P, P), BF16)
    identf = kb.sb("identf", (P, P), F32)
    wdT = kb.sb("l1a_wdT", (P, 4096), BF16)
    wupT = kb.sb("l1a_wupT", (P, 4096), BF16)
    inwT = kb.sb("l1a_inwT", (P, 512), BF16)
    outwT = kb.sb("l1a_outwT", (P, 256), BF16)
    fcdT = kb.sb("l1a_fcdT", (P, 512), BF16)
    cwcb = kb.sb("l1a_cwcb", (P, 4), F32)
    drep = kb.sb("l1a_drep", (P, 2), F32)
    normw = kb.sb("l1a_normw", (P, 2), F32)
    bdown = kb.sb("l1a_bdown", (P, 1), F32)
    bupT = kb.sb("l1a_bupT", (P, 32), F32)
    g1 = kb.sb("l1a_g1", (1, 8), F32)
    onesb = kb.sb("l1a_onesb", (P, 1), BF16)
    onesf = kb.sb("l1a_onesf", (1, P), F32)
    gd = kb.sb("gdscr", (P, 64), BF16)

    kb.dma(nc.sync, identb[:, :], aps['identb'][:, :], 'dSc')
    kb.dma(nc.sync, identf[:, :], aps['identf'][:, :], 'dSc')
    kb.dma(nc.gpsimd, wdT[:, :], aps['wdT'][:, :], 'dGc')
    kb.dma(nc.gpsimd, wupT[:, :], aps['wupT'][:, :], 'dGc')
    kb.dma(nc.gpsimd, inwT[:, :], aps['inwT'][:, :], 'dGc')
    kb.dma(nc.gpsimd, outwT[:, :], aps['outwT'][:, :], 'dGc')
    kb.dma(nc.gpsimd, fcdT[:, :], aps['fcdT'][:, :], 'dGc')
    kb.dma(nc.sync, cwcb[:, :], aps['cwcb'][:, :], 'dSc')
    kb.dma(nc.sync, drep[:, :], aps['drep'][:, :], 'dSc')
    kb.dma(nc.sync, normw[:, :], aps['normw'][:, :], 'dSc')
    kb.dma(nc.sync, bdown[:, :], aps['bdown'][:, :], 'dSc')
    kb.dma(nc.sync, bupT[:, :], aps['bupT'][:, :], 'dSc')
    kb.dma(nc.sync, g1[0:1, 0:1], aps['g1'][:, :], 'dSc')
    t_const = kb.tok('dSc')
    t_constg = kb.tok('dGc')

    kb.vpre(t_constg, t_const)
    kb.vpost(v.memset(onesb[:, :], 1.0))
    kb.vpre()
    kb.vpost(v.memset(onesf[:, :], 1.0))
    for c in range(2):
        kb.vpre()
        kb.vpost(v.tensor_scalar(outwT[:, 128 * c:128 * (c + 1)],
                                 outwT[:, 128 * c:128 * (c + 1)],
                                 normw[:, c:c + 1], None, ALU.mult))
    t_w = kb.tok('cV')
    pS = [kb.ps("pS0", (P, 512), F32), kb.ps("pS1", (P, 512), F32)]
    kb.gpre(t_w)
    tgw = kb.gpost(nc.gpsimd.tensor_copy(gd[0:1, 0:2], onesf[0:1, P - 2:P]))
    kb.wait('T', nc.tensor, tgw, st.get('l1a_pS0_war'))
    ins = nc.tensor.matmul(pS[0][:, 0:1], onesf[0:1, :], g1[0:1, 0:1],
                           start=True, stop=True)
    tt = kb.inc(ins, 'sT')
    alf = kb.sb("l1a_alf", (P, 4), F32)
    kb.apre(tt)
    kb.apost(a.activation(alf[:, 0:1], pS[0][:, 0:1], AF.Sigmoid))
    kb.vpre(kb.tok('cA'))
    kb.vpost(v.tensor_scalar(alf[:, 1:2], alf[:, 0:1], 1.0, None, ALU.add))
    kb.vpre()
    tal = kb.vpost(v.tensor_scalar(alf[:, 2:3], alf[:, 0:1], -1.0, 1.0,
                                   ALU.mult, ALU.add))
    st['l1a_pS0_war'] = kb.tok('cA')

    xn1 = kb.sb("l1a_xn1", (P, 4 * D), BF16)
    x1T = kb.sb("l1a_x1T", (P, 32 * 512), BF16)
    x1T_r = x1T[:, :].rearrange("p (f t) -> p f t", f=32)
    lres = kb.sb("l1a_res", (P, 4 * D), F32)
    xst = kb.sb("xst0", (P, D), F32)
    h0T = kb.sb("l1a_h0T", (P, 512), BF16)
    h1T = kb.sb("l1a_h1T", (P, 512), BF16)
    h2T = kb.sb("l1a_h2T", (P, 512), BF16)
    hfin = kb.sb("l1a_hfin", (P, 512), BF16)
    utile = kb.sb("l1a_utile", (P, 1024), F32)
    w = {'inwT': inwT, 'outwT': outwT, 'fcdT': fcdT, 'cwcb': cwcb,
         'drep': drep, 'onesb': onesb, 'onesf': onesf}

    pA = [kb.ps("pA%d" % k, (P, 512), F32) for k in range(4)]
    for tc in range(2):
        for k4 in range(4):
            kb.wait('S', nc.sync, st.get('l1a_xst_war'), st.get('l1a_res_stored'))
            tx = kb.dma(nc.sync, xst[:, :],
                        res_in[512 * tc + P * k4:512 * tc + P * (k4 + 1), :], 'dSx')
            tln = emit_ln_tile(kb, xst[:, :], xn1[:, D * k4:D * (k4 + 1)], tx,
                               war_toks=(st.get('l1a_xn_war'),))
            st['l1a_xst_war'] = tln
            kb.gpre(tln)
            tg = kb.gpost(nc.gpsimd.tensor_copy(
                gd[:, :], xn1[:, D * k4 + D - 64:D * (k4 + 1)]))
            kb.wait('T', nc.tensor, tg, t_constg)
            emit_transpose_tile(kb, xn1[:, D * k4:D * (k4 + 1)], x1T_r,
                                P * k4, identb[:, :])
        t_tr = (kb.tr_evac[0], kb.tr_evac[1])
        kb.wait('T', nc.tensor, st.get('l1a_pS0_war'), *t_tr)
        for f in range(32):
            ins = nc.tensor.matmul(pS[0][:, :], wdT[:, 128 * f:128 * (f + 1)],
                                   x1T[:, 512 * f:512 * (f + 1)],
                                   start=(f == 0), stop=(f == 31))
        tt = kb.inc(ins, 'sT')
        kb.apre(tt, tal, st.get('l1a_h0_war'))
        th0 = kb.apost(a.activation(h0T[:, :], pS[0][:, :], AF.Gelu,
                                    bias=bdown[:, 0:1]))
        st['l1a_pS0_war'] = th0
        kb.gpre(th0)
        tgh = kb.gpost(nc.gpsimd.tensor_copy(gd[:, :], h0T[:, 512 - 64:512]))
        st['l1a_u_guard'] = tgh
        kb.wait('T', nc.tensor, tgh)
        _l1a_hydra(kb, h0T[:, :], h1T[:, :], w)
        kb.gpre(kb.tok('cA'))
        tg1 = kb.gpost(nc.gpsimd.tensor_copy(gd[:, :], h1T[:, 512 - 64:512]))
        kb.wait('T', nc.tensor, tg1)
        _l1a_hydra(kb, h1T[:, :], h2T[:, :], w)
        kb.vpre(kb.tok('cA'))
        kb.vpost(v.tensor_scalar(hfin[:, :], h2T[:, :], alf[:, 2:3], None,
                                 ALU.mult))
        kb.vpre()
        thf = kb.vpost(v.scalar_tensor_tensor(hfin[:, :], h0T[:, :],
                                              alf[:, 1:2], hfin[:, :],
                                              ALU.mult, ALU.add))
        st['l1a_h0_war'] = thf
        kb.gpre(thf)
        tghf = kb.gpost(nc.gpsimd.tensor_copy(gd[:, :], hfin[:, 512 - 64:512]))
        kb.wait('S', nc.sync, st.get('l1a_res_stored'), st.get('l1a_lres_war'))
        for k4 in range(4):
            kb.dma(nc.sync, lres[:, D * k4:D * (k4 + 1)],
                   res_in[512 * tc + P * k4:512 * tc + P * (k4 + 1), :], 'dSr')
        t_res = kb.tok('dSr')
        for k4 in range(4):
            kb.vpre(t_res)
            kb.vpost(v.tensor_tensor(lres[:, D * k4:D * (k4 + 1)],
                                     lres[:, D * k4:D * (k4 + 1)],
                                     xn1[:, D * k4:D * (k4 + 1)], ALU.add))
        st['l1a_xn_war'] = kb.tok('cV')
        ut_war = [None, None]
        pa2_war = st.get('l1a_pA2_war')
        pend = None

        def _flush_up(pend):
            nonlocal pa2_war
            po, pb, ptgu = pend
            kb.wait('T', nc.tensor, ptgu, pa2_war)
            for m in range(4):
                ins = nc.tensor.transpose(
                    pA[2][:, 128 * m:128 * (m + 1)],
                    utile[:, 512 * pb + 128 * m:512 * pb + 128 * (m + 1)],
                    identf[:, :])
            tt2 = kb.inc(ins, 'sT')
            ut_war[pb] = tt2
            kb.vpre(tt2)
            for m in range(4):
                ins = v.tensor_tensor(lres[:, D * m + P * po:D * m + P * (po + 1)],
                                      lres[:, D * m + P * po:D * m + P * (po + 1)],
                                      pA[2][:, 128 * m:128 * (m + 1)], ALU.add)
                kb.vpost(ins)
                if m < 3:
                    kb.vpre()
            pa2_war = kb.tok('cV')

        for oc in range(32):
            ps = pS[oc % 2]
            kb.wait('T', nc.tensor, tghf, st.get('l1a_pS%d_war' % (oc % 2)))
            ins = nc.tensor.matmul(ps[:, :], wupT[:, 128 * oc:128 * (oc + 1)],
                                   hfin[:, :], start=True, stop=True)
            tt = kb.inc(ins, 'sT')
            if pend is not None:
                _flush_up(pend)
            kb.apre(tt, ut_war[oc % 2])
            tev = kb.apost(a.activation(utile[:, 512 * (oc % 2):512 * (oc % 2 + 1)],
                                        ps[:, :], AF.Identity,
                                        bias=bupT[:, oc:oc + 1]))
            st['l1a_pS%d_war' % (oc % 2)] = tev
            kb.gpre(tev)
            tgu = kb.gpost(nc.gpsimd.tensor_copy(
                gd[:, :], utile[:, 512 * (oc % 2) + 448:512 * (oc % 2) + 512]))
            pend = (oc, oc % 2, tgu)
        _flush_up(pend)
        st['l1a_pA2_war'] = pa2_war
        gdf = kb.sb("gdscrf", (P, 64), F32)
        for k4 in range(4):
            kb.gpre(pa2_war)
            tgr = kb.gpost(nc.gpsimd.tensor_copy(
                gdf[:, :], lres[:, D * k4 + D - 64:D * (k4 + 1)]))
            kb.wait('A', a, tgr)
            kb.dma(nc.scalar, res_out[512 * tc + P * k4:512 * tc + P * (k4 + 1), :],
                   lres[:, D * k4:D * (k4 + 1)], 'dV')
        st['l1a_res_stored'] = kb.tok('dV')
        st['l1a_lres_war'] = pa2_war


def _l1f_build_bp(kb, stg_f, tload, scT, nt, bp_out):
    nc = kb.nc
    v = nc.vector
    st = kb.state
    inv = kb.sb("l1f_inv", (P, 86), F32)
    stg = kb.sb("l1f_wg", (P, 4096), BF16)[:, 0:86 * 32]
    gd = kb.sb("gdscr", (P, 64), BF16)
    identb = kb.sb("identb", (P, P), BF16)
    pT = kb.ps("pT0", (P, 1024), BF16)
    kb.vpre(tload, st.get('l1f_inv_war'))
    kb.vpost(v.reciprocal(inv[:, 0:nt], scT))
    kb.vpre()
    kb.vpost(v.tensor_scalar(inv[:, 0:nt], inv[:, 0:nt], 4.0, None, ALU.mult))
    for ot in range(nt):
        kb.vpre(st.get('l1f_bstage_war') if ot == 0 else None)
        kb.vpost(v.tensor_scalar(stg[:, 32 * ot:32 * (ot + 1)],
                                 stg_f[:, 32 * ot:32 * (ot + 1)],
                                 inv[:, ot:ot + 1], None, ALU.mult))
    tv = kb.tok('cV')
    st['l1f_inv_war'] = tv
    kb.gpre(tv)
    tg = kb.gpost(nc.gpsimd.tensor_copy(gd[:, 0:32],
                                        stg[:, 32 * (nt - 1):32 * nt]))
    evtok = None
    for g in range((nt + 7) // 8):
        n = min(8, nt - 8 * g)
        kb.wait('T', nc.tensor, tg, evtok, st.get('l1f_pT_war'))
        for m in range(n):
            ins = nc.tensor.transpose(pT[0:32, 128 * m:128 * (m + 1)],
                                      stg[:, 32 * (8 * g + m):32 * (8 * g + m + 1)],
                                      identb[:, :])
        tt = kb.inc(ins, 'sT')
        kb.vpre(tt)
        evtok = kb.vpost(v.tensor_copy(bp_out[0:32, 1024 * g:1024 * g + 128 * n],
                                       pT[0:32, 0:128 * n]))
    st['l1f_bstage_war'] = evtok
    st['l1f_pT_war'] = evtok
    return evtok


def emit_l1_ffn(kb, aps):
    nc = kb.nc
    v, a = nc.vector, nc.scalar
    res_in, res_out = aps['res_in'], aps['res_out']
    st = kb.state
    identb = kb.sb("identb", (P, P), BF16)
    kb.dma(nc.sync, identb[:, :], aps['identb'][:, :], 'dSc')

    hcol = kb.sb("l1f_hcol", (P, 86 * 512), BF16)
    xnT = hcol[:, 0:32 * 1024]
    xnT_r = xnT.rearrange("p (f t) -> p f t", f=32)
    gBp = hcol[0:32, 32 * 1024:32 * 1024 + 86 * 128]
    xst = kb.sb("xst0", (P, D), F32)
    xn = kb.sb("l1f_xn", (P, D), BF16)
    gd = kb.sb("gdscr", (P, 64), BF16)
    uBp = kb.sb("l1f_uBp", (32, 86 * 128), BF16)
    B2p = kb.sb("l1f_B2p", (32, 32 * 128), BF16)
    wd = kb.sb("l1f_wd", (P, 2 * 8 * 512), BF16)
    gA_sb = wd[:, 0:1024]
    uA_sb = wd[:, 1024:2048]
    A2_sb = wd[:, 2048:2048 + 86 * 32]
    t1g = kb.sb("l1f_t1g", (32, 1024), BF16)
    t1u = kb.sb("l1f_t1u", (32, 1024), BF16)
    t2 = kb.sb("l1f_t2", (32, 1024), BF16)
    gscT = kb.sb("l1f_gsc", (P, 86), F32)
    uscT = kb.sb("l1f_usc", (P, 86), F32)
    dscT = kb.sb("l1f_dsc", (P, 32), F32)
    gsc005 = kb.sb("l1f_gsc5", (P, 86), F32)
    usc005 = kb.sb("l1f_usc5", (P, 86), F32)
    xsrow = kb.sb("l1f_xsrow", (1, 1024), BF16)
    rsrow = kb.sb("l1f_rsrow", (1, 1024), BF16)
    m75 = kb.sb("l1f_m75", (1, 512), BF16)
    onesb = kb.sb("l1a_onesb", (P, 1), BF16)
    onesf = kb.sb("l1a_onesf", (1, P), F32)
    s2bc = kb.sb("l1f_s2bc", (P, 4096), F32)
    dsrow = s2bc[0:1, :]
    stg_f = s2bc[:, 0:86 * 32]
    wg = kb.sb("l1f_wg", (P, 4096), BF16)
    wu = kb.sb("l1f_wu", (P, 4096), BF16)
    h_sb = kb.sb("l1f_h", (P, 4 * 512), BF16)
    gs_sb = kb.sb("l1f_gs", (P, 512), BF16)
    us_sb = kb.sb("l1f_us", (P, 512), BF16)
    ytmp = xst[:, 0:2 * 512]
    pS = [kb.ps("pS0", (P, 512), F32), kb.ps("pS1", (P, 512), F32)]
    pA = [kb.ps("pA%d" % k, (P, 512), F32) for k in range(4)]

    kb.dma(nc.sync, gscT[:, :], aps['gscT'][:, :], 'dSc')
    kb.dma(nc.sync, uscT[:, :], aps['uscT'][:, :], 'dSc')
    kb.dma(nc.sync, dscT[:, :], aps['dscT'][:, :], 'dSc')
    kb.dma(nc.gpsimd, gA_sb, aps['gA'][:, :], 'dGc')
    kb.dma(nc.gpsimd, uA_sb, aps['uA'][:, :], 'dGc')
    kb.dma(nc.gpsimd, A2_sb, aps['A2'][:, :], 'dGc')
    t_const = kb.tok('dSc')
    t_constg = kb.tok('dGc')
    kb.vpre(t_const, t_constg)
    kb.vpost(v.memset(onesb[:, :], 1.0))
    kb.vpre()
    kb.vpost(v.memset(onesf[:, :], 1.0))
    kb.vpre()
    kb.vpost(v.memset(m75[0:1, :], -7.5))
    kb.vpre()
    kb.vpost(v.tensor_scalar(gsc005[:, :], gscT[:, :], 0.005, None, ALU.mult))
    kb.vpre()
    kb.vpost(v.tensor_scalar(usc005[:, :], uscT[:, :], 0.005, None, ALU.mult))
    for k in range(8):
        kb.wait('S', nc.sync, st.get('l1f_xst_war'))
        tx = kb.dma(nc.sync, xst[:, :], res_in[P * k:P * (k + 1), :], 'dSx')
        tln = emit_ln_tile(kb, xst[:, :], xn[:, :], tx,
                           war_toks=(st.get('l1f_xn_war'),))
        st['l1f_xst_war'] = tln
        kb.gpre(tln)
        tg = kb.gpost(nc.gpsimd.tensor_copy(gd[:, :], xn[:, D - 64:D]))
        kb.wait('T', nc.tensor, tg, t_const)
        st['l1f_xn_war'] = emit_transpose_tile(kb, xn[:, :], xnT_r, P * k,
                                               identb[:, :])
    t_tr = (kb.tr_evac[0], kb.tr_evac[1])
    kb.gpre(kb.tok('cV'))
    tgo = kb.gpost(nc.gpsimd.tensor_copy(gd[0:1, 0:2], onesf[0:1, P - 2:P]))
    for tc in range(2):
        kb.wait('T', nc.tensor, tgo, st.get('l1f_ps0_war'), *t_tr)
        for f in range(32):
            ins = nc.tensor.matmul(pS[0][0:1, :], onesb[:, 0:1],
                                   xnT[:, 1024 * f + 512 * tc:1024 * f + 512 * (tc + 1)],
                                   start=(f == 0), stop=(f == 31))
        tt = kb.inc(ins, 'sT')
        kb.vpre(tt)
        st['l1f_ps0_war'] = kb.vpost(
            v.tensor_copy(xsrow[0:1, 512 * tc:512 * (tc + 1)], pS[0][0:1, :]))
    kb.gpre(kb.tok('cV'))
    tgxs = kb.gpost(nc.gpsimd.tensor_copy(gd[0:1, :], xsrow[0:1, 1024 - 64:1024]))

    tl = kb.dma(nc.sync, stg_f[:, :], aps['gBT'][:, :], 'dSb2')
    tb1 = _l1f_build_bp(kb, stg_f, tl, gscT[:, :], 86, gBp)
    kb.wait('S', nc.sync, kb.tok('cV'))
    tl = kb.dma(nc.sync, stg_f[:, :], aps['uBT'][:, :], 'dSb2')
    tb2 = _l1f_build_bp(kb, stg_f, tl, uscT[:, :], 86, uBp)
    kb.wait('S', nc.sync, kb.tok('cV'))
    tl = kb.dma(nc.sync, stg_f[:, 0:32 * 32], aps['B2T'][:, :], 'dSb2')
    tb3 = _l1f_build_bp(kb, stg_f, tl, dscT[:, :], 32, B2p)
    for (t1, A_sb, warkey) in ((t1g, gA_sb, 'l1f_t1g_war'),
                               (t1u, uA_sb, 'l1f_t1u_war')):
        for tc in range(2):
            kb.wait('T', nc.tensor, t_constg, st.get('l1f_ps1_war'), *t_tr)
            for f in range(32):
                ins = nc.tensor.matmul(
                    pS[1][0:32, :], A_sb[:, 32 * f:32 * (f + 1)],
                    xnT[:, 1024 * f + 512 * tc:1024 * f + 512 * (tc + 1)],
                    start=(f == 0), stop=(f == 31))
            tt = kb.inc(ins, 'sT')
            kb.vpre(tt)
            st['l1f_ps1_war'] = kb.vpost(
                v.tensor_copy(t1[0:32, 512 * tc:512 * (tc + 1)], pS[1][0:32, :]))
    kb.wait('S', nc.sync, kb.tok('cV'))
    tdsl = kb.dma(nc.sync, s2bc[0:1, :], aps['dsrow'][:, :], 'dSc2')
    kb.vpre(tdsl, st.get('l1f_inv_war'), st.get('l1f_bstage_war'))
    tds = kb.vpost(v.tensor_scalar(dsrow[0:1, :], dsrow[0:1, :], 0.005, None,
                                   ALU.mult))
    kb.gpre(tds)
    tgd = kb.gpost(nc.gpsimd.tensor_copy(gd[0:1, :],
                                         dsrow[0:1, 4096 - 64:4096]))
    for i in range(8):
        ps = pS[i % 2]
        kb.wait('T', nc.tensor, tgd, st.get('l1f_ps%d_war' % (i % 2)))
        ins = nc.tensor.matmul(ps[:, :], onesf[0:1, :],
                               dsrow[0:1, 512 * i:512 * (i + 1)],
                               start=True, stop=True)
        tt = kb.inc(ins, 'sT')
        kb.vpre(tt)
        st['l1f_ps%d_war' % (i % 2)] = kb.vpost(
            v.tensor_copy(s2bc[:, 512 * i:512 * (i + 1)], ps[:, :]))

    kb.gpre(kb.tok('cV'))
    tgr1 = kb.gpost(nc.gpsimd.tensor_copy(gd[0:32, 0:32],
                                          t1g[0:32, 1024 - 32:1024]))
    kb.gpre()
    tgrow = kb.gpost(nc.gpsimd.tensor_copy(gd[0:32, 0:32],
                                           t1u[0:32, 1024 - 32:1024]))

    wgs = [wg, kb.sb("l1f_wgb", (P, 4096), BF16)]
    wus = [wu, kb.sb("l1f_wub", (P, 4096), BF16)]
    wg_war = [None, None]
    wu_war = [None, None]
    ev_war = {'g': None, 'u': None}
    h_war = [None, None, None, None]
    hst = [None, None, None, None]
    for ot in range(86):
        par = ot % 2
        wgc, wuc = wgs[par], wus[par]
        tok0 = (st.get('l1f_pT_war') if ot == 0
                else (st.get('l1f_xst_war') if ot == 1 else None))
        kb.wait('S', nc.sync, wg_war[par], tok0)
        tqg = kb.dma(nc.sync, wgc[:, :],
                     aps['gqT'][P * ot:P * (ot + 1), :], 'dGg')
        kb.wait('A', a, wu_war[par], tok0)
        tqu = kb.dma(nc.scalar, wuc[:, :],
                     aps['uqT'][P * ot:P * (ot + 1), :], 'dAu')
        for tc in range(2):
            slot = 2 * par + tc
            pslot = 2 * (1 - par) + tc
            kb.wait('T', nc.tensor, tqg, ev_war['g'], tgrow, tgr1, tgo, tgxs,
                    tb1, tb2, tb3)
            for hc in range(32):
                ins = nc.tensor.matmul(
                    pS[0][:, :], wgc[:, 128 * hc:128 * (hc + 1)],
                    xnT[:, 1024 * hc + 512 * tc:1024 * hc + 512 * (tc + 1)],
                    start=(hc == 0), stop=False)
            ins = nc.tensor.matmul(pS[0][:, :], m75[0:1, 0:P],
                                   xsrow[0:1, 512 * tc:512 * (tc + 1)],
                                   start=False, stop=False)
            ins = nc.tensor.matmul(pS[0][:, :], gBp[0:32, 128 * ot:128 * (ot + 1)],
                                   t1g[0:32, 512 * tc:512 * (tc + 1)],
                                   start=False, stop=True)
            ttg = kb.inc(ins, 'sT')
            kb.wait('T', nc.tensor, tqu, ev_war['u'])
            for hc in range(32):
                ins = nc.tensor.matmul(
                    pS[1][:, :], wuc[:, 128 * hc:128 * (hc + 1)],
                    xnT[:, 1024 * hc + 512 * tc:1024 * hc + 512 * (tc + 1)],
                    start=(hc == 0), stop=False)
            ins = nc.tensor.matmul(pS[1][:, :], m75[0:1, 0:P],
                                   xsrow[0:1, 512 * tc:512 * (tc + 1)],
                                   start=False, stop=False)
            ins = nc.tensor.matmul(pS[1][:, :], uBp[0:32, 128 * ot:128 * (ot + 1)],
                                   t1u[0:32, 512 * tc:512 * (tc + 1)],
                                   start=False, stop=True)
            ttu = kb.inc(ins, 'sT')
            wu_war[par] = ttu
            if tc == 1:
                wg_war[par] = ttg
            kb.apre(ttg, st.get('l1f_gs_war'))
            tevg = kb.apost(a.activation(gs_sb[:, :], pS[0][:, :], AF.Silu,
                                         scale=gsc005[:, ot:ot + 1]))
            ev_war['g'] = tevg
            kb.vpre(ttu, st.get('l1f_us_war'))
            tevu = kb.vpost(v.tensor_scalar(us_sb[:, :], pS[1][:, :],
                                            usc005[:, ot:ot + 1], None, ALU.mult))
            ev_war['u'] = tevu
            kb.vpre(tevg, h_war[slot], hst[slot])
            th = kb.vpost(v.tensor_tensor(h_sb[:, 512 * slot:512 * (slot + 1)],
                                          gs_sb[:, :], us_sb[:, :], ALU.mult))
            st['l1f_gs_war'] = th
            st['l1f_us_war'] = th
            kb.wait('A', a, th)
            hst[slot] = kb.dma(nc.scalar,
                               aps['h_scratch'][P * ot:P * (ot + 1),
                                                512 * tc:512 * (tc + 1)],
                               h_sb[:, 512 * slot:512 * (slot + 1)],
                               'dGh%d' % slot)
            if ot > 0:
                kb.wait('T', nc.tensor, st.get('l1f_h_guard%d' % pslot))
                ins = nc.tensor.matmul(pA[2 + tc][0:32, :],
                                       A2_sb[:, 32 * (ot - 1):32 * ot],
                                       h_sb[:, 512 * pslot:512 * (pslot + 1)],
                                       start=(ot == 1), stop=False)
                h_war[pslot] = kb.inc(ins, 'sT')
            kb.gpre(th)
            st['l1f_h_guard%d' % slot] = kb.gpost(
                nc.gpsimd.tensor_copy(gd[:, :],
                                      h_sb[:, 512 * (slot + 1) - 64:512 * (slot + 1)]))
    for tc in range(2):
        slot = 2 * (85 % 2) + tc
        kb.wait('T', nc.tensor, st.get('l1f_h_guard%d' % slot))
        ins = nc.tensor.matmul(pA[2 + tc][0:32, :], A2_sb[:, 32 * 85:32 * 86],
                               h_sb[:, 512 * slot:512 * (slot + 1)],
                               start=False, stop=True)
        h_war[slot] = kb.inc(ins, 'sT')
        kb.apre(h_war[slot], st.get('l1f_t2_war'))
        st['l1f_t2_war'] = kb.apost(
            a.activation(t2[0:32, 512 * tc:512 * (tc + 1)], pA[2 + tc][0:32, :],
                         AF.Identity))
    kb.gpre(st['l1f_t2_war'])
    tgt2 = kb.gpost(nc.gpsimd.tensor_copy(gd[0:32, 0:32], t2[0:32, 992:1024]))

    kb.gpre()
    tpre = kb.dma(nc.gpsimd, res_out[:, :], res_in[:, :], 'dGcp')

    yst = [None, None]
    ytp_war = None
    for tc in range(2):
        hwaits = (kb.tok('dGh0'), kb.tok('dGh1'), kb.tok('dGh2'),
                  kb.tok('dGh3'), ('sT', kb.cnt['sT']), st.get('l1f_hcol_war'))
        th = []
        for (en, qe, _), (i0, i1) in zip(
                (('S', nc.sync, 0), ('A', nc.scalar, 0), ('G', nc.gpsimd, 0)),
                ((0, 29), (29, 58), (58, 86))):
            kb.wait(en, qe, *hwaits)
            th.append(kb.dma(
                qe, hcol[:, 512 * i0:512 * i1].rearrange("p (i t) -> p i t",
                                                         i=i1 - i0),
                aps['h_scratch'][P * i0:P * i1, 512 * tc:512 * (tc + 1)]
                .rearrange("(i p) t -> p i t", p=P), 'dSh'))
        kb.wait('T', nc.tensor, *th, st.get('l1f_ps0_war'))
        for i in range(86):
            ins = nc.tensor.matmul(pS[0][0:1, :], onesb[:, 0:1],
                                   hcol[:, 512 * i:512 * (i + 1)],
                                   start=(i == 0), stop=(i == 85))
        tt = kb.inc(ins, 'sT')
        kb.vpre(tt)
        trs = kb.vpost(v.tensor_copy(rsrow[0:1, 512 * tc:512 * (tc + 1)],
                                     pS[0][0:1, :]))
        st['l1f_ps0_war'] = trs
        kb.gpre(trs)
        tgrs = kb.gpost(nc.gpsimd.tensor_copy(
            gd[0:1, :], rsrow[0:1, 512 * (tc + 1) - 64:512 * (tc + 1)]))
        wd_tok = [None, None, None, None]
        wd_war = [None, None, None, None]
        QS = (('G', nc.gpsimd, 'dSd0'), ('A', nc.scalar, 'dSd1'),
              ('S', nc.sync, 'dSd2'))
        for o2c in range(8):
            for g in range(22):
                slot = g % 4
                n = min(4, 86 - 4 * g)
                qeng, qe, qsem = QS[g % 3]
                kb.wait(qeng, qe, wd_war[slot], st.get('l1f_t2_war'))
                wd_tok[slot] = kb.dma(
                    qe,
                    wd[:, 2048 * slot:2048 * slot + 512 * n]
                    .rearrange("p (i o) -> p i o", i=n),
                    aps['dqT'][(o2c * 86 + 4 * g) * P:(o2c * 86 + 4 * g + n) * P, :]
                    .rearrange("(i p) o -> p i o", p=P), qsem)
                kb.wait('T', nc.tensor, wd_tok[slot], tgt2, tgrs,
                        ytp_war if g == 0 else None)
                for m in range(n):
                    i = 4 * g + m
                    for k in range(4):
                        ins = nc.tensor.matmul(
                            pA[k][:, :],
                            hcol[:, 512 * i + 128 * k:512 * i + 128 * (k + 1)],
                            wd[:, 2048 * slot + 512 * m:2048 * slot + 512 * (m + 1)],
                            start=(i == 0), stop=False)
                wd_war[slot] = kb.inc(ins, 'sT')
            for k in range(4):
                ins = nc.tensor.matmul(
                    pA[k][:, :],
                    rsrow[0:1, 512 * tc + 128 * k:512 * tc + 128 * (k + 1)],
                    m75[0:1, :], start=False, stop=False)
                ins = nc.tensor.matmul(
                    pA[k][:, :],
                    t2[0:32, 512 * tc + 128 * k:512 * tc + 128 * (k + 1)],
                    B2p[0:32, 512 * o2c:512 * (o2c + 1)],
                    start=False, stop=True)
                tt = kb.inc(ins, 'sT')
                ypar = k % 2
                kb.vpre(tt, yst[ypar])
                tev = kb.vpost(v.tensor_tensor(
                    ytmp[:, 512 * ypar:512 * (ypar + 1)], pA[k][:, :],
                    s2bc[:, 512 * o2c:512 * (o2c + 1)], ALU.mult))
                kb.gpre(tev, tpre)
                yst[ypar] = kb.dma(
                    nc.gpsimd,
                    res_out[512 * tc + P * k:512 * tc + P * (k + 1),
                            512 * o2c:512 * (o2c + 1)],
                    ytmp[:, 512 * ypar:512 * (ypar + 1)], 'dGy%d' % ypar,
                    accum_op=ALU.add)
            ytp_war = kb.tok('cV')
        st['l1f_hcol_war'] = kb.tok('sT')
    kb.wait('S', nc.sync, kb.tok('dGy0'), kb.tok('dGy1'))


def _prep_l0f_weights(q_packed, q_scales, q_zeros, delta_weight, bias):
    b = np.ascontiguousarray(q_packed).view(np.uint8).reshape(-1, 4)[:, 0]
    b2 = b.reshape(4096, 2048)
    qbT = np.ascontiguousarray(
        b2.reshape(32, 128, 16, 128).transpose(0, 3, 2, 1)).reshape(32 * 128, 2048)
    qbT = np.concatenate([qbT & np.uint8(15), qbT >> np.uint8(4)], axis=1)
    F, JC, Q = np.meshgrid(np.arange(128), np.arange(16), np.arange(2),
                           indexing='ij')
    ii = 2 * (128 * JC + F) + Q
    s2d = q_scales.reshape(4096, 32)
    cc = ii // 128
    sexpT = np.ascontiguousarray(
        s2d[:, cc.ravel()].reshape(4096, 128, 16, 2)
        .transpose(1, 2, 3, 0).reshape(128, 16, 2, 32, 128)
        .transpose(3, 0, 1, 2, 4)).reshape(32 * 128, 4096).astype(np.float32)
    deltaT = np.ascontiguousarray(
        delta_weight[:, ii.ravel()].reshape(4096, 128, 16, 2)
        .transpose(1, 2, 3, 0).reshape(128, 16, 2, 32, 128)
        .transpose(3, 0, 1, 2, 4)).reshape(32 * 128, 4096).astype(BF)
    s2dT = np.ascontiguousarray(q_scales.reshape(4096, 32).T)
    z2dT = np.ascontiguousarray(q_zeros.reshape(4096, 32).T)
    indE = np.zeros((128, 32, 32), dtype=BF)
    for pc in range(32):
        q, jc = pc // 16, pc % 16
        for f in range(128):
            c = 2 * jc + (1 if f >= 64 else 0)
            indE[f, pc, c] = 1.0
    indE = indE.reshape(128, 1024)
    biasT = np.ascontiguousarray(bias.reshape(32, 128).T).astype(np.float32)
    return {"qbT": qbT, "sexpT": sexpT, "deltaT": deltaT, "s2dT": s2dT,
            "z2dT": z2dT, "indE": indE, "biasT": biasT,
            "identb": np.eye(P, dtype=BF),
            "identf": np.eye(P, dtype=np.float32)}


def _l0f_aps(nc):
    return {
        'qbT': nc.dram_tensor("qbT", [32 * P, 4096], U8, kind="ExternalInput").ap(),
        'sexpT': nc.dram_tensor("sexpT", [32 * P, 4096], F32, kind="ExternalInput").ap(),
        'deltaT': nc.dram_tensor("deltaT", [32 * P, 4096], BF16, kind="ExternalInput").ap(),
        's2dT': nc.dram_tensor("s2dT", [32, 4096], F32, kind="ExternalInput").ap(),
        'z2dT': nc.dram_tensor("z2dT", [32, 4096], F32, kind="ExternalInput").ap(),
        'indE': nc.dram_tensor("indE", [P, 1024], BF16, kind="ExternalInput").ap(),
        'biasT': nc.dram_tensor("biasT", [P, 32], F32, kind="ExternalInput").ap(),
        'identb': _dram(nc, "identb", [P, P], BF16),
        'identf': _dram(nc, "identf", [P, P], F32),
        'weff_scratch': nc.dram_tensor("weff_scratch", [32 * P, 4096], BF16,
                                       kind="Internal").ap(),
    }


def build_l0f_program():
    nc = bass.Bass("TRN2", target_bir_lowering=False)
    aps = _l0f_aps(nc)
    aps['res_in'] = nc.dram_tensor("res_in", [TOKC, D], F32, kind="ExternalInput").ap()
    aps['res_out'] = nc.dram_tensor("res_out", [TOKC, D], F32, kind="ExternalOutput").ap()
    kb = KB(nc)
    kb.OFFSETS = _mk_offsets()
    emit_l0_ffn(kb, aps)
    kb.wait('S', nc.sync, kb.tok('dGy0'), kb.tok('dGy1'))
    kb.close()
    return nc


def kernel_l0f_only(r1_full, q_packed, q_scales, q_zeros, delta_weight, bias):
    r1 = np.asarray(r1_full).reshape(T, D)
    w = _prep_l0f_weights(np.asarray(q_packed), np.asarray(q_scales),
                          np.asarray(q_zeros), np.asarray(delta_weight),
                          np.asarray(bias))
    nc = build_l0f_program()
    in_maps = []
    for c in range(NCORE):
        m = dict(w)
        m['res_in'] = np.ascontiguousarray(
            np.concatenate([r1[STR * c:STR * (c + 1)],
                            r1[STR * (c + 8):STR * (c + 9)]]))
        in_maps.append(m)
    res = run_bass_kernel_spmd(nc, in_maps, core_ids=list(range(NCORE)))
    out = np.empty((T, D), np.float32)
    for c in range(NCORE):
        r = np.asarray(res.results[c]["res_out"])
        out[STR * c:STR * (c + 1)] = r[:STR]
        out[STR * (c + 8):STR * (c + 9)] = r[STR:]
    return out.reshape(1, T, D)


def _prep_l1a_weights(p):
    wd = np.ascontiguousarray(
        p['w_down'].T.reshape(32, 128, 128).transpose(1, 0, 2).reshape(128, 4096)
    ).astype(np.float32)
    wu = np.ascontiguousarray(p['w_up'].T).astype(np.float32)
    inw = np.ascontiguousarray(p['hy_in_w'][:512, :].T).astype(np.float32)
    outw = np.ascontiguousarray(
        p['hy_out_w'].T.reshape(2, 128, 128).transpose(1, 0, 2).reshape(128, 256)
    ).astype(np.float32)
    fr = p['hy_fcD_w'][np.repeat(np.arange(4), 64), :].T
    fcd = np.ascontiguousarray(
        fr.reshape(2, 128, 2, 128).transpose(1, 0, 2, 3).reshape(128, 512)
    ).astype(np.float32)
    cw = p['hy_conv_w'][0:256, 0, 6].reshape(2, 128).T
    cb = p['hy_conv_b'][0:256].reshape(2, 128).T
    cwcb = np.ascontiguousarray(np.concatenate([cw, cb], 1)).astype(np.float32)
    drep = np.ascontiguousarray(
        np.repeat(p['hy_D'], 64).reshape(2, 128).T).astype(np.float32)
    normw = np.ascontiguousarray(
        p['hy_norm_w'].reshape(2, 128).T).astype(np.float32)
    return {
        'wdT': wd, 'wupT': wu, 'inwT': inw, 'outwT': outw, 'fcdT': fcd,
        'cwcb': cwcb, 'drep': drep, 'normw': normw,
        'bdown': np.ascontiguousarray(p['b_down'].reshape(128, 1)).astype(np.float32),
        'bupT': np.ascontiguousarray(p['b_up'].reshape(32, 128).T).astype(np.float32),
        'g1': np.asarray(p['gate1']).reshape(1, 1).astype(np.float32),
        'identb': np.eye(P, dtype=BF),
        'identf': np.eye(P, dtype=np.float32),
    }


def _l1a_aps(nc):
    d = {'identb': _dram(nc, "identb", [P, P], BF16),
         'identf': _dram(nc, "identf", [P, P], F32)}
    for nm, shp, dt in (('wdT', (P, 4096), F32), ('wupT', (P, 4096), F32),
                        ('inwT', (P, 512), F32), ('outwT', (P, 256), F32),
                        ('fcdT', (P, 512), F32), ('cwcb', (P, 4), F32),
                        ('drep', (P, 2), F32), ('normw', (P, 2), F32),
                        ('bdown', (P, 1), F32), ('bupT', (P, 32), F32),
                        ('g1', (1, 1), F32), ):
        d[nm] = nc.dram_tensor(nm, list(shp), dt, kind="ExternalInput").ap()
    return d


def build_l1a_program():
    nc = bass.Bass("TRN2", target_bir_lowering=False)
    aps = _l1a_aps(nc)
    aps['res_in'] = nc.dram_tensor("res_in", [TOKC, D], F32, kind="ExternalInput").ap()
    aps['res_out'] = nc.dram_tensor("res_out", [TOKC, D], F32, kind="ExternalOutput").ap()
    kb = KB(nc)
    kb.OFFSETS = _mk_offsets()
    emit_l1_attn(kb, aps)
    kb.wait('S', nc.sync, kb.tok('dV'))
    kb.close()
    return nc


def kernel_l1a_only(r2_full, p):
    r2 = np.asarray(r2_full).reshape(T, D)
    w = _prep_l1a_weights(p)
    nc = build_l1a_program()
    in_maps = []
    for c in range(NCORE):
        m = dict(w)
        m['res_in'] = np.ascontiguousarray(
            np.concatenate([r2[STR * c:STR * (c + 1)],
                            r2[STR * (c + 8):STR * (c + 9)]]))
        in_maps.append(m)
    res = run_bass_kernel_spmd(nc, in_maps, core_ids=list(range(NCORE)))
    out = np.empty((T, D), np.float32)
    for c in range(NCORE):
        r = np.asarray(res.results[c]["res_out"])
        out[STR * c:STR * (c + 1)] = r[:STR]
        out[STR * (c + 8):STR * (c + 9)] = r[STR:]
    return out.reshape(1, T, D)


def _prep_l1f_weights(i4):
    def qbytes(q):
        return np.ascontiguousarray(q).view(np.uint8).reshape(-1, 4)[:, 0]
    gb = qbytes(i4['gate_q']).reshape(11008, 4096)
    ub = qbytes(i4['up_q']).reshape(11008, 4096)
    db = qbytes(i4['down_q']).reshape(4096, 11008)
    gqT = np.ascontiguousarray(
        gb.T.reshape(32, 128, 86, 128).transpose(2, 1, 0, 3)).reshape(
            86 * 128, 4096).astype(BF)
    uqT = np.ascontiguousarray(
        ub.T.reshape(32, 128, 86, 128).transpose(2, 1, 0, 3)).reshape(
            86 * 128, 4096).astype(BF)
    dqT = np.ascontiguousarray(
        db.T.reshape(86, 128, 8, 512).transpose(2, 0, 1, 3)).reshape(
            8 * 86 * 128, 512).astype(BF)
    gBT = np.ascontiguousarray(
        i4['gate_B'].T.reshape(86, 128, 32).transpose(1, 0, 2)).reshape(128, 86 * 32)
    uBT = np.ascontiguousarray(
        i4['up_B'].T.reshape(86, 128, 32).transpose(1, 0, 2)).reshape(128, 86 * 32)
    B2T = np.ascontiguousarray(
        i4['down_B'].T.reshape(32, 128, 32).transpose(1, 0, 2)).reshape(128, 32 * 32)
    gA = np.ascontiguousarray(
        i4['gate_A'].reshape(32, 128, 32).transpose(1, 0, 2)).reshape(128, 1024)
    uA = np.ascontiguousarray(
        i4['up_A'].reshape(32, 128, 32).transpose(1, 0, 2)).reshape(128, 1024)
    A2 = np.ascontiguousarray(
        i4['down_A'].reshape(86, 128, 32).transpose(1, 0, 2)).reshape(128, 86 * 32)
    return {
        'gqT': gqT, 'uqT': uqT, 'dqT': dqT,
        'gBT': gBT.astype(np.float32), 'uBT': uBT.astype(np.float32),
        'B2T': B2T.astype(np.float32),
        'gA': gA.astype(np.float32), 'uA': uA.astype(np.float32),
        'A2': A2.astype(np.float32),
        'gscT': np.ascontiguousarray(i4['gate_scale'].reshape(86, 128).T).astype(np.float32),
        'uscT': np.ascontiguousarray(i4['up_scale'].reshape(86, 128).T).astype(np.float32),
        'dscT': np.ascontiguousarray(i4['down_scale'].reshape(32, 128).T).astype(np.float32),
        'dsrow': np.ascontiguousarray(i4['down_scale'].reshape(1, 4096)).astype(np.float32),
        'identb': np.eye(P, dtype=BF),
    }


def _l1f_aps(nc):
    d = {
        'gqT': nc.dram_tensor("gqT", [86 * P, 4096], BF16, kind="ExternalInput").ap(),
        'uqT': nc.dram_tensor("uqT", [86 * P, 4096], BF16, kind="ExternalInput").ap(),
        'dqT': nc.dram_tensor("dqT", [8 * 86 * P, 512], BF16, kind="ExternalInput").ap(),
        'gBT': nc.dram_tensor("gBT", [P, 86 * 32], F32, kind="ExternalInput").ap(),
        'uBT': nc.dram_tensor("uBT", [P, 86 * 32], F32, kind="ExternalInput").ap(),
        'B2T': nc.dram_tensor("B2T", [P, 32 * 32], F32, kind="ExternalInput").ap(),
        'gA': nc.dram_tensor("gA", [P, 1024], F32, kind="ExternalInput").ap(),
        'uA': nc.dram_tensor("uA", [P, 1024], F32, kind="ExternalInput").ap(),
        'A2': nc.dram_tensor("A2", [P, 86 * 32], F32, kind="ExternalInput").ap(),
        'gscT': nc.dram_tensor("gscT", [P, 86], F32, kind="ExternalInput").ap(),
        'uscT': nc.dram_tensor("uscT", [P, 86], F32, kind="ExternalInput").ap(),
        'dscT': nc.dram_tensor("dscT", [P, 32], F32, kind="ExternalInput").ap(),
        'dsrow': nc.dram_tensor("dsrow", [1, 4096], F32, kind="ExternalInput").ap(),
        'identb': _dram(nc, "identb", [P, P], BF16),
        'h_scratch': nc.dram_tensor("h_scratch", [86 * P, 1024], BF16,
                                    kind="Internal").ap(),
    }
    return d


def build_l1f_program():
    nc = bass.Bass("TRN2", target_bir_lowering=False)
    aps = _l1f_aps(nc)
    aps['res_in'] = nc.dram_tensor("res_in", [TOKC, D], F32, kind="ExternalInput").ap()
    aps['res_out'] = nc.dram_tensor("res_out", [TOKC, D], F32, kind="ExternalOutput").ap()
    kb = KB(nc)
    kb.OFFSETS = _mk_offsets()
    emit_l1_ffn(kb, aps)
    kb.close()
    return nc


def kernel_l1f_only(r3_full, i4):
    r3 = np.asarray(r3_full).reshape(T, D)
    w = _prep_l1f_weights(i4)
    nc = build_l1f_program()
    in_maps = []
    for c in range(NCORE):
        m = dict(w)
        m['res_in'] = np.ascontiguousarray(
            np.concatenate([r3[STR * c:STR * (c + 1)],
                            r3[STR * (c + 8):STR * (c + 9)]]))
        in_maps.append(m)
    res = run_bass_kernel_spmd(nc, in_maps, core_ids=list(range(NCORE)))
    out = np.empty((T, D), np.float32)
    for c in range(NCORE):
        r = np.asarray(res.results[c]["res_out"])
        out[STR * c:STR * (c + 1)] = r[:STR]
        out[STR * (c + 8):STR * (c + 9)] = r[STR:]
    return out.reshape(1, T, D)


def _core_slices(x):
    return [np.ascontiguousarray(np.concatenate(
        [x[STR * c:STR * (c + 1)], x[STR * (c + 8):STR * (c + 9)]]))
        for c in range(NCORE)]


def _gather(res, key="res_out"):
    out = np.empty((T, D), np.float32)
    for c in range(NCORE):
        r = np.asarray(res.results[c][key])
        out[STR * c:STR * (c + 1)] = r[:STR]
        out[STR * (c + 8):STR * (c + 9)] = r[STR:]
    return out


def _unused_multilaunch(L0_attn_x, L0_attn_bv_q, L0_attn_bv_k, L0_attn_bv_v,
           L0_ffn_q_packed, L0_ffn_q_scales, L0_ffn_q_zeros,
           L0_ffn_delta_weight, L0_ffn_bias,
           L1_attn_w_down, L1_attn_b_down, L1_attn_w_up, L1_attn_b_up,
           L1_attn_gate1, L1_attn_hy_in_w, L1_attn_hy_conv_w, L1_attn_hy_conv_b,
           L1_attn_hy_dt_bias, L1_attn_hy_A_log, L1_attn_hy_D, L1_attn_hy_fcD_w,
           L1_attn_hy_norm_w, L1_attn_hy_out_w,
           L1_ffn_gate_q, L1_ffn_gate_scale, L1_ffn_gate_A, L1_ffn_gate_B,
           L1_ffn_up_q, L1_ffn_up_scale, L1_ffn_up_A, L1_ffn_up_B,
           L1_ffn_down_q, L1_ffn_down_scale, L1_ffn_down_A, L1_ffn_down_B):
    x = np.ascontiguousarray(np.asarray(L0_attn_x).reshape(T, D))
    nc = build_l0a_program()
    in_maps = [_prep_l0a(c, x, np.asarray(L0_attn_bv_q),
                         np.asarray(L0_attn_bv_k), np.asarray(L0_attn_bv_v))
               for c in range(NCORE)]
    res = run_bass_kernel_spmd(nc, in_maps, core_ids=list(range(NCORE)))
    r1 = _gather(res)
    w = _prep_l0f_weights(np.asarray(L0_ffn_q_packed), np.asarray(L0_ffn_q_scales),
                          np.asarray(L0_ffn_q_zeros), np.asarray(L0_ffn_delta_weight),
                          np.asarray(L0_ffn_bias))
    nc = build_l0f_program()
    sl = _core_slices(r1)
    res = run_bass_kernel_spmd(nc, [dict(w, res_in=sl[c]) for c in range(NCORE)],
                               core_ids=list(range(NCORE)))
    r2 = _gather(res)
    p = {'w_down': np.asarray(L1_attn_w_down), 'b_down': np.asarray(L1_attn_b_down),
         'w_up': np.asarray(L1_attn_w_up), 'b_up': np.asarray(L1_attn_b_up),
         'gate1': np.asarray(L1_attn_gate1), 'hy_in_w': np.asarray(L1_attn_hy_in_w),
         'hy_conv_w': np.asarray(L1_attn_hy_conv_w),
         'hy_conv_b': np.asarray(L1_attn_hy_conv_b),
         'hy_D': np.asarray(L1_attn_hy_D), 'hy_fcD_w': np.asarray(L1_attn_hy_fcD_w),
         'hy_norm_w': np.asarray(L1_attn_hy_norm_w),
         'hy_out_w': np.asarray(L1_attn_hy_out_w)}
    w = _prep_l1a_weights(p)
    nc = build_l1a_program()
    sl = _core_slices(r2)
    res = run_bass_kernel_spmd(nc, [dict(w, res_in=sl[c]) for c in range(NCORE)],
                               core_ids=list(range(NCORE)))
    r3 = _gather(res)
    i4 = {'gate_q': np.asarray(L1_ffn_gate_q), 'gate_scale': np.asarray(L1_ffn_gate_scale),
          'gate_A': np.asarray(L1_ffn_gate_A), 'gate_B': np.asarray(L1_ffn_gate_B),
          'up_q': np.asarray(L1_ffn_up_q), 'up_scale': np.asarray(L1_ffn_up_scale),
          'up_A': np.asarray(L1_ffn_up_A), 'up_B': np.asarray(L1_ffn_up_B),
          'down_q': np.asarray(L1_ffn_down_q), 'down_scale': np.asarray(L1_ffn_down_scale),
          'down_A': np.asarray(L1_ffn_down_A), 'down_B': np.asarray(L1_ffn_down_B)}
    w = _prep_l1f_weights(i4)
    nc = build_l1f_program()
    sl = _core_slices(r3)
    res = run_bass_kernel_spmd(nc, [dict(w, res_in=sl[c]) for c in range(NCORE)],
                               core_ids=list(range(NCORE)))
    r4 = _gather(res)
    return r4.reshape(1, T, D)


_BASE = 16512
_BS = 194944 + _BASE


def _mk_offsets():
    o = {}
    o['xst0'] = 176128
    small = [('identb', 256), ('identf', 512), ('gdscr', 256), ('gdscrf', 256),
             ('ln_bst', 256), ('ln_mv', 64), ('ones1b', 256),
             ('l1a_onesb', 64), ('l1a_onesf', 512)]
    off = 192512
    for nm, sz in small:
        o[nm] = off
        off += sz
    for nm, a in (('acc', 0), ('kqQT', 65536), ('vbb', 65536), ('xnT', 98304),
                  ('kqb', 98304), ('xn_s', 131072), ('l0a_xep', 131072),
                  ('mask_sb', 172032), ('l0a_xb', 176128), ('attnT', 194944)):
        o[nm] = a
    o['gdscr2'] = 211328
    o['gdscr3'] = 211456
    for nm, a in (('l0f_res', 0), ('l0f_ystg', 32768), ('l0f_rc', 36864),
                  ('xnpT', 65536), ('l0f_wt', 131072),
                  ('l0f_sexp', 147456), ('l0f_delta', 163840),
                  ('l0f_xnp', 147456), ('l0f_qb', 172032)):
        o[nm] = a
    for nm, sz in (('l0f_bias', 128), ('indE', 2048), ('l0f_xsum', 2048),
                   ('l0f_zs', 8192)):
        o[nm] = o.get('_l0f_cur', 194944)
        o['_l0f_cur'] = o[nm] + sz
    o.pop('_l0f_cur')
    for nm, a in (('l1a_res', 0), ('l1a_xn1', 65536), ('l1a_x1T', 98304),
                  ('l1a_wdT', 131072), ('l1a_wupT', 139264)):
        o[nm] = a
    cur = 147456
    for nm, sz in (('l1a_inwT', 1024), ('l1a_outwT', 512), ('l1a_fcdT', 1024),
                   ('l1a_cwcb', 64), ('l1a_drep', 64), ('l1a_normw', 64),
                   ('l1a_bdown', 64), ('l1a_bupT', 128), ('l1a_g1', 64),
                   ('l1a_alf', 64), ('l1a_rr', 2048), ('l1a_zsil', 2048),
                   ('l1a_xs', 2048), ('l1a_y', 2048), ('l1a_yz', 2048),
                   ('l1a_yz2', 2048), ('l1a_y2', 2048), ('l1a_h0T', 1024),
                   ('l1a_h1T', 1024), ('l1a_h2T', 1024), ('l1a_hfin', 1024),
                   ('l1a_utile', 4096)):
        o[nm] = cur
        cur += sz
    for nm, a in (('l1f_hcol', 0), ('l1f_s2bc', 88064), ('l1f_uBp', 104448),
                  ('l1f_B2p', 126976), ('l1f_wd', 135168), ('l1f_wg', 151552),
                  ('l1f_wu', 159744), ('l1f_xn', 167936),
                  ('l1f_wgb', 176128), ('l1f_wub', 184320)):
        o[nm] = a
    cur = 194944
    for nm, sz in (('l1f_gsc', 384), ('l1f_usc', 384), ('l1f_dsc', 128),
                   ('l1f_gsc5', 384), ('l1f_usc5', 384), ('l1f_inv', 384),
                   ('l1f_xsrow', 2048), ('l1f_rsrow', 2048), ('l1f_m75', 1024),
                   ('l1f_t1g', 2048), ('l1f_t1u', 2048), ('l1f_t2', 2048),
                   ('l1f_gs', 1024), ('l1f_us', 1024)):
        o[nm] = cur
        cur += sz
    o['l1f_h'] = 167936 + 4096
    return {k: v + _BASE for k, v in o.items()}


def build_fused_program():
    nc = bass.Bass("TRN2", target_bir_lowering=False)
    aps0 = _l0a_aps(nc)
    aps1 = _l0f_aps(nc)
    aps2 = _l1a_aps(nc)
    aps3 = _l1f_aps(nc)
    res1 = nc.dram_tensor("res1", [TOKC, D], F32, kind="Internal").ap()
    res2 = nc.dram_tensor("res2", [TOKC, D], F32, kind="Internal").ap()
    res3 = nc.dram_tensor("res3", [TOKC, D], F32, kind="Internal").ap()
    out = nc.dram_tensor("res_out", [TOKC, D], F32, kind="ExternalOutput").ap()
    aps0['res_out'] = res1
    aps1['res_in'], aps1['res_out'] = res1, res2
    aps2['res_in'], aps2['res_out'] = res2, res3
    aps3['res_in'], aps3['res_out'] = res3, out
    kb = KB(nc)
    kb.OFFSETS = _mk_offsets()
    emit_l0_attn(kb, aps0)
    kb.block_barrier()
    emit_l0_ffn(kb, aps1)
    kb.block_barrier()
    emit_l1_attn(kb, aps2)
    kb.block_barrier()
    emit_l1_ffn(kb, aps3)
    kb.block_barrier()
    kb.close()
    return nc


def kernel_fused(inp):
    x = np.ascontiguousarray(np.asarray(inp['L0_attn_x']).reshape(T, D))
    _prep_l0a.xbf = np.ascontiguousarray(x.astype(BF))
    w0 = [_prep_l0a(c, x, np.asarray(inp['L0_attn_bv_q']),
                    np.asarray(inp['L0_attn_bv_k']),
                    np.asarray(inp['L0_attn_bv_v'])) for c in range(NCORE)]
    w1 = _prep_l0f_weights(np.asarray(inp['L0_ffn_q_packed']),
                           np.asarray(inp['L0_ffn_q_scales']),
                           np.asarray(inp['L0_ffn_q_zeros']),
                           np.asarray(inp['L0_ffn_delta_weight']),
                           np.asarray(inp['L0_ffn_bias']))
    p = {k[len('L1_attn_'):]: np.asarray(v) for k, v in inp.items()
         if k.startswith('L1_attn_')}
    w2 = _prep_l1a_weights(p)
    i4 = {k[len('L1_ffn_'):]: np.asarray(v) for k, v in inp.items()
          if k.startswith('L1_ffn_')}
    w3 = _prep_l1f_weights(i4)
    nc = build_fused_program()
    in_maps = []
    for c in range(NCORE):
        m = {}
        m.update(w0[c])
        m.update(w1)
        m.update(w2)
        m.update(w3)
        in_maps.append(m)
    res = run_bass_kernel_spmd(nc, in_maps, core_ids=list(range(NCORE)))
    return _gather(res).reshape(1, T, D)


def kernel(**inputs):
    return kernel_fused(inputs)
